# version 08
# speedup vs baseline: 1.6876x; 1.0053x over previous
import math
import numpy as np
import ml_dtypes
import concourse.bass as bass
import concourse.mybir as mybir
from concourse.bass_utils import run_bass_kernel_spmd

F32 = mybir.dt.float32
BF16 = mybir.dt.bfloat16
U32 = mybir.dt.uint32
AF = mybir.ActivationFunctionType
ALU = mybir.AluOpType
AX = mybir.AxisListType
bfdt = ml_dtypes.bfloat16

ENGINES = ("sync", "scalar", "vector", "tensor", "gpsimd")
T = 2048
D = 1024
NT = 16
NCH = 8


class Prog:
    def __init__(self, nc, same_engine_sync=True):
        self.nc = nc
        self.ops = []
        self.same_engine_sync = same_engine_sync

    def op(self, engine, fn, reads=(), writes=(), dma=False, tag=""):
        assert engine in ENGINES, engine
        self.ops.append(dict(engine=engine, fn=fn, reads=tuple(reads),
                             writes=tuple(writes), dma=bool(dma), tag=tag,
                             bar=self._bar_front, scope=self._scope))
        return len(self.ops) - 1

    _bar_front = None
    _scope = None

    def set_scope(self, name):
        self._scope = name

    def barrier(self):
        last = {}
        for i, o in enumerate(self.ops):
            last[o["engine"]] = i
        self._bar_front = tuple(sorted(last.values()))

    def dma(self, engine, out, in_, reads=(), writes=(), tag="dma", **kw):
        return self.op(engine, lambda e: e.dma_start(out, in_, **kw),
                       reads=reads, writes=writes, dma=True, tag=tag)

    def mm(self, out, lhsT, rhs, start=True, stop=True, reads=(), writes=(),
           tag="mm", **kw):
        return self.op("tensor",
                       lambda e: e.matmul(out, lhsT, rhs, start=start, stop=stop, **kw),
                       reads=reads, writes=writes, tag=tag)

    def trans(self, out, in_, ident, reads=(), writes=(), tag="tr", **kw):
        return self.op("tensor", lambda e: e.transpose(out, in_, ident, **kw),
                       reads=reads, writes=writes, tag=tag)

    def act(self, out, in_, func, bias=0.0, scale=1.0, engine="scalar",
            reads=(), writes=(), tag="act", **kw):
        return self.op(engine,
                       lambda e: e.activation(out, in_, func, bias=bias, scale=scale, **kw),
                       reads=reads, writes=writes, tag=tag)

    def tt(self, out, in0, in1, alu, engine="vector", reads=(), writes=(), tag="tt"):
        return self.op(engine, lambda e: e.tensor_tensor(out, in0, in1, alu),
                       reads=reads, writes=writes, tag=tag)

    def ts(self, out, in0, s1, alu, s2=None, alu2=None, engine="vector",
           reads=(), writes=(), tag="ts"):
        if s2 is None:
            return self.op(engine, lambda e: e.tensor_single_scalar(out, in0, s1, alu),
                           reads=reads, writes=writes, tag=tag)
        return self.op(engine, lambda e: e.tensor_scalar(out, in0, s1, s2, alu, alu2),
                       reads=reads, writes=writes, tag=tag)

    def stt(self, out, in0, scalar, in1, op0, op1, engine="vector",
            reads=(), writes=(), tag="stt"):
        return self.op(engine,
                       lambda e: e.scalar_tensor_tensor(out, in0, scalar, in1, op0, op1),
                       reads=reads, writes=writes, tag=tag)

    def scan(self, out, d0, d1, initial, op0, op1, engine="vector",
             reads=(), writes=(), tag="scan"):
        return self.op(engine,
                       lambda e: e.tensor_tensor_scan(out, d0, d1, initial, op0, op1),
                       reads=reads, writes=writes, tag=tag)

    def red(self, out, in_, alu, engine="vector", negate=False, reads=(), writes=(),
            tag="red"):
        return self.op(engine,
                       lambda e: e.tensor_reduce(out, in_, AX.X, alu, negate=negate),
                       reads=reads, writes=writes, tag=tag)

    def copy(self, out, in_, engine="vector", reads=(), writes=(), tag="cp"):
        return self.op(engine, lambda e: e.tensor_copy(out, in_),
                       reads=reads, writes=writes, tag=tag)

    def memset(self, ap, val, engine="vector", reads=(), writes=(), tag="memset"):
        return self.op(engine, lambda e: e.memset(ap, val),
                       reads=reads, writes=writes, tag=tag)

    def build(self):
        nc = self.nc
        n = len(self.ops)
        last_writer = {}
        readers_since = {}
        deps = [set() for _ in range(n)]
        for i, o in enumerate(self.ops):
            for k in o["reads"]:
                if k in last_writer:
                    deps[i].add(last_writer[k])
            for k in o["writes"]:
                if k in last_writer:
                    deps[i].add(last_writer[k])
                for r in readers_since.get(k, ()):
                    deps[i].add(r)
            for k in o["reads"]:
                readers_since.setdefault(k, []).append(i)
            for k in o["writes"]:
                last_writer[k] = i
                readers_since[k] = []
            if o.get("bar"):
                for q in o["bar"]:
                    deps[i].add(q)
            deps[i].discard(i)
        if not self.same_engine_sync:
            for i, o in enumerate(self.ops):
                if o["engine"] == "tensor" and not o["dma"]:
                    deps[i] = {q for q in deps[i]
                               if self.ops[q]["engine"] != "tensor"
                               or self.ops[q]["dma"]}
        needed = [False] * n
        for i in range(n):
            for q in deps[i]:
                needed[q] = True
        for i, o in enumerate(self.ops):
            if o["dma"]:
                needed[i] = True
        sems = {}
        for e in ENGINES:
            sems[(e, "c")] = nc.alloc_semaphore(name=f"sc_{e}")
            sems[(e, "d")] = nc.alloc_semaphore(name=f"sd_{e}")
        counts = [None] * n
        cum = {k: 0 for k in sems}
        for i, o in enumerate(self.ops):
            if needed[i]:
                k = (o["engine"], "d" if o["dma"] else "c")
                cum[k] += 16 if o["dma"] else 1
                counts[i] = (k, cum[k])
        waits = [dict() for _ in range(n)]
        eng_clock = {e: {} for e in ENGINES}
        op_clock = [None] * n
        for i, o in enumerate(self.ops):
            e = o["engine"]
            cl = eng_clock[e]
            for q in sorted(deps[i]):
                k, v = counts[q]
                if cl.get(k, 0) >= v:
                    continue
                waits[i][k] = max(waits[i].get(k, 0), v)
            for q in sorted(deps[i]):
                k, v = counts[q]
                if waits[i].get(k, 0) >= v and op_clock[q]:
                    for kk, vv in op_clock[q].items():
                        if cl.get(kk, 0) < vv:
                            cl[kk] = vv
            for k, v in waits[i].items():
                cl[k] = max(cl.get(k, 0), v)
            if counts[i] is not None:
                k, v = counts[i]
                snap = dict(cl)
                snap[k] = v
                op_clock[i] = snap
        final_counts = dict(cum)
        per_engine = {e: [] for e in ENGINES}
        for i, o in enumerate(self.ops):
            per_engine[o["engine"]].append(i)

        with nc.Block() as block:
            def mk(e_name):
                def body(eng):
                    for i in per_engine[e_name]:
                        o = self.ops[i]
                        for k, v in sorted(waits[i].items()):
                            eng.wait_ge(sems[k], v)
                        ins = o["fn"](eng)
                        if needed[i]:
                            k, v = counts[i]
                            ins.then_inc(sems[k], 16 if o["dma"] else 1)
                return body

            def gp_body(eng):
                mk("gpsimd")(eng)
                for k, v in final_counts.items():
                    if v > 0 and k != ("gpsimd", "c"):
                        eng.wait_ge(sems[k], v)
            block.sync(mk("sync"))
            block.scalar(mk("scalar"))
            block.vector(mk("vector"))
            block.tensor(mk("tensor"))
            block.gpsimd(gp_body)
        self._nwaits = sum(len(w) for w in waits)
        self._nincs = sum(1 for x in needed if x)
        return self


class Env:
    def __init__(self, nc):
        self.nc = nc
        self.p = Prog(nc, same_engine_sync=False)
        self.scope = None

    def sb(self, name, shape, dt=F32):
        if self.scope is not None:
            return self.scope.enter_context(self.nc.sbuf_tensor(name, shape, dt))
        return self.nc.alloc_sbuf_tensor(name, shape, dt)

    def psum(self, name, shape, dt=F32):
        return self.nc.alloc_psum_tensor(name, shape, dt)

    def const(self, name, arr):
        return self.nc.inline_tensor(np.ascontiguousarray(arr), name=name).ap()

    def load_const_sb(self, name, arr, engine="sync"):
        a = np.ascontiguousarray(arr)
        dt = {np.dtype(np.float32): F32, np.dtype(bfdt): BF16,
              np.dtype(np.uint32): U32}[a.dtype]
        t = self.sb(name, list(a.shape), dt)
        self.p.dma(engine, t[tuple(slice(None) for _ in a.shape)],
                   self.const(name + "_c", a), writes=[name], tag="ldc_" + name)
        return t

    def wload(self, dst_ap, src_ap, key, reads=(), engine="gpsimd"):
        self.p.dma(engine, dst_ap, src_ap, reads=reads, writes=[key],
                   tag="w_" + key)


def pk1(c0, c1):
    return ["PA%d" % b for b in range(c0 // 512, (c1 + 511) // 512)]


def pk2(c0, c1):
    return ["PB%d" % b for b in range(c0 // 512, (c1 + 511) // 512)]


def make_consts_l1(env):
    sel2 = np.zeros((2, 128), np.float32)
    sel2[0, 0:64] = 1.0
    sel2[1, 64:128] = 1.0
    env.pairsel2 = env.load_const_sb("pairsel2", sel2.astype(bfdt))
    ind = np.zeros((15, 15 * 128), np.float32)
    for e in range(15):
        ind[e, e * 128:(e + 1) * 128] = 1.0
    env.ind15 = env.load_const_sb("ind15", ind.astype(bfdt))
    env.ones512 = env.load_const_sb("ones512", np.ones((1, 512), bfdt))
    env.idf = env.load_const_sb("idf", np.eye(128, dtype=np.float32))


def make_consts(env):
    env.id128 = env.load_const_sb("id128", np.eye(128, dtype=bfdt))

    env.sumask = env.load_const_sb(
        "sumask", np.triu(np.ones((128, 128), np.float32), 1).astype(bfdt))
    env.causal_keep = env.load_const_sb(
        "causal_keep", np.triu(np.ones((128, 128), np.float32), 0).astype(bfdt))
    env.ones_row = env.load_const_sb("ones_row", np.ones((1, 128), bfdt))
    env.eps_ln = env.load_const_sb("eps_ln", np.full((128, 1), 1e-5, np.float32))
    env.eps8 = env.load_const_sb("eps8", np.full((128, 1), 1e-8, np.float32))
    env.ones_col128 = env.load_const_sb("ones_col128", np.ones((128, 1), bfdt))
    sel = np.zeros((128, 2), np.float32)
    sel[0:64, 0] = 1.0
    sel[64:128, 1] = 1.0
    env.pairselT = env.load_const_sb("pairselT", sel.astype(bfdt))

    env.scr15b = env.sb("scr15b", [128, 15], BF16)
    env.PS1 = env.psum("PS1", [128, 2048], F32)
    env.PS2 = env.psum("PS2", [128, 2048], F32)


def emit_ln_phase(env, x_dram, xnfm_key="xn_fm", xn_tok_dump=None):
    p, nc = env.p, env.nc
    with (
        nc.sbuf_tensor("g_xtok", [128, NT * D], F32) as x_tok,
        nc.sbuf_tensor("g_xntok", [128, NT * D], BF16) as xn_tok,
        nc.sbuf_tensor("g_lnst", [128, 5 * NT], F32) as st,
        nc.sbuf_tensor("g_lnsq", [128, D], F32) as sq,
    ):
        xv = x_dram.rearrange("(n p) d -> p n d", p=128)
        p.dma("sync", x_tok[:, :].rearrange("p (n d) -> p n d", n=NT), xv,
              writes=["x_tok"], tag="ld_x")
        s1, s2 = st[:, 0:NT], st[:, NT:2 * NT]
        rstd, nb = st[:, 2 * NT:3 * NT], st[:, 3 * NT:4 * NT]
        var = st[:, 4 * NT:5 * NT]
        for tt in range(NT):
            xt = x_tok[:, tt * D:(tt + 1) * D]
            p.op("scalar", lambda e, xt=xt, tt=tt: e.activation(
                    sq[:, :], xt, AF.Square, accum_out=s2[:, tt:tt + 1]),
                 reads=["x_tok"], writes=["ln_sq", "ln_s2"], tag=f"ln_sq{tt}")
            p.op("scalar", lambda e, xt=xt, tt=tt: e.activation(
                    sq[:, :], xt, AF.Copy, accum_out=s1[:, tt:tt + 1]),
                 reads=["x_tok", "ln_sq"], writes=["ln_sq", "ln_s1"],
                 tag=f"ln_s1{tt}")
        p.ts(s1, s1, 1.0 / D, ALU.mult, reads=["ln_s1"], writes=["ln_mean"],
             tag="ln_mean")
        p.tt(var, s1, s1, ALU.mult, reads=["ln_mean"], writes=["ln_m2"],
             tag="ln_m2")
        p.stt(var, s2, 1.0 / D, var, ALU.mult, ALU.subtract,
              reads=["ln_s2", "ln_m2"], writes=["ln_var"], tag="ln_var")
        p.act(rstd, var, AF.Sqrt, bias=env.eps_ln[:, 0:1],
              reads=["ln_var", "eps_ln"], writes=["ln_sd"], tag="ln_sd")
        p.op("vector", lambda e: e.reciprocal(rstd, rstd),
             reads=["ln_sd"], writes=["ln_rstd"], tag="ln_rstd")
        p.stt(nb, s1, -1.0, rstd, ALU.mult, ALU.mult,
              reads=["ln_mean", "ln_rstd"], writes=["ln_nb"], tag="ln_nb")
        trps = env.PS1.ap().bitcast(BF16)
        xnv = env.xn_fm[:, :].rearrange("p (c t) -> p c t", c=NCH)
        for tt in range(NT):
            p.op("scalar", lambda e, tt=tt: e.activation(
                    xn_tok[:, tt * D:(tt + 1) * D], x_tok[:, tt * D:(tt + 1) * D],
                    AF.Identity, bias=nb[:, tt:tt + 1], scale=rstd[:, tt:tt + 1]),
                 reads=["x_tok", "ln_rstd", "ln_nb"], writes=["xn_tok"],
                 tag=f"ln_ap{tt}")
            for c in range(NCH):
                p.trans(trps[:, c * 128:(c + 1) * 128],
                        xn_tok[:, tt * D + c * 128: tt * D + (c + 1) * 128],
                        env.id128[:, :],
                        reads=["xn_tok", "id128"], writes=pk1(0, 512), tag="ln_tr")
            p.copy(xnv[:, :, tt * 128:(tt + 1) * 128],
                   trps[:, 0:1024].rearrange("p (c t) -> p c t", c=NCH),
                   reads=pk1(0, 512), writes=[xnfm_key], tag=f"ln_trc{tt}")
            if xn_tok_dump is not None:
                p.dma("gpsimd",
                      xn_tok_dump.rearrange("(n p) d -> n p d", p=128)[tt],
                      xn_tok[:, tt * D:(tt + 1) * D],
                      reads=["xn_tok"], writes=["xnd_w"], tag="ln_xnd")


def emit_sg_residual(env, x_in_dram, x_out_dram, g_src, s_src, tag="sgr"):
    p, nc = env.p, env.nc
    xiv = x_in_dram.rearrange("(n p) d -> p n d", p=128)
    xov = x_out_dram.rearrange("(n p) d -> p n d", p=128)
    with (
        nc.sbuf_tensor("sg_buf", [128, NCH * 512], BF16) as sgb,
        nc.sbuf_tensor("sg_xt", [128, 2 * D], F32) as xt2,
    ):
        for tb in range(4):
            for dc in range(NCH):
                gap, gr = g_src(dc, tb)
                sap, sr = s_src(dc, tb)
                p.tt(sgb[:, dc * 512:(dc + 1) * 512], gap, sap, ALU.mult,
                     reads=list(gr) + list(sr), writes=["sg_buf"],
                     tag=tag + f"_m{tb}_{dc}")
            for j in range(4):
                tt = tb * 4 + j
                xt = xt2[:, (tt % 2) * D:(tt % 2) * D + D]
                p.dma("sync", xt, xiv[:, tt, :], reads=["sg_xt" + str(tt % 2)],
                      writes=["sg_x" + str(tt % 2)], tag=tag + "_ldx")
                trps = env.PS1.ap().bitcast(BF16)
                for c in range(NCH):
                    p.trans(trps[:, c * 128:(c + 1) * 128],
                            sgb[:, c * 512 + j * 128: c * 512 + (j + 1) * 128],
                            env.id128[:, :],
                            reads=["sg_buf", "id128"], writes=pk1(0, 512),
                            tag=tag + "_tr")
                p.tt(xt, xt, trps[:, 0:1024], ALU.add,
                     reads=pk1(0, 512) + ["sg_x" + str(tt % 2)],
                     writes=["sg_xt" + str(tt % 2)], tag=tag + f"_add{tt}")
                p.dma("gpsimd", xov[:, tt, :], xt,
                      reads=["sg_xt" + str(tt % 2)], writes=["sg_x" + str(tt % 2)],
                      tag="out_x")


def host_prep_l1(inp):
    w = {}
    wq = np.asarray(inp["L1_attn_w_q"])
    wk = np.asarray(inp["L1_attn_w_k"])
    wv = np.asarray(inp["L1_attn_w_v"])
    wo = np.asarray(inp["L1_attn_w_o"])
    w["l1_wq"] = np.ascontiguousarray(
        wq.transpose(1, 0, 2).reshape(D, D)).reshape(NCH, 128, D)
    w["l1_wk"] = np.ascontiguousarray(
        wk.transpose(1, 0, 2).reshape(D, D)).reshape(NCH, 128, D)
    w["l1_wv"] = np.ascontiguousarray(
        wv.transpose(1, 0, 2).reshape(D, D)).reshape(NCH, 128, D)
    w["l1_wo"] = np.ascontiguousarray(wo).reshape(NCH, 128, D)
    w["l1_routerW"] = np.ascontiguousarray(
        np.asarray(inp["L1_ffn_router_W"])).reshape(NCH, 128, 15)
    w["l1_routerb"] = np.ascontiguousarray(
        np.asarray(inp["L1_ffn_router_b"]).reshape(15, 1))
    W1all = np.concatenate([np.asarray(inp["L1_ffn_sW1"])[None],
                            np.asarray(inp["L1_ffn_eW1"])], 0)
    W2all = np.concatenate([np.asarray(inp["L1_ffn_sW2"])[None],
                            np.asarray(inp["L1_ffn_eW2"])], 0)
    b1all = np.concatenate([np.asarray(inp["L1_ffn_sb1"])[None],
                            np.asarray(inp["L1_ffn_eb1"])], 0)
    b2all = np.concatenate([np.asarray(inp["L1_ffn_sb2"])[None],
                            np.asarray(inp["L1_ffn_eb2"])], 0)
    w["l1_W1"] = np.ascontiguousarray(W1all.reshape(16, NCH, 128, D))
    w["l1_W2"] = np.ascontiguousarray(W2all.reshape(16, NCH, 128, D))
    w["l1_b1col"] = np.ascontiguousarray(
        b1all.reshape(16, NCH, 128).transpose(2, 0, 1).reshape(128, 16 * NCH))
    w["l1_eb2"] = np.ascontiguousarray(b2all[1:])
    w["l1_sb2row"] = np.ascontiguousarray(b2all[0:1])
    return w


L1_WSPECS = {
    "l1_wq": (NCH, 128, D), "l1_wk": (NCH, 128, D), "l1_wv": (NCH, 128, D),
    "l1_wo": (NCH, 128, D), "l1_routerW": (NCH, 128, 15), "l1_routerb": (15, 1),
    "l1_W1": (16, NCH, 128, D), "l1_W2": (16, NCH, 128, D),
    "l1_b1col": (128, 16 * NCH), "l1_b2": (16, D),
}


def emit_l1_attn(env, W, s_sink):
    p, nc = env.p, env.nc
    l1_bscr = nc.dram_tensor("l1_bscr", [2], F32, kind="Internal").ap()
    PS1, PS2 = env.PS1.ap(), env.PS2.ap()
    p.barrier()
    with (
        nc.sbuf_tensor("l1_y", [128, NCH * T], BF16) as y_fm,
        nc.sbuf_tensor("l1_sums", [16, T], F32) as sums_all,
        nc.sbuf_tensor("l1_r", [16, T], BF16) as r_bf,
        nc.sbuf_tensor("l1_srow", [128, 1024], F32) as srow,
        nc.sbuf_tensor("l1_rrep", [128, T], BF16) as rrep,
        nc.sbuf_tensor("l1_mcol", [2, 4], F32) as mcol,
        nc.sbuf_tensor("l1_qp", [128, T], BF16) as qp,
        nc.sbuf_tensor("l1_kp", [128, T], BF16) as kp,
        nc.sbuf_tensor("l1_vp", [128, NT * 130], BF16) as vp,
        nc.sbuf_tensor("l1_wp", [128, 3 * NCH * 128], BF16) as wpair,
        nc.sbuf_tensor("l1_pt", [128, 2048], BF16) as ptil,
        nc.sbuf_tensor("l1_sqt", [128, T], BF16) as sqt,

        nc.sbuf_tensor("l1_bh", [128, 2], F32) as bhneg,
        nc.sbuf_tensor("l1_br", [2, 2], F32) as brow2,
        nc.sbuf_tensor("l1_brb", [1, 2], BF16) as brow_bf,
        nc.sbuf_tensor("l1_wo_sb", [128, NCH * D], BF16) as wo_sb,
    ):
        env.wload(wo_sb[:, :].rearrange("p (c o) -> p c o", c=NCH),
                  W["l1_wo"].rearrange("c p o -> p c o"), "l1_wo")
        for hp in range(8):
            for i, nm in enumerate(("l1_wq", "l1_wk", "l1_wv")):
                env.wload(
                    wpair[:, i * 1024:(i + 1) * 1024].rearrange(
                        "p (c o) -> p c o", c=NCH),
                    W[nm][:, :, hp * 128:(hp + 1) * 128].rearrange(
                        "c p o -> p c o"),
                    "l1_wpair")
            for wi, (dst, dk) in enumerate(((qp, "l1_qp"), (kp, "l1_kp"))):
                for qc in range(4):
                    c0 = (qc % 2) * 512
                    ps = PS1[:, c0:c0 + 512]
                    for kc in range(NCH):
                        p.mm(ps,
                             wpair[:, wi * 1024 + kc * 128: wi * 1024 + (kc + 1) * 128],
                             env.xn_fm[:, kc * T + qc * 512: kc * T + qc * 512 + 512],
                             start=(kc == 0), stop=(kc == NCH - 1),
                             reads=["l1_wpair", "xn_fm"],
                             writes=pk1(c0, c0 + 512), tag=f"l1{dk}mm")
                    p.act(dst[:, qc * 512:(qc + 1) * 512], ps, AF.Copy,
                          reads=pk1(c0, c0 + 512), writes=[dk], tag=f"l1{dk}ev")
            for tt in range(NT):
                c0 = 1024 + (tt % 2) * 512
                ps = PS1[:, c0:c0 + 128]
                for kc in range(NCH):
                    p.mm(ps, env.xn_fm[:, kc * T + tt * 128: kc * T + (tt + 1) * 128],
                         wpair[:, 2048 + kc * 128: 2048 + (kc + 1) * 128],
                         start=(kc == 0), stop=(kc == NCH - 1),
                         reads=["l1_wpair", "xn_fm"], writes=pk1(c0, c0 + 128),
                         tag=f"l1vmm{tt}")
                p.copy(vp[:, tt * 130:(tt + 1) * 130].rearrange(
                           "p (h j) -> p h j", h=2)[:, :, 0:64],
                       ps.rearrange("p (h j) -> p h j", h=2),
                       reads=pk1(c0, c0 + 128), writes=["l1_vp"],
                       tag=f"l1vev{tt}")
            p.memset(vp[:, :].rearrange("p (n h j) -> p n h j", n=NT, h=2)
                     [:, :, :, 64:65],
                     1.0, reads=["l1_vp"], writes=["l1_vp"], tag="l1_vones")
            for o, (srcb, sk) in enumerate(((qp, "l1_qp"), (kp, "l1_kp"))):
                p.act(sqt[:, :], srcb[:, :], AF.Square, reads=[sk],
                      writes=["l1_sqt"], tag=f"l1sq{o}")
                for qc in range(4):
                    ps = PS2[0:2, 1536:2048]
                    p.mm(ps, env.pairselT[:, :], sqt[:, qc * 512:(qc + 1) * 512],
                         reads=["l1_sqt", "pairselT"], writes=pk2(1536, 2048),
                         tag=f"l1nrm{o}")
                    p.red(mcol[:, qc:qc + 1], ps, ALU.max,
                          reads=pk2(1536, 2048), writes=["l1_mc"],
                          tag=f"l1nrmr{o}")
                p.red(brow2[:, o:o + 1], mcol[:, 0:4], ALU.max,
                      reads=["l1_mc"], writes=["l1_brq" if o == 0 else "l1_brk"],
                      tag=f"l1b{o}")
            p.tt(brow2[:, 0:1], brow2[:, 0:1], brow2[:, 1:2], ALU.mult,
                 reads=["l1_brq", "l1_brk"], writes=["l1_bp"], tag="l1bp")
            p.act(brow2[:, 0:1], brow2[:, 0:1], AF.Sqrt, reads=["l1_bp"],
                  writes=["l1_bs"], tag="l1bs")
            p.ts(brow2[:, 0:1], brow2[:, 0:1], -0.125, ALU.mult,
                 reads=["l1_bs"], writes=["l1_bn"], tag="l1bn")
            p.dma("gpsimd", l1_bscr, brow2[:, 0:1],
                  reads=["l1_bn", "l1_bh"], writes=["l1_bscrd"],
                  tag="l1btr")
            p.dma("sync", bhneg[:, 0:2],
                  bass.AP(l1_bscr.tensor, 0, [[0, 128], [1, 2]]),
                  reads=["l1_bscrd", "l1_bh"], writes=["l1_bh"],
                  tag="l1brep")
            for h in range(2):
                hg = 2 * hp + h
                for qh in range(2):
                    pi = (2 * h + qh) % 2
                    a0 = pi * 1024
                    av = PS2[0:65, a0:a0 + 1024]
                    for kt in range(NT):
                        s0 = (kt % 2) * 1024
                        pth = ptil[:, s0:s0 + 1024]
                        sc = PS1[:, s0:s0 + 1024]
                        for q2 in range(2):
                            p.mm(sc[:, q2 * 512:(q2 + 1) * 512],
                                 kp[64 * h:64 * h + 64, kt * 128:(kt + 1) * 128],
                                 qp[64 * h:64 * h + 64,
                                    qh * 1024 + q2 * 512: qh * 1024 + q2 * 512 + 512],
                                 reads=["l1_qp", "l1_kp"],
                                 writes=pk1(s0 + q2 * 512, s0 + q2 * 512 + 512),
                                 tag=f"l1sc{hg}")
                        p.act(pth, sc, AF.Exp, scale=0.125,
                              bias=bhneg[:, h:h + 1],
                              reads=pk1(s0, s0 + 1024) + ["l1_bh",
                                                          f"l1_pt{kt % 2}"],
                              writes=[f"l1_pt{kt % 2}"], tag=f"l1exp{hg}")
                        for q2 in range(2):
                            p.mm(av[:, q2 * 512:(q2 + 1) * 512],
                                 vp[:, kt * 130 + 65 * h: kt * 130 + 65 * h + 65],
                                 pth[:, q2 * 512:(q2 + 1) * 512],
                                 start=(kt == 0), stop=(kt == NT - 1),
                                 reads=[f"l1_pt{kt % 2}", "l1_vp"],
                                 writes=pk2(a0 + q2 * 512, a0 + q2 * 512 + 512),
                                 tag=f"l1av{hg}")
                    p.act(srow[64:65, :], av[64:65, :], AF.Copy,
                          reads=pk2(a0, a0 + 1024) + ["l1_srow"],
                          writes=["l1_srow"], tag=f"l1sumst{hg}")
                    p.dma("gpsimd",
                          sums_all[hg:hg + 1, qh * 1024: qh * 1024 + 1024],
                          srow[64:65, :], reads=["l1_srow"],
                          writes=["l1_sums", "l1_srow"], tag=f"l1sum{hg}")
                    p.act(y_fm[64 * h:64 * h + 64,
                               hp * T + qh * 1024: hp * T + qh * 1024 + 1024],
                          av[0:64, :], AF.Copy, reads=pk2(a0, a0 + 1024),
                          writes=["l1_yraw"], tag=f"l1yev{hg}")
        p.act(sums_all[:, :], sums_all[:, :], AF.Ln, reads=["l1_sums"],
              writes=["l1_lns"], tag="l1_lns")
        p.act(r_bf[:, :], sums_all[:, :], AF.Exp, scale=-1.0,
              reads=["l1_lns"], writes=["l1_r"], tag="l1_rinv")
        p.dma("gpsimd", env.rscr, r_bf[:, :], reads=["l1_r"],
              writes=["l1_rscr"], tag="l1rst")
        for c in range(NCH):
            for h in range(2):
                srcap = bass.AP(env.rscr.tensor, (2 * c + h) * T,
                                [[0, 64], [1, T]])
                p.dma("sync", rrep[64 * h:64 * h + 64, :], srcap,
                      reads=["l1_rscr", "l1_rrep"], writes=["l1_rrep"],
                      tag="l1rbc")
            p.tt(y_fm[:, c * T:(c + 1) * T], y_fm[:, c * T:(c + 1) * T],
                 rrep[:, :], ALU.mult,
                 reads=["l1_rrep", "l1_yraw"], writes=["l1_y"],
                 tag=f"l1ynrm{c}")
        for dc in range(NCH):
            for qc in range(4):
                c0 = (qc % 2) * 512
                ps = PS2[:, c0:c0 + 512]
                for kc in range(NCH):
                    p.mm(ps, wo_sb[:, kc * D + dc * 128: kc * D + (dc + 1) * 128],
                         y_fm[:, kc * T + qc * 512: kc * T + qc * 512 + 512],
                         start=(kc == 0), stop=(kc == NCH - 1),
                         reads=["l1_wo", "l1_y"], writes=pk2(c0, c0 + 512),
                         tag="l1omm")
                s_sink(dc, qc, ps, pk2(c0, c0 + 512))
    p.barrier()


def emit_l1_ffn(env, W, g_sink):
    p, nc = env.p, env.nc
    PS1, PS2 = env.PS1.ap(), env.PS2.ap()
    p.barrier()
    with (
        nc.sbuf_tensor("l1_racc", [128, NCH * T], F32) as racc,
        nc.sbuf_tensor("l1_cw", [15, T], BF16) as cw_fm,
        nc.sbuf_tensor("l1_cwr", [128, 512], BF16) as cwrep,
        nc.sbuf_tensor("l1_hsb", [128, NCH * 512], BF16) as h_sb,
        nc.sbuf_tensor("l1_eW", [128, 2 * NCH * D], BF16) as eW,
        nc.sbuf_tensor("l1_b1c_sb", [128, 16 * NCH], F32) as b1c,
        nc.sbuf_tensor("l1_eb2_sb", [15, D], BF16) as eb2_sb,
        nc.sbuf_tensor("l1_sb2_sb", [1, D], BF16) as sb2_sb,
        nc.sbuf_tensor("l1_rt", [128, 65], F32) as rt,
        nc.sbuf_tensor("l1_s15f", [15, 128], F32) as scr15f,
        nc.sbuf_tensor("l1_rtb", [15, 1], F32) as rb_sb,
        nc.sbuf_tensor("l1_rW", [128, NCH * 16], BF16) as rW_sb,
    ):
        env.wload(b1c[:, :], W["l1_b1col"], "l1_b1c")
        env.wload(eb2_sb[:, :], W["l1_eb2"], "l1_eb2")
        env.wload(sb2_sb[:, :], W["l1_sb2row"], "l1_sb2")
        env.wload(rW_sb[:, :].rearrange("p (c o) -> p c o", c=NCH)[:, :, 0:15],
                  W["l1_routerW"].rearrange("c p o -> p c o"), "l1_rW")
        p.dma("sync", rb_sb[:, :], W["l1_routerb"], writes=["l1_rb"], tag="l1rbld")
        for qc in range(4):
            ps = PS1[0:15, 0:512]
            for kc in range(NCH):
                p.mm(ps, rW_sb[:, kc * 16: kc * 16 + 15],
                     env.xn_fm[:, kc * T + qc * 512: kc * T + qc * 512 + 512],
                     start=(kc == 0), stop=(kc == NCH - 1),
                     reads=["l1_rW", "xn_fm"], writes=pk1(0, 512), tag="l1rlmm")
            for j in range(4):
                tt = qc * 4 + j
                p.act(scr15f[0:15, 0:128], ps[0:15, j * 128:(j + 1) * 128],
                      AF.Identity, bias=rb_sb[:, 0:1],
                      reads=pk1(0, 512) + ["l1_rb"], writes=["scr15f"],
                      tag="l1rl15")
                trp = PS2[:, 1536:1664]
                p.trans(trp[0:128, 0:15], scr15f[0:15, 0:128],
                        env.idf[0:15, 0:15],
                        reads=["scr15f", "idf"], writes=pk2(1536, 2048),
                        tag="l1rtr")
                p.copy(rt[:, 0:15], trp[0:128, 0:15],
                       reads=pk2(1536, 2048), writes=["l1_rt"], tag="l1rtc")
                p.red(rt[:, 16:17], rt[:, 0:15], ALU.max, negate=True,
                      reads=["l1_rt"], writes=["l1_rtm"], tag="l1sm_m")
                p.op("scalar", lambda e: e.activation(
                        rt[:, 0:15], rt[:, 0:15], AF.Exp,
                        bias=rt[:, 16:17], accum_out=rt[:, 17:18]),
                     reads=["l1_rt", "l1_rtm"], writes=["l1_rt", "l1_rts"],
                     tag="l1sm_e")
                p.op("vector", lambda e: e.reciprocal(rt[:, 18:19], rt[:, 17:18]),
                     reads=["l1_rts"], writes=["l1_rtr"], tag="l1sm_r")
                p.ts(rt[:, 0:15], rt[:, 0:15], rt[:, 18:19], ALU.mult,
                     reads=["l1_rt", "l1_rtr"], writes=["l1_rt"], tag="l1sm_p")
                p.red(rt[:, 19:20], rt[:, 0:15], ALU.max, reads=["l1_rt"],
                      writes=["l1_m1"], tag="l1t_m1")
                p.ts(rt[:, 20:35], rt[:, 0:15], rt[:, 19:20], ALU.is_ge,
                     reads=["l1_rt", "l1_m1"], writes=["l1_e1"], tag="l1t_e1")
                p.tt(rt[:, 35:50], rt[:, 0:15], rt[:, 20:35], ALU.subtract,
                     reads=["l1_rt", "l1_e1"], writes=["l1_z1"], tag="l1t_z1")
                p.red(rt[:, 19:20], rt[:, 35:50], ALU.max, reads=["l1_z1"],
                      writes=["l1_m2"], tag="l1t_m2")
                p.ts(rt[:, 50:65], rt[:, 35:50], rt[:, 19:20], ALU.is_ge,
                     reads=["l1_z1", "l1_m2"], writes=["l1_e2"], tag="l1t_e2")
                p.tt(rt[:, 35:50], rt[:, 35:50], rt[:, 50:65], ALU.subtract,
                     reads=["l1_z1", "l1_e2"], writes=["l1_z2"], tag="l1t_z2")
                p.red(rt[:, 19:20], rt[:, 35:50], ALU.max, reads=["l1_z2"],
                      writes=["l1_m3"], tag="l1t_m3")
                p.ts(rt[:, 35:50], rt[:, 35:50], rt[:, 19:20], ALU.is_ge,
                     reads=["l1_z2", "l1_m3"], writes=["l1_e3"], tag="l1t_e3")
                p.tt(rt[:, 50:65], rt[:, 50:65], rt[:, 35:50], ALU.add,
                     reads=["l1_e2", "l1_e3"], writes=["l1_e23"], tag="l1t_e23")
                p.tt(rt[:, 50:65], rt[:, 50:65], rt[:, 20:35], ALU.add,
                     reads=["l1_e23", "l1_e1"], writes=["l1_e123"], tag="l1t_s")
                p.tt(rt[:, 50:65], rt[:, 50:65], rt[:, 0:15], ALU.mult,
                     reads=["l1_e123", "l1_rt"], writes=["l1_cwt"], tag="l1t_cw")
                p.copy(env.scr15b[0:128, 0:15], rt[:, 50:65],
                       reads=["l1_cwt"], writes=["scr15b"], tag="l1cwb")
                p.trans(PS2.bitcast(BF16)[0:15, 3072:3200],
                        env.scr15b[0:128, 0:15], env.id128[:, :],
                        reads=["scr15b", "id128"], writes=pk2(1536, 2048),
                        tag="l1cwtr")
                p.copy(cw_fm[0:15, tt * 128:(tt + 1) * 128],
                       PS2.bitcast(BF16)[0:15, 3072:3200],
                       reads=pk2(1536, 2048), writes=["l1_cw"], tag="l1cwc")
        for e in range(16):
            env.wload(eW[:, 0:NCH * D].rearrange("p (c o) -> p c o", c=NCH),
                      W["l1_W1"][e].rearrange("c p o -> p c o"), "l1_eW")
            env.wload(eW[:, NCH * D:2 * NCH * D].rearrange("p (c o) -> p c o", c=NCH),
                      W["l1_W2"][e].rearrange("c p o -> p c o"), "l1_eW")
            for tb in range(4):
                if e > 0:
                    p.mm(PS2[:, 1536:2048],
                         env.ind15[:, (e - 1) * 128: e * 128],
                         cw_fm[0:15, tb * 512:(tb + 1) * 512],
                         reads=["l1_cw", "ind15"], writes=pk2(1536, 2048),
                         tag="l1cwrep")
                    p.copy(cwrep[:, :], PS2[:, 1536:2048],
                           reads=pk2(1536, 2048), writes=["l1_cwr"],
                           tag="l1cwrc")
                for hc in range(NCH):
                    c0 = (hc % 2) * 512
                    ps = PS1[:, c0:c0 + 512]
                    for kc in range(NCH):
                        p.mm(ps, eW[:, kc * D + hc * 128: kc * D + (hc + 1) * 128],
                             env.xn_fm[:, kc * T + tb * 512: kc * T + tb * 512 + 512],
                             start=(kc == 0), stop=(kc == NCH - 1),
                             reads=["l1_eW", "xn_fm"], writes=pk1(c0, c0 + 512),
                             tag="l1hmm")
                    p.act(h_sb[:, hc * 512:(hc + 1) * 512], ps, AF.Relu,
                          bias=b1c[:, e * NCH + hc: e * NCH + hc + 1],
                          reads=pk1(c0, c0 + 512) + ["l1_b1c"], writes=["l1_hs"],
                          tag="l1hev")
                    if e > 0:
                        p.tt(h_sb[:, hc * 512:(hc + 1) * 512],
                             h_sb[:, hc * 512:(hc + 1) * 512], cwrep[:, :],
                             ALU.mult, reads=["l1_hs", "l1_cwr"],
                             writes=["l1_h"], tag="l1hm")
                    else:
                        p.copy(h_sb[:, hc * 512:(hc + 1) * 512],
                               h_sb[:, hc * 512:(hc + 1) * 512],
                               reads=["l1_hs"], writes=["l1_h"], tag="l1hm0")
                for dc in range(NCH):
                    c0 = 1024 + (dc % 2) * 512
                    ps = PS1[:, c0:c0 + 512]
                    for hc in range(NCH):
                        p.mm(ps, eW[:, NCH * D + hc * D + dc * 128:
                                     NCH * D + hc * D + (dc + 1) * 128],
                             h_sb[:, hc * 512:(hc + 1) * 512],
                             start=(hc == 0), stop=(hc == NCH - 1),
                             reads=["l1_eW", "l1_h"], writes=pk1(c0, c0 + 512),
                             tag="l1ymm")
                    dst = racc[:, dc * T + tb * 512: dc * T + tb * 512 + 512]
                    if e == 0:
                        p.copy(dst, ps, reads=pk1(c0, c0 + 512),
                               writes=["l1_racc"], tag="l1rc0")
                    else:
                        p.tt(dst, dst, ps, ALU.add,
                             reads=pk1(c0, c0 + 512) + ["l1_racc"],
                             writes=["l1_racc"], tag="l1rca")
        for dc in range(NCH):
            for qc in range(4):
                c0 = (qc % 2) * 512
                ps = PS1[:, c0:c0 + 512]
                p.mm(ps, eb2_sb[:, dc * 128:(dc + 1) * 128],
                     cw_fm[0:15, qc * 512:(qc + 1) * 512],
                     start=True, stop=False,
                     reads=["l1_eb2", "l1_cw"], writes=pk1(c0, c0 + 512),
                     tag="l1b2mm")
                p.mm(ps, sb2_sb[:, dc * 128:(dc + 1) * 128],
                     env.ones512[:, :], start=False, stop=True,
                     reads=["l1_sb2", "ones512"], writes=pk1(c0, c0 + 512),
                     tag="l1sb2mm")
                dst = racc[:, dc * T + qc * 512: dc * T + qc * 512 + 512]
                p.tt(dst, dst, ps, ALU.add,
                     reads=pk1(c0, c0 + 512) + ["l1_racc"], writes=["l1_racc"],
                     tag="l1b2a")
        if hasattr(env, "dbg_hook_ffn"):
            p.dma("gpsimd", env.dbg_hook_ffn, racc[:, :], reads=["l1_racc"],
                  tag="dracc")
            p.dma("gpsimd", env.dbg_cw, cw_fm[:, :], reads=["l1_cw"], tag="dcw")
        for tb in range(4):
            for dc in range(NCH):
                g_sink(dc, tb, racc[:, dc * T + tb * 512: dc * T + tb * 512 + 512],
                       ["l1_racc"])
    p.barrier()
def make_residual_streamer(env, x_in_dram, x_out_dram, s_scr, blk=512,
                           trps_region=None):
    p, nc = env.p, env.nc
    sgb = env.sb("rs_sgb", [128, NCH * blk], BF16)
    sld = env.sb("rs_sld", [128, 2 * blk], BF16)
    gtmp = env.sb("rs_gt", [128, 2 * blk], BF16)
    xt2 = env.sb("rs_xt", [128, D], F32)
    xiv = x_in_dram.rearrange("(n p) d -> p n d", p=128)
    xov = x_out_dram.rearrange("(n p) d -> p n d", p=128)
    ntile = blk // 128

    def g_sink(dc, tb, ap, rkeys):
        par = dc % 2
        sl = sld[:, par * blk:(par + 1) * blk]
        gt = gtmp[:, par * blk:(par + 1) * blk]
        p.dma("sync", sl, s_scr[dc, :, tb * blk:(tb + 1) * blk],
              reads=[f"rs_sld{par}"], writes=[f"rs_sld{par}", "s_scr_r"],
              tag="rs_lds")
        p.act(gt, ap, AF.Silu, reads=list(rkeys), writes=[f"rs_gt{par}"],
              tag="rs_silu")
        p.tt(sgb[:, dc * blk:(dc + 1) * blk], gt, sl, ALU.mult,
             reads=[f"rs_gt{par}", f"rs_sld{par}"], writes=["rs_sgb"],
             tag="rs_sg")
        if dc == NCH - 1:
            reg = trps_region if trps_region is not None else (0, 512)
            trps = env.PS1.ap().bitcast(BF16)[:, reg[0] * 2:reg[0] * 2 + 1024]
            for j in range(ntile):
                tt = tb * ntile + j
                xt = xt2[:, :]
                p.dma("sync", xt, xiv[:, tt, :], reads=["rs_x0"],
                      writes=["rs_x0"], tag="rs_ldx")
                for c in range(NCH):
                    p.trans(trps[:, c * 128:(c + 1) * 128],
                            sgb[:, c * blk + j * 128: c * blk + (j + 1) * 128],
                            env.id128[:, :],
                            reads=["rs_sgb", "id128"],
                            writes=pk1(reg[0], reg[1]), tag="rs_tr")
                p.tt(xt, xt, trps[:, 0:1024], ALU.add,
                     reads=pk1(reg[0], reg[1]) + ["rs_x0"],
                     writes=["rs_x0"], tag="rs_add")
                p.dma("gpsimd", xov[:, tt, :], xt,
                      reads=["rs_x0"], writes=["rs_x0"], tag="out_x")
    return g_sink


def make_s_sink(env, s_scr, mode="add_xn"):
    p = env.p
    sst = env.sb("ss_st", [128, 512], BF16)

    def s_sink(dc, qc, ap, rkeys):
        st = sst[:, 0:512]
        if mode == "add_xn":
            p.tt(st, ap, env.xn_fm[:, dc * T + qc * 512: dc * T + qc * 512 + 512],
                 ALU.add, reads=list(rkeys) + ["xn_fm", "ss_st"],
                 writes=["ss_st"], tag="ss_add")
        else:
            p.copy(st, ap, reads=list(rkeys) + ["ss_st"],
                   writes=["ss_st"], tag="ss_cp")
        p.dma("gpsimd", s_scr[dc, :, qc * 512:(qc + 1) * 512], st,
              reads=["ss_st"], writes=["ss_st", "s_scr_w"], tag="ss_out")
    return s_sink


DEBUG_DUMP = False


def build_layer1(wshapes):
    nc = bass.Bass("TRN2", target_bir_lowering=False)
    kind_dbg = "ExternalOutput" if DEBUG_DUMP else "Internal"
    x_in = nc.dram_tensor("x_in", [T, D], F32, kind="ExternalInput").ap()
    x_out = nc.dram_tensor("x_out", [T, D], F32, kind="ExternalOutput").ap()
    s_scr = nc.dram_tensor("s_scr", [NCH, 128, T], BF16, kind=kind_dbg).ap()
    rscr = nc.dram_tensor("rscr", [16, T], BF16, kind=kind_dbg).ap()
    W = {k: nc.dram_tensor(k, list(v), F32, kind="ExternalInput").ap()
         for k, v in wshapes.items()}
    env = Env(nc)
    make_consts(env)
    make_consts_l1(env)
    env.scr15 = env.sb("scr15", [15, 128], BF16)
    env.rscr = rscr
    env.xn_fm = env.sb("xn_fm", [128, NCH * T], BF16)
    emit_ln_phase(env, x_in)
    if DEBUG_DUMP:
        dxn = nc.dram_tensor("dbg_xn", [128, NCH * T], BF16,
                             kind="ExternalOutput").ap()
        env.p.dma("gpsimd", dxn, env.xn_fm[:, :], reads=["xn_fm"], tag="dxn")
    emit_l1_attn(env, W, make_s_sink(env, s_scr, mode="add_xn"))
    if DEBUG_DUMP:
        env.dbg_hook_ffn = nc.dram_tensor("dbg_racc", [128, NCH * T], F32,
                                          kind="ExternalOutput").ap()
        env.dbg_cw = nc.dram_tensor("dbg_cw", [15, T], BF16,
                                    kind="ExternalOutput").ap()
    g_sink = make_residual_streamer(env, x_in, x_out, s_scr)
    emit_l1_ffn(env, W, g_sink)
    env.p.build()
    return nc


LAYER_BUILDERS = {1: (host_prep_l1, build_layer1)}
_BUILD_CACHE = {}


def run_layer(li, x_np, inputs, cores=8):
    prep, builder = LAYER_BUILDERS[li]
    w = prep(inputs)
    wshapes = {k: v.shape for k, v in w.items()}
    key = (li, tuple(sorted(wshapes.items())))
    if key not in _BUILD_CACHE:
        _BUILD_CACHE[key] = builder(wshapes)
    nc = _BUILD_CACHE[key]
    in_maps = []
    for c in range(cores):
        m = {"x_in": np.ascontiguousarray(x_np[c], dtype=np.float32)}
        for k, v in w.items():
            m[k] = np.ascontiguousarray(v, dtype=np.float32)
        in_maps.append(m)
    res = run_bass_kernel_spmd(nc, in_maps, core_ids=list(range(cores)))
    return np.stack([r["x_out"] for r in res.results])


def kernel(**inputs):
    x = np.ascontiguousarray(np.asarray(inputs["L0_attn_x"], dtype=np.float32))
    B = x.shape[0]
    for li in (0, 1, 2, 3):
        if li in LAYER_BUILDERS:
            x = run_layer(li, x, inputs, cores=B)
    return x
def host_prep_l3(inp):
    w = {}
    Wqkv = np.asarray(inp["L3_attn_W_qkv"])
    H, DK = 16, 64
    wq = Wqkv[:, 0:D].reshape(D, H, DK)
    wk = Wqkv[:, D:2 * D].reshape(D, H, DK)
    wv = Wqkv[:, 2 * D:3 * D].reshape(D, H, DK)
    perm = np.concatenate([np.arange(0, DK, 2), np.arange(1, DK, 2)])
    wq = wq[:, :, perm].reshape(D, D)
    wk = wk[:, :, perm].reshape(D, D)
    wv = wv.reshape(D, D)
    w["l3_wq"] = np.ascontiguousarray(wq).reshape(NCH, 128, D)
    w["l3_wk"] = np.ascontiguousarray(wk).reshape(NCH, 128, D)
    w["l3_wv"] = np.ascontiguousarray(wv).reshape(NCH, 128, D)
    w["l3_wo"] = np.ascontiguousarray(np.asarray(inp["L3_attn_W_out"])).reshape(NCH, 128, D)
    def bitrev(j, nb):
        r = 0
        for i in range(nb):
            r = (r << 1) | ((j >> i) & 1)
        return r
    leafperm = [bitrev(j, 4) for j in range(16)]
    lvl_rows = {d: [(2 ** d - 1) + bitrev(j, d) for j in range(2 ** d)]
                for d in range(4)}
    for f, pfx in enumerate("qkv"):
        nw = np.asarray(inp[f"L3_ffn_{pfx}_node_w"])
        nb = np.asarray(inp[f"L3_ffn_{pfx}_node_b"])
        for d in range(4):
            rows = lvl_rows[d]
            w[f"l3_nw{f}{d}"] = np.ascontiguousarray(
                nw[rows].T.reshape(NCH, 128, 2 ** d))
            w[f"l3_nb{f}{d}"] = np.ascontiguousarray(
                nb[rows].reshape(2 ** d, 1))
        w1 = np.asarray(inp[f"L3_ffn_{pfx}_w1"])[leafperm]
        b1 = np.asarray(inp[f"L3_ffn_{pfx}_b1"])[leafperm]
        w2 = np.asarray(inp[f"L3_ffn_{pfx}_w2"])[leafperm]
        b2 = np.asarray(inp[f"L3_ffn_{pfx}_b2"])[leafperm]
        w[f"l3_w1{f}"] = np.ascontiguousarray(
            w1.reshape(16, NCH, 128, 64).transpose(1, 2, 0, 3).reshape(NCH, 128, 16 * 64))
        w[f"l3_b1{f}"] = np.ascontiguousarray(b1.T)
        w[f"l3_w2{f}"] = np.ascontiguousarray(w2)
        w[f"l3_b2{f}"] = np.ascontiguousarray(b2)
    return w


def l3_make_consts(env):
    pos = np.arange(T)
    i32 = np.arange(32)
    freqs = 1.0 / (10000.0 ** (2 * i32 / 64.0))
    ang = pos[None, :] * freqs[:, None]
    cos32, sin32 = np.cos(ang), np.sin(ang)
    C = np.concatenate([cos32, cos32, cos32, cos32], 0)
    S = np.concatenate([-sin32, sin32, -sin32, sin32], 0)
    env.l3_cos = env.load_const_sb("l3_cos", C.astype(bfdt))
    env.l3_sin = env.load_const_sb("l3_sin", S.astype(bfdt))

    for d in range(4):
        ge = np.zeros((2 ** d, 16), np.float32)
        lt = np.zeros((2 ** d, 16), np.float32)
        for j in range(16):
            pre = j % (2 ** d)
            if (j >> d) & 1:
                ge[pre, j] = 1.0
            else:
                lt[pre, j] = 1.0
        env.__dict__[f"l3_selge{d}"] = env.load_const_sb(f"l3_selge{d}", ge.astype(bfdt))
        env.__dict__[f"l3_sellt{d}"] = env.load_const_sb(f"l3_sellt{d}", lt.astype(bfdt))
    il = np.zeros((16, 16 * 64), np.float32)
    for l in range(16):
        il[l, l * 64:(l + 1) * 64] = 1.0
    env.l3_indl = env.load_const_sb("l3_indl", il.astype(bfdt))


def emit_l3_attn(env, W, s_sink):
    p, nc = env.p, env.nc
    PS1, PS2 = env.PS1.ap(), env.PS2.ap()
    l3_bscr = nc.dram_tensor("l3_bscr", [2], F32, kind="Internal").ap()
    p.barrier()
    from contextlib import ExitStack as _ES
    with _ES() as _es:
        _e = _es.enter_context
        y_fm = _e(nc.sbuf_tensor("l3_y", [128, NCH * T], BF16))
        sums_all = _e(nc.sbuf_tensor("l3_sums", [16, T], F32))
        r_bf = _e(nc.sbuf_tensor("l3_r", [16, T], BF16))
        srow = _e(nc.sbuf_tensor("l3_srow", [128, T], F32))
        rrep = _e(nc.sbuf_tensor("l3_rrep", [128, T], BF16))
        mcol = _e(nc.sbuf_tensor("l3_mcol", [2, 4], F32))
        qp = _e(nc.sbuf_tensor("l3_qp", [128, T], BF16))
        kp = _e(nc.sbuf_tensor("l3_kp", [128, T], BF16))
        vp = _e(nc.sbuf_tensor("l3_vp", [128, NT * 130], BF16))
        wpair = _e(nc.sbuf_tensor("l3_wp", [128, 3 * NCH * 128], BF16))
        ptil = _e(nc.sbuf_tensor("l3_pt", [128, 2 * 2048], BF16))
        swp = _e(nc.sbuf_tensor("l3_sw", [128, T], BF16))
        utmp = _e(nc.sbuf_tensor("l3_u", [128, T], BF16))
        sqt = _e(nc.sbuf_tensor("l3_sqt", [128, T], BF16))
        bhneg = _e(nc.sbuf_tensor("l3_bh", [128, 2], F32))
        brow2 = _e(nc.sbuf_tensor("l3_br", [2, 2], F32))
        brow_bf = _e(nc.sbuf_tensor("l3_brb", [1, 2], BF16))
        wo_sb = _e(nc.sbuf_tensor("l3_wo_sb", [128, NCH * D], BF16))
        env.wload(wo_sb[:, :].rearrange("p (c o) -> p c o", c=NCH),
                  W["l3_wo"].rearrange("c p o -> p c o"), "l3_wo")
        for hp in range(8):
            for i, nm in enumerate(("l3_wq", "l3_wk", "l3_wv")):
                env.wload(
                    wpair[:, i * 1024:(i + 1) * 1024].rearrange(
                        "p (c o) -> p c o", c=NCH),
                    W[nm][:, :, hp * 128:(hp + 1) * 128].rearrange(
                        "c p o -> p c o"),
                    "l3_wpair")
            for wi, (dst, dk) in enumerate(((qp, "l3_qp"), (kp, "l3_kp"))):
                for qc in range(4):
                    c0 = (qc % 2) * 512
                    ps = PS1[:, c0:c0 + 512]
                    for kc in range(NCH):
                        p.mm(ps,
                             wpair[:, wi * 1024 + kc * 128: wi * 1024 + (kc + 1) * 128],
                             env.xn_fm[:, kc * T + qc * 512: kc * T + qc * 512 + 512],
                             start=(kc == 0), stop=(kc == NCH - 1),
                             reads=["l3_wpair", "xn_fm"],
                             writes=pk1(c0, c0 + 512), tag=f"l3{dk}mm")
                    p.act(dst[:, qc * 512:(qc + 1) * 512], ps, AF.Copy,
                          reads=pk1(c0, c0 + 512), writes=[dk + "_raw"],
                          tag=f"l3{dk}ev")
                for b0, b1 in ((0, 32), (32, 0), (64, 96), (96, 64)):
                    p.dma("gpsimd", swp[b0:b0 + 32, :], dst[b1:b1 + 32, :],
                          reads=[dk + "_raw", "l3_swp"], writes=["l3_swp"],
                          tag=f"l3sw{dk}")
                eng = "vector" if wi == 0 else "gpsimd"
                p.tt(utmp[:, :], dst[:, :], env.l3_cos[:, :], ALU.mult,
                     engine=eng, reads=[dk + "_raw", "l3_cos", "l3_u"],
                     writes=["l3_u"], tag=f"l3rc{dk}")
                p.tt(swp[:, :], swp[:, :], env.l3_sin[:, :], ALU.mult,
                     engine=eng, reads=["l3_swp", "l3_sin"], writes=["l3_swp"],
                     tag=f"l3rs{dk}")
                p.tt(dst[:, :], utmp[:, :], swp[:, :], ALU.add,
                     engine=eng, reads=["l3_u", "l3_swp"],
                     writes=[dk, dk + "_raw"], tag=f"l3ra{dk}")
            for tt in range(NT):
                c0 = 1024 + (tt % 2) * 512
                ps = PS1[:, c0:c0 + 128]
                for kc in range(NCH):
                    p.mm(ps, env.xn_fm[:, kc * T + tt * 128: kc * T + (tt + 1) * 128],
                         wpair[:, 2048 + kc * 128: 2048 + (kc + 1) * 128],
                         start=(kc == 0), stop=(kc == NCH - 1),
                         reads=["l3_wpair", "xn_fm"], writes=pk1(c0, c0 + 128),
                         tag=f"l3vmm{tt}")
                p.copy(vp[:, tt * 130:(tt + 1) * 130].rearrange(
                           "p (h j) -> p h j", h=2)[:, :, 0:64],
                       ps.rearrange("p (h j) -> p h j", h=2),
                       reads=pk1(c0, c0 + 128), writes=["l3_vp"],
                       tag=f"l3vev{tt}")
            p.memset(vp[:, :].rearrange("p (n h j) -> p n h j", n=NT, h=2)
                     [:, :, :, 64:65],
                     1.0, reads=["l3_vp"], writes=["l3_vp"], tag="l3_vones")
            for o, (srcb, sk) in enumerate(((qp, "l3_qp"), (kp, "l3_kp"))):
                p.act(sqt[:, :], srcb[:, :], AF.Square, reads=[sk],
                      writes=["l3_sqt"], tag=f"l3sq{o}")
                for qc in range(4):
                    ps = PS2[0:2, 1536:2048]
                    p.mm(ps, env.pairselT[:, :], sqt[:, qc * 512:(qc + 1) * 512],
                         reads=["l3_sqt", "pairselT"], writes=pk2(1536, 2048),
                         tag=f"l3nrm{o}")
                    p.red(mcol[:, qc:qc + 1], ps, ALU.max,
                          reads=pk2(1536, 2048), writes=["l3_mc"],
                          tag=f"l3nrmr{o}")
                p.red(brow2[:, o:o + 1], mcol[:, 0:4], ALU.max,
                      reads=["l3_mc"], writes=[f"l3_br{o}"], tag=f"l3b{o}")
            p.tt(brow2[:, 0:1], brow2[:, 0:1], brow2[:, 1:2], ALU.mult,
                 reads=["l3_br0", "l3_br1"], writes=["l3_bp"], tag="l3bp")
            p.act(brow2[:, 0:1], brow2[:, 0:1], AF.Sqrt, reads=["l3_bp"],
                  writes=["l3_bs"], tag="l3bs")
            p.ts(brow2[:, 0:1], brow2[:, 0:1], -0.125, ALU.mult,
                 reads=["l3_bs"], writes=["l3_bn"], tag="l3bn")
            p.dma("gpsimd", l3_bscr, brow2[:, 0:1],
                  reads=["l3_bn", "l3_bh"], writes=["l3_bscrd"],
                  tag="l3btr")
            p.dma("sync", bhneg[:, 0:2],
                  bass.AP(l3_bscr.tensor, 0, [[0, 128], [1, 2]]),
                  reads=["l3_bscrd", "l3_bh"], writes=["l3_bh"],
                  tag="l3brep")
            for h in range(2):
                hg = 2 * hp + h
                for kt in range(NT):
                    qcb0 = kt // 4
                    s0 = (kt % 2) * 1024
                    po = (kt % 2) * 2048
                    pky = f"l3_pt{kt % 2}"
                    for ch in range(qcb0, 4):
                        sc = PS1[:, s0 + (ch % 2) * 512: s0 + (ch % 2) * 512 + 512]
                        p.mm(sc,
                             kp[64 * h:64 * h + 64, kt * 128:(kt + 1) * 128],
                             qp[64 * h:64 * h + 64, ch * 512:(ch + 1) * 512],
                             reads=["l3_qp", "l3_kp"],
                             writes=pk1(s0 + (ch % 2) * 512,
                                        s0 + (ch % 2) * 512 + 512),
                             tag=f"l3sc{hg}")
                        p.act(ptil[:, po + ch * 512: po + (ch + 1) * 512],
                              sc, AF.Exp,
                              scale=0.125, bias=bhneg[:, h:h + 1],
                              reads=pk1(s0 + (ch % 2) * 512,
                                        s0 + (ch % 2) * 512 + 512) + ["l3_bh",
                                                                      pky],
                              writes=[pky], tag=f"l3exp{hg}")
                    m = kt % 4
                    if m > 0:
                        p.memset(ptil[:, po + qcb0 * 512: po + qcb0 * 512 + m * 128],
                                 0.0, reads=[pky], writes=[pky],
                                 tag=f"l3mz{hg}")
                    p.tt(ptil[:, po + kt * 128: po + (kt + 1) * 128],
                         ptil[:, po + kt * 128: po + (kt + 1) * 128],
                         env.causal_keep[:, :], ALU.mult,
                         reads=[pky, "causal_keep"], writes=[pky],
                         tag=f"l3msk{hg}")
                    for ch in range(qcb0, 4):
                        av = PS2[0:65, ch * 512:(ch + 1) * 512]
                        p.mm(av,
                             vp[:, kt * 130 + 65 * h: kt * 130 + 65 * h + 65],
                             ptil[:, po + ch * 512: po + (ch + 1) * 512],
                             start=(kt == 0), stop=(kt == min(4 * ch + 3, NT - 1)),
                             reads=[pky, "l3_vp"],
                             writes=pk2(ch * 512, (ch + 1) * 512),
                             tag=f"l3av{hg}")
                p.act(srow[64:65, 0:T], PS2[64:65, 0:T], AF.Copy,
                      reads=pk2(0, 2048) + ["l3_srow"],
                      writes=["l3_srow"], tag=f"l3sumst{hg}")
                p.dma("gpsimd", sums_all[hg:hg + 1, :], srow[64:65, 0:T],
                      reads=["l3_srow"], writes=["l3_sums", "l3_srow"],
                      tag=f"l3sum{hg}")
                p.act(y_fm[64 * h:64 * h + 64, hp * T:(hp + 1) * T],
                      PS2[0:64, 0:T], AF.Copy, reads=pk2(0, 2048),
                      writes=["l3_yraw"], tag=f"l3yev{hg}")
        p.act(sums_all[:, :], sums_all[:, :], AF.Ln, reads=["l3_sums"],
              writes=["l3_lns"], tag="l3_lns")
        p.act(r_bf[:, :], sums_all[:, :], AF.Exp, scale=-1.0,
              reads=["l3_lns"], writes=["l3_r"], tag="l3_rinv")
        p.dma("gpsimd", env.rscr, r_bf[:, :], reads=["l3_r"],
              writes=["l3_rscr"], tag="l3rst")
        for c in range(NCH):
            for h in range(2):
                srcap = bass.AP(env.rscr.tensor, (2 * c + h) * T,
                                [[0, 64], [1, T]])
                p.dma("sync", rrep[64 * h:64 * h + 64, :], srcap,
                      reads=["l3_rscr", "l3_rrep"], writes=["l3_rrep"],
                      tag="l3rbc")
            p.tt(y_fm[:, c * T:(c + 1) * T], y_fm[:, c * T:(c + 1) * T],
                 rrep[:, :], ALU.mult,
                 reads=["l3_rrep", "l3_yraw"], writes=["l3_y"],
                 tag=f"l3ynrm{c}")
        for dc in range(NCH):
            for qc in range(4):
                c0 = (qc % 2) * 512
                ps = PS2[:, c0:c0 + 512]
                for kc in range(NCH):
                    p.mm(ps, wo_sb[:, kc * D + dc * 128: kc * D + (dc + 1) * 128],
                         y_fm[:, kc * T + qc * 512: kc * T + qc * 512 + 512],
                         start=(kc == 0), stop=(kc == NCH - 1),
                         reads=["l3_wo", "l3_y"], writes=pk2(c0, c0 + 512),
                         tag="l3omm")
                s_sink(dc, qc, ps, pk2(c0, c0 + 512))
    p.barrier()
def emit_l3_ffn(env, W, g_sink):
    p, nc = env.p, env.nc
    PS1, PS2 = env.PS1.ap(), env.PS2.ap()
    p.barrier()
    with (
        nc.sbuf_tensor("l3f_q", [128, NCH * T], BF16) as q_fm,
        nc.sbuf_tensor("l3f_k", [128, NCH * T], BF16) as k_fm,
        nc.sbuf_tensor("l3f_v", [128, NCH * T], BF16) as v_fm,
    ):
        vt = env.xn_fm
        from contextlib import ExitStack as _ES
        for f, (dst, dstk) in enumerate(((q_fm, "l3f_q"), (k_fm, "l3f_k"),
                                         (v_fm, "l3f_v"))):
            with _ES() as _es:
                _e = _es.enter_context
                nw_sb = _e(nc.sbuf_tensor(f"l3f_nw{f}", [128, NCH * 15], BF16))
                nb_sb = _e(nc.sbuf_tensor(f"l3f_nb{f}", [8, 8], F32))
                ge_sb = _e(nc.sbuf_tensor(f"l3f_ge{f}", [8, 512], BF16))
                lt_sb = _e(nc.sbuf_tensor(f"l3f_lt{f}", [8, 512], BF16))
                sel_sb = _e(nc.sbuf_tensor(f"l3f_sel{f}", [16, T], BF16))
                w1_sb = _e(nc.sbuf_tensor(f"l3f_w1{f}", [128, 16 * NCH * 64], BF16))
                w2_sb = _e(nc.sbuf_tensor(f"l3f_w2{f}", [64, 16 * D], BF16))
                b1_sb = _e(nc.sbuf_tensor(f"l3f_b1{f}", [64, 16], F32))
                b2_sb = _e(nc.sbuf_tensor(f"l3f_b2{f}", [16, D], BF16))
                h_sb = _e(nc.sbuf_tensor(f"l3f_h{f}", [64, 512], BF16))
                for d in range(4):
                    p.dma("gpsimd",
                          nw_sb[:, :].rearrange("p (c o) -> p c o", c=NCH)
                          [:, :, (2 ** d - 1):(2 ** (d + 1) - 1)],
                          W[f"l3_nw{f}{d}"].rearrange("c p o -> p c o"),
                          writes=["l3f_nw"], tag="w_l3f_nw",
                          allow_slow_non_contiguous=True)
                    p.dma("sync", nb_sb[0:2 ** d, d:d + 1], W[f"l3_nb{f}{d}"],
                          writes=["l3f_nb"], tag="l3fnb")
                for d in range(4):
                    p.ts(nb_sb[0:2 ** d, 4 + d:5 + d], nb_sb[0:2 ** d, d:d + 1],
                         -1.0, ALU.mult, reads=["l3f_nb"], writes=["l3f_nbn"],
                         tag="l3fnbn")
                env.wload(w1_sb[:, :].rearrange("p (c lo) -> p c lo", c=NCH),
                          W[f"l3_w1{f}"].rearrange("c p lo -> p c lo"),
                          "l3f_w1")
                env.wload(w2_sb[:, :].rearrange("p (l d) -> p l d", l=16),
                          W[f"l3_w2{f}"].rearrange("l p d -> p l d"), "l3f_w2")
                p.dma("sync", b1_sb[:, :], W[f"l3_b1{f}"], writes=["l3f_b1"],
                      tag="l3fb1")
                env.wload(b2_sb[:, :], W[f"l3_b2{f}"], "l3f_b2")
                for d in range(4):
                    nd = 2 ** d
                    for qc in range(4):
                        ps = PS1[0:nd, 0:512]
                        for kc in range(NCH):
                            p.mm(ps, nw_sb[:, kc * 15 + nd - 1:
                                           kc * 15 + 2 * nd - 1],
                                 env.xn_fm[:, kc * T + qc * 512:
                                           kc * T + qc * 512 + 512],
                                 start=(kc == 0), stop=(kc == NCH - 1),
                                 reads=["l3f_nw", "xn_fm"], writes=pk1(0, 512),
                                 tag=f"l3fsc{f}{d}")
                        p.ts(ge_sb[0:nd, :], ps, nb_sb[0:nd, 4 + d:5 + d],
                             ALU.is_ge, reads=pk1(0, 512) + ["l3f_nbn"],
                             writes=["l3f_ge"], tag=f"l3fge{d}")
                        p.ts(lt_sb[0:nd, :], ps, nb_sb[0:nd, 4 + d:5 + d],
                             ALU.is_lt, reads=pk1(0, 512) + ["l3f_nbn"],
                             writes=["l3f_lt"], tag=f"l3flt{d}")
                        ps2 = PS1[0:16, 512:1024]
                        p.mm(ps2, env.__dict__[f"l3_selge{d}"][:, :],
                             ge_sb[0:nd, :], start=True, stop=False,
                             reads=["l3f_ge", f"l3_selge{d}"],
                             writes=pk1(512, 1024), tag=f"l3fek{d}")
                        p.mm(ps2, env.__dict__[f"l3_sellt{d}"][:, :],
                             lt_sb[0:nd, :], start=False, stop=True,
                             reads=["l3f_lt", f"l3_sellt{d}"],
                             writes=pk1(512, 1024), tag=f"l3fek{d}b")
                        if d == 0:
                            p.copy(sel_sb[0:16, qc * 512:(qc + 1) * 512], ps2,
                                   reads=pk1(512, 1024), writes=["l3f_sel"],
                                   tag="l3fsel0")
                        else:
                            p.tt(sel_sb[0:16, qc * 512:(qc + 1) * 512],
                                 sel_sb[0:16, qc * 512:(qc + 1) * 512], ps2,
                                 ALU.mult, reads=pk1(512, 1024) + ["l3f_sel"],
                                 writes=["l3f_sel"], tag=f"l3fsel{d}")
                for tb in range(8):
                    c0 = tb * 256
                    for l in range(16):
                        psr = PS1[0:64, 0:256]
                        p.mm(psr, env.l3_indl[:, l * 64:(l + 1) * 64],
                             sel_sb[0:16, c0:c0 + 256],
                             reads=["l3f_sel", "l3_indl"], writes=pk1(0, 512),
                             tag="l3frep")
                        hps = PS1[0:64, 512 + (l % 2) * 256:
                                  512 + (l % 2) * 256 + 256]
                        for kc in range(NCH):
                            p.mm(hps, w1_sb[:, (kc * 16 + l) * 64:
                                            (kc * 16 + l + 1) * 64],
                                 env.xn_fm[:, kc * T + c0: kc * T + c0 + 256],
                                 start=(kc == 0), stop=(kc == NCH - 1),
                                 reads=["l3f_w1", "xn_fm"],
                                 writes=pk1(512 + (l % 2) * 256,
                                            768 + (l % 2) * 256),
                                 tag="l3fhmm")
                        hv = h_sb[0:64, (l % 2) * 256:(l % 2) * 256 + 256]
                        p.act(hv, hps, AF.Relu, bias=b1_sb[:, l:l + 1],
                              reads=pk1(512 + (l % 2) * 256,
                                        768 + (l % 2) * 256) + ["l3f_b1"],
                              writes=[f"l3f_h{l % 2}"], tag="l3fhev")
                        p.tt(hv, hv, psr, ALU.mult,
                             reads=pk1(0, 512) + [f"l3f_h{l % 2}"],
                             writes=[f"l3f_h{l % 2}"], tag="l3fhm")
                        for dc in range(NCH):
                            p.mm(PS2[:, dc * 256:(dc + 1) * 256],
                                 w2_sb[:, l * D + dc * 128: l * D + (dc + 1) * 128],
                                 hv, start=(l == 0), stop=False,
                                 reads=[f"l3f_h{l % 2}", "l3f_w2"],
                                 writes=pk2(dc * 256, (dc + 1) * 256),
                                 tag="l3fw2")
                    for dc in range(NCH):
                        p.mm(PS2[:, dc * 256:(dc + 1) * 256],
                             b2_sb[:, dc * 128:(dc + 1) * 128],
                             sel_sb[0:16, c0:c0 + 256], start=False, stop=True,
                             reads=["l3f_sel", "l3f_b2"],
                             writes=pk2(dc * 256, (dc + 1) * 256), tag="l3fb2")
                        p.act(dst[:, dc * T + c0: dc * T + c0 + 256],
                              PS2[:, dc * 256:(dc + 1) * 256], AF.Copy,
                              reads=pk2(dc * 256, (dc + 1) * 256),
                              writes=[dstk], tag="l3fout")
        p.barrier()
        trps = env.PS1.ap().bitcast(BF16)
        for tt in range(NT):
            for c in range(NCH):
                p.trans(trps[:, c * 128:(c + 1) * 128],
                        v_fm[:, c * T + tt * 128: c * T + (tt + 1) * 128],
                        env.id128[:, :],
                        reads=["l3f_v", "id128"], writes=pk1(0, 512),
                        tag="l3fvtr")
            p.copy(vt[:, tt * 1024:(tt + 1) * 1024], trps[:, 0:1024],
                   reads=pk1(0, 512), writes=["l3f_vt"], tag="l3fvtc")
        with _ES() as _es2:
            _e2 = _es2.enter_context
            sqt = _e2(nc.sbuf_tensor("l3f_sq", [128, T], BF16))
            bsc = _e2(nc.sbuf_tensor("l3f_b", [1, 10], F32))
            bscb = _e2(nc.sbuf_tensor("l3f_bb", [1, 1], BF16))
            bcol = _e2(nc.sbuf_tensor("l3f_bh", [128, 1], F32))
            ptil = _e2(nc.sbuf_tensor("l3f_pt", [128, 256], BF16))
            rbr = _e2(nc.sbuf_tensor("l3f_rw", [1, 256], BF16))
            rrep = _e2(nc.sbuf_tensor("l3f_rr", [128, 256], BF16))
            g_sb = _e2(nc.sbuf_tensor("l3f_g", [128, 2 * 256], F32))
            for o, (srcb, sk) in enumerate(((q_fm, "l3f_q"), (k_fm, "l3f_k"))):
                for qc in range(4):
                    ps = PS1[0:1, 1536:2048]
                    for c in range(NCH):
                        p.act(sqt[:, qc * 512:(qc + 1) * 512],
                              srcb[:, c * T + qc * 512: c * T + qc * 512 + 512],
                              AF.Square, reads=[sk], writes=["l3f_sq"],
                              tag=f"l3fsq{o}")
                        p.mm(ps, env.ones_col128[:, :],
                             sqt[:, qc * 512:(qc + 1) * 512],
                             start=(c == 0), stop=(c == NCH - 1),
                             reads=["l3f_sq", "ones_col128"],
                             writes=pk1(1536, 2048), tag=f"l3fn{o}")
                    p.red(bsc[0:1, 4 * o + qc:4 * o + qc + 1], ps, ALU.max,
                          reads=pk1(1536, 2048),
                          writes=[f"l3f_m{'qk'[o]}{qc}"], tag=f"l3fnr{o}")
            p.red(bsc[0:1, 8:9], bsc[0:1, 0:4], ALU.max,
                  reads=["l3f_mq0", "l3f_mq1", "l3f_mq2", "l3f_mq3"],
                  writes=["l3f_mq"], tag="l3fmq")
            p.red(bsc[0:1, 9:10], bsc[0:1, 4:8], ALU.max,
                  reads=["l3f_mk0", "l3f_mk1", "l3f_mk2", "l3f_mk3"],
                  writes=["l3f_mk"], tag="l3fmk")
            p.tt(bsc[0:1, 8:9], bsc[0:1, 8:9], bsc[0:1, 9:10], ALU.mult,
                 reads=["l3f_mq", "l3f_mk"], writes=["l3f_bp"], tag="l3fbp")
            p.act(bsc[0:1, 8:9], bsc[0:1, 8:9], AF.Sqrt, reads=["l3f_bp"],
                  writes=["l3f_bs"], tag="l3fbs")
            p.ts(bscb[0:1, 0:1], bsc[0:1, 8:9], -0.03125, ALU.mult,
                 reads=["l3f_bs"], writes=["l3f_bn"], tag="l3fbn")
            p.mm(PS1[:, 1536:1537], env.ones_row[:, :], bscb[0:1, 0:1],
                 reads=["l3f_bn", "ones_row"], writes=pk1(1536, 2048),
                 tag="l3fbrep")
            p.act(bcol[:, :], PS1[:, 1536:1537], AF.Copy,
                  reads=pk1(1536, 2048), writes=["l3f_bh"], tag="l3fbev")
            for qb in range(8):
                c0 = qb * 256
                for kt in range(NT):
                    s0 = 1024 + (kt % 2) * 256
                    sc = PS1[:, s0:s0 + 256]
                    sck = pk1(1024, 1536)
                    for kc in range(NCH):
                        p.mm(sc, k_fm[:, kc * T + kt * 128: kc * T + (kt + 1) * 128],
                             q_fm[:, kc * T + c0: kc * T + c0 + 256],
                             start=(kc == 0), stop=(kc == NCH - 1),
                             reads=["l3f_q", "l3f_k"], writes=sck,
                             tag="l3fasc")
                    p.act(ptil[:, :], sc, AF.Exp, scale=0.03125,
                          bias=bcol[:, 0:1], reads=sck + ["l3f_bh"],
                          writes=["l3f_pt"], tag="l3faexp")
                    p.mm(PS1[0:1, 1792:2048], env.ones_col128[:, :],
                         ptil[:, :], start=(kt == 0), stop=(kt == NT - 1),
                         reads=["l3f_pt", "ones_col128"], writes=pk1(1536, 2048),
                         tag="l3fasum")
                    for dc in range(NCH):
                        p.mm(PS2[:, dc * 256:(dc + 1) * 256],
                             vt[:, kt * 1024 + dc * 128: kt * 1024 + (dc + 1) * 128],
                             ptil[:, :], start=(kt == 0), stop=(kt == NT - 1),
                             reads=["l3f_pt", "l3f_vt"],
                             writes=pk2(dc * 256, (dc + 1) * 256), tag="l3fav")
                p.act(rbr[0:1, :], PS1[0:1, 1792:2048], AF.Ln,
                      reads=pk1(1536, 2048), writes=["l3f_ln"], tag="l3faln")
                p.act(rbr[0:1, :], rbr[0:1, :], AF.Exp, scale=-1.0,
                      reads=["l3f_ln"], writes=["l3f_rw"], tag="l3farw")
                p.mm(PS1[:, 1024:1280], env.ones_row[:, :], rbr[0:1, :],
                     reads=["l3f_rw", "ones_row"], writes=pk1(1024, 1536),
                     tag="l3farep")
                p.act(rrep[:, :], PS1[:, 1024:1280], AF.Copy,
                      reads=pk1(1024, 1536), writes=["l3f_rr"], tag="l3farev")
                for dc in range(NCH):
                    gv = g_sb[:, (dc % 2) * 256:(dc % 2) * 256 + 256]
                    p.tt(gv, PS2[:, dc * 256:(dc + 1) * 256], rrep[:, :],
                         ALU.mult,
                         reads=pk2(dc * 256, (dc + 1) * 256) + ["l3f_rr",
                                                                f"l3f_g{dc % 2}"],
                         writes=[f"l3f_g{dc % 2}"], tag="l3fgn")
                    g_sink(dc, qb, gv, [f"l3f_g{dc % 2}"])
    p.barrier()


def build_layer3(wshapes):
    nc = bass.Bass("TRN2", target_bir_lowering=False)
    kind_dbg = "ExternalOutput" if DEBUG_DUMP else "Internal"
    x_in = nc.dram_tensor("x_in", [T, D], F32, kind="ExternalInput").ap()
    x_out = nc.dram_tensor("x_out", [T, D], F32, kind="ExternalOutput").ap()
    s_scr = nc.dram_tensor("s_scr", [NCH, 128, T], BF16, kind=kind_dbg).ap()
    rscr = nc.dram_tensor("rscr", [16, T], BF16, kind="Internal").ap()
    W = {k: nc.dram_tensor(k, list(v), F32, kind="ExternalInput").ap()
         for k, v in wshapes.items()}
    env = Env(nc)
    make_consts(env)
    l3_make_consts(env)
    env.rscr = rscr
    env.xn_fm = env.sb("xn_fm", [128, NCH * T], BF16)
    emit_ln_phase(env, x_in)
    if DEBUG_DUMP:
        dxn = nc.dram_tensor("dbg_xn", [128, NCH * T], BF16,
                             kind="ExternalOutput").ap()
        env.p.dma("gpsimd", dxn, env.xn_fm[:, :], reads=["xn_fm"], tag="dxn")
    emit_l3_attn(env, W, make_s_sink(env, s_scr, mode="plain"))
    if DEBUG_DUMP:
        env.dbg_q = nc.dram_tensor("dbg_q", [128, NCH * T], BF16,
                                   kind="ExternalOutput").ap()
        env.dbg_sel = nc.dram_tensor("dbg_sel", [16, T], BF16,
                                     kind="ExternalOutput").ap()
    g_sink = make_residual_streamer(env, x_in, x_out, s_scr, blk=256,
                                    trps_region=(0, 512))
    emit_l3_ffn(env, W, g_sink)
    env.p.build()
    return nc


LAYER_BUILDERS[3] = (host_prep_l3, build_layer3)
def host_prep_l2(inp):
    w = {}
    w["l2_WK"] = np.ascontiguousarray(np.asarray(inp["L2_attn_W_K"])).reshape(NCH, 128, 2048)
    w["l2_Wugg"] = np.ascontiguousarray(np.asarray(inp["L2_attn_W_ugg"])).reshape(NCH, 128, 6144)
    w["l2_Wout"] = np.ascontiguousarray(np.asarray(inp["L2_attn_W_out"])).reshape(16, 128, D)
    w["l2_rms"] = np.ascontiguousarray(np.asarray(inp["L2_attn_rms_scale"])).reshape(NCH, 128).T.copy()
    w["l2_split"] = np.ascontiguousarray(np.asarray(inp["L2_attn_split_scale"])).reshape(NCH, 128).T.copy()
    w["l2_Wc"] = np.ascontiguousarray(np.asarray(inp["L2_ffn_Wc"])).reshape(NCH, 128, 128)
    w["l2_bc"] = np.ascontiguousarray(np.asarray(inp["L2_ffn_bc"])).reshape(128, 1)
    W1 = np.asarray(inp["L2_ffn_W1"]).reshape(128, D, 32)
    W2 = np.asarray(inp["L2_ffn_W2"]).reshape(128, D, 32)
    b1 = np.asarray(inp["L2_ffn_b1"]).reshape(128, 32)
    b2 = np.asarray(inp["L2_ffn_b2"]).reshape(128, D)
    w["l2_W1"] = np.ascontiguousarray(W1.transpose(0, 2, 1)
                                      .reshape(128, 32, NCH, 128)
                                      .transpose(0, 2, 3, 1))
    w["l2_W1"] = np.ascontiguousarray(W1.reshape(128, NCH, 128, 32))
    w["l2_W2t"] = np.ascontiguousarray(W2.transpose(0, 2, 1))
    w["l2_b1T"] = np.ascontiguousarray(b1.T)
    w["l2_b2"] = np.ascontiguousarray(b2)
    return w


def emit_l2_attn(env, W, s_sink):
    p, nc = env.p, env.nc
    PS1, PS2 = env.PS1.ap(), env.PS2.ap()
    p.barrier()
    from contextlib import ExitStack as _ES
    with _ES() as _es:
        _e = _es.enter_context
        xn2 = _e(nc.sbuf_tensor("l2_xn2", [128, NCH * T], BF16))
        so = _e(nc.sbuf_tensor("l2_so", [128, T], BF16))
        soq = _e(nc.sbuf_tensor("l2_soq", [128, 16 * 512], BF16))
        nrow = _e(nc.sbuf_tensor("l2_nr", [1, T], F32))
        rrep = _e(nc.sbuf_tensor("l2_rrep", [128, T], BF16))
        sv = _e(nc.sbuf_tensor("l2_sv_sb", [128, NCH], F32))
        sq = _e(nc.sbuf_tensor("l2_sq", [128, 512], BF16))
        wst = _e(nc.sbuf_tensor("l2_wst", [128, 2 * 4 * NCH * 128], BF16))
        kc_sb = _e(nc.sbuf_tensor("l2_kc", [128, T], F32))
        k1m = _e(nc.sbuf_tensor("l2_k1m", [128, T], F32))
        shf = _e(nc.sbuf_tensor("l2_shf", [128, T + 1], F32))
        ub = _e(nc.sbuf_tensor("l2_ub", [128, T], F32))
        t1 = _e(nc.sbuf_tensor("l2_t1", [128, T], F32))
        gsig = _e(nc.sbuf_tensor("l2_gs", [128, T], BF16))
        sct = _e(nc.sbuf_tensor("l2_sct", [128, T], F32))
        wo_sb = _e(nc.sbuf_tensor("l2_wo_sb", [128, 16 * D], BF16))
        env.wload(wo_sb[:, :].rearrange("p (c o) -> p c o", c=16),
                  W["l2_Wout"].rearrange("c p o -> p c o"), "l2_wo")
        p.dma("sync", sv[:, 0:NCH], W["l2_rms"], writes=["l2_rms"], tag="l2rms")
        p.dma("sync", xn2[:, 0:16].bitcast(F32), W["l2_split"],
              writes=["l2_splt"], tag="l2splt")
        p.tt(sv[:, :], sv[:, :], xn2[:, 0:16].bitcast(F32),
             ALU.mult, reads=["l2_rms", "l2_splt"], writes=["l2_sv"],
             tag="l2sv")
        for qc in range(4):
            ps = PS1[0:1, 1536:2048]
            for c in range(NCH):
                p.act(sq[:, 0:512],
                      env.xn_fm[:, c * T + qc * 512: c * T + qc * 512 + 512],
                      AF.Square, reads=["xn_fm"], writes=["l2_sq"], tag="l2sq")
                p.mm(ps, env.ones_col128[:, :], sq[:, 0:512],
                     start=(c == 0), stop=(c == NCH - 1),
                     reads=["l2_sq", "ones_col128"], writes=pk1(1536, 2048),
                     tag="l2nm")
            p.act(nrow[0:1, qc * 512:(qc + 1) * 512], ps, AF.Sqrt,
                  scale=1.0 / D, reads=pk1(1536, 2048), writes=["l2_nr"],
                  tag="l2nsq")
        p.act(nrow[0:1, :], nrow[0:1, :], AF.Ln, bias=env.eps8[0:1, 0:1],
              reads=["l2_nr", "eps8"], writes=["l2_nl"], tag="l2nl")
        p.act(nrow[0:1, :], nrow[0:1, :], AF.Exp, scale=-1.0,
              reads=["l2_nl"], writes=["l2_nr2"], tag="l2nr2")
        p.copy(sct[0:1, 0:T], nrow[0:1, :], reads=["l2_nr2"], writes=["l2_rbf"],
               tag="l2rbf")
        p.dma("gpsimd", env.rscr[0:1, :], sct[0:1, 0:T], reads=["l2_rbf"],
              writes=["l2_rd"], tag="l2rst")
        srcap = bass.AP(env.rscr.tensor, 0, [[0, 128], [1, T]])
        p.dma("sync", rrep[:, :], srcap, reads=["l2_rd"], writes=["l2_rrep"],
              tag="l2rbc")
        for c in range(NCH):
            p.stt(xn2[:, c * T:(c + 1) * T],
                  env.xn_fm[:, c * T:(c + 1) * T], sv[:, c:c + 1],
                  rrep[:, :], ALU.mult, ALU.mult,
                  reads=["xn_fm", "l2_sv", "l2_rrep", "l2_splt"],
                  writes=["l2_xn2"], tag=f"l2xn2_{c}")
        for hc in range(16):
            env.wload(wst[:, 0:1024].rearrange("p (c o) -> p c o", c=NCH),
                      W["l2_WK"][:, :, hc * 128:(hc + 1) * 128].rearrange(
                          "c p o -> p c o"), "l2_wst")
            for j, off in enumerate((0, 2048, 4096)):
                env.wload(wst[:, (j + 1) * 1024:(j + 2) * 1024].rearrange(
                              "p (c o) -> p c o", c=NCH),
                          W["l2_Wugg"][:, :, off + hc * 128: off + (hc + 1) * 128]
                          .rearrange("c p o -> p c o"), "l2_wst")
            for qc in range(4):
                c0 = (qc % 2) * 512
                ps = PS1[:, c0:c0 + 512]
                for kc in range(NCH):
                    p.mm(ps, wst[:, kc * 128:(kc + 1) * 128],
                         xn2[:, kc * T + qc * 512: kc * T + qc * 512 + 512],
                         start=(kc == 0), stop=(kc == NCH - 1),
                         reads=["l2_wst", "l2_xn2"], writes=pk1(c0, c0 + 512),
                         tag="l2kmm")
                p.act(kc_sb[:, qc * 512:(qc + 1) * 512], ps, AF.Sigmoid,
                      reads=pk1(c0, c0 + 512), writes=["l2_kc"], tag="l2kev")
                p.act(k1m[:, qc * 512:(qc + 1) * 512], ps, AF.Sigmoid,
                      scale=-1.0, reads=pk1(c0, c0 + 512), writes=["l2_k1m"],
                      tag="l2k1ev")
            for wi, (dst, dk, fn) in enumerate((
                    (t1, "l2_t1", AF.Sigmoid), (ub, "l2_ub", AF.Copy),
                    (gsig, "l2_gs", AF.Sigmoid))):
                wofs = (2 if wi == 0 else (1 if wi == 1 else 3)) * 1024
                for qc in range(4):
                    c0 = (qc % 2) * 512
                    ps = PS1[:, c0:c0 + 512]
                    for kc in range(NCH):
                        p.mm(ps, wst[:, wofs + kc * 128: wofs + (kc + 1) * 128],
                             xn2[:, kc * T + qc * 512: kc * T + qc * 512 + 512],
                             start=(kc == 0), stop=(kc == NCH - 1),
                             reads=["l2_wst", "l2_xn2"],
                             writes=pk1(c0, c0 + 512), tag=f"l2umm{wi}")
                    p.act(dst[:, qc * 512:(qc + 1) * 512], ps, fn,
                          reads=pk1(c0, c0 + 512), writes=[dk], tag=f"l2uev{wi}")
            p.tt(ub[:, :], ub[:, :], t1[:, :], ALU.mult,
                 reads=["l2_ub", "l2_t1"], writes=["l2_ub"], tag="l2ub1")
            p.tt(ub[:, :], ub[:, :], k1m[:, :], ALU.mult,
                 reads=["l2_ub", "l2_k1m"], writes=["l2_ub"], tag="l2ub2")
            p.memset(shf[:, 0:1], 0.0, writes=["l2_shf"], tag="l2shz")
            p.copy(shf[:, 1:T + 1], kc_sb[:, :], reads=["l2_kc", "l2_shf"],
                   writes=["l2_shf"], tag="l2shc")
            p.scan(sct[:, :], shf[:, 0:T], ub[:, :], 0.0, ALU.mult, ALU.add,
                   reads=["l2_shf", "l2_ub"], writes=["l2_sct", "l2_rbf"],
                   tag="l2scan")
            p.tt(so[:, :], sct[:, :], gsig[:, :], ALU.mult,
                 reads=["l2_sct", "l2_gs", "l2_so"], writes=["l2_so"],
                 tag="l2soev")
            p.dma("gpsimd", env.l2_sodram[hc], so[:, :],
                  reads=["l2_so"], writes=["l2_sod"], tag="l2sost")
        for qc in range(4):
            p.dma("sync", soq[:, :].rearrange("p (h j) -> p h j", h=16),
                  env.l2_sodram.rearrange("h p j -> p h j")
                  [:, :, qc * 512:(qc + 1) * 512],
                  reads=["l2_sod", "l2_soq"], writes=["l2_soq"], tag="l2sold")
            for dc in range(NCH):
                c0 = (dc % 2) * 512
                ps = PS2[:, c0:c0 + 512]
                for hc in range(16):
                    p.mm(ps, wo_sb[:, hc * D + dc * 128: hc * D + (dc + 1) * 128],
                         soq[:, hc * 512:(hc + 1) * 512],
                         start=(hc == 0), stop=(hc == 15),
                         reads=["l2_wo", "l2_soq"], writes=pk2(c0, c0 + 512),
                         tag="l2omm")
                s_sink(dc, qc, ps, pk2(c0, c0 + 512))
    p.barrier()
def emit_l2_ffn(env, W, xnd, yed, finale):
    p, nc = env.p, env.nc
    PS1, PS2 = env.PS1.ap(), env.PS2.ap()
    p.barrier()
    from contextlib import ExitStack as _ES
    from concourse.bass_core_rust import IndirectOffsetOnAxis
    tie = env.load_const_sb(
        "l2_tie", np.ascontiguousarray(np.broadcast_to(
            np.tile(np.linspace(0.0, 1e-6, 32, dtype=np.float32), 64),
            (128, T))))
    iota32 = env.load_const_sb(
        "l2_iota32", np.ascontiguousarray(np.broadcast_to(
            np.tile(np.arange(32, dtype=np.float32), 64), (128, T))
            .astype(bfdt)))
    g32 = env.load_const_sb(
        "l2_g32", np.ascontiguousarray(np.broadcast_to(
            (np.arange(64, dtype=np.float32) * 32), (128, 64))))
    with _ES() as _es:
        _e = _es.enter_context
        wc_sb = _e(nc.sbuf_tensor("l2_wc_sb", [128, NCH * 128], BF16))
        bc_sb = _e(nc.sbuf_tensor("l2_bc_sb", [128, 1], F32))
        lg = _e(nc.sbuf_tensor("l2_lg", [128, T], F32))
        gmax = _e(nc.sbuf_tensor("l2_gmax", [128, 64], F32))
        perm = _e(nc.sbuf_tensor("l2_perm", [128, T], BF16))
        loc = _e(nc.sbuf_tensor("l2_loc", [128, 64], F32))
        idxb = _e(nc.sbuf_tensor("l2_idx", [128, 64], U32))
        idx2 = _e(nc.sbuf_tensor("l2_idx2", [128, 64], U32))
        vtok = _e(nc.sbuf_tensor("l2_vtok", [128, 2 * D], BF16))
        xsel = _e(nc.sbuf_tensor("l2_xsel", [128, 2 * D], BF16))
        w1_sb = _e(nc.sbuf_tensor("l2_w1s", [128, 4 * NCH * 32], BF16))
        w2_sb = _e(nc.sbuf_tensor("l2_w2s", [32, 4 * D], BF16))
        b1_sb = _e(nc.sbuf_tensor("l2_b1s", [32, 128], F32))
        b2_sb = _e(nc.sbuf_tensor("l2_b2s", [1, 4 * D], BF16))
        h_sb = _e(nc.sbuf_tensor("l2_hs", [32, 4 * 64], BF16))
        ye_sb = _e(nc.sbuf_tensor("l2_yes", [64, 2 * D], BF16))
        yeg = _e(nc.sbuf_tensor("l2_yeg", [128, 2 * D], BF16))
        env.wload(wc_sb[:, :].rearrange("p (c o) -> p c o", c=NCH),
                  W["l2_Wc"].rearrange("c p o -> p c o"), "l2_wcw")
        p.dma("sync", bc_sb[:, :], W["l2_bc"], writes=["l2_bcb"], tag="l2bc")
        p.dma("sync", b1_sb[:, :], W["l2_b1T"], writes=["l2_b1s"], tag="l2b1")
        for qc in range(4):
            c0 = (qc % 2) * 512
            ps = PS1[:, c0:c0 + 512]
            for kc in range(NCH):
                p.mm(ps, wc_sb[:, kc * 128:(kc + 1) * 128],
                     env.xn_fm[:, kc * T + qc * 512: kc * T + qc * 512 + 512],
                     start=(kc == 0), stop=(kc == NCH - 1),
                     reads=["l2_wcw", "xn_fm"], writes=pk1(c0, c0 + 512),
                     tag="l2lmm")
            p.act(lg[:, qc * 512:(qc + 1) * 512], ps, AF.Identity,
                  bias=bc_sb[:, 0:1], reads=pk1(c0, c0 + 512) + ["l2_bcb"],
                  writes=["l2_lg"], tag="l2lev")
        p.tt(lg[:, :], lg[:, :], tie[:, :], ALU.add,
             reads=["l2_lg", "l2_tie"], writes=["l2_lg"], tag="l2tie")
        lg3 = lg[:, :].rearrange("p (g j) -> p g j", g=64)
        p.red(gmax[:, :], lg3, ALU.max, reads=["l2_lg"], writes=["l2_gm"],
              tag="l2gm")
        gmb = bass.AP(gmax, 0, [[64, 128], [1, 64], [0, 32]])
        p.tt(perm[:, :].rearrange("p (g j) -> p g j", g=64), lg3, gmb,
             ALU.is_ge, reads=["l2_lg", "l2_gm"], writes=["l2_perm"],
             tag="l2pm")
        p.tt(lg[:, :], perm[:, :], iota32[:, :], ALU.mult,
             reads=["l2_perm", "l2_iota32", "l2_lg"], writes=["l2_lgi"],
             tag="l2li")
        p.red(loc[:, :], lg3, ALU.add, reads=["l2_lgi"], writes=["l2_loc"],
              tag="l2loc")
        p.tt(loc[:, :], loc[:, :], g32[:, :], ALU.add,
             reads=["l2_loc", "l2_g32"], writes=["l2_loc"], tag="l2locg")
        p.copy(idxb[:, :], loc[:, :], reads=["l2_loc"], writes=["l2_idxb"],
               tag="l2idx")
        p.dma("gpsimd", env.l2_idxd, idxb[:, :], reads=["l2_idxb"],
              writes=["l2_idxd"], tag="l2idxst")
        for eh in range(2):
            srcap = bass.AP(env.l2_idxd.tensor, eh * 64,
                            [[1, 64], [128, 64]])
            p.dma("sync", idx2[eh * 64:(eh + 1) * 64, :], srcap,
                  reads=["l2_idxd"], writes=["l2_idx2"], tag="l2idxld",
                  allow_slow_non_contiguous=True)
        for gi in range(64):
            par = gi % 2
            p.op("gpsimd", lambda e, gi=gi, par=par: e.indirect_dma_start(
                    vtok[:, par * D:(par + 1) * D], None, xnd,
                    IndirectOffsetOnAxis(idx2[:, gi:gi + 1], 0)),
                 reads=["l2_idx2", "xnd_w", f"l2_vt{par}"],
                 writes=[f"l2_vt{par}"], dma=True, tag="l2gather")
            trps = PS1.bitcast(BF16)[:, 2048 + par * 1024:
                                     2048 + par * 1024 + 1024]
            tk = pk1(1024 + par * 512, 1536 + par * 512)
            for c in range(NCH):
                p.trans(trps[:, c * 128:(c + 1) * 128],
                        vtok[:, par * D + c * 128: par * D + (c + 1) * 128],
                        env.id128[:, :], reads=[f"l2_vt{par}", "id128"],
                        writes=tk, tag="l2vtr")
            p.copy(xsel[:, par * D:(par + 1) * D], trps[:, :],
                   reads=tk, writes=[f"l2_xs{par}"], tag="l2xsc")
            for eh in range(2):
                es = 2 * gi + eh
                sl = 2 * par + eh
                env.wload(w1_sb[:, sl * NCH * 32:(sl + 1) * NCH * 32]
                          .rearrange("p (c o) -> p c o", c=NCH),
                          W["l2_W1"][es].rearrange("c p o -> p c o"),
                          f"l2_w1{sl}")
                env.wload(w2_sb[0:32, sl * D:(sl + 1) * D], W["l2_W2t"][es],
                          f"l2_w2{sl}")
                env.wload(b2_sb[0:1, sl * D:(sl + 1) * D],
                          W["l2_b2"][es:es + 1, :], f"l2_b2{sl}")
                hps = PS1[0:32, 512 + sl * 64: 512 + sl * 64 + 64]
                hk = pk1(512, 1024)
                for kc in range(NCH):
                    p.mm(hps, w1_sb[:, sl * NCH * 32 + kc * 32:
                                    sl * NCH * 32 + (kc + 1) * 32],
                         xsel[:, par * D + kc * 128 + eh * 64:
                              par * D + kc * 128 + eh * 64 + 64],
                         start=(kc == 0), stop=(kc == NCH - 1),
                         reads=[f"l2_w1{sl}", f"l2_xs{par}"],
                         writes=hk, tag="l2hmm")
                p.act(h_sb[0:32, sl * 64:(sl + 1) * 64], hps, AF.Relu,
                      bias=b1_sb[:, es:es + 1], reads=hk + ["l2_b1s"],
                      writes=[f"l2_h{sl}"], tag="l2hev")
                yps = PS2[0:64, eh * 1024:(eh + 1) * 1024]
                for nh in range(2):
                    yk = pk2(eh * 1024 + nh * 512, eh * 1024 + nh * 512 + 512)
                    p.mm(yps[:, nh * 512:(nh + 1) * 512],
                         h_sb[0:32, sl * 64:(sl + 1) * 64],
                         w2_sb[0:32, sl * D + nh * 512: sl * D + nh * 512 + 512],
                         start=True, stop=False,
                         reads=[f"l2_h{sl}", f"l2_w2{sl}"], writes=yk,
                         tag="l2ymm")
                    p.mm(yps[:, nh * 512:(nh + 1) * 512],
                         env.ones_row[:, 0:64],
                         b2_sb[0:1, sl * D + nh * 512: sl * D + nh * 512 + 512],
                         start=False, stop=True,
                         reads=[f"l2_b2{sl}", "ones_row"], writes=yk,
                         tag="l2yb2")
                p.act(ye_sb[0:64, eh * D:(eh + 1) * D], yps, AF.Copy,
                      reads=pk2(eh * 1024, (eh + 1) * 1024),
                      writes=[f"l2_ye{eh}"], tag="l2yev")
                p.dma("gpsimd", yed[es, :, :], ye_sb[0:64, eh * D:(eh + 1) * D],
                      reads=[f"l2_ye{eh}"], writes=["l2_yed"], tag="l2yout")
        for g in range(64):
            par = g % 2
            p.dma("sync", yeg[:, par * D:(par + 1) * D], yed[:, g, :],
                  reads=["l2_yed", f"l2_yg{par}"], writes=[f"l2_yg{par}"],
                  tag="l2ygl")
            ps = PS1[0:32, par * 1024: par * 1024 + 1024]
            for nh in range(2):
                p.mm(ps[:, nh * 512:(nh + 1) * 512],
                     perm[:, g * 32:(g + 1) * 32],
                     yeg[:, par * D + nh * 512: par * D + nh * 512 + 512],
                     reads=["l2_perm", f"l2_yg{par}"],
                     writes=pk1(par * 1024 + nh * 512,
                                par * 1024 + nh * 512 + 512), tag="l2fmm")
            finale(g, ps, pk1(par * 1024, par * 1024 + 1024))
    p.barrier()


def build_layer2(wshapes):
    nc = bass.Bass("TRN2", target_bir_lowering=False)
    kind_dbg = "ExternalOutput" if DEBUG_DUMP else "Internal"
    x_in = nc.dram_tensor("x_in", [T, D], F32, kind="ExternalInput").ap()
    x_out = nc.dram_tensor("x_out", [T, D], F32, kind="ExternalOutput").ap()
    s_scr2 = nc.dram_tensor("s_scr2", [T, D], BF16, kind=kind_dbg).ap()
    xnd = nc.dram_tensor("xnd", [T, D], BF16, kind=kind_dbg).ap()
    yed = nc.dram_tensor("yed", [128, 64, D], BF16, kind=kind_dbg).ap()
    rscr = nc.dram_tensor("rscr", [16, T], BF16, kind="Internal").ap()
    sodram = nc.dram_tensor("l2_sod", [16, 128, T], BF16, kind=kind_dbg).ap()
    idxd = nc.dram_tensor("l2_idxd", [128, 64], U32, kind="Internal").ap()
    W = {k: nc.dram_tensor(k, list(v), F32, kind="ExternalInput").ap()
         for k, v in wshapes.items()}
    env = Env(nc)
    make_consts(env)
    env.rscr = rscr
    env.l2_sodram = sodram
    env.l2_idxd = idxd
    env.xn_fm = env.sb("xn_fm", [128, NCH * T], BF16)
    emit_ln_phase(env, x_in, xn_tok_dump=xnd)
    p = env.p
    sst = env.sb("ss2_st", [128, 512], BF16)
    stok = env.sb("ss2_tk", [128, 512], BF16)

    def s_sink_tok(dc, qc, ap, rkeys):
        p.tt(sst[:, :], ap,
             env.xn_fm[:, dc * T + qc * 512: dc * T + qc * 512 + 512],
             ALU.add, reads=list(rkeys) + ["xn_fm", "ss2_st"],
             writes=["ss2_st"], tag="ss2_add")
        trps = env.PS1.ap().bitcast(BF16)[:, 0:1024]
        for j in range(4):
            p.trans(trps[:, j * 128:(j + 1) * 128],
                    sst[:, j * 128:(j + 1) * 128], env.id128[:, :],
                    reads=["ss2_st", "id128"], writes=pk1(0, 512),
                    tag="ss2_tr")
        p.copy(stok[:, :], trps[:, 0:512], reads=pk1(0, 512),
               writes=["ss2_tk"], tag="ss2_cp")
        for j in range(4):
            tt = qc * 4 + j
            p.dma("gpsimd", s_scr2.rearrange("(n p) d -> n p d", p=128)
                  [tt, :, dc * 128:(dc + 1) * 128],
                  stok[:, j * 128:(j + 1) * 128],
                  reads=["ss2_tk"], writes=["s2w"], tag="ss2_out")
    emit_l2_attn(env, W, s_sink_tok)
    gb = env.sb("l2f_g", [32, 2 * D], BF16)
    sb2 = env.sb("l2f_s", [32, 2 * D], BF16)
    xb = env.sb("l2f_x", [32, 2 * D], F32)
    xiv = x_in.rearrange("(n p) d -> n p d", p=32)
    xov = x_out.rearrange("(n p) d -> n p d", p=32)
    siv = s_scr2.rearrange("(n p) d -> n p d", p=32)

    def finale(g, ap, rkeys):
        par = g % 2
        p.dma("sync", sb2[:, par * D:(par + 1) * D], siv[g, :, :],
              reads=["s2w", f"l2f_s{par}"], writes=[f"l2f_s{par}"],
              tag="l2f_lds")
        p.dma("sync", xb[:, par * D:(par + 1) * D], xiv[g, :, :],
              reads=[f"l2f_x{par}"], writes=[f"l2f_x{par}"], tag="l2f_ldx")
        p.act(gb[:, par * D:(par + 1) * D], ap, AF.Silu,
              reads=list(rkeys) + [f"l2f_g{par}"], writes=[f"l2f_g{par}"],
              tag="l2f_silu")
        p.tt(gb[:, par * D:(par + 1) * D], gb[:, par * D:(par + 1) * D],
             sb2[:, par * D:(par + 1) * D], ALU.mult,
             reads=[f"l2f_g{par}", f"l2f_s{par}"], writes=[f"l2f_g{par}"],
             tag="l2f_sg")
        p.tt(xb[:, par * D:(par + 1) * D], xb[:, par * D:(par + 1) * D],
             gb[:, par * D:(par + 1) * D], ALU.add,
             reads=[f"l2f_g{par}", f"l2f_x{par}"], writes=[f"l2f_x{par}"],
             tag="l2f_add")
        p.dma("gpsimd", xov[g, :, :], xb[:, par * D:(par + 1) * D],
              reads=[f"l2f_x{par}"], writes=[f"l2f_x{par}"], tag="out_x")
    emit_l2_ffn(env, W, xnd, yed, finale)
    env.p.build()
    return nc


LAYER_BUILDERS[2] = (host_prep_l2, build_layer2)
def host_prep_l0(inp):
    w = {}
    for nm, key in (("Wq", "l0_wq"), ("Wk", "l0_wk"), ("Wv", "l0_wv")):
        w[key] = np.ascontiguousarray(
            np.asarray(inp[f"L0_attn_{nm}"])).reshape(NCH, 128, D)
    Wo = np.asarray(inp["L0_attn_Wo"]).reshape(16, 64, D)
    perm = np.zeros((16, 64, D), Wo.dtype)
    pr = np.empty((NCH, 2), np.int64)
    for h in range(16):
        g, hl = h // 4, h % 4
        ch, rh = 2 * g + (hl & 1), hl >> 1
        perm[2 * ch + rh] = Wo[h]
    w["l0_wo"] = np.ascontiguousarray(perm.reshape(D, D)).reshape(NCH, 128, D)
    w["l0_ab"] = np.ascontiguousarray(np.concatenate(
        [np.asarray(inp["L0_attn_alpha_raw"]).reshape(1, 16),
         np.asarray(inp["L0_attn_beta_raw"]).reshape(1, 16)], 1))
    DEPTH, PAR, NN = 7, 8, 255
    W_in = np.asarray(inp["L0_ffn_W_in"])
    b_in = np.asarray(inp["L0_ffn_b_in"])
    W_out = np.asarray(inp["L0_ffn_W_out"])
    def bitrev(j, nb):
        r = 0
        for i in range(nb):
            r = (r << 1) | ((j >> i) & 1)
        return r
    LOFF = [0, 1, 2, 3, 4, 5, 7, 11]
    NPAD = 19 * 128
    Wp = np.zeros((NPAD, D), np.float32)
    bp = np.zeros((NPAD,), np.float32)
    Wop = np.zeros((NPAD, D), np.float32)
    for d in range(8):
        for j in range(2 ** d):
            node = (2 ** d - 1) + bitrev(j, d)
            for par in range(PAR):
                src = par * NN + node
                dst = LOFF[d] * 128 + j * 8 + par
                Wp[dst] = W_in[src]
                bp[dst] = b_in[src]
                Wop[dst] = W_out[:, src]
    w["l0_winT"] = np.ascontiguousarray(Wp.T).reshape(NCH, 128, NPAD)
    w["l0_bin"] = np.ascontiguousarray(bp.reshape(19, 128).T)
    w["l0_woutT"] = np.ascontiguousarray(Wop).reshape(19, 128, D)
    return w


def l0_make_consts(env):
    sel2 = np.zeros((2, 128), np.float32)
    sel2[0, 0:64] = 1.0
    sel2[1, 64:128] = 1.0
    env.pairsel2 = env.load_const_sb("pairsel2", sel2.astype(bfdt))
    sl = np.tril(np.ones((128, 128), np.float32), -1)
    su = np.triu(np.ones((128, 128), np.float32), 1)
    eye = np.eye(128, dtype=np.float32)
    env.l0_sl4 = env.load_const_sb("l0_sl4", np.tile(sl, (1, 4)).astype(bfdt))
    env.l0_su4 = env.load_const_sb("l0_su4", np.tile(su, (1, 4)).astype(bfdt))
    env.l0_id4 = env.load_const_sb("l0_id4", np.tile(eye, (1, 4)).astype(bfdt))
    env.l0_tree = {}
    for d in range(7):
        nin = 8 * 2 ** d
        nout = 8 * 2 ** (d + 1)
        rows_in = min(128, nin)
        for ot in range((nout + 127) // 128):
            orow = min(128, nout - ot * 128)
            js = range((ot * 128) // 8, (ot * 128 + orow) // 8)
            pj0 = (js.start) % (2 ** d)
            st = (pj0 * 8) // 128
            selp = np.zeros((rows_in, orow), np.float32)
            selge = np.zeros((rows_in, orow), np.float32)
            sellt = np.zeros((rows_in, orow), np.float32)
            for j in js:
                b, pj = j >> d, j % (2 ** d)
                prow = pj * 8 - st * 128
                assert 0 <= prow < rows_in
                for par in range(8):
                    selp[prow + par, (j * 8 + par) - ot * 128] = 1.0
                    (selge if b else sellt)[prow + par,
                                            (j * 8 + par) - ot * 128] = 1.0
            env.l0_tree[(d, ot)] = (st, rows_in,
                                    env.load_const_sb(f"l0_selp{d}_{ot}",
                                                      selp.astype(bfdt)),
                                    env.load_const_sb(f"l0_selge{d}_{ot}",
                                                      selge.astype(bfdt)),
                                    env.load_const_sb(f"l0_sellt{d}_{ot}",
                                                      sellt.astype(bfdt)))


def emit_l0_attn(env, W, s_sink):
    p, nc = env.p, env.nc
    PS1, PS2 = env.PS1.ap(), env.PS2.ap()
    p.barrier()
    from contextlib import ExitStack as _ES
    with _ES() as _es:
        _e = _es.enter_context
        y_fm = _e(nc.sbuf_tensor("l0_y", [128, NCH * T], BF16))
        abrow = _e(nc.sbuf_tensor("l0_abrow", [1, 48], F32))
        abrep = _e(nc.sbuf_tensor("l0_abrep", [128, 32], F32))
        rnb4 = _e(nc.sbuf_tensor("l0_rnb", [1, 32], BF16))
        _es_grp = _ES()
        _eg = _es_grp.enter_context
        qf = _eg(nc.sbuf_tensor("l0_qf", [64, 4 * T], BF16))
        kaf = _eg(nc.sbuf_tensor("l0_kaf", [64, 4 * T], BF16))
        ktok = _eg(nc.sbuf_tensor("l0_kt", [128, 4 * 1024], BF16))
        vtok = _eg(nc.sbuf_tensor("l0_vt", [128, 4 * 1024], BF16))
        wst = _eg(nc.sbuf_tensor("l0_wst", [128, 3 * NCH * 256], BF16))
        ptk = _eg(nc.sbuf_tensor("l0_ptk", [128, 512], BF16))
        sqs = _eg(nc.sbuf_tensor("l0_sqs", [128, 512], BF16))
        nrm = _eg(nc.sbuf_tensor("l0_nrm", [128, 64], F32))
        rnh = _eg(nc.sbuf_tensor("l0_rnh", [128, 64], F32))
        tl_b = _eg(nc.sbuf_tensor("l0_tl", [128, 2048], BF16))
        x_b = _eg(nc.sbuf_tensor("l0_x", [128, 4 * 2048], BF16))
        xt_b = _eg(nc.sbuf_tensor("l0_xt", [128, 2048], BF16))
        pm_b = _eg(nc.sbuf_tensor("l0_pm", [128, 4 * 2048], BF16))
        hbuf = _eg(nc.sbuf_tensor("l0_h", [128, 512], BF16))
        cs_b = _eg(nc.sbuf_tensor("l0_cs", [128, 4 * 64], BF16))
        us_b = _eg(nc.sbuf_tensor("l0_us", [128, 4 * 64], BF16))
        m_b = _eg(nc.sbuf_tensor("l0_m", [64, 4 * 64], BF16))
        p.dma("sync", abrow[0:1, 0:32], W["l0_ab"], writes=["l0_abr"],
              tag="l0ab")
        p.act(abrow[0:1, 0:32], abrow[0:1, 0:32], AF.Sigmoid,
              reads=["l0_abr"], writes=["l0_abs"], tag="l0abs")
        p.ts(abrow[0:1, 0:32], abrow[0:1, 0:32], 0.5, ALU.mult,
             reads=["l0_abs"], writes=["l0_ab2"], tag="l0ab2")
        p.act(abrow[0:1, 32:48], abrow[0:1, 0:16], AF.Sqrt,
              reads=["l0_ab2"], writes=["l0_sqa"], tag="l0sqa")
        p.op("vector", lambda e: e.reciprocal(abrow[0:1, 0:16],
                                              abrow[0:1, 32:48]),
             reads=["l0_sqa"], writes=["l0_rsqa"], tag="l0rsq")
        p.tt(abrow[0:1, 16:32], abrow[0:1, 16:32], abrow[0:1, 0:16], ALU.mult,
             reads=["l0_ab2", "l0_rsqa"], writes=["l0_bp"], tag="l0bp")
        p.act(rnb4[0:1, 0:16], abrow[0:1, 32:48], AF.Copy,
              reads=["l0_sqa"], writes=["l0_sqab"], tag="l0sqcast")
        p.act(rnb4[0:1, 16:32], abrow[0:1, 16:32], AF.Copy,
              reads=["l0_bp"], writes=["l0_bpb"], tag="l0bpcast")
        p.mm(PS2[:, 1536:1552], env.ones_row[:, :], rnb4[0:1, 0:16],
             reads=["l0_sqab", "ones_row"], writes=pk2(1536, 2048),
             tag="l0sqarep")
        p.act(abrep[:, 0:16], PS2[:, 1536:1552], AF.Copy,
              reads=pk2(1536, 2048), writes=["l0_sqarep"], tag="l0sqev")
        p.mm(PS2[:, 1536:1552], env.ones_row[:, :], rnb4[0:1, 16:32],
             reads=["l0_bpb", "ones_row"], writes=pk2(1536, 2048),
             tag="l0bprep")
        p.act(abrep[:, 16:32], PS2[:, 1536:1552], AF.Copy,
              reads=pk2(1536, 2048), writes=["l0_bprep"], tag="l0bpev")
        for g in range(4):
            p.barrier()
            for i, nm in enumerate(("l0_wq", "l0_wk", "l0_wv")):
                env.wload(wst[:, i * 2048:(i + 1) * 2048].rearrange(
                              "p (c o) -> p c o", c=NCH),
                          W[nm][:, :, g * 256:(g + 1) * 256].rearrange(
                              "c p o -> p c o"), "l0_wst")
            for tt in range(NT):
                pr = tt % 2
                psq = PS1[:, pr * 256: pr * 256 + 256]
                for kc in range(NCH):
                    p.mm(psq, env.xn_fm[:, kc * T + tt * 128: kc * T + (tt + 1) * 128],
                         wst[:, 0 * 2048 + kc * 256: 0 * 2048 + (kc + 1) * 256],
                         start=(kc == 0), stop=(kc == NCH - 1),
                         reads=["l0_wst", "xn_fm"], writes=[f"PA0{pr}"],
                         tag="l0pq")
                psk = PS1[:, 512 + pr * 256: 512 + pr * 256 + 256]
                for kc in range(NCH):
                    p.mm(psk, env.xn_fm[:, kc * T + tt * 128: kc * T + (tt + 1) * 128],
                         wst[:, 1 * 2048 + kc * 256: 1 * 2048 + (kc + 1) * 256],
                         start=(kc == 0), stop=(kc == NCH - 1),
                         reads=["l0_wst", "xn_fm"], writes=[f"PA1{pr}"],
                         tag="l0pk")
                psv = PS1[:, 1024 + pr * 256: 1024 + pr * 256 + 256]
                for kc in range(NCH):
                    p.mm(psv, env.xn_fm[:, kc * T + tt * 128: kc * T + (tt + 1) * 128],
                         wst[:, 2 * 2048 + kc * 256: 2 * 2048 + (kc + 1) * 256],
                         start=(kc == 0), stop=(kc == NCH - 1),
                         reads=["l0_wst", "xn_fm"], writes=[f"PA2{pr}"],
                         tag="l0pv")
                p.act(vtok[:, :].rearrange("p (h c o) -> p h c o", h=4, c=NT)
                      [:, :, tt, :],
                      psv.rearrange("p (h o) -> p h o", h=4),
                      AF.Copy, reads=[f"PA2{pr}"], writes=["l0_vt"],
                      tag="l0vev")
                ptkh = ptk[:, pr * 256: pr * 256 + 256]
                sqsh = sqs[:, pr * 256: pr * 256 + 256]
                p.act(ptkh, psk, AF.Copy, reads=[f"PA1{pr}", f"l0_ptk{pr}"],
                      writes=[f"l0_ptk{pr}"], tag="l0kev")
                p.act(sqsh, ptkh, AF.Square, reads=[f"l0_ptk{pr}", f"l0_sqs{pr}"],
                      writes=[f"l0_sqs{pr}"], tag="l0ksq")
                p.red(nrm[:, tt * 4:(tt + 1) * 4],
                      sqsh.rearrange("p (h o) -> p h o", h=4), ALU.add,
                      reads=[f"l0_sqs{pr}"], writes=["l0_nrm"], tag="l0knr")
                p.copy(ktok[:, :].rearrange("p (h c o) -> p h c o", h=4, c=NT)
                       [:, :, tt, :],
                       ptkh.rearrange("p (h o) -> p h o", h=4),
                       reads=[f"l0_ptk{pr}"], writes=["l0_ktraw"], tag="l0kst")
                for hl in range(4):
                    p.trans(PS1.bitcast(BF16)[0:64, 3072 + pr * 512 + hl * 128:
                                              3072 + pr * 512 + hl * 128 + 128],
                            ptkh[:, hl * 64:(hl + 1) * 64].bitcast(BF16)
                            if False else ptkh[:, hl * 64:(hl + 1) * 64],
                            env.id128[:, :],
                            reads=[f"l0_ptk{pr}", "id128"],
                            writes=[f"PA3{pr}"], tag="l0qtr") if False else None
                p.act(ptkh, psq, AF.Copy, reads=[f"PA0{pr}", f"l0_ptk{pr}",
                                                 "l0_ktraw"],
                      writes=[f"l0_ptk{pr}"], tag="l0qev")
                for hl in range(4):
                    p.trans(PS1.bitcast(BF16)[0:64, 3072 + pr * 512 + hl * 128:
                                              3072 + pr * 512 + hl * 128 + 128],
                            ptkh[:, hl * 64:(hl + 1) * 64],
                            env.id128[:, :],
                            reads=[f"l0_ptk{pr}", "id128"],
                            writes=[f"PA3{pr}"], tag="l0qtr")
                p.copy(qf[0:64, :].rearrange("p (h c o) -> p h c o", h=4, c=NT)
                       [:, :, tt, :],
                       PS1.bitcast(BF16)[0:64, 3072 + pr * 512:
                                         3072 + pr * 512 + 512].rearrange(
                           "p (h o) -> p h o", h=4),
                       reads=[f"PA3{pr}"], writes=["l0_qf"], tag="l0qtc")
            p.act(rnh[:, :], nrm[:, :], AF.Sqrt, reads=["l0_nrm"],
                  writes=["l0_rh1"], tag="l0rh1")
            p.ts(rnh[:, :], rnh[:, :], 1e-12, ALU.max, reads=["l0_rh1"],
                 writes=["l0_rh2"], tag="l0rh2")
            p.act(rnh[:, :], rnh[:, :], AF.Ln, reads=["l0_rh2"],
                  writes=["l0_rh3"], tag="l0rh3")
            p.act(rnh[:, :], rnh[:, :], AF.Exp, scale=-1.0, reads=["l0_rh3"],
                  writes=["l0_rh4"], tag="l0rh4")
            for hl in range(4):
                h = 4 * g + hl
                p.ts(rnh[:, :].rearrange("p (c h) -> p c h", c=NT)[:, :, hl],
                     rnh[:, :].rearrange("p (c h) -> p c h", c=NT)[:, :, hl],
                     abrep[:, h:h + 1], ALU.mult,
                     reads=["l0_rh4", "l0_sqarep"], writes=["l0_rh"],
                     tag="l0rh5")
            bpv = bass.AP(abrep, 16 + 4 * g, [[32, 128], [1, 4], [0, 1024]])
            p.tt(vtok[:, :].rearrange("p (h co) -> p h co", h=4),
                 vtok[:, :].rearrange("p (h co) -> p h co", h=4), bpv, ALU.mult,
                 reads=["l0_vt", "l0_bprep"], writes=["l0_vt"], tag="l0vbp")
            for tt in range(NT):
                pr = tt % 2
                ktv = ktok[:, :].rearrange("p (h c o) -> p h c o", h=4, c=NT)
                rnv = bass.AP(rnh, tt * 4,
                              [[64, 128], [1, 4], [0, 64]])
                p.tt(ktv[:, :, tt, :], ktv[:, :, tt, :], rnv, ALU.mult,
                     reads=["l0_ktraw", "l0_rh", "l0_kt"], writes=["l0_kt"],
                     tag="l0kap")
                for hl in range(4):
                    p.trans(PS1.bitcast(BF16)[0:64, 3072 + pr * 512 + hl * 128:
                                              3072 + pr * 512 + hl * 128 + 128],
                            ktv[:, hl, tt, :], env.id128[:, :],
                            reads=["l0_kt", "id128"], writes=[f"PA3{pr}"],
                            tag="l0katr")
                p.copy(kaf[0:64, :].rearrange("p (h c o) -> p h c o", h=4, c=NT)
                       [:, :, tt, :],
                       PS1.bitcast(BF16)[0:64, 3072 + pr * 512:
                                         3072 + pr * 512 + 512].rearrange(
                           "p (h o) -> p h o", h=4),
                       reads=[f"PA3{pr}"], writes=["l0_kaf"], tag="l0katc")
            p.barrier()
            for hl in range(4):
                kfh = kaf[0:64, hl * T:(hl + 1) * T]
                qfh = qf[0:64, hl * T:(hl + 1) * T]
                for cg in range(4):
                    ps = PS1[:, (cg % 2) * 512:(cg % 2) * 512 + 512]
                    pky = pk1((cg % 2) * 512, (cg % 2) * 512 + 512)
                    for j in range(4):
                        c = 4 * cg + j
                        p.mm(ps[:, j * 128:(j + 1) * 128],
                             kfh[:, c * 128:(c + 1) * 128],
                             kfh[:, c * 128:(c + 1) * 128],
                             reads=["l0_kaf"], writes=pky, tag="l0gram")
                    p.tt(hbuf[:, :], ps, env.l0_sl4[:, :], ALU.mult,
                         reads=pky + ["l0_sl4"], writes=["l0_hb"], tag="l0msl")
                    p.tt(tl_b[:, cg * 512:(cg + 1) * 512],
                         hbuf[:, :], env.l0_id4[:, :], ALU.add, engine="gpsimd",
                         reads=["l0_hb", "l0_id4"], writes=["l0_tl"], tag="l0tl")
                    p.stt(xt_b[:, cg * 512:(cg + 1) * 512],
                          hbuf[:, :], -1.0, env.l0_id4[:, :], ALU.mult, ALU.add,
                          reads=["l0_hb", "l0_id4"], writes=["l0_xt"],
                          tag="l0x0t")
                    p.tt(hbuf[:, :], ps, env.l0_su4[:, :], ALU.mult,
                         reads=pky + ["l0_su4"], writes=["l0_hb"], tag="l0msu")
                    p.stt(x_b[:, hl * 2048 + cg * 512: hl * 2048 + (cg + 1) * 512],
                          hbuf[:, :], -1.0, env.l0_id4[:, :], ALU.mult, ALU.add,
                          reads=["l0_hb", "l0_id4"],
                          writes=["l0_x"], tag="l0x0")
                    for j in range(4):
                        c = 4 * cg + j
                        p.mm(ps[:, j * 128:(j + 1) * 128],
                             kfh[:, c * 128:(c + 1) * 128],
                             qfh[:, c * 128:(c + 1) * 128],
                             reads=["l0_kaf", "l0_qf"], writes=pky, tag="l0pq2")
                    p.tt(pm_b[:, hl * 2048 + cg * 512: hl * 2048 + (cg + 1) * 512],
                         ps, env.l0_su4[:, :], ALU.mult,
                         reads=pky + ["l0_su4"], writes=["l0_pm"], tag="l0pmm")
                for it in range(2):
                    for cg in range(4):
                        psH = PS1[:, (cg % 2) * 512:(cg % 2) * 512 + 512]
                        pkH = pk1((cg % 2) * 512, (cg % 2) * 512 + 512)
                        base = hl * 2048 + cg * 512
                        tb = cg * 512
                        for j in range(4):
                            p.mm(psH[:, j * 128:(j + 1) * 128],
                                 tl_b[:, tb + j * 128: tb + (j + 1) * 128],
                                 x_b[:, base + j * 128: base + (j + 1) * 128],
                                 reads=["l0_tl", "l0_x"], writes=pkH,
                                 tag="l0nH")
                        p.act(hbuf[:, :], psH, AF.Copy, reads=pkH,
                              writes=["l0_hb"], tag="l0nHe")
                        psX = PS1[:, 1024 + (cg % 2) * 512:
                                  1024 + (cg % 2) * 512 + 512]
                        pkX = pk1(1024 + (cg % 2) * 512, 1536 + (cg % 2) * 512)
                        for j in range(4):
                            p.mm(psX[:, j * 128:(j + 1) * 128],
                                 xt_b[:, tb + j * 128: tb + (j + 1) * 128],
                                 hbuf[:, j * 128:(j + 1) * 128],
                                 reads=["l0_xt", "l0_hb"], writes=pkX,
                                 tag="l0nXH")
                        p.stt(x_b[:, base:base + 512], x_b[:, base:base + 512],
                              2.0, psX, ALU.mult, ALU.subtract,
                              reads=pkX + ["l0_x"], writes=["l0_x"], tag="l0nX")
                        if it < 1:
                            psT = PS1.bitcast(BF16)[:, (cg % 2) * 1024:
                                                    (cg % 2) * 1024 + 1024]
                            pkT = pk1((cg % 2) * 512, (cg % 2) * 512 + 512)
                            for j in range(4):
                                p.trans(psT[:, j * 256:j * 256 + 128],
                                        x_b[:, base + j * 128:
                                            base + (j + 1) * 128],
                                        env.id128[:, :],
                                        reads=["l0_x", "id128"], writes=pkT,
                                        tag="l0nT")
                            p.copy(xt_b[:, tb:tb + 512].rearrange(
                                       "p (j o) -> p j o", j=4),
                                   psT[:, 0:1024].rearrange(
                                       "p (j o) -> p j o", j=4)[:, :, 0:128],
                                   reads=pkT, writes=["l0_xt"], tag="l0nTe")
            for c in range(NT):
                for hl in range(4):
                    ch = 2 * g + (hl & 1)
                    rh = (hl >> 1) * 64
                    kfh = kaf[0:64, hl * T + c * 128: hl * T + (c + 1) * 128]
                    qfh = qf[0:64, hl * T + c * 128: hl * T + (c + 1) * 128]
                    vslice = vtok[:, (hl * NT + c) * 64:(hl * NT + c + 1) * 64]
                    kslice = ktok[:, (hl * NT + c) * 64:(hl * NT + c + 1) * 64]
                    PSB, c0, pkB = ((PS2, 1024, ["PB2"]) if hl == 0 else
                                    (PS2, 1536, ["PB3"]) if hl == 1 else
                                    (PS1, 1024, ["PA2"]) if hl == 2 else
                                    (PS1, 1536, ["PA3"]))
                    psC = PSB[:, c0: c0 + 64]
                    if c > 0:
                        p.mm(psC, kfh, m_b[0:64, hl * 64:(hl + 1) * 64],
                             reads=["l0_kaf", f"l0_m{hl}"], writes=pkB,
                             tag="l0C2")
                        p.tt(cs_b[:, hl * 64:(hl + 1) * 64], vslice, psC,
                             ALU.subtract, reads=pkB + ["l0_vt"],
                             writes=[f"l0_cs{hl}"], tag="l0Ce")
                    psU = PSB[:, c0 + 64: c0 + 128]
                    p.mm(psU, x_b[:, hl * 2048 + c * 128: hl * 2048 + (c + 1) * 128],
                         vslice if c == 0 else cs_b[:, hl * 64:(hl + 1) * 64],
                         reads=["l0_x", "l0_vt", f"l0_cs{hl}"], writes=pkB,
                         tag="l0U")
                    p.act(us_b[:, hl * 64:(hl + 1) * 64], psU, AF.Copy,
                          reads=pkB + [f"l0_us{hl}"], writes=[f"l0_us{hl}"],
                          tag="l0Ue")
                    psY = PSB[rh:rh + 64, c0 + 128: c0 + 256]
                    if c > 0:
                        p.mm(psY, m_b[0:64, hl * 64:(hl + 1) * 64], qfh,
                             start=True, stop=False,
                             reads=[f"l0_m{hl}", "l0_qf"], writes=pkB,
                             tag="l0Y1")
                    p.mm(psY, us_b[:, hl * 64:(hl + 1) * 64],
                         pm_b[:, hl * 2048 + c * 128: hl * 2048 + (c + 1) * 128],
                         start=(c == 0), stop=True,
                         reads=[f"l0_us{hl}", "l0_pm"], writes=pkB, tag="l0Y2")
                    p.copy(y_fm[rh:rh + 64, ch * T + c * 128: ch * T + (c + 1) * 128],
                           psY, reads=pkB + ["l0_yf"], writes=["l0_yf"],
                           tag="l0Ye")
                    if hl < 2:
                        psM = PS2[0:64, hl * 512: hl * 512 + 64]
                        pkM = [f"PB{hl}"]
                    else:
                        psM = PS1[0:64, (hl - 2) * 512: (hl - 2) * 512 + 64]
                        pkM = [f"PA{hl - 2}"]
                    p.mm(psM, kslice, us_b[:, hl * 64:(hl + 1) * 64],
                         start=(c == 0), stop=(c == NT - 1),
                         reads=["l0_kt", f"l0_us{hl}"], writes=pkM, tag="l0M")
                    if c < NT - 1:
                        p.act(m_b[0:64, hl * 64:(hl + 1) * 64], psM, AF.Copy,
                              reads=pkM + [f"l0_m{hl}"], writes=[f"l0_m{hl}"],
                              tag="l0Me")
        if DEBUG_DUMP:
            for nm_, tns, sh in (("dbg_y", y_fm, [128, NCH * T]),
                                 ("dbg_qf", qf, [64, 4 * T]),
                                 ("dbg_kaf", kaf, [64, 4 * T]),
                                 ("dbg_ktok", ktok, [128, 4096]),
                                 ("dbg_vtok", vtok, [128, 4096]),
                                 ("dbg_xb", x_b, [128, 4 * 2048]),
                                 ("dbg_pm", pm_b, [128, 4 * 2048])):
                dd_ = nc.dram_tensor(nm_, sh, BF16, kind="ExternalOutput").ap()
                p.dma("gpsimd", dd_, tns[0:sh[0], :],
                      reads=["l0_yf", "l0_qf", "l0_kaf", "l0_kt", "l0_vt",
                             "l0_x", "l0_pm"], tag=nm_)
        _es_grp.close()
        p.barrier()
        wo_sb = _e(nc.sbuf_tensor("l0_wo_sb", [128, NCH * D], BF16))
        env.wload(wo_sb[:, :].rearrange("p (c o) -> p c o", c=NCH),
                  W["l0_wo"].rearrange("c p o -> p c o"), "l0_wo")
        for dc in range(NCH):
            for qc in range(4):
                c0 = (qc % 2) * 512
                ps = PS2[:, c0:c0 + 512]
                for kc in range(NCH):
                    p.mm(ps, wo_sb[:, kc * D + dc * 128: kc * D + (dc + 1) * 128],
                         y_fm[:, kc * T + qc * 512: kc * T + qc * 512 + 512],
                         start=(kc == 0), stop=(kc == NCH - 1),
                         reads=["l0_wo", "l0_yf"], writes=pk2(c0, c0 + 512),
                         tag="l0omm")
                s_sink(dc, qc, ps, pk2(c0, c0 + 512))
    p.barrier()


def emit_l0_ffn(env, W, g_sink):
    p, nc = env.p, env.nc
    PS1, PS2 = env.PS1.ap(), env.PS2.ap()
    p.barrier()
    LOFF = [0, 1, 2, 3, 4, 5, 7, 11, 19]
    from contextlib import ExitStack as _ES
    with _ES() as _es:
        _e = _es.enter_context
        acts = _e(nc.sbuf_tensor("l0f_acts", [128, 19 * 512], BF16))
        Fb = _e(nc.sbuf_tensor("l0f_F", [128, 19 * 512], BF16))
        win = _e(nc.sbuf_tensor("l0f_win", [128, 2 * NCH * 128], BF16))
        wout = _e(nc.sbuf_tensor("l0f_wout", [128, 2 * 19 * 128], BF16))
        binc = _e(nc.sbuf_tensor("l0f_bin", [128, 19], F32))
        dd = _e(nc.sbuf_tensor("l0f_dd", [128, 512], BF16))
        ge_sb = _e(nc.sbuf_tensor("l0f_ge", [128, 8 * 512], BF16))
        lt_sb = _e(nc.sbuf_tensor("l0f_lt", [128, 8 * 512], BF16))
        p.dma("sync", binc[:, :], W["l0_bin"], writes=["l0f_bin"], tag="l0fb")
        for tb in range(4):
            c0 = tb * 512
            p.memset(Fb[0:8, 0:512], 1.0, reads=["l0f_F"], writes=["l0f_F"],
                     tag="l0fF0")
            for lt in range(19):
                env.wload(win[:, (lt % 2) * 1024:(lt % 2) * 1024 + 1024]
                          .rearrange("p (c o) -> p c o", c=NCH),
                          W["l0_winT"][:, :, lt * 128:(lt + 1) * 128]
                          .rearrange("c p o -> p c o"), f"l0f_win{lt % 2}")
                ps = PS1[:, (lt % 2) * 512:(lt % 2) * 512 + 512]
                pky = pk1((lt % 2) * 512, (lt % 2) * 512 + 512)
                for kc in range(NCH):
                    p.mm(ps, win[:, (lt % 2) * 1024 + kc * 128:
                                 (lt % 2) * 1024 + (kc + 1) * 128],
                         env.xn_fm[:, kc * T + c0: kc * T + c0 + 512],
                         start=(kc == 0), stop=(kc == NCH - 1),
                         reads=[f"l0f_win{lt % 2}", "xn_fm"], writes=pky,
                         tag="l0flmm")
                p.act(acts[:, lt * 512:(lt + 1) * 512], ps, AF.Silu,
                      bias=binc[:, lt:lt + 1], reads=pky + ["l0f_bin"],
                      writes=["l0f_acts"], tag="l0flev")
            for d in range(7):
                for st in range(max(1, (8 * 2 ** d) // 128)):
                    lt = LOFF[d] + st
                    nrow = min(128, 8 * 2 ** d)
                    p.ts(ge_sb[0:nrow, st * 512:(st + 1) * 512],
                         acts[0:nrow, lt * 512:(lt + 1) * 512], 0.0, ALU.is_ge,
                         reads=["l0f_acts"], writes=["l0f_ge"], tag="l0fge")
                    p.ts(lt_sb[0:nrow, st * 512:(st + 1) * 512],
                         acts[0:nrow, lt * 512:(lt + 1) * 512], 0.0, ALU.is_lt,
                         reads=["l0f_acts"], writes=["l0f_lt"], tag="l0flt")
                nin = 8 * 2 ** d
                nout = 8 * 2 ** (d + 1)
                for ot in range((nout + 127) // 128):
                    orow = min(128, nout - ot * 128)
                    psP = PS1[:, 1024:1536]
                    psD = PS1[:, 1536:2048]
                    st, rows, selp, selg, sell = env.l0_tree[(d, ot)]
                    p.mm(psP[0:orow, 0:512], selp[0:rows, 0:orow],
                         Fb[0:rows, (LOFF[d] + st) * 512:
                            (LOFF[d] + st + 1) * 512],
                         reads=["l0f_F", f"l0_selp{d}_{ot}"],
                         writes=pk1(1024, 1536), tag="l0fP")
                    p.mm(psD[0:orow, 0:512], selg[0:rows, 0:orow],
                         ge_sb[0:rows, st * 512:(st + 1) * 512],
                         start=True, stop=False,
                         reads=["l0f_ge", f"l0_selge{d}_{ot}"],
                         writes=pk1(1536, 2048), tag="l0fDg")
                    p.mm(psD[0:orow, 0:512], sell[0:rows, 0:orow],
                         lt_sb[0:rows, st * 512:(st + 1) * 512],
                         start=False, stop=True,
                         reads=["l0f_lt", f"l0_sellt{d}_{ot}"],
                         writes=pk1(1536, 2048), tag="l0fDl")
                    p.act(dd[0:orow, 0:512], psD[0:orow, 0:512], AF.Copy,
                          reads=pk1(1536, 2048), writes=["l0f_dd"], tag="l0fDe")
                    p.tt(Fb[0:orow, (LOFF[d + 1] + ot) * 512:
                            (LOFF[d + 1] + ot + 1) * 512],
                         psP[0:orow, 0:512], dd[0:orow, 0:512], ALU.mult,
                         reads=pk1(1024, 1536) + ["l0f_dd", "l0f_F"],
                         writes=["l0f_F"], tag="l0fFm")
            for lt in range(19):
                p.tt(acts[:, lt * 512:(lt + 1) * 512],
                     acts[:, lt * 512:(lt + 1) * 512],
                     Fb[:, lt * 512:(lt + 1) * 512], ALU.mult,
                     reads=["l0f_F", "l0f_acts"], writes=["l0f_acts"],
                     tag="l0fmask")
            for dc in range(NCH):
                env.wload(wout[:, (dc % 2) * 19 * 128:
                               (dc % 2) * 19 * 128 + 19 * 128]
                          .rearrange("p (l o) -> p l o", l=19),
                          W["l0_woutT"][:, :, dc * 128:(dc + 1) * 128]
                          .rearrange("l p o -> p l o"), f"l0f_wo{dc % 2}")
                ps = PS2[:, (dc % 2) * 512:(dc % 2) * 512 + 512]
                pky = pk2((dc % 2) * 512, (dc % 2) * 512 + 512)
                for lt in range(19):
                    p.mm(ps, wout[:, (dc % 2) * 19 * 128 + lt * 128:
                                  (dc % 2) * 19 * 128 + (lt + 1) * 128],
                         acts[:, lt * 512:(lt + 1) * 512],
                         start=(lt == 0), stop=(lt == 18),
                         reads=[f"l0f_wo{dc % 2}", "l0f_acts"], writes=pky,
                         tag="l0fomm")
                g_sink(dc, tb, ps, pky)
    p.barrier()


def build_layer0(wshapes):
    nc = bass.Bass("TRN2", target_bir_lowering=False)
    kind_dbg = "ExternalOutput" if DEBUG_DUMP else "Internal"
    x_in = nc.dram_tensor("x_in", [T, D], F32, kind="ExternalInput").ap()
    x_out = nc.dram_tensor("x_out", [T, D], F32, kind="ExternalOutput").ap()
    s_scr = nc.dram_tensor("s_scr", [NCH, 128, T], BF16, kind=kind_dbg).ap()
    rscr = nc.dram_tensor("rscr", [16, T], BF16, kind="Internal").ap()
    W = {k: nc.dram_tensor(k, list(v), F32, kind="ExternalInput").ap()
         for k, v in wshapes.items()}
    env = Env(nc)
    make_consts(env)
    l0_make_consts(env)
    env.rscr = rscr
    env.xn_fm = env.sb("xn_fm", [128, NCH * T], BF16)
    emit_ln_phase(env, x_in)
    if DEBUG_DUMP:
        dxn = nc.dram_tensor("dbg_xn", [128, NCH * T], BF16,
                             kind="ExternalOutput").ap()
        env.p.dma("gpsimd", dxn, env.xn_fm[:, :], reads=["xn_fm"], tag="dxn")
    emit_l0_attn(env, W, make_s_sink(env, s_scr, mode="plain"))
    g_sink = make_residual_streamer(env, x_in, x_out, s_scr, blk=512)
    emit_l0_ffn(env, W, g_sink)
    env.p.build()
    return nc


LAYER_BUILDERS[0] = (host_prep_l0, build_layer0)


def _dedupe_names(nc):
    seen = {}

    def uniq(name):
        n = seen.get(name, 0)
        seen[name] = n + 1
        return name if n == 0 else f"{name}__{n}"
    orig_sb = nc.sbuf_tensor
    nc.sbuf_tensor = lambda name, *a, **kw: orig_sb(uniq(name), *a, **kw)
    orig_asb = nc.alloc_sbuf_tensor
    nc.alloc_sbuf_tensor = lambda name, *a, **kw: orig_asb(uniq(name), *a, **kw)
    orig_it = nc.inline_tensor
    nc.inline_tensor = lambda arr, name=None, **kw: orig_it(
        arr, name=uniq(name), **kw)


def build_fused(wshapes):
    from contextlib import ExitStack
    nc = bass.Bass("TRN2", target_bir_lowering=False)
    _dedupe_names(nc)
    x_in = nc.dram_tensor("x_in", [T, D], F32, kind="ExternalInput").ap()
    x_out = nc.dram_tensor("x_out", [T, D], F32, kind="ExternalOutput").ap()
    x1 = nc.dram_tensor("x_b1", [T, D], F32, kind="Internal").ap()
    x2 = nc.dram_tensor("x_b2", [T, D], F32, kind="Internal").ap()
    x3 = nc.dram_tensor("x_b3", [T, D], F32, kind="Internal").ap()
    s_scr = nc.dram_tensor("s_scr", [NCH, 128, T], BF16, kind="Internal").ap()
    rscr = nc.dram_tensor("rscr", [16, T], BF16, kind="Internal").ap()
    s_scr2 = nc.dram_tensor("s_scr2", [T, D], BF16, kind="Internal").ap()
    xnd = nc.dram_tensor("xnd", [T, D], BF16, kind="Internal").ap()
    yed = nc.dram_tensor("yed", [128, 64, D], BF16, kind="Internal").ap()
    sodram = nc.dram_tensor("l2_sod", [16, 128, T], BF16, kind="Internal").ap()
    idxd = nc.dram_tensor("l2_idxd", [128, 64], U32, kind="Internal").ap()
    W = {k: nc.dram_tensor(k, list(v), F32, kind="ExternalInput").ap()
         for k, v in wshapes.items()}
    env = Env(nc)
    make_consts(env)
    env.rscr = rscr
    env.l2_sodram = sodram
    env.l2_idxd = idxd
    env.xn_fm = env.sb("xn_fm", [128, NCH * T], BF16)
    p = env.p

    env.scope = ExitStack()
    p.set_scope("L0ln")
    l0_make_consts(env)
    emit_ln_phase(env, x_in)
    p.set_scope("L0attn")
    emit_l0_attn(env, W, make_s_sink(env, s_scr, mode="plain"))
    p.set_scope("L0ffn")
    emit_l0_ffn(env, W, make_residual_streamer(env, x_in, x1, s_scr, blk=512))
    p.barrier()
    env.scope.close()
    env.scope = None

    env.scope = ExitStack()
    p.set_scope("L1ln")
    make_consts_l1(env)
    env.scr15 = env.sb("scr15", [15, 128], BF16)
    emit_ln_phase(env, x1)
    p.set_scope("L1attn")
    emit_l1_attn(env, W, make_s_sink(env, s_scr, mode="add_xn"))
    p.set_scope("L1ffn")
    emit_l1_ffn(env, W, make_residual_streamer(env, x1, x2, s_scr))
    p.barrier()
    env.scope.close()
    env.scope = None

    env.scope = ExitStack()
    p.set_scope("L2ln")
    emit_ln_phase(env, x2, xn_tok_dump=xnd)
    sst2 = env.sb("ss2_st", [128, 512], BF16)
    stok = env.sb("ss2_tk", [128, 512], BF16)

    def s_sink_tok(dc, qc, ap, rkeys):
        p.tt(sst2[:, :], ap,
             env.xn_fm[:, dc * T + qc * 512: dc * T + qc * 512 + 512],
             ALU.add, reads=list(rkeys) + ["xn_fm", "ss2_st"],
             writes=["ss2_st"], tag="ss2_add")
        trps = env.PS1.ap().bitcast(BF16)[:, 0:1024]
        for j in range(4):
            p.trans(trps[:, j * 128:(j + 1) * 128],
                    sst2[:, j * 128:(j + 1) * 128], env.id128[:, :],
                    reads=["ss2_st", "id128"], writes=pk1(0, 512),
                    tag="ss2_tr")
        p.copy(stok[:, :], trps[:, 0:512], reads=pk1(0, 512),
               writes=["ss2_tk"], tag="ss2_cp")
        for j in range(4):
            tt = qc * 4 + j
            p.dma("gpsimd", s_scr2.rearrange("(n p) d -> n p d", p=128)
                  [tt, :, dc * 128:(dc + 1) * 128],
                  stok[:, j * 128:(j + 1) * 128],
                  reads=["ss2_tk"], writes=["s2w"], tag="ss2_out")
    p.set_scope("L2attn")
    emit_l2_attn(env, W, s_sink_tok)
    gb2 = env.sb("l2f_g", [32, 2 * D], BF16)
    sb2 = env.sb("l2f_s", [32, 2 * D], BF16)
    xb2 = env.sb("l2f_x", [32, 2 * D], F32)
    xiv2 = x2.rearrange("(n p) d -> n p d", p=32)
    xov2 = x3.rearrange("(n p) d -> n p d", p=32)
    siv2 = s_scr2.rearrange("(n p) d -> n p d", p=32)

    def finale(g, ap, rkeys):
        par = g % 2
        p.dma("sync", sb2[:, par * D:(par + 1) * D], siv2[g, :, :],
              reads=["s2w", f"l2f_s{par}"], writes=[f"l2f_s{par}"],
              tag="l2f_lds")
        p.dma("sync", xb2[:, par * D:(par + 1) * D], xiv2[g, :, :],
              reads=[f"l2f_x{par}"], writes=[f"l2f_x{par}"], tag="l2f_ldx")
        p.act(gb2[:, par * D:(par + 1) * D], ap, AF.Silu,
              reads=list(rkeys) + [f"l2f_g{par}"], writes=[f"l2f_g{par}"],
              tag="l2f_silu")
        p.tt(gb2[:, par * D:(par + 1) * D], gb2[:, par * D:(par + 1) * D],
             sb2[:, par * D:(par + 1) * D], ALU.mult,
             reads=[f"l2f_g{par}", f"l2f_s{par}"], writes=[f"l2f_g{par}"],
             tag="l2f_sg")
        p.tt(xb2[:, par * D:(par + 1) * D], xb2[:, par * D:(par + 1) * D],
             gb2[:, par * D:(par + 1) * D], ALU.add,
             reads=[f"l2f_g{par}", f"l2f_x{par}"], writes=[f"l2f_x{par}"],
             tag="l2f_add")
        p.dma("gpsimd", xov2[g, :, :], xb2[:, par * D:(par + 1) * D],
              reads=[f"l2f_x{par}"], writes=[f"l2f_x{par}"], tag="out_x")
    p.set_scope("L2ffn")
    emit_l2_ffn(env, W, xnd, yed, finale)
    p.barrier()
    env.scope.close()
    env.scope = None

    env.scope = ExitStack()
    p.set_scope("L3ln")
    l3_make_consts(env)
    emit_ln_phase(env, x3)
    p.set_scope("L3attn")
    emit_l3_attn(env, W, make_s_sink(env, s_scr, mode="plain"))
    p.set_scope("L3ffn")
    emit_l3_ffn(env, W, make_residual_streamer(env, x3, x_out, s_scr, blk=256,
                                               trps_region=(0, 512)))
    p.set_scope(None)
    p.build()
    env.scope = None
    return nc


_FUSED_CACHE = {}


def kernel(**inputs):
    x = np.ascontiguousarray(np.asarray(inputs["L0_attn_x"], dtype=np.float32))
    B = x.shape[0]
    w = {}
    for li in (0, 1, 2, 3):
        w.update(LAYER_BUILDERS[li][0](inputs))
    wshapes = {k: v.shape for k, v in w.items()}
    key = tuple(sorted(wshapes.items()))
    if key not in _FUSED_CACHE:
        _FUSED_CACHE[key] = build_fused(wshapes)
    nc = _FUSED_CACHE[key]
    in_maps = []
    for c in range(B):
        m = {"x_in": np.ascontiguousarray(x[c], np.float32)}
        for k, v in w.items():
            m[k] = np.ascontiguousarray(v, np.float32)
        in_maps.append(m)
    res = run_bass_kernel_spmd(nc, in_maps, core_ids=list(range(B)))
    return np.stack([np.asarray(r["x_out"]) for r in res.results]).astype(
        np.float32)


# version 10
# speedup vs baseline: 1.7813x; 1.0555x over previous
import math
import numpy as np
import ml_dtypes
import concourse.bass as bass
import concourse.mybir as mybir
from concourse.bass_utils import run_bass_kernel_spmd

F32 = mybir.dt.float32
BF16 = mybir.dt.bfloat16
U32 = mybir.dt.uint32
AF = mybir.ActivationFunctionType
ALU = mybir.AluOpType
AX = mybir.AxisListType
bfdt = ml_dtypes.bfloat16

ENGINES = ("sync", "scalar", "vector", "tensor", "gpsimd")
T = 2048
D = 1024
NT = 16
NCH = 8


class Prog:
    def __init__(self, nc, same_engine_sync=True):
        self.nc = nc
        self.ops = []
        self.same_engine_sync = same_engine_sync

    def op(self, engine, fn, reads=(), writes=(), dma=False, tag=""):
        assert engine in ENGINES, engine
        self.ops.append(dict(engine=engine, fn=fn, reads=tuple(reads),
                             writes=tuple(writes), dma=bool(dma), tag=tag,
                             bar=self._bar_front, scope=self._scope))
        return len(self.ops) - 1

    _bar_front = None
    _scope = None

    def set_scope(self, name):
        self._scope = name

    def barrier(self):
        last = {}
        for i, o in enumerate(self.ops):
            last[o["engine"]] = i
        self._bar_front = tuple(sorted(last.values()))

    def dma(self, engine, out, in_, reads=(), writes=(), tag="dma", **kw):
        return self.op(engine, lambda e: e.dma_start(out, in_, **kw),
                       reads=reads, writes=writes, dma=True, tag=tag)

    def mm(self, out, lhsT, rhs, start=True, stop=True, reads=(), writes=(),
           tag="mm", **kw):
        return self.op("tensor",
                       lambda e: e.matmul(out, lhsT, rhs, start=start, stop=stop, **kw),
                       reads=reads, writes=writes, tag=tag)

    def trans(self, out, in_, ident, reads=(), writes=(), tag="tr", **kw):
        return self.op("tensor", lambda e: e.transpose(out, in_, ident, **kw),
                       reads=reads, writes=writes, tag=tag)

    def act(self, out, in_, func, bias=0.0, scale=1.0, engine="scalar",
            reads=(), writes=(), tag="act", **kw):
        return self.op(engine,
                       lambda e: e.activation(out, in_, func, bias=bias, scale=scale, **kw),
                       reads=reads, writes=writes, tag=tag)

    def tt(self, out, in0, in1, alu, engine="vector", reads=(), writes=(), tag="tt"):
        return self.op(engine, lambda e: e.tensor_tensor(out, in0, in1, alu),
                       reads=reads, writes=writes, tag=tag)

    def ts(self, out, in0, s1, alu, s2=None, alu2=None, engine="vector",
           reads=(), writes=(), tag="ts"):
        if s2 is None:
            return self.op(engine, lambda e: e.tensor_single_scalar(out, in0, s1, alu),
                           reads=reads, writes=writes, tag=tag)
        return self.op(engine, lambda e: e.tensor_scalar(out, in0, s1, s2, alu, alu2),
                       reads=reads, writes=writes, tag=tag)

    def stt(self, out, in0, scalar, in1, op0, op1, engine="vector",
            reads=(), writes=(), tag="stt"):
        return self.op(engine,
                       lambda e: e.scalar_tensor_tensor(out, in0, scalar, in1, op0, op1),
                       reads=reads, writes=writes, tag=tag)

    def scan(self, out, d0, d1, initial, op0, op1, engine="vector",
             reads=(), writes=(), tag="scan"):
        return self.op(engine,
                       lambda e: e.tensor_tensor_scan(out, d0, d1, initial, op0, op1),
                       reads=reads, writes=writes, tag=tag)

    def red(self, out, in_, alu, engine="vector", negate=False, reads=(), writes=(),
            tag="red"):
        return self.op(engine,
                       lambda e: e.tensor_reduce(out, in_, AX.X, alu, negate=negate),
                       reads=reads, writes=writes, tag=tag)

    def copy(self, out, in_, engine="vector", reads=(), writes=(), tag="cp"):
        return self.op(engine, lambda e: e.tensor_copy(out, in_),
                       reads=reads, writes=writes, tag=tag)

    def memset(self, ap, val, engine="vector", reads=(), writes=(), tag="memset"):
        return self.op(engine, lambda e: e.memset(ap, val),
                       reads=reads, writes=writes, tag=tag)

    def build(self):
        nc = self.nc
        n = len(self.ops)
        last_writer = {}
        readers_since = {}
        deps = [set() for _ in range(n)]
        for i, o in enumerate(self.ops):
            for k in o["reads"]:
                if k in last_writer:
                    deps[i].add(last_writer[k])
            for k in o["writes"]:
                if k in last_writer:
                    deps[i].add(last_writer[k])
                for r in readers_since.get(k, ()):
                    deps[i].add(r)
            for k in o["reads"]:
                readers_since.setdefault(k, []).append(i)
            for k in o["writes"]:
                last_writer[k] = i
                readers_since[k] = []
            if o.get("bar"):
                for q in o["bar"]:
                    deps[i].add(q)
            deps[i].discard(i)
        if not self.same_engine_sync:
            for i, o in enumerate(self.ops):
                if o["engine"] == "tensor" and not o["dma"]:
                    deps[i] = {q for q in deps[i]
                               if self.ops[q]["engine"] != "tensor"
                               or self.ops[q]["dma"]}
        needed = [False] * n
        for i in range(n):
            for q in deps[i]:
                needed[q] = True
        for i, o in enumerate(self.ops):
            if o["dma"]:
                needed[i] = True
        sems = {}
        for e in ENGINES:
            sems[(e, "c")] = nc.alloc_semaphore(name=f"sc_{e}")
            sems[(e, "d")] = nc.alloc_semaphore(name=f"sd_{e}")
        counts = [None] * n
        cum = {k: 0 for k in sems}
        for i, o in enumerate(self.ops):
            if needed[i]:
                k = (o["engine"], "d" if o["dma"] else "c")
                cum[k] += 16 if o["dma"] else 1
                counts[i] = (k, cum[k])
        waits = [dict() for _ in range(n)]
        eng_clock = {e: {} for e in ENGINES}
        op_clock = [None] * n
        for i, o in enumerate(self.ops):
            e = o["engine"]
            cl = eng_clock[e]
            for q in sorted(deps[i]):
                k, v = counts[q]
                if cl.get(k, 0) >= v:
                    continue
                waits[i][k] = max(waits[i].get(k, 0), v)
            for q in sorted(deps[i]):
                k, v = counts[q]
                if waits[i].get(k, 0) >= v and op_clock[q]:
                    for kk, vv in op_clock[q].items():
                        if cl.get(kk, 0) < vv:
                            cl[kk] = vv
            for k, v in waits[i].items():
                cl[k] = max(cl.get(k, 0), v)
            if counts[i] is not None:
                k, v = counts[i]
                snap = dict(cl)
                snap[k] = v
                op_clock[i] = snap
        final_counts = dict(cum)
        per_engine = {e: [] for e in ENGINES}
        for i, o in enumerate(self.ops):
            per_engine[o["engine"]].append(i)

        with nc.Block() as block:
            def mk(e_name):
                def body(eng):
                    for i in per_engine[e_name]:
                        o = self.ops[i]
                        for k, v in sorted(waits[i].items()):
                            eng.wait_ge(sems[k], v)
                        ins = o["fn"](eng)
                        if needed[i]:
                            k, v = counts[i]
                            ins.then_inc(sems[k], 16 if o["dma"] else 1)
                return body

            def gp_body(eng):
                mk("gpsimd")(eng)
                for k, v in final_counts.items():
                    if v > 0 and k != ("gpsimd", "c"):
                        eng.wait_ge(sems[k], v)
            block.sync(mk("sync"))
            block.scalar(mk("scalar"))
            block.vector(mk("vector"))
            block.tensor(mk("tensor"))
            block.gpsimd(gp_body)
        self._nwaits = sum(len(w) for w in waits)
        self._nincs = sum(1 for x in needed if x)
        return self


class Env:
    def __init__(self, nc):
        self.nc = nc
        self.p = Prog(nc, same_engine_sync=False)
        self.scope = None

    def sb(self, name, shape, dt=F32):
        if self.scope is not None:
            return self.scope.enter_context(self.nc.sbuf_tensor(name, shape, dt))
        return self.nc.alloc_sbuf_tensor(name, shape, dt)

    def psum(self, name, shape, dt=F32):
        return self.nc.alloc_psum_tensor(name, shape, dt)

    def const(self, name, arr):
        return self.nc.inline_tensor(np.ascontiguousarray(arr), name=name).ap()

    def load_const_sb(self, name, arr, engine="sync"):
        a = np.ascontiguousarray(arr)
        dt = {np.dtype(np.float32): F32, np.dtype(bfdt): BF16,
              np.dtype(np.uint32): U32}[a.dtype]
        t = self.sb(name, list(a.shape), dt)
        self.p.dma(engine, t[tuple(slice(None) for _ in a.shape)],
                   self.const(name + "_c", a), writes=[name], tag="ldc_" + name)
        return t

    def wload(self, dst_ap, src_ap, key, reads=(), engine="gpsimd"):
        self.p.dma(engine, dst_ap, src_ap, reads=reads, writes=[key],
                   tag="w_" + key)


def pk1(c0, c1):
    return ["PA%d" % b for b in range(c0 // 512, (c1 + 511) // 512)]


def pk2(c0, c1):
    return ["PB%d" % b for b in range(c0 // 512, (c1 + 511) // 512)]


def make_consts_l1(env):
    sel2 = np.zeros((2, 128), np.float32)
    sel2[0, 0:64] = 1.0
    sel2[1, 64:128] = 1.0
    env.pairsel2 = env.load_const_sb("pairsel2", sel2.astype(bfdt))
    ind = np.zeros((15, 15 * 128), np.float32)
    for e in range(15):
        ind[e, e * 128:(e + 1) * 128] = 1.0
    env.ind15 = env.load_const_sb("ind15", ind.astype(bfdt))
    env.ones512 = env.load_const_sb("ones512", np.ones((1, 512), bfdt))
    env.idf = env.load_const_sb("idf", np.eye(128, dtype=np.float32))


def make_consts(env):
    env.id128 = env.load_const_sb("id128", np.eye(128, dtype=bfdt))

    env.sumask = env.load_const_sb(
        "sumask", np.triu(np.ones((128, 128), np.float32), 1).astype(bfdt))
    env.causal_keep = env.load_const_sb(
        "causal_keep", np.triu(np.ones((128, 128), np.float32), 0).astype(bfdt))
    env.ones_row = env.load_const_sb("ones_row", np.ones((1, 128), bfdt))
    env.eps_ln = env.load_const_sb("eps_ln", np.full((128, 1), 1e-5, np.float32))
    env.eps8 = env.load_const_sb("eps8", np.full((128, 1), 1e-8, np.float32))
    env.ones_col128 = env.load_const_sb("ones_col128", np.ones((128, 1), bfdt))
    sel = np.zeros((128, 2), np.float32)
    sel[0:64, 0] = 1.0
    sel[64:128, 1] = 1.0
    env.pairselT = env.load_const_sb("pairselT", sel.astype(bfdt))

    env.scr15b = env.sb("scr15b", [128, 15], BF16)
    env.PS1 = env.psum("PS1", [128, 2048], F32)
    env.PS2 = env.psum("PS2", [128, 2048], F32)


def emit_ln_phase(env, x_dram, xnfm_key="xn_fm", xn_tok_dump=None):
    p, nc = env.p, env.nc
    with (
        nc.sbuf_tensor("g_xtok", [128, NT * D], F32) as x_tok,
        nc.sbuf_tensor("g_xntok", [128, NT * D], BF16) as xn_tok,
        nc.sbuf_tensor("g_lnst", [128, 5 * NT], F32) as st,
        nc.sbuf_tensor("g_lnsq", [128, D], F32) as sq,
    ):
        xv = x_dram.rearrange("(n p) d -> p n d", p=128)
        p.dma("sync", x_tok[:, :].rearrange("p (n d) -> p n d", n=NT), xv,
              writes=["x_tok"], tag="ld_x")
        s1, s2 = st[:, 0:NT], st[:, NT:2 * NT]
        rstd, nb = st[:, 2 * NT:3 * NT], st[:, 3 * NT:4 * NT]
        var = st[:, 4 * NT:5 * NT]
        for tt in range(NT):
            xt = x_tok[:, tt * D:(tt + 1) * D]
            p.op("scalar", lambda e, xt=xt, tt=tt: e.activation(
                    sq[:, :], xt, AF.Square, accum_out=s2[:, tt:tt + 1]),
                 reads=["x_tok"], writes=["ln_sq", "ln_s2"], tag=f"ln_sq{tt}")
            p.op("scalar", lambda e, xt=xt, tt=tt: e.activation(
                    sq[:, :], xt, AF.Copy, accum_out=s1[:, tt:tt + 1]),
                 reads=["x_tok", "ln_sq"], writes=["ln_sq", "ln_s1"],
                 tag=f"ln_s1{tt}")
        p.ts(s1, s1, 1.0 / D, ALU.mult, reads=["ln_s1"], writes=["ln_mean"],
             tag="ln_mean")
        p.tt(var, s1, s1, ALU.mult, reads=["ln_mean"], writes=["ln_m2"],
             tag="ln_m2")
        p.stt(var, s2, 1.0 / D, var, ALU.mult, ALU.subtract,
              reads=["ln_s2", "ln_m2"], writes=["ln_var"], tag="ln_var")
        p.act(rstd, var, AF.Sqrt, bias=env.eps_ln[:, 0:1],
              reads=["ln_var", "eps_ln"], writes=["ln_sd"], tag="ln_sd")
        p.op("vector", lambda e: e.reciprocal(rstd, rstd),
             reads=["ln_sd"], writes=["ln_rstd"], tag="ln_rstd")
        p.stt(nb, s1, -1.0, rstd, ALU.mult, ALU.mult,
              reads=["ln_mean", "ln_rstd"], writes=["ln_nb"], tag="ln_nb")
        trps = env.PS1.ap().bitcast(BF16)
        xnv = env.xn_fm[:, :].rearrange("p (c t) -> p c t", c=NCH)
        for tt in range(NT):
            p.op("scalar", lambda e, tt=tt: e.activation(
                    xn_tok[:, tt * D:(tt + 1) * D], x_tok[:, tt * D:(tt + 1) * D],
                    AF.Identity, bias=nb[:, tt:tt + 1], scale=rstd[:, tt:tt + 1]),
                 reads=["x_tok", "ln_rstd", "ln_nb"], writes=["xn_tok"],
                 tag=f"ln_ap{tt}")
            for c in range(NCH):
                p.trans(trps[:, c * 128:(c + 1) * 128],
                        xn_tok[:, tt * D + c * 128: tt * D + (c + 1) * 128],
                        env.id128[:, :],
                        reads=["xn_tok", "id128"], writes=pk1(0, 512), tag="ln_tr")
            p.copy(xnv[:, :, tt * 128:(tt + 1) * 128],
                   trps[:, 0:1024].rearrange("p (c t) -> p c t", c=NCH),
                   reads=pk1(0, 512), writes=[xnfm_key], tag=f"ln_trc{tt}")
            if xn_tok_dump is not None:
                p.dma("gpsimd",
                      xn_tok_dump.rearrange("(n p) d -> n p d", p=128)[tt],
                      xn_tok[:, tt * D:(tt + 1) * D],
                      reads=["xn_tok"], writes=["xnd_w"], tag="ln_xnd")


def emit_sg_residual(env, x_in_dram, x_out_dram, g_src, s_src, tag="sgr"):
    p, nc = env.p, env.nc
    xiv = x_in_dram.rearrange("(n p) d -> p n d", p=128)
    xov = x_out_dram.rearrange("(n p) d -> p n d", p=128)
    with (
        nc.sbuf_tensor("sg_buf", [128, NCH * 512], BF16) as sgb,
        nc.sbuf_tensor("sg_xt", [128, 2 * D], F32) as xt2,
    ):
        for tb in range(4):
            for dc in range(NCH):
                gap, gr = g_src(dc, tb)
                sap, sr = s_src(dc, tb)
                p.tt(sgb[:, dc * 512:(dc + 1) * 512], gap, sap, ALU.mult,
                     reads=list(gr) + list(sr), writes=["sg_buf"],
                     tag=tag + f"_m{tb}_{dc}")
            for j in range(4):
                tt = tb * 4 + j
                xt = xt2[:, (tt % 2) * D:(tt % 2) * D + D]
                p.dma("sync", xt, xiv[:, tt, :], reads=["sg_xt" + str(tt % 2)],
                      writes=["sg_x" + str(tt % 2)], tag=tag + "_ldx")
                trps = env.PS1.ap().bitcast(BF16)
                for c in range(NCH):
                    p.trans(trps[:, c * 128:(c + 1) * 128],
                            sgb[:, c * 512 + j * 128: c * 512 + (j + 1) * 128],
                            env.id128[:, :],
                            reads=["sg_buf", "id128"], writes=pk1(0, 512),
                            tag=tag + "_tr")
                p.tt(xt, xt, trps[:, 0:1024], ALU.add,
                     reads=pk1(0, 512) + ["sg_x" + str(tt % 2)],
                     writes=["sg_xt" + str(tt % 2)], tag=tag + f"_add{tt}")
                p.dma("gpsimd", xov[:, tt, :], xt,
                      reads=["sg_xt" + str(tt % 2)], writes=["sg_x" + str(tt % 2)],
                      tag="out_x")


def host_prep_l1(inp):
    w = {}
    wq = np.asarray(inp["L1_attn_w_q"])
    wk = np.asarray(inp["L1_attn_w_k"])
    wv = np.asarray(inp["L1_attn_w_v"])
    wo = np.asarray(inp["L1_attn_w_o"])
    w["l1_wq"] = np.ascontiguousarray(
        wq.transpose(1, 0, 2).reshape(D, D)).reshape(NCH, 128, D)
    w["l1_wk"] = np.ascontiguousarray(
        wk.transpose(1, 0, 2).reshape(D, D)).reshape(NCH, 128, D)
    w["l1_wv"] = np.ascontiguousarray(
        wv.transpose(1, 0, 2).reshape(D, D)).reshape(NCH, 128, D)
    w["l1_wo"] = np.ascontiguousarray(wo).reshape(NCH, 128, D)
    w["l1_routerW"] = np.ascontiguousarray(
        np.asarray(inp["L1_ffn_router_W"])).reshape(NCH, 128, 15)
    w["l1_routerb"] = np.ascontiguousarray(
        np.asarray(inp["L1_ffn_router_b"]).reshape(15, 1))
    W1all = np.concatenate([np.asarray(inp["L1_ffn_sW1"])[None],
                            np.asarray(inp["L1_ffn_eW1"])], 0)
    W2all = np.concatenate([np.asarray(inp["L1_ffn_sW2"])[None],
                            np.asarray(inp["L1_ffn_eW2"])], 0)
    b1all = np.concatenate([np.asarray(inp["L1_ffn_sb1"])[None],
                            np.asarray(inp["L1_ffn_eb1"])], 0)
    b2all = np.concatenate([np.asarray(inp["L1_ffn_sb2"])[None],
                            np.asarray(inp["L1_ffn_eb2"])], 0)
    w["l1_W1"] = np.ascontiguousarray(W1all.reshape(16, NCH, 128, D))
    w["l1_W2"] = np.ascontiguousarray(W2all.reshape(16, NCH, 128, D))
    w["l1_b1col"] = np.ascontiguousarray(
        b1all.reshape(16, NCH, 128).transpose(2, 0, 1).reshape(128, 16 * NCH))
    w["l1_eb2"] = np.ascontiguousarray(b2all[1:])
    w["l1_sb2row"] = np.ascontiguousarray(b2all[0:1])
    return w


L1_WSPECS = {
    "l1_wq": (NCH, 128, D), "l1_wk": (NCH, 128, D), "l1_wv": (NCH, 128, D),
    "l1_wo": (NCH, 128, D), "l1_routerW": (NCH, 128, 15), "l1_routerb": (15, 1),
    "l1_W1": (16, NCH, 128, D), "l1_W2": (16, NCH, 128, D),
    "l1_b1col": (128, 16 * NCH), "l1_b2": (16, D),
}


def emit_l1_attn(env, W, s_sink):
    p, nc = env.p, env.nc
    l1_bscr = nc.dram_tensor("l1_bscr", [2], F32, kind="Internal").ap()
    PS1, PS2 = env.PS1.ap(), env.PS2.ap()
    p.barrier()
    with (
        nc.sbuf_tensor("l1_y", [128, NCH * T], BF16) as y_fm,
        nc.sbuf_tensor("l1_sums", [16, T], F32) as sums_all,
        nc.sbuf_tensor("l1_r", [16, T], BF16) as r_bf,
        nc.sbuf_tensor("l1_srow", [128, 1024], F32) as srow,
        nc.sbuf_tensor("l1_rrep", [128, T], BF16) as rrep,
        nc.sbuf_tensor("l1_mcol", [2, 4], F32) as mcol,
        nc.sbuf_tensor("l1_qp", [128, T], BF16) as qp,
        nc.sbuf_tensor("l1_kp", [128, T], BF16) as kp,
        nc.sbuf_tensor("l1_vp", [128, NT * 130], BF16) as vp,
        nc.sbuf_tensor("l1_wp", [128, 3 * NCH * 128], BF16) as wpair,
        nc.sbuf_tensor("l1_pt", [128, 2048], BF16) as ptil,
        nc.sbuf_tensor("l1_sqt", [128, T], BF16) as sqt,

        nc.sbuf_tensor("l1_bh", [128, 2], F32) as bhneg,
        nc.sbuf_tensor("l1_br", [2, 2], F32) as brow2,
        nc.sbuf_tensor("l1_brb", [1, 2], BF16) as brow_bf,
        nc.sbuf_tensor("l1_wo_sb", [128, NCH * D], BF16) as wo_sb,
    ):
        env.wload(wo_sb[:, :].rearrange("p (c o) -> p c o", c=NCH),
                  W["l1_wo"].rearrange("c p o -> p c o"), "l1_wo")
        for hp in range(8):
            for i, nm in enumerate(("l1_wq", "l1_wk", "l1_wv")):
                env.wload(
                    wpair[:, i * 1024:(i + 1) * 1024].rearrange(
                        "p (c o) -> p c o", c=NCH),
                    W[nm][:, :, hp * 128:(hp + 1) * 128].rearrange(
                        "c p o -> p c o"),
                    f"l1_wpair{i}")
            for wi, (dst, dk) in enumerate(((qp, "l1_qp"), (kp, "l1_kp"))):
                for qc in range(4):
                    c0 = (qc % 2) * 512
                    ps = PS1[:, c0:c0 + 512]
                    for kc in range(NCH):
                        p.mm(ps,
                             wpair[:, wi * 1024 + kc * 128: wi * 1024 + (kc + 1) * 128],
                             env.xn_fm[:, kc * T + qc * 512: kc * T + qc * 512 + 512],
                             start=(kc == 0), stop=(kc == NCH - 1),
                             reads=[f"l1_wpair{wi}", "xn_fm"],
                             writes=pk1(c0, c0 + 512), tag=f"l1{dk}mm")
                    p.act(dst[:, qc * 512:(qc + 1) * 512], ps, AF.Copy,
                          reads=pk1(c0, c0 + 512), writes=[dk], tag=f"l1{dk}ev")
            for tt in range(NT):
                c0 = 1024 + (tt % 2) * 512
                ps = PS1[:, c0:c0 + 128]
                for kc in range(NCH):
                    p.mm(ps, env.xn_fm[:, kc * T + tt * 128: kc * T + (tt + 1) * 128],
                         wpair[:, 2048 + kc * 128: 2048 + (kc + 1) * 128],
                         start=(kc == 0), stop=(kc == NCH - 1),
                         reads=["l1_wpair2", "xn_fm"], writes=pk1(c0, c0 + 128),
                         tag=f"l1vmm{tt}")
                p.copy(vp[:, tt * 130:(tt + 1) * 130].rearrange(
                           "p (h j) -> p h j", h=2)[:, :, 0:64],
                       ps.rearrange("p (h j) -> p h j", h=2),
                       reads=pk1(c0, c0 + 128), writes=["l1_vp"],
                       tag=f"l1vev{tt}")
            p.memset(vp[:, :].rearrange("p (n h j) -> p n h j", n=NT, h=2)
                     [:, :, :, 64:65],
                     1.0, reads=["l1_vp"], writes=["l1_vp"], tag="l1_vones")
            for o, (srcb, sk) in enumerate(((qp, "l1_qp"), (kp, "l1_kp"))):
                p.act(sqt[:, :], srcb[:, :], AF.Square, reads=[sk],
                      writes=["l1_sqt"], tag=f"l1sq{o}")
                for qc in range(4):
                    ps = PS2[0:2, 1536:2048]
                    p.mm(ps, env.pairselT[:, :], sqt[:, qc * 512:(qc + 1) * 512],
                         reads=["l1_sqt", "pairselT"], writes=pk2(1536, 2048),
                         tag=f"l1nrm{o}")
                    p.red(mcol[:, qc:qc + 1], ps, ALU.max,
                          reads=pk2(1536, 2048), writes=["l1_mc"],
                          tag=f"l1nrmr{o}")
                p.red(brow2[:, o:o + 1], mcol[:, 0:4], ALU.max,
                      reads=["l1_mc"], writes=["l1_brq" if o == 0 else "l1_brk"],
                      tag=f"l1b{o}")
            p.tt(brow2[:, 0:1], brow2[:, 0:1], brow2[:, 1:2], ALU.mult,
                 reads=["l1_brq", "l1_brk"], writes=["l1_bp"], tag="l1bp")
            p.act(brow2[:, 0:1], brow2[:, 0:1], AF.Sqrt, reads=["l1_bp"],
                  writes=["l1_bs"], tag="l1bs")
            p.ts(brow2[:, 0:1], brow2[:, 0:1], -0.125, ALU.mult,
                 reads=["l1_bs"], writes=["l1_bn"], tag="l1bn")
            p.dma("gpsimd", l1_bscr, brow2[:, 0:1],
                  reads=["l1_bn", "l1_bh"], writes=["l1_bscrd"],
                  tag="l1btr")
            p.dma("sync", bhneg[:, 0:2],
                  bass.AP(l1_bscr.tensor, 0, [[0, 128], [1, 2]]),
                  reads=["l1_bscrd", "l1_bh"], writes=["l1_bh"],
                  tag="l1brep")
            for h in range(2):
                hg = 2 * hp + h
                for qh in range(2):
                    pi = (2 * h + qh) % 2
                    a0 = pi * 1024
                    av = PS2[0:65, a0:a0 + 1024]
                    for kt in range(NT):
                        s0 = (kt % 2) * 1024
                        pth = ptil[:, s0:s0 + 1024]
                        sc = PS1[:, s0:s0 + 1024]
                        for q2 in range(2):
                            p.mm(sc[:, q2 * 512:(q2 + 1) * 512],
                                 kp[64 * h:64 * h + 64, kt * 128:(kt + 1) * 128],
                                 qp[64 * h:64 * h + 64,
                                    qh * 1024 + q2 * 512: qh * 1024 + q2 * 512 + 512],
                                 reads=["l1_qp", "l1_kp"],
                                 writes=pk1(s0 + q2 * 512, s0 + q2 * 512 + 512),
                                 tag=f"l1sc{hg}")
                        p.act(pth, sc, AF.Exp, scale=0.125,
                              bias=bhneg[:, h:h + 1],
                              reads=pk1(s0, s0 + 1024) + ["l1_bh",
                                                          f"l1_pt{kt % 2}"],
                              writes=[f"l1_pt{kt % 2}"], tag=f"l1exp{hg}")
                        for q2 in range(2):
                            p.mm(av[:, q2 * 512:(q2 + 1) * 512],
                                 vp[:, kt * 130 + 65 * h: kt * 130 + 65 * h + 65],
                                 pth[:, q2 * 512:(q2 + 1) * 512],
                                 start=(kt == 0), stop=(kt == NT - 1),
                                 reads=[f"l1_pt{kt % 2}", "l1_vp"],
                                 writes=pk2(a0 + q2 * 512, a0 + q2 * 512 + 512),
                                 tag=f"l1av{hg}")
                    p.act(srow[64:65, :], av[64:65, :], AF.Copy,
                          reads=pk2(a0, a0 + 1024) + ["l1_srow"],
                          writes=["l1_srow"], tag=f"l1sumst{hg}")
                    p.dma("gpsimd",
                          sums_all[hg:hg + 1, qh * 1024: qh * 1024 + 1024],
                          srow[64:65, :], reads=["l1_srow"],
                          writes=["l1_sums", "l1_srow"], tag=f"l1sum{hg}")
                    p.act(y_fm[64 * h:64 * h + 64,
                               hp * T + qh * 1024: hp * T + qh * 1024 + 1024],
                          av[0:64, :], AF.Copy, reads=pk2(a0, a0 + 1024),
                          writes=["l1_yraw"], tag=f"l1yev{hg}")
        p.act(sums_all[:, :], sums_all[:, :], AF.Ln, reads=["l1_sums"],
              writes=["l1_lns"], tag="l1_lns")
        p.act(r_bf[:, :], sums_all[:, :], AF.Exp, scale=-1.0,
              reads=["l1_lns"], writes=["l1_r"], tag="l1_rinv")
        p.dma("gpsimd", env.rscr, r_bf[:, :], reads=["l1_r"],
              writes=["l1_rscr"], tag="l1rst")
        for c in range(NCH):
            for h in range(2):
                srcap = bass.AP(env.rscr.tensor, (2 * c + h) * T,
                                [[0, 64], [1, T]])
                p.dma("sync", rrep[64 * h:64 * h + 64, :], srcap,
                      reads=["l1_rscr", "l1_rrep"], writes=["l1_rrep"],
                      tag="l1rbc")
            p.tt(y_fm[:, c * T:(c + 1) * T], y_fm[:, c * T:(c + 1) * T],
                 rrep[:, :], ALU.mult,
                 reads=["l1_rrep", "l1_yraw"], writes=["l1_y"],
                 tag=f"l1ynrm{c}")
        for dc in range(NCH):
            for qc in range(4):
                c0 = (qc % 2) * 512
                ps = PS2[:, c0:c0 + 512]
                for kc in range(NCH):
                    p.mm(ps, wo_sb[:, kc * D + dc * 128: kc * D + (dc + 1) * 128],
                         y_fm[:, kc * T + qc * 512: kc * T + qc * 512 + 512],
                         start=(kc == 0), stop=(kc == NCH - 1),
                         reads=["l1_wo", "l1_y"], writes=pk2(c0, c0 + 512),
                         tag="l1omm")
                s_sink(dc, qc, ps, pk2(c0, c0 + 512))
    p.barrier()


def emit_l1_ffn(env, W, g_sink):
    p, nc = env.p, env.nc
    PS1, PS2 = env.PS1.ap(), env.PS2.ap()
    p.barrier()
    with (
        nc.sbuf_tensor("l1_racc", [128, NCH * T], F32) as racc,
        nc.sbuf_tensor("l1_cw", [15, T], BF16) as cw_fm,
        nc.sbuf_tensor("l1_cwr", [128, 512], BF16) as cwrep,
        nc.sbuf_tensor("l1_hsb", [128, NCH * 512], BF16) as h_sb,
        nc.sbuf_tensor("l1_eW", [128, 2 * NCH * D], BF16) as eW,
        nc.sbuf_tensor("l1_b1c_sb", [128, 16 * NCH], F32) as b1c,
        nc.sbuf_tensor("l1_eb2_sb", [15, D], BF16) as eb2_sb,
        nc.sbuf_tensor("l1_sb2_sb", [1, D], BF16) as sb2_sb,
        nc.sbuf_tensor("l1_rt", [128, 65], F32) as rt,
        nc.sbuf_tensor("l1_s15f", [15, 128], F32) as scr15f,
        nc.sbuf_tensor("l1_rtb", [15, 1], F32) as rb_sb,
        nc.sbuf_tensor("l1_rW", [128, NCH * 16], BF16) as rW_sb,
    ):
        env.wload(b1c[:, :], W["l1_b1col"], "l1_b1c")
        env.wload(eb2_sb[:, :], W["l1_eb2"], "l1_eb2")
        env.wload(sb2_sb[:, :], W["l1_sb2row"], "l1_sb2")
        env.wload(rW_sb[:, :].rearrange("p (c o) -> p c o", c=NCH)[:, :, 0:15],
                  W["l1_routerW"].rearrange("c p o -> p c o"), "l1_rW")
        p.dma("sync", rb_sb[:, :], W["l1_routerb"], writes=["l1_rb"], tag="l1rbld")
        for qc in range(4):
            ps = PS1[0:15, 0:512]
            for kc in range(NCH):
                p.mm(ps, rW_sb[:, kc * 16: kc * 16 + 15],
                     env.xn_fm[:, kc * T + qc * 512: kc * T + qc * 512 + 512],
                     start=(kc == 0), stop=(kc == NCH - 1),
                     reads=["l1_rW", "xn_fm"], writes=pk1(0, 512), tag="l1rlmm")
            for j in range(4):
                tt = qc * 4 + j
                p.act(scr15f[0:15, 0:128], ps[0:15, j * 128:(j + 1) * 128],
                      AF.Identity, bias=rb_sb[:, 0:1],
                      reads=pk1(0, 512) + ["l1_rb"], writes=["scr15f"],
                      tag="l1rl15")
                trp = PS2[:, 1536:1664]
                p.trans(trp[0:128, 0:15], scr15f[0:15, 0:128],
                        env.idf[0:15, 0:15],
                        reads=["scr15f", "idf"], writes=pk2(1536, 2048),
                        tag="l1rtr")
                p.copy(rt[:, 0:15], trp[0:128, 0:15],
                       reads=pk2(1536, 2048), writes=["l1_rt"], tag="l1rtc")
                p.red(rt[:, 16:17], rt[:, 0:15], ALU.max, negate=True,
                      reads=["l1_rt"], writes=["l1_rtm"], tag="l1sm_m")
                p.op("scalar", lambda e: e.activation(
                        rt[:, 0:15], rt[:, 0:15], AF.Exp,
                        bias=rt[:, 16:17], accum_out=rt[:, 17:18]),
                     reads=["l1_rt", "l1_rtm"], writes=["l1_rt", "l1_rts"],
                     tag="l1sm_e")
                p.op("vector", lambda e: e.reciprocal(rt[:, 18:19], rt[:, 17:18]),
                     reads=["l1_rts"], writes=["l1_rtr"], tag="l1sm_r")
                p.ts(rt[:, 0:15], rt[:, 0:15], rt[:, 18:19], ALU.mult,
                     reads=["l1_rt", "l1_rtr"], writes=["l1_rt"], tag="l1sm_p")
                p.red(rt[:, 19:20], rt[:, 0:15], ALU.max, reads=["l1_rt"],
                      writes=["l1_m1"], tag="l1t_m1")
                p.ts(rt[:, 20:35], rt[:, 0:15], rt[:, 19:20], ALU.is_ge,
                     reads=["l1_rt", "l1_m1"], writes=["l1_e1"], tag="l1t_e1")
                p.tt(rt[:, 35:50], rt[:, 0:15], rt[:, 20:35], ALU.subtract,
                     reads=["l1_rt", "l1_e1"], writes=["l1_z1"], tag="l1t_z1")
                p.red(rt[:, 19:20], rt[:, 35:50], ALU.max, reads=["l1_z1"],
                      writes=["l1_m2"], tag="l1t_m2")
                p.ts(rt[:, 50:65], rt[:, 35:50], rt[:, 19:20], ALU.is_ge,
                     reads=["l1_z1", "l1_m2"], writes=["l1_e2"], tag="l1t_e2")
                p.tt(rt[:, 35:50], rt[:, 35:50], rt[:, 50:65], ALU.subtract,
                     reads=["l1_z1", "l1_e2"], writes=["l1_z2"], tag="l1t_z2")
                p.red(rt[:, 19:20], rt[:, 35:50], ALU.max, reads=["l1_z2"],
                      writes=["l1_m3"], tag="l1t_m3")
                p.ts(rt[:, 35:50], rt[:, 35:50], rt[:, 19:20], ALU.is_ge,
                     reads=["l1_z2", "l1_m3"], writes=["l1_e3"], tag="l1t_e3")
                p.tt(rt[:, 50:65], rt[:, 50:65], rt[:, 35:50], ALU.add,
                     reads=["l1_e2", "l1_e3"], writes=["l1_e23"], tag="l1t_e23")
                p.tt(rt[:, 50:65], rt[:, 50:65], rt[:, 20:35], ALU.add,
                     reads=["l1_e23", "l1_e1"], writes=["l1_e123"], tag="l1t_s")
                p.tt(rt[:, 50:65], rt[:, 50:65], rt[:, 0:15], ALU.mult,
                     reads=["l1_e123", "l1_rt"], writes=["l1_cwt"], tag="l1t_cw")
                p.copy(env.scr15b[0:128, 0:15], rt[:, 50:65],
                       reads=["l1_cwt"], writes=["scr15b"], tag="l1cwb")
                p.trans(PS2.bitcast(BF16)[0:15, 3072:3200],
                        env.scr15b[0:128, 0:15], env.id128[:, :],
                        reads=["scr15b", "id128"], writes=pk2(1536, 2048),
                        tag="l1cwtr")
                p.copy(cw_fm[0:15, tt * 128:(tt + 1) * 128],
                       PS2.bitcast(BF16)[0:15, 3072:3200],
                       reads=pk2(1536, 2048), writes=["l1_cw"], tag="l1cwc")
        for e in range(16):
            env.wload(eW[:, 0:NCH * D].rearrange("p (c o) -> p c o", c=NCH),
                      W["l1_W1"][e].rearrange("c p o -> p c o"), "l1_eW1")
            env.wload(eW[:, NCH * D:2 * NCH * D].rearrange("p (c o) -> p c o", c=NCH),
                      W["l1_W2"][e].rearrange("c p o -> p c o"), "l1_eW2")
            for tb in range(4):
                if e > 0:
                    p.mm(PS2[:, 1536:2048],
                         env.ind15[:, (e - 1) * 128: e * 128],
                         cw_fm[0:15, tb * 512:(tb + 1) * 512],
                         reads=["l1_cw", "ind15"], writes=pk2(1536, 2048),
                         tag="l1cwrep")
                    p.copy(cwrep[:, :], PS2[:, 1536:2048],
                           reads=pk2(1536, 2048), writes=["l1_cwr"],
                           tag="l1cwrc")
                for hc in range(NCH):
                    c0 = (hc % 2) * 512
                    ps = PS1[:, c0:c0 + 512]
                    for kc in range(NCH):
                        p.mm(ps, eW[:, kc * D + hc * 128: kc * D + (hc + 1) * 128],
                             env.xn_fm[:, kc * T + tb * 512: kc * T + tb * 512 + 512],
                             start=(kc == 0), stop=(kc == NCH - 1),
                             reads=["l1_eW1", "xn_fm"], writes=pk1(c0, c0 + 512),
                             tag="l1hmm")
                    p.act(h_sb[:, hc * 512:(hc + 1) * 512], ps, AF.Relu,
                          bias=b1c[:, e * NCH + hc: e * NCH + hc + 1],
                          reads=pk1(c0, c0 + 512) + ["l1_b1c"], writes=["l1_hs"],
                          tag="l1hev")
                    if e > 0:
                        p.tt(h_sb[:, hc * 512:(hc + 1) * 512],
                             h_sb[:, hc * 512:(hc + 1) * 512], cwrep[:, :],
                             ALU.mult, reads=["l1_hs", "l1_cwr"],
                             writes=["l1_h"], tag="l1hm")
                    else:
                        p.copy(h_sb[:, hc * 512:(hc + 1) * 512],
                               h_sb[:, hc * 512:(hc + 1) * 512],
                               reads=["l1_hs"], writes=["l1_h"], tag="l1hm0")
                for dc in range(NCH):
                    c0 = 1024 + (dc % 2) * 512
                    ps = PS1[:, c0:c0 + 512]
                    for hc in range(NCH):
                        p.mm(ps, eW[:, NCH * D + hc * D + dc * 128:
                                     NCH * D + hc * D + (dc + 1) * 128],
                             h_sb[:, hc * 512:(hc + 1) * 512],
                             start=(hc == 0), stop=(hc == NCH - 1),
                             reads=["l1_eW2", "l1_h"], writes=pk1(c0, c0 + 512),
                             tag="l1ymm")
                    dst = racc[:, dc * T + tb * 512: dc * T + tb * 512 + 512]
                    if e == 0:
                        p.copy(dst, ps, reads=pk1(c0, c0 + 512),
                               writes=["l1_racc"], tag="l1rc0")
                    else:
                        p.tt(dst, dst, ps, ALU.add,
                             reads=pk1(c0, c0 + 512) + ["l1_racc"],
                             writes=["l1_racc"], tag="l1rca")
        for dc in range(NCH):
            for qc in range(4):
                c0 = (qc % 2) * 512
                ps = PS1[:, c0:c0 + 512]
                p.mm(ps, eb2_sb[:, dc * 128:(dc + 1) * 128],
                     cw_fm[0:15, qc * 512:(qc + 1) * 512],
                     start=True, stop=False,
                     reads=["l1_eb2", "l1_cw"], writes=pk1(c0, c0 + 512),
                     tag="l1b2mm")
                p.mm(ps, sb2_sb[:, dc * 128:(dc + 1) * 128],
                     env.ones512[:, :], start=False, stop=True,
                     reads=["l1_sb2", "ones512"], writes=pk1(c0, c0 + 512),
                     tag="l1sb2mm")
                dst = racc[:, dc * T + qc * 512: dc * T + qc * 512 + 512]
                p.tt(dst, dst, ps, ALU.add,
                     reads=pk1(c0, c0 + 512) + ["l1_racc"], writes=["l1_racc"],
                     tag="l1b2a")
        if hasattr(env, "dbg_hook_ffn"):
            p.dma("gpsimd", env.dbg_hook_ffn, racc[:, :], reads=["l1_racc"],
                  tag="dracc")
            p.dma("gpsimd", env.dbg_cw, cw_fm[:, :], reads=["l1_cw"], tag="dcw")
        for tb in range(4):
            for dc in range(NCH):
                g_sink(dc, tb, racc[:, dc * T + tb * 512: dc * T + tb * 512 + 512],
                       ["l1_racc"])
    p.barrier()
def make_residual_streamer(env, x_in_dram, x_out_dram, s_scr, blk=512,
                           trps_region=None):
    p, nc = env.p, env.nc
    sgb = env.sb("rs_sgb", [128, NCH * blk], BF16)
    sld = env.sb("rs_sld", [128, 2 * blk], BF16)
    gtmp = env.sb("rs_gt", [128, 2 * blk], BF16)
    xt2 = env.sb("rs_xt", [128, D], F32)
    xiv = x_in_dram.rearrange("(n p) d -> p n d", p=128)
    xov = x_out_dram.rearrange("(n p) d -> p n d", p=128)
    ntile = blk // 128

    def g_sink(dc, tb, ap, rkeys):
        par = dc % 2
        sl = sld[:, par * blk:(par + 1) * blk]
        gt = gtmp[:, par * blk:(par + 1) * blk]
        p.dma("sync", sl, s_scr[dc, :, tb * blk:(tb + 1) * blk],
              reads=[f"rs_sld{par}"], writes=[f"rs_sld{par}", "s_scr_r"],
              tag="rs_lds")
        p.act(gt, ap, AF.Silu, reads=list(rkeys), writes=[f"rs_gt{par}"],
              tag="rs_silu")
        p.tt(sgb[:, dc * blk:(dc + 1) * blk], gt, sl, ALU.mult,
             reads=[f"rs_gt{par}", f"rs_sld{par}"], writes=["rs_sgb"],
             tag="rs_sg")
        if dc == NCH - 1:
            reg = trps_region if trps_region is not None else (0, 512)
            trps = env.PS1.ap().bitcast(BF16)[:, reg[0] * 2:reg[0] * 2 + 1024]
            for j in range(ntile):
                tt = tb * ntile + j
                xt = xt2[:, :]
                p.dma("sync", xt, xiv[:, tt, :], reads=["rs_x0"],
                      writes=["rs_x0"], tag="rs_ldx")
                for c in range(NCH):
                    p.trans(trps[:, c * 128:(c + 1) * 128],
                            sgb[:, c * blk + j * 128: c * blk + (j + 1) * 128],
                            env.id128[:, :],
                            reads=["rs_sgb", "id128"],
                            writes=pk1(reg[0], reg[1]), tag="rs_tr")
                p.tt(xt, xt, trps[:, 0:1024], ALU.add,
                     reads=pk1(reg[0], reg[1]) + ["rs_x0"],
                     writes=["rs_x0"], tag="rs_add")
                p.dma("gpsimd", xov[:, tt, :], xt,
                      reads=["rs_x0"], writes=["rs_x0"], tag="out_x")
    return g_sink


def make_s_sink(env, s_scr, mode="add_xn"):
    p = env.p
    sst = env.sb("ss_st", [128, 512], BF16)

    def s_sink(dc, qc, ap, rkeys):
        st = sst[:, 0:512]
        if mode == "add_xn":
            p.tt(st, ap, env.xn_fm[:, dc * T + qc * 512: dc * T + qc * 512 + 512],
                 ALU.add, reads=list(rkeys) + ["xn_fm", "ss_st"],
                 writes=["ss_st"], tag="ss_add")
        else:
            p.copy(st, ap, reads=list(rkeys) + ["ss_st"],
                   writes=["ss_st"], tag="ss_cp")
        p.dma("gpsimd", s_scr[dc, :, qc * 512:(qc + 1) * 512], st,
              reads=["ss_st"], writes=["ss_st", "s_scr_w"], tag="ss_out")
    return s_sink


DEBUG_DUMP = False


def build_layer1(wshapes):
    nc = bass.Bass("TRN2", target_bir_lowering=False)
    kind_dbg = "ExternalOutput" if DEBUG_DUMP else "Internal"
    x_in = nc.dram_tensor("x_in", [T, D], F32, kind="ExternalInput").ap()
    x_out = nc.dram_tensor("x_out", [T, D], F32, kind="ExternalOutput").ap()
    s_scr = nc.dram_tensor("s_scr", [NCH, 128, T], BF16, kind=kind_dbg).ap()
    rscr = nc.dram_tensor("rscr", [16, T], BF16, kind=kind_dbg).ap()
    W = {k: nc.dram_tensor(k, list(v), F32, kind="ExternalInput").ap()
         for k, v in wshapes.items()}
    env = Env(nc)
    make_consts(env)
    make_consts_l1(env)
    env.scr15 = env.sb("scr15", [15, 128], BF16)
    env.rscr = rscr
    env.xn_fm = env.sb("xn_fm", [128, NCH * T], BF16)
    emit_ln_phase(env, x_in)
    if DEBUG_DUMP:
        dxn = nc.dram_tensor("dbg_xn", [128, NCH * T], BF16,
                             kind="ExternalOutput").ap()
        env.p.dma("gpsimd", dxn, env.xn_fm[:, :], reads=["xn_fm"], tag="dxn")
    emit_l1_attn(env, W, make_s_sink(env, s_scr, mode="add_xn"))
    if DEBUG_DUMP:
        env.dbg_hook_ffn = nc.dram_tensor("dbg_racc", [128, NCH * T], F32,
                                          kind="ExternalOutput").ap()
        env.dbg_cw = nc.dram_tensor("dbg_cw", [15, T], BF16,
                                    kind="ExternalOutput").ap()
    g_sink = make_residual_streamer(env, x_in, x_out, s_scr)
    emit_l1_ffn(env, W, g_sink)
    env.p.build()
    return nc


LAYER_BUILDERS = {1: (host_prep_l1, build_layer1)}
_BUILD_CACHE = {}


def run_layer(li, x_np, inputs, cores=8):
    prep, builder = LAYER_BUILDERS[li]
    w = prep(inputs)
    wshapes = {k: v.shape for k, v in w.items()}
    key = (li, tuple(sorted(wshapes.items())))
    if key not in _BUILD_CACHE:
        _BUILD_CACHE[key] = builder(wshapes)
    nc = _BUILD_CACHE[key]
    in_maps = []
    for c in range(cores):
        m = {"x_in": np.ascontiguousarray(x_np[c], dtype=np.float32)}
        for k, v in w.items():
            m[k] = np.ascontiguousarray(v, dtype=np.float32)
        in_maps.append(m)
    res = run_bass_kernel_spmd(nc, in_maps, core_ids=list(range(cores)))
    return np.stack([r["x_out"] for r in res.results])


def kernel(**inputs):
    x = np.ascontiguousarray(np.asarray(inputs["L0_attn_x"], dtype=np.float32))
    B = x.shape[0]
    for li in (0, 1, 2, 3):
        if li in LAYER_BUILDERS:
            x = run_layer(li, x, inputs, cores=B)
    return x
def host_prep_l3(inp):
    w = {}
    Wqkv = np.asarray(inp["L3_attn_W_qkv"])
    H, DK = 16, 64
    wq = Wqkv[:, 0:D].reshape(D, H, DK)
    wk = Wqkv[:, D:2 * D].reshape(D, H, DK)
    wv = Wqkv[:, 2 * D:3 * D].reshape(D, H, DK)
    perm = np.concatenate([np.arange(0, DK, 2), np.arange(1, DK, 2)])
    wq = wq[:, :, perm].reshape(D, D)
    wk = wk[:, :, perm].reshape(D, D)
    wv = wv.reshape(D, D)
    w["l3_wq"] = np.ascontiguousarray(wq).reshape(NCH, 128, D)
    w["l3_wk"] = np.ascontiguousarray(wk).reshape(NCH, 128, D)
    w["l3_wv"] = np.ascontiguousarray(wv).reshape(NCH, 128, D)
    w["l3_wo"] = np.ascontiguousarray(np.asarray(inp["L3_attn_W_out"])).reshape(NCH, 128, D)
    def bitrev(j, nb):
        r = 0
        for i in range(nb):
            r = (r << 1) | ((j >> i) & 1)
        return r
    leafperm = [bitrev(j, 4) for j in range(16)]
    lvl_rows = {d: [(2 ** d - 1) + bitrev(j, d) for j in range(2 ** d)]
                for d in range(4)}
    for f, pfx in enumerate("qkv"):
        nw = np.asarray(inp[f"L3_ffn_{pfx}_node_w"])
        nb = np.asarray(inp[f"L3_ffn_{pfx}_node_b"])
        for d in range(4):
            rows = lvl_rows[d]
            w[f"l3_nw{f}{d}"] = np.ascontiguousarray(
                nw[rows].T.reshape(NCH, 128, 2 ** d))
            w[f"l3_nb{f}{d}"] = np.ascontiguousarray(
                nb[rows].reshape(2 ** d, 1))
        w1 = np.asarray(inp[f"L3_ffn_{pfx}_w1"])[leafperm]
        b1 = np.asarray(inp[f"L3_ffn_{pfx}_b1"])[leafperm]
        w2 = np.asarray(inp[f"L3_ffn_{pfx}_w2"])[leafperm]
        b2 = np.asarray(inp[f"L3_ffn_{pfx}_b2"])[leafperm]
        w[f"l3_w1{f}"] = np.ascontiguousarray(
            w1.reshape(16, NCH, 128, 64).transpose(1, 2, 0, 3).reshape(NCH, 128, 16 * 64))
        w[f"l3_b1{f}"] = np.ascontiguousarray(b1.T)
        w[f"l3_w2{f}"] = np.ascontiguousarray(w2)
        w[f"l3_b2{f}"] = np.ascontiguousarray(b2)
    return w


def l3_make_consts(env):
    pos = np.arange(T)
    i32 = np.arange(32)
    freqs = 1.0 / (10000.0 ** (2 * i32 / 64.0))
    ang = pos[None, :] * freqs[:, None]
    cos32, sin32 = np.cos(ang), np.sin(ang)
    C = np.concatenate([cos32, cos32, cos32, cos32], 0)
    S = np.concatenate([-sin32, sin32, -sin32, sin32], 0)
    env.l3_cos = env.load_const_sb("l3_cos", C.astype(bfdt))
    env.l3_sin = env.load_const_sb("l3_sin", S.astype(bfdt))

    for d in range(4):
        ge = np.zeros((2 ** d, 16), np.float32)
        lt = np.zeros((2 ** d, 16), np.float32)
        for j in range(16):
            pre = j % (2 ** d)
            if (j >> d) & 1:
                ge[pre, j] = 1.0
            else:
                lt[pre, j] = 1.0
        env.__dict__[f"l3_selge{d}"] = env.load_const_sb(f"l3_selge{d}", ge.astype(bfdt))
        env.__dict__[f"l3_sellt{d}"] = env.load_const_sb(f"l3_sellt{d}", lt.astype(bfdt))
    il = np.zeros((16, 16 * 64), np.float32)
    for l in range(16):
        il[l, l * 64:(l + 1) * 64] = 1.0
    env.l3_indl = env.load_const_sb("l3_indl", il.astype(bfdt))


def emit_l3_attn(env, W, s_sink):
    p, nc = env.p, env.nc
    PS1, PS2 = env.PS1.ap(), env.PS2.ap()
    l3_bscr = nc.dram_tensor("l3_bscr", [2], F32, kind="Internal").ap()
    p.barrier()
    from contextlib import ExitStack as _ES
    with _ES() as _es:
        _e = _es.enter_context
        y_fm = _e(nc.sbuf_tensor("l3_y", [128, NCH * T], BF16))
        sums_all = _e(nc.sbuf_tensor("l3_sums", [16, T], F32))
        r_bf = _e(nc.sbuf_tensor("l3_r", [16, T], BF16))
        srow = _e(nc.sbuf_tensor("l3_srow", [128, T], F32))
        rrep = _e(nc.sbuf_tensor("l3_rrep", [128, T], BF16))
        mcol = _e(nc.sbuf_tensor("l3_mcol", [2, 4], F32))
        qp = _e(nc.sbuf_tensor("l3_qp", [128, T], BF16))
        kp = _e(nc.sbuf_tensor("l3_kp", [128, T], BF16))
        vp = _e(nc.sbuf_tensor("l3_vp", [128, NT * 130], BF16))
        wpair = _e(nc.sbuf_tensor("l3_wp", [128, 3 * NCH * 128], BF16))
        ptil = _e(nc.sbuf_tensor("l3_pt", [128, 2 * 2048], BF16))
        swp = _e(nc.sbuf_tensor("l3_sw", [128, T], BF16))
        utmp = _e(nc.sbuf_tensor("l3_u", [128, T], BF16))
        sqt = _e(nc.sbuf_tensor("l3_sqt", [128, T], BF16))
        bhneg = _e(nc.sbuf_tensor("l3_bh", [128, 2], F32))
        brow2 = _e(nc.sbuf_tensor("l3_br", [2, 2], F32))
        brow_bf = _e(nc.sbuf_tensor("l3_brb", [1, 2], BF16))
        wo_sb = _e(nc.sbuf_tensor("l3_wo_sb", [128, NCH * D], BF16))
        env.wload(wo_sb[:, :].rearrange("p (c o) -> p c o", c=NCH),
                  W["l3_wo"].rearrange("c p o -> p c o"), "l3_wo")
        for hp in range(8):
            for i, nm in enumerate(("l3_wq", "l3_wk", "l3_wv")):
                env.wload(
                    wpair[:, i * 1024:(i + 1) * 1024].rearrange(
                        "p (c o) -> p c o", c=NCH),
                    W[nm][:, :, hp * 128:(hp + 1) * 128].rearrange(
                        "c p o -> p c o"),
                    f"l3_wpair{i}")
            for wi, (dst, dk) in enumerate(((qp, "l3_qp"), (kp, "l3_kp"))):
                for qc in range(4):
                    c0 = (qc % 2) * 512
                    ps = PS1[:, c0:c0 + 512]
                    for kc in range(NCH):
                        p.mm(ps,
                             wpair[:, wi * 1024 + kc * 128: wi * 1024 + (kc + 1) * 128],
                             env.xn_fm[:, kc * T + qc * 512: kc * T + qc * 512 + 512],
                             start=(kc == 0), stop=(kc == NCH - 1),
                             reads=[f"l3_wpair{wi}", "xn_fm"],
                             writes=pk1(c0, c0 + 512), tag=f"l3{dk}mm")
                    p.act(dst[:, qc * 512:(qc + 1) * 512], ps, AF.Copy,
                          reads=pk1(c0, c0 + 512), writes=[dk + "_raw"],
                          tag=f"l3{dk}ev")
                for b0, b1 in ((0, 32), (32, 0), (64, 96), (96, 64)):
                    p.dma("gpsimd", swp[b0:b0 + 32, :], dst[b1:b1 + 32, :],
                          reads=[dk + "_raw", "l3_swp"], writes=["l3_swp"],
                          tag=f"l3sw{dk}")
                eng = "vector" if wi == 0 else "gpsimd"
                p.tt(utmp[:, :], dst[:, :], env.l3_cos[:, :], ALU.mult,
                     engine=eng, reads=[dk + "_raw", "l3_cos", "l3_u"],
                     writes=["l3_u"], tag=f"l3rc{dk}")
                p.tt(swp[:, :], swp[:, :], env.l3_sin[:, :], ALU.mult,
                     engine=eng, reads=["l3_swp", "l3_sin"], writes=["l3_swp"],
                     tag=f"l3rs{dk}")
                p.tt(dst[:, :], utmp[:, :], swp[:, :], ALU.add,
                     engine=eng, reads=["l3_u", "l3_swp"],
                     writes=[dk, dk + "_raw"], tag=f"l3ra{dk}")
            for tt in range(NT):
                c0 = 1024 + (tt % 2) * 512
                ps = PS1[:, c0:c0 + 128]
                for kc in range(NCH):
                    p.mm(ps, env.xn_fm[:, kc * T + tt * 128: kc * T + (tt + 1) * 128],
                         wpair[:, 2048 + kc * 128: 2048 + (kc + 1) * 128],
                         start=(kc == 0), stop=(kc == NCH - 1),
                         reads=["l3_wpair2", "xn_fm"], writes=pk1(c0, c0 + 128),
                         tag=f"l3vmm{tt}")
                p.copy(vp[:, tt * 130:(tt + 1) * 130].rearrange(
                           "p (h j) -> p h j", h=2)[:, :, 0:64],
                       ps.rearrange("p (h j) -> p h j", h=2),
                       reads=pk1(c0, c0 + 128), writes=["l3_vp"],
                       tag=f"l3vev{tt}")
            p.memset(vp[:, :].rearrange("p (n h j) -> p n h j", n=NT, h=2)
                     [:, :, :, 64:65],
                     1.0, reads=["l3_vp"], writes=["l3_vp"], tag="l3_vones")
            for o, (srcb, sk) in enumerate(((qp, "l3_qp"), (kp, "l3_kp"))):
                p.act(sqt[:, :], srcb[:, :], AF.Square, reads=[sk],
                      writes=["l3_sqt"], tag=f"l3sq{o}")
                for qc in range(4):
                    ps = PS2[0:2, 1536:2048]
                    p.mm(ps, env.pairselT[:, :], sqt[:, qc * 512:(qc + 1) * 512],
                         reads=["l3_sqt", "pairselT"], writes=pk2(1536, 2048),
                         tag=f"l3nrm{o}")
                    p.red(mcol[:, qc:qc + 1], ps, ALU.max,
                          reads=pk2(1536, 2048), writes=["l3_mc"],
                          tag=f"l3nrmr{o}")
                p.red(brow2[:, o:o + 1], mcol[:, 0:4], ALU.max,
                      reads=["l3_mc"], writes=[f"l3_br{o}"], tag=f"l3b{o}")
            p.tt(brow2[:, 0:1], brow2[:, 0:1], brow2[:, 1:2], ALU.mult,
                 reads=["l3_br0", "l3_br1"], writes=["l3_bp"], tag="l3bp")
            p.act(brow2[:, 0:1], brow2[:, 0:1], AF.Sqrt, reads=["l3_bp"],
                  writes=["l3_bs"], tag="l3bs")
            p.ts(brow2[:, 0:1], brow2[:, 0:1], -0.125, ALU.mult,
                 reads=["l3_bs"], writes=["l3_bn"], tag="l3bn")
            p.dma("gpsimd", l3_bscr, brow2[:, 0:1],
                  reads=["l3_bn", "l3_bh"], writes=["l3_bscrd"],
                  tag="l3btr")
            p.dma("sync", bhneg[:, 0:2],
                  bass.AP(l3_bscr.tensor, 0, [[0, 128], [1, 2]]),
                  reads=["l3_bscrd", "l3_bh"], writes=["l3_bh"],
                  tag="l3brep")
            for h in range(2):
                hg = 2 * hp + h
                for kt in range(NT):
                    qcb0 = kt // 4
                    s0 = (kt % 2) * 1024
                    po = (kt % 2) * 2048
                    pky = f"l3_pt{kt % 2}"
                    for ch in range(qcb0, 4):
                        sc = PS1[:, s0 + (ch % 2) * 512: s0 + (ch % 2) * 512 + 512]
                        p.mm(sc,
                             kp[64 * h:64 * h + 64, kt * 128:(kt + 1) * 128],
                             qp[64 * h:64 * h + 64, ch * 512:(ch + 1) * 512],
                             reads=["l3_qp", "l3_kp"],
                             writes=pk1(s0 + (ch % 2) * 512,
                                        s0 + (ch % 2) * 512 + 512),
                             tag=f"l3sc{hg}")
                        p.act(ptil[:, po + ch * 512: po + (ch + 1) * 512],
                              sc, AF.Exp,
                              scale=0.125, bias=bhneg[:, h:h + 1],
                              reads=pk1(s0 + (ch % 2) * 512,
                                        s0 + (ch % 2) * 512 + 512) + ["l3_bh",
                                                                      pky],
                              writes=[pky], tag=f"l3exp{hg}")
                    m = kt % 4
                    if m > 0:
                        p.memset(ptil[:, po + qcb0 * 512: po + qcb0 * 512 + m * 128],
                                 0.0, reads=[pky], writes=[pky],
                                 tag=f"l3mz{hg}")
                    p.tt(ptil[:, po + kt * 128: po + (kt + 1) * 128],
                         ptil[:, po + kt * 128: po + (kt + 1) * 128],
                         env.causal_keep[:, :], ALU.mult,
                         reads=[pky, "causal_keep"], writes=[pky],
                         tag=f"l3msk{hg}")
                    for ch in range(qcb0, 4):
                        av = PS2[0:65, ch * 512:(ch + 1) * 512]
                        p.mm(av,
                             vp[:, kt * 130 + 65 * h: kt * 130 + 65 * h + 65],
                             ptil[:, po + ch * 512: po + (ch + 1) * 512],
                             start=(kt == 0), stop=(kt == min(4 * ch + 3, NT - 1)),
                             reads=[pky, "l3_vp"],
                             writes=pk2(ch * 512, (ch + 1) * 512),
                             tag=f"l3av{hg}")
                p.act(srow[64:65, 0:T], PS2[64:65, 0:T], AF.Copy,
                      reads=pk2(0, 2048) + ["l3_srow"],
                      writes=["l3_srow"], tag=f"l3sumst{hg}")
                p.dma("gpsimd", sums_all[hg:hg + 1, :], srow[64:65, 0:T],
                      reads=["l3_srow"], writes=["l3_sums", "l3_srow"],
                      tag=f"l3sum{hg}")
                p.act(y_fm[64 * h:64 * h + 64, hp * T:(hp + 1) * T],
                      PS2[0:64, 0:T], AF.Copy, reads=pk2(0, 2048),
                      writes=["l3_yraw"], tag=f"l3yev{hg}")
        p.act(sums_all[:, :], sums_all[:, :], AF.Ln, reads=["l3_sums"],
              writes=["l3_lns"], tag="l3_lns")
        p.act(r_bf[:, :], sums_all[:, :], AF.Exp, scale=-1.0,
              reads=["l3_lns"], writes=["l3_r"], tag="l3_rinv")
        p.dma("gpsimd", env.rscr, r_bf[:, :], reads=["l3_r"],
              writes=["l3_rscr"], tag="l3rst")
        for c in range(NCH):
            for h in range(2):
                srcap = bass.AP(env.rscr.tensor, (2 * c + h) * T,
                                [[0, 64], [1, T]])
                p.dma("sync", rrep[64 * h:64 * h + 64, :], srcap,
                      reads=["l3_rscr", "l3_rrep"], writes=["l3_rrep"],
                      tag="l3rbc")
            p.tt(y_fm[:, c * T:(c + 1) * T], y_fm[:, c * T:(c + 1) * T],
                 rrep[:, :], ALU.mult,
                 reads=["l3_rrep", "l3_yraw"], writes=["l3_y"],
                 tag=f"l3ynrm{c}")
        for dc in range(NCH):
            for qc in range(4):
                c0 = (qc % 2) * 512
                ps = PS2[:, c0:c0 + 512]
                for kc in range(NCH):
                    p.mm(ps, wo_sb[:, kc * D + dc * 128: kc * D + (dc + 1) * 128],
                         y_fm[:, kc * T + qc * 512: kc * T + qc * 512 + 512],
                         start=(kc == 0), stop=(kc == NCH - 1),
                         reads=["l3_wo", "l3_y"], writes=pk2(c0, c0 + 512),
                         tag="l3omm")
                s_sink(dc, qc, ps, pk2(c0, c0 + 512))
    p.barrier()
def emit_l3_ffn(env, W, g_sink):
    p, nc = env.p, env.nc
    PS1, PS2 = env.PS1.ap(), env.PS2.ap()
    p.barrier()
    with (
        nc.sbuf_tensor("l3f_q", [128, NCH * T], BF16) as q_fm,
        nc.sbuf_tensor("l3f_k", [128, NCH * T], BF16) as k_fm,
        nc.sbuf_tensor("l3f_v", [128, NCH * T], BF16) as v_fm,
    ):
        vt = env.xn_fm
        from contextlib import ExitStack as _ES
        for f, (dst, dstk) in enumerate(((q_fm, "l3f_q"), (k_fm, "l3f_k"),
                                         (v_fm, "l3f_v"))):
            with _ES() as _es:
                _e = _es.enter_context
                nw_sb = _e(nc.sbuf_tensor(f"l3f_nw{f}", [128, NCH * 15], BF16))
                nb_sb = _e(nc.sbuf_tensor(f"l3f_nb{f}", [8, 8], F32))
                ge_sb = _e(nc.sbuf_tensor(f"l3f_ge{f}", [8, 512], BF16))
                lt_sb = _e(nc.sbuf_tensor(f"l3f_lt{f}", [8, 512], BF16))
                sel_sb = _e(nc.sbuf_tensor(f"l3f_sel{f}", [16, T], BF16))
                w1_sb = _e(nc.sbuf_tensor(f"l3f_w1{f}", [128, 16 * NCH * 64], BF16))
                w2_sb = _e(nc.sbuf_tensor(f"l3f_w2{f}", [64, 16 * D], BF16))
                b1_sb = _e(nc.sbuf_tensor(f"l3f_b1{f}", [64, 16], F32))
                b2_sb = _e(nc.sbuf_tensor(f"l3f_b2{f}", [16, D], BF16))
                h_sb = _e(nc.sbuf_tensor(f"l3f_h{f}", [64, 512], BF16))
                for d in range(4):
                    p.dma("gpsimd",
                          nw_sb[:, :].rearrange("p (c o) -> p c o", c=NCH)
                          [:, :, (2 ** d - 1):(2 ** (d + 1) - 1)],
                          W[f"l3_nw{f}{d}"].rearrange("c p o -> p c o"),
                          writes=["l3f_nw"], tag="w_l3f_nw",
                          allow_slow_non_contiguous=True)
                    p.dma("sync", nb_sb[0:2 ** d, d:d + 1], W[f"l3_nb{f}{d}"],
                          writes=["l3f_nb"], tag="l3fnb")
                for d in range(4):
                    p.ts(nb_sb[0:2 ** d, 4 + d:5 + d], nb_sb[0:2 ** d, d:d + 1],
                         -1.0, ALU.mult, reads=["l3f_nb"], writes=["l3f_nbn"],
                         tag="l3fnbn")
                env.wload(w1_sb[:, :].rearrange("p (c lo) -> p c lo", c=NCH),
                          W[f"l3_w1{f}"].rearrange("c p lo -> p c lo"),
                          "l3f_w1")
                env.wload(w2_sb[:, :].rearrange("p (l d) -> p l d", l=16),
                          W[f"l3_w2{f}"].rearrange("l p d -> p l d"), "l3f_w2")
                p.dma("sync", b1_sb[:, :], W[f"l3_b1{f}"], writes=["l3f_b1"],
                      tag="l3fb1")
                env.wload(b2_sb[:, :], W[f"l3_b2{f}"], "l3f_b2")
                for d in range(4):
                    nd = 2 ** d
                    for qc in range(4):
                        ps = PS1[0:nd, 0:512]
                        for kc in range(NCH):
                            p.mm(ps, nw_sb[:, kc * 15 + nd - 1:
                                           kc * 15 + 2 * nd - 1],
                                 env.xn_fm[:, kc * T + qc * 512:
                                           kc * T + qc * 512 + 512],
                                 start=(kc == 0), stop=(kc == NCH - 1),
                                 reads=["l3f_nw", "xn_fm"], writes=pk1(0, 512),
                                 tag=f"l3fsc{f}{d}")
                        p.ts(ge_sb[0:nd, :], ps, nb_sb[0:nd, 4 + d:5 + d],
                             ALU.is_ge, reads=pk1(0, 512) + ["l3f_nbn"],
                             writes=["l3f_ge"], tag=f"l3fge{d}")
                        p.ts(lt_sb[0:nd, :], ps, nb_sb[0:nd, 4 + d:5 + d],
                             ALU.is_lt, reads=pk1(0, 512) + ["l3f_nbn"],
                             writes=["l3f_lt"], tag=f"l3flt{d}")
                        ps2 = PS1[0:16, 512:1024]
                        p.mm(ps2, env.__dict__[f"l3_selge{d}"][:, :],
                             ge_sb[0:nd, :], start=True, stop=False,
                             reads=["l3f_ge", f"l3_selge{d}"],
                             writes=pk1(512, 1024), tag=f"l3fek{d}")
                        p.mm(ps2, env.__dict__[f"l3_sellt{d}"][:, :],
                             lt_sb[0:nd, :], start=False, stop=True,
                             reads=["l3f_lt", f"l3_sellt{d}"],
                             writes=pk1(512, 1024), tag=f"l3fek{d}b")
                        if d == 0:
                            p.copy(sel_sb[0:16, qc * 512:(qc + 1) * 512], ps2,
                                   reads=pk1(512, 1024), writes=["l3f_sel"],
                                   tag="l3fsel0")
                        else:
                            p.tt(sel_sb[0:16, qc * 512:(qc + 1) * 512],
                                 sel_sb[0:16, qc * 512:(qc + 1) * 512], ps2,
                                 ALU.mult, reads=pk1(512, 1024) + ["l3f_sel"],
                                 writes=["l3f_sel"], tag=f"l3fsel{d}")
                for tb in range(8):
                    c0 = tb * 256
                    for l in range(16):
                        psr = PS1[0:64, 0:256]
                        p.mm(psr, env.l3_indl[:, l * 64:(l + 1) * 64],
                             sel_sb[0:16, c0:c0 + 256],
                             reads=["l3f_sel", "l3_indl"], writes=pk1(0, 512),
                             tag="l3frep")
                        hps = PS1[0:64, 512 + (l % 2) * 256:
                                  512 + (l % 2) * 256 + 256]
                        for kc in range(NCH):
                            p.mm(hps, w1_sb[:, (kc * 16 + l) * 64:
                                            (kc * 16 + l + 1) * 64],
                                 env.xn_fm[:, kc * T + c0: kc * T + c0 + 256],
                                 start=(kc == 0), stop=(kc == NCH - 1),
                                 reads=["l3f_w1", "xn_fm"],
                                 writes=pk1(512 + (l % 2) * 256,
                                            768 + (l % 2) * 256),
                                 tag="l3fhmm")
                        hv = h_sb[0:64, (l % 2) * 256:(l % 2) * 256 + 256]
                        p.act(hv, hps, AF.Relu, bias=b1_sb[:, l:l + 1],
                              reads=pk1(512 + (l % 2) * 256,
                                        768 + (l % 2) * 256) + ["l3f_b1"],
                              writes=[f"l3f_h{l % 2}"], tag="l3fhev")
                        p.tt(hv, hv, psr, ALU.mult,
                             reads=pk1(0, 512) + [f"l3f_h{l % 2}"],
                             writes=[f"l3f_h{l % 2}"], tag="l3fhm")
                        for dc in range(NCH):
                            p.mm(PS2[:, dc * 256:(dc + 1) * 256],
                                 w2_sb[:, l * D + dc * 128: l * D + (dc + 1) * 128],
                                 hv, start=(l == 0), stop=False,
                                 reads=[f"l3f_h{l % 2}", "l3f_w2"],
                                 writes=pk2(dc * 256, (dc + 1) * 256),
                                 tag="l3fw2")
                    for dc in range(NCH):
                        p.mm(PS2[:, dc * 256:(dc + 1) * 256],
                             b2_sb[:, dc * 128:(dc + 1) * 128],
                             sel_sb[0:16, c0:c0 + 256], start=False, stop=True,
                             reads=["l3f_sel", "l3f_b2"],
                             writes=pk2(dc * 256, (dc + 1) * 256), tag="l3fb2")
                        p.act(dst[:, dc * T + c0: dc * T + c0 + 256],
                              PS2[:, dc * 256:(dc + 1) * 256], AF.Copy,
                              reads=pk2(dc * 256, (dc + 1) * 256),
                              writes=[dstk], tag="l3fout")
        p.barrier()
        trps = env.PS1.ap().bitcast(BF16)
        for tt in range(NT):
            for c in range(NCH):
                p.trans(trps[:, c * 128:(c + 1) * 128],
                        v_fm[:, c * T + tt * 128: c * T + (tt + 1) * 128],
                        env.id128[:, :],
                        reads=["l3f_v", "id128"], writes=pk1(0, 512),
                        tag="l3fvtr")
            p.copy(vt[:, tt * 1024:(tt + 1) * 1024], trps[:, 0:1024],
                   reads=pk1(0, 512), writes=["l3f_vt"], tag="l3fvtc")
        with _ES() as _es2:
            _e2 = _es2.enter_context
            sqt = _e2(nc.sbuf_tensor("l3f_sq", [128, T], BF16))
            bsc = _e2(nc.sbuf_tensor("l3f_b", [1, 10], F32))
            bscb = _e2(nc.sbuf_tensor("l3f_bb", [1, 1], BF16))
            bcol = _e2(nc.sbuf_tensor("l3f_bh", [128, 1], F32))
            ptil = _e2(nc.sbuf_tensor("l3f_pt", [128, 256], BF16))
            rbr = _e2(nc.sbuf_tensor("l3f_rw", [1, 256], BF16))
            rrep = _e2(nc.sbuf_tensor("l3f_rr", [128, 256], BF16))
            g_sb = _e2(nc.sbuf_tensor("l3f_g", [128, 2 * 256], F32))
            for o, (srcb, sk) in enumerate(((q_fm, "l3f_q"), (k_fm, "l3f_k"))):
                for qc in range(4):
                    ps = PS1[0:1, 1536:2048]
                    for c in range(NCH):
                        p.act(sqt[:, qc * 512:(qc + 1) * 512],
                              srcb[:, c * T + qc * 512: c * T + qc * 512 + 512],
                              AF.Square, reads=[sk], writes=["l3f_sq"],
                              tag=f"l3fsq{o}")
                        p.mm(ps, env.ones_col128[:, :],
                             sqt[:, qc * 512:(qc + 1) * 512],
                             start=(c == 0), stop=(c == NCH - 1),
                             reads=["l3f_sq", "ones_col128"],
                             writes=pk1(1536, 2048), tag=f"l3fn{o}")
                    p.red(bsc[0:1, 4 * o + qc:4 * o + qc + 1], ps, ALU.max,
                          reads=pk1(1536, 2048),
                          writes=[f"l3f_m{'qk'[o]}{qc}"], tag=f"l3fnr{o}")
            p.red(bsc[0:1, 8:9], bsc[0:1, 0:4], ALU.max,
                  reads=["l3f_mq0", "l3f_mq1", "l3f_mq2", "l3f_mq3"],
                  writes=["l3f_mq"], tag="l3fmq")
            p.red(bsc[0:1, 9:10], bsc[0:1, 4:8], ALU.max,
                  reads=["l3f_mk0", "l3f_mk1", "l3f_mk2", "l3f_mk3"],
                  writes=["l3f_mk"], tag="l3fmk")
            p.tt(bsc[0:1, 8:9], bsc[0:1, 8:9], bsc[0:1, 9:10], ALU.mult,
                 reads=["l3f_mq", "l3f_mk"], writes=["l3f_bp"], tag="l3fbp")
            p.act(bsc[0:1, 8:9], bsc[0:1, 8:9], AF.Sqrt, reads=["l3f_bp"],
                  writes=["l3f_bs"], tag="l3fbs")
            p.ts(bscb[0:1, 0:1], bsc[0:1, 8:9], -0.03125, ALU.mult,
                 reads=["l3f_bs"], writes=["l3f_bn"], tag="l3fbn")
            p.mm(PS1[:, 1536:1537], env.ones_row[:, :], bscb[0:1, 0:1],
                 reads=["l3f_bn", "ones_row"], writes=pk1(1536, 2048),
                 tag="l3fbrep")
            p.act(bcol[:, :], PS1[:, 1536:1537], AF.Copy,
                  reads=pk1(1536, 2048), writes=["l3f_bh"], tag="l3fbev")
            for qb in range(8):
                c0 = qb * 256
                for kt in range(NT):
                    s0 = 1024 + (kt % 2) * 256
                    sc = PS1[:, s0:s0 + 256]
                    sck = pk1(1024, 1536)
                    for kc in range(NCH):
                        p.mm(sc, k_fm[:, kc * T + kt * 128: kc * T + (kt + 1) * 128],
                             q_fm[:, kc * T + c0: kc * T + c0 + 256],
                             start=(kc == 0), stop=(kc == NCH - 1),
                             reads=["l3f_q", "l3f_k"], writes=sck,
                             tag="l3fasc")
                    p.act(ptil[:, :], sc, AF.Exp, scale=0.03125,
                          bias=bcol[:, 0:1], reads=sck + ["l3f_bh"],
                          writes=["l3f_pt"], tag="l3faexp")
                    p.mm(PS1[0:1, 1792:2048], env.ones_col128[:, :],
                         ptil[:, :], start=(kt == 0), stop=(kt == NT - 1),
                         reads=["l3f_pt", "ones_col128"], writes=pk1(1536, 2048),
                         tag="l3fasum")
                    for dc in range(NCH):
                        p.mm(PS2[:, dc * 256:(dc + 1) * 256],
                             vt[:, kt * 1024 + dc * 128: kt * 1024 + (dc + 1) * 128],
                             ptil[:, :], start=(kt == 0), stop=(kt == NT - 1),
                             reads=["l3f_pt", "l3f_vt"],
                             writes=pk2(dc * 256, (dc + 1) * 256), tag="l3fav")
                p.act(rbr[0:1, :], PS1[0:1, 1792:2048], AF.Ln,
                      reads=pk1(1536, 2048), writes=["l3f_ln"], tag="l3faln")
                p.act(rbr[0:1, :], rbr[0:1, :], AF.Exp, scale=-1.0,
                      reads=["l3f_ln"], writes=["l3f_rw"], tag="l3farw")
                p.mm(PS1[:, 1024:1280], env.ones_row[:, :], rbr[0:1, :],
                     reads=["l3f_rw", "ones_row"], writes=pk1(1024, 1536),
                     tag="l3farep")
                p.act(rrep[:, :], PS1[:, 1024:1280], AF.Copy,
                      reads=pk1(1024, 1536), writes=["l3f_rr"], tag="l3farev")
                for dc in range(NCH):
                    gv = g_sb[:, (dc % 2) * 256:(dc % 2) * 256 + 256]
                    p.tt(gv, PS2[:, dc * 256:(dc + 1) * 256], rrep[:, :],
                         ALU.mult,
                         reads=pk2(dc * 256, (dc + 1) * 256) + ["l3f_rr",
                                                                f"l3f_g{dc % 2}"],
                         writes=[f"l3f_g{dc % 2}"], tag="l3fgn")
                    g_sink(dc, qb, gv, [f"l3f_g{dc % 2}"])
    p.barrier()


def build_layer3(wshapes):
    nc = bass.Bass("TRN2", target_bir_lowering=False)
    kind_dbg = "ExternalOutput" if DEBUG_DUMP else "Internal"
    x_in = nc.dram_tensor("x_in", [T, D], F32, kind="ExternalInput").ap()
    x_out = nc.dram_tensor("x_out", [T, D], F32, kind="ExternalOutput").ap()
    s_scr = nc.dram_tensor("s_scr", [NCH, 128, T], BF16, kind=kind_dbg).ap()
    rscr = nc.dram_tensor("rscr", [16, T], BF16, kind="Internal").ap()
    W = {k: nc.dram_tensor(k, list(v), F32, kind="ExternalInput").ap()
         for k, v in wshapes.items()}
    env = Env(nc)
    make_consts(env)
    l3_make_consts(env)
    env.rscr = rscr
    env.xn_fm = env.sb("xn_fm", [128, NCH * T], BF16)
    emit_ln_phase(env, x_in)
    if DEBUG_DUMP:
        dxn = nc.dram_tensor("dbg_xn", [128, NCH * T], BF16,
                             kind="ExternalOutput").ap()
        env.p.dma("gpsimd", dxn, env.xn_fm[:, :], reads=["xn_fm"], tag="dxn")
    emit_l3_attn(env, W, make_s_sink(env, s_scr, mode="plain"))
    if DEBUG_DUMP:
        env.dbg_q = nc.dram_tensor("dbg_q", [128, NCH * T], BF16,
                                   kind="ExternalOutput").ap()
        env.dbg_sel = nc.dram_tensor("dbg_sel", [16, T], BF16,
                                     kind="ExternalOutput").ap()
    g_sink = make_residual_streamer(env, x_in, x_out, s_scr, blk=256,
                                    trps_region=(0, 512))
    emit_l3_ffn(env, W, g_sink)
    env.p.build()
    return nc


LAYER_BUILDERS[3] = (host_prep_l3, build_layer3)
def host_prep_l2(inp):
    w = {}
    w["l2_WK"] = np.ascontiguousarray(np.asarray(inp["L2_attn_W_K"])).reshape(NCH, 128, 2048)
    w["l2_Wugg"] = np.ascontiguousarray(np.asarray(inp["L2_attn_W_ugg"])).reshape(NCH, 128, 6144)
    w["l2_Wout"] = np.ascontiguousarray(np.asarray(inp["L2_attn_W_out"])).reshape(16, 128, D)
    w["l2_rms"] = np.ascontiguousarray(np.asarray(inp["L2_attn_rms_scale"])).reshape(NCH, 128).T.copy()
    w["l2_split"] = np.ascontiguousarray(np.asarray(inp["L2_attn_split_scale"])).reshape(NCH, 128).T.copy()
    w["l2_Wc"] = np.ascontiguousarray(np.asarray(inp["L2_ffn_Wc"])).reshape(NCH, 128, 128)
    w["l2_bc"] = np.ascontiguousarray(np.asarray(inp["L2_ffn_bc"])).reshape(128, 1)
    W1 = np.asarray(inp["L2_ffn_W1"]).reshape(128, D, 32)
    W2 = np.asarray(inp["L2_ffn_W2"]).reshape(128, D, 32)
    b1 = np.asarray(inp["L2_ffn_b1"]).reshape(128, 32)
    b2 = np.asarray(inp["L2_ffn_b2"]).reshape(128, D)
    w["l2_W1"] = np.ascontiguousarray(W1.transpose(0, 2, 1)
                                      .reshape(128, 32, NCH, 128)
                                      .transpose(0, 2, 3, 1))
    w["l2_W1"] = np.ascontiguousarray(W1.reshape(128, NCH, 128, 32))
    w["l2_W2t"] = np.ascontiguousarray(W2.transpose(0, 2, 1))
    w["l2_b1T"] = np.ascontiguousarray(b1.T)
    w["l2_b2"] = np.ascontiguousarray(b2)
    return w


def emit_l2_attn(env, W, s_sink):
    p, nc = env.p, env.nc
    PS1, PS2 = env.PS1.ap(), env.PS2.ap()
    p.barrier()
    from contextlib import ExitStack as _ES
    with _ES() as _es:
        _e = _es.enter_context
        xn2 = _e(nc.sbuf_tensor("l2_xn2", [128, NCH * T], BF16))
        so = _e(nc.sbuf_tensor("l2_so", [128, T], BF16))
        soq = _e(nc.sbuf_tensor("l2_soq", [128, 16 * 512], BF16))
        nrow = _e(nc.sbuf_tensor("l2_nr", [1, T], F32))
        rrep = _e(nc.sbuf_tensor("l2_rrep", [128, T], BF16))
        sv = _e(nc.sbuf_tensor("l2_sv_sb", [128, NCH], F32))
        sq = _e(nc.sbuf_tensor("l2_sq", [128, 512], BF16))
        wst = _e(nc.sbuf_tensor("l2_wst", [128, 2 * 4 * NCH * 128], BF16))
        kc_sb = _e(nc.sbuf_tensor("l2_kc", [128, T], F32))
        k1m = _e(nc.sbuf_tensor("l2_k1m", [128, T], F32))
        shf = _e(nc.sbuf_tensor("l2_shf", [128, T + 1], F32))
        ub = _e(nc.sbuf_tensor("l2_ub", [128, T], F32))
        t1 = _e(nc.sbuf_tensor("l2_t1", [128, T], F32))
        gsig = _e(nc.sbuf_tensor("l2_gs", [128, T], BF16))
        sct = _e(nc.sbuf_tensor("l2_sct", [128, T], F32))
        wo_sb = _e(nc.sbuf_tensor("l2_wo_sb", [128, 16 * D], BF16))
        env.wload(wo_sb[:, :].rearrange("p (c o) -> p c o", c=16),
                  W["l2_Wout"].rearrange("c p o -> p c o"), "l2_wo")
        p.dma("sync", sv[:, 0:NCH], W["l2_rms"], writes=["l2_rms"], tag="l2rms")
        p.dma("sync", xn2[:, 0:16].bitcast(F32), W["l2_split"],
              writes=["l2_splt"], tag="l2splt")
        p.tt(sv[:, :], sv[:, :], xn2[:, 0:16].bitcast(F32),
             ALU.mult, reads=["l2_rms", "l2_splt"], writes=["l2_sv"],
             tag="l2sv")
        for qc in range(4):
            ps = PS1[0:1, 1536:2048]
            for c in range(NCH):
                p.act(sq[:, 0:512],
                      env.xn_fm[:, c * T + qc * 512: c * T + qc * 512 + 512],
                      AF.Square, reads=["xn_fm"], writes=["l2_sq"], tag="l2sq")
                p.mm(ps, env.ones_col128[:, :], sq[:, 0:512],
                     start=(c == 0), stop=(c == NCH - 1),
                     reads=["l2_sq", "ones_col128"], writes=pk1(1536, 2048),
                     tag="l2nm")
            p.act(nrow[0:1, qc * 512:(qc + 1) * 512], ps, AF.Sqrt,
                  scale=1.0 / D, reads=pk1(1536, 2048), writes=["l2_nr"],
                  tag="l2nsq")
        p.act(nrow[0:1, :], nrow[0:1, :], AF.Ln, bias=env.eps8[0:1, 0:1],
              reads=["l2_nr", "eps8"], writes=["l2_nl"], tag="l2nl")
        p.act(nrow[0:1, :], nrow[0:1, :], AF.Exp, scale=-1.0,
              reads=["l2_nl"], writes=["l2_nr2"], tag="l2nr2")
        p.copy(sct[0:1, 0:T], nrow[0:1, :], reads=["l2_nr2"], writes=["l2_rbf"],
               tag="l2rbf")
        p.dma("gpsimd", env.rscr[0:1, :], sct[0:1, 0:T], reads=["l2_rbf"],
              writes=["l2_rd"], tag="l2rst")
        srcap = bass.AP(env.rscr.tensor, 0, [[0, 128], [1, T]])
        p.dma("sync", rrep[:, :], srcap, reads=["l2_rd"], writes=["l2_rrep"],
              tag="l2rbc")
        for c in range(NCH):
            p.stt(xn2[:, c * T:(c + 1) * T],
                  env.xn_fm[:, c * T:(c + 1) * T], sv[:, c:c + 1],
                  rrep[:, :], ALU.mult, ALU.mult,
                  reads=["xn_fm", "l2_sv", "l2_rrep", "l2_splt"],
                  writes=["l2_xn2"], tag=f"l2xn2_{c}")
        for hc in range(16):
            env.wload(wst[:, 0:1024].rearrange("p (c o) -> p c o", c=NCH),
                      W["l2_WK"][:, :, hc * 128:(hc + 1) * 128].rearrange(
                          "c p o -> p c o"), "l2_wstK")
            for j, off in enumerate((0, 2048, 4096)):
                env.wload(wst[:, (j + 1) * 1024:(j + 2) * 1024].rearrange(
                              "p (c o) -> p c o", c=NCH),
                          W["l2_Wugg"][:, :, off + hc * 128: off + (hc + 1) * 128]
                          .rearrange("c p o -> p c o"), f"l2_wstU{j + 1}")
            for qc in range(4):
                c0 = (qc % 2) * 512
                ps = PS1[:, c0:c0 + 512]
                for kc in range(NCH):
                    p.mm(ps, wst[:, kc * 128:(kc + 1) * 128],
                         xn2[:, kc * T + qc * 512: kc * T + qc * 512 + 512],
                         start=(kc == 0), stop=(kc == NCH - 1),
                         reads=["l2_wstK", "l2_xn2"], writes=pk1(c0, c0 + 512),
                         tag="l2kmm")
                p.act(kc_sb[:, qc * 512:(qc + 1) * 512], ps, AF.Sigmoid,
                      reads=pk1(c0, c0 + 512), writes=["l2_kc"], tag="l2kev")
                p.act(k1m[:, qc * 512:(qc + 1) * 512], ps, AF.Sigmoid,
                      scale=-1.0, reads=pk1(c0, c0 + 512), writes=["l2_k1m"],
                      tag="l2k1ev")
            for wi, (dst, dk, fn) in enumerate((
                    (t1, "l2_t1", AF.Sigmoid), (ub, "l2_ub", AF.Copy),
                    (gsig, "l2_gs", AF.Sigmoid))):
                wofs = (2 if wi == 0 else (1 if wi == 1 else 3)) * 1024
                for qc in range(4):
                    c0 = (qc % 2) * 512
                    ps = PS1[:, c0:c0 + 512]
                    for kc in range(NCH):
                        p.mm(ps, wst[:, wofs + kc * 128: wofs + (kc + 1) * 128],
                             xn2[:, kc * T + qc * 512: kc * T + qc * 512 + 512],
                             start=(kc == 0), stop=(kc == NCH - 1),
                             reads=[f"l2_wstU{wofs // 1024}", "l2_xn2"],
                             writes=pk1(c0, c0 + 512), tag=f"l2umm{wi}")
                    p.act(dst[:, qc * 512:(qc + 1) * 512], ps, fn,
                          reads=pk1(c0, c0 + 512), writes=[dk], tag=f"l2uev{wi}")
            p.tt(ub[:, :], ub[:, :], t1[:, :], ALU.mult,
                 reads=["l2_ub", "l2_t1"], writes=["l2_ub"], tag="l2ub1")
            p.tt(ub[:, :], ub[:, :], k1m[:, :], ALU.mult,
                 reads=["l2_ub", "l2_k1m"], writes=["l2_ub"], tag="l2ub2")
            p.memset(shf[:, 0:1], 0.0, writes=["l2_shf"], tag="l2shz")
            p.copy(shf[:, 1:T + 1], kc_sb[:, :], reads=["l2_kc", "l2_shf"],
                   writes=["l2_shf"], tag="l2shc")
            p.scan(sct[:, :], shf[:, 0:T], ub[:, :], 0.0, ALU.mult, ALU.add,
                   reads=["l2_shf", "l2_ub"], writes=["l2_sct", "l2_rbf"],
                   tag="l2scan")
            p.tt(so[:, :], sct[:, :], gsig[:, :], ALU.mult,
                 reads=["l2_sct", "l2_gs", "l2_so"], writes=["l2_so"],
                 tag="l2soev")
            p.dma("gpsimd", env.l2_sodram[hc], so[:, :],
                  reads=["l2_so"], writes=["l2_sod"], tag="l2sost")
        for qc in range(4):
            p.dma("sync", soq[:, :].rearrange("p (h j) -> p h j", h=16),
                  env.l2_sodram.rearrange("h p j -> p h j")
                  [:, :, qc * 512:(qc + 1) * 512],
                  reads=["l2_sod", "l2_soq"], writes=["l2_soq"], tag="l2sold")
            for dc in range(NCH):
                c0 = (dc % 2) * 512
                ps = PS2[:, c0:c0 + 512]
                for hc in range(16):
                    p.mm(ps, wo_sb[:, hc * D + dc * 128: hc * D + (dc + 1) * 128],
                         soq[:, hc * 512:(hc + 1) * 512],
                         start=(hc == 0), stop=(hc == 15),
                         reads=["l2_wo", "l2_soq"], writes=pk2(c0, c0 + 512),
                         tag="l2omm")
                s_sink(dc, qc, ps, pk2(c0, c0 + 512))
    p.barrier()
def emit_l2_ffn(env, W, xnd, yed, finale):
    p, nc = env.p, env.nc
    PS1, PS2 = env.PS1.ap(), env.PS2.ap()
    p.barrier()
    from contextlib import ExitStack as _ES
    from concourse.bass_core_rust import IndirectOffsetOnAxis
    tie = env.load_const_sb(
        "l2_tie", np.ascontiguousarray(np.broadcast_to(
            np.tile(np.linspace(0.0, 1e-6, 32, dtype=np.float32), 64),
            (128, T))))
    iota32 = env.load_const_sb(
        "l2_iota32", np.ascontiguousarray(np.broadcast_to(
            np.tile(np.arange(32, dtype=np.float32), 64), (128, T))
            .astype(bfdt)))
    g32 = env.load_const_sb(
        "l2_g32", np.ascontiguousarray(np.broadcast_to(
            (np.arange(64, dtype=np.float32) * 32), (128, 64))))
    with _ES() as _es:
        _e = _es.enter_context
        wc_sb = _e(nc.sbuf_tensor("l2_wc_sb", [128, NCH * 128], BF16))
        bc_sb = _e(nc.sbuf_tensor("l2_bc_sb", [128, 1], F32))
        lg = _e(nc.sbuf_tensor("l2_lg", [128, T], F32))
        gmax = _e(nc.sbuf_tensor("l2_gmax", [128, 64], F32))
        perm = _e(nc.sbuf_tensor("l2_perm", [128, T], BF16))
        loc = _e(nc.sbuf_tensor("l2_loc", [128, 64], F32))
        idxb = _e(nc.sbuf_tensor("l2_idx", [128, 64], U32))
        idx2 = _e(nc.sbuf_tensor("l2_idx2", [128, 64], U32))
        vtok = _e(nc.sbuf_tensor("l2_vtok", [128, 2 * D], BF16))
        xsel = _e(nc.sbuf_tensor("l2_xsel", [128, 2 * D], BF16))
        w1_sb = _e(nc.sbuf_tensor("l2_w1s", [128, 4 * NCH * 32], BF16))
        w2_sb = _e(nc.sbuf_tensor("l2_w2s", [32, 4 * D], BF16))
        b1_sb = _e(nc.sbuf_tensor("l2_b1s", [32, 128], F32))
        b2_sb = _e(nc.sbuf_tensor("l2_b2s", [1, 4 * D], BF16))
        h_sb = _e(nc.sbuf_tensor("l2_hs", [32, 4 * 64], BF16))
        ye_sb = _e(nc.sbuf_tensor("l2_yes", [64, 2 * D], BF16))
        yeg = _e(nc.sbuf_tensor("l2_yeg", [128, 2 * D], BF16))
        env.wload(wc_sb[:, :].rearrange("p (c o) -> p c o", c=NCH),
                  W["l2_Wc"].rearrange("c p o -> p c o"), "l2_wcw")
        p.dma("sync", bc_sb[:, :], W["l2_bc"], writes=["l2_bcb"], tag="l2bc")
        p.dma("sync", b1_sb[:, :], W["l2_b1T"], writes=["l2_b1s"], tag="l2b1")
        for qc in range(4):
            c0 = (qc % 2) * 512
            ps = PS1[:, c0:c0 + 512]
            for kc in range(NCH):
                p.mm(ps, wc_sb[:, kc * 128:(kc + 1) * 128],
                     env.xn_fm[:, kc * T + qc * 512: kc * T + qc * 512 + 512],
                     start=(kc == 0), stop=(kc == NCH - 1),
                     reads=["l2_wcw", "xn_fm"], writes=pk1(c0, c0 + 512),
                     tag="l2lmm")
            p.act(lg[:, qc * 512:(qc + 1) * 512], ps, AF.Identity,
                  bias=bc_sb[:, 0:1], reads=pk1(c0, c0 + 512) + ["l2_bcb"],
                  writes=["l2_lg"], tag="l2lev")
        p.tt(lg[:, :], lg[:, :], tie[:, :], ALU.add,
             reads=["l2_lg", "l2_tie"], writes=["l2_lg"], tag="l2tie")
        lg3 = lg[:, :].rearrange("p (g j) -> p g j", g=64)
        p.red(gmax[:, :], lg3, ALU.max, reads=["l2_lg"], writes=["l2_gm"],
              tag="l2gm")
        gmb = bass.AP(gmax, 0, [[64, 128], [1, 64], [0, 32]])
        p.tt(perm[:, :].rearrange("p (g j) -> p g j", g=64), lg3, gmb,
             ALU.is_ge, reads=["l2_lg", "l2_gm"], writes=["l2_perm"],
             tag="l2pm")
        p.tt(lg[:, :], perm[:, :], iota32[:, :], ALU.mult,
             reads=["l2_perm", "l2_iota32", "l2_lg"], writes=["l2_lgi"],
             tag="l2li")
        p.red(loc[:, :], lg3, ALU.add, reads=["l2_lgi"], writes=["l2_loc"],
              tag="l2loc")
        p.tt(loc[:, :], loc[:, :], g32[:, :], ALU.add,
             reads=["l2_loc", "l2_g32"], writes=["l2_loc"], tag="l2locg")
        p.copy(idxb[:, :], loc[:, :], reads=["l2_loc"], writes=["l2_idxb"],
               tag="l2idx")
        p.dma("gpsimd", env.l2_idxd, idxb[:, :], reads=["l2_idxb"],
              writes=["l2_idxd"], tag="l2idxst")
        for eh in range(2):
            srcap = bass.AP(env.l2_idxd.tensor, eh * 64,
                            [[1, 64], [128, 64]])
            p.dma("sync", idx2[eh * 64:(eh + 1) * 64, :], srcap,
                  reads=["l2_idxd"], writes=["l2_idx2"], tag="l2idxld",
                  allow_slow_non_contiguous=True)
        for gi in range(64):
            par = gi % 2
            p.op("gpsimd", lambda e, gi=gi, par=par: e.indirect_dma_start(
                    vtok[:, par * D:(par + 1) * D], None, xnd,
                    IndirectOffsetOnAxis(idx2[:, gi:gi + 1], 0)),
                 reads=["l2_idx2", "xnd_w", f"l2_vt{par}"],
                 writes=[f"l2_vt{par}"], dma=True, tag="l2gather")
            trps = PS1.bitcast(BF16)[:, 2048 + par * 1024:
                                     2048 + par * 1024 + 1024]
            tk = pk1(1024 + par * 512, 1536 + par * 512)
            for c in range(NCH):
                p.trans(trps[:, c * 128:(c + 1) * 128],
                        vtok[:, par * D + c * 128: par * D + (c + 1) * 128],
                        env.id128[:, :], reads=[f"l2_vt{par}", "id128"],
                        writes=tk, tag="l2vtr")
            p.copy(xsel[:, par * D:(par + 1) * D], trps[:, :],
                   reads=tk, writes=[f"l2_xs{par}"], tag="l2xsc")
            for eh in range(2):
                es = 2 * gi + eh
                sl = 2 * par + eh
                env.wload(w1_sb[:, sl * NCH * 32:(sl + 1) * NCH * 32]
                          .rearrange("p (c o) -> p c o", c=NCH),
                          W["l2_W1"][es].rearrange("c p o -> p c o"),
                          f"l2_w1{sl}")
                env.wload(w2_sb[0:32, sl * D:(sl + 1) * D], W["l2_W2t"][es],
                          f"l2_w2{sl}")
                env.wload(b2_sb[0:1, sl * D:(sl + 1) * D],
                          W["l2_b2"][es:es + 1, :], f"l2_b2{sl}")
                hps = PS1[0:32, 512 + sl * 64: 512 + sl * 64 + 64]
                hk = pk1(512, 1024)
                for kc in range(NCH):
                    p.mm(hps, w1_sb[:, sl * NCH * 32 + kc * 32:
                                    sl * NCH * 32 + (kc + 1) * 32],
                         xsel[:, par * D + kc * 128 + eh * 64:
                              par * D + kc * 128 + eh * 64 + 64],
                         start=(kc == 0), stop=(kc == NCH - 1),
                         reads=[f"l2_w1{sl}", f"l2_xs{par}"],
                         writes=hk, tag="l2hmm")
                p.act(h_sb[0:32, sl * 64:(sl + 1) * 64], hps, AF.Relu,
                      bias=b1_sb[:, es:es + 1], reads=hk + ["l2_b1s"],
                      writes=[f"l2_h{sl}"], tag="l2hev")
                yps = PS2[0:64, eh * 1024:(eh + 1) * 1024]
                for nh in range(2):
                    yk = pk2(eh * 1024 + nh * 512, eh * 1024 + nh * 512 + 512)
                    p.mm(yps[:, nh * 512:(nh + 1) * 512],
                         h_sb[0:32, sl * 64:(sl + 1) * 64],
                         w2_sb[0:32, sl * D + nh * 512: sl * D + nh * 512 + 512],
                         start=True, stop=False,
                         reads=[f"l2_h{sl}", f"l2_w2{sl}"], writes=yk,
                         tag="l2ymm")
                    p.mm(yps[:, nh * 512:(nh + 1) * 512],
                         env.ones_row[:, 0:64],
                         b2_sb[0:1, sl * D + nh * 512: sl * D + nh * 512 + 512],
                         start=False, stop=True,
                         reads=[f"l2_b2{sl}", "ones_row"], writes=yk,
                         tag="l2yb2")
                p.act(ye_sb[0:64, eh * D:(eh + 1) * D], yps, AF.Copy,
                      reads=pk2(eh * 1024, (eh + 1) * 1024),
                      writes=[f"l2_ye{eh}"], tag="l2yev")
                p.dma("gpsimd", yed[es, :, :], ye_sb[0:64, eh * D:(eh + 1) * D],
                      reads=[f"l2_ye{eh}"], writes=["l2_yed"], tag="l2yout")
        for g in range(64):
            par = g % 2
            p.dma("sync", yeg[:, par * D:(par + 1) * D], yed[:, g, :],
                  reads=["l2_yed", f"l2_yg{par}"], writes=[f"l2_yg{par}"],
                  tag="l2ygl")
            ps = PS1[0:32, par * 1024: par * 1024 + 1024]
            for nh in range(2):
                p.mm(ps[:, nh * 512:(nh + 1) * 512],
                     perm[:, g * 32:(g + 1) * 32],
                     yeg[:, par * D + nh * 512: par * D + nh * 512 + 512],
                     reads=["l2_perm", f"l2_yg{par}"],
                     writes=pk1(par * 1024 + nh * 512,
                                par * 1024 + nh * 512 + 512), tag="l2fmm")
            finale(g, ps, pk1(par * 1024, par * 1024 + 1024))
    p.barrier()


def build_layer2(wshapes):
    nc = bass.Bass("TRN2", target_bir_lowering=False)
    kind_dbg = "ExternalOutput" if DEBUG_DUMP else "Internal"
    x_in = nc.dram_tensor("x_in", [T, D], F32, kind="ExternalInput").ap()
    x_out = nc.dram_tensor("x_out", [T, D], F32, kind="ExternalOutput").ap()
    s_scr2 = nc.dram_tensor("s_scr2", [T, D], BF16, kind=kind_dbg).ap()
    xnd = nc.dram_tensor("xnd", [T, D], BF16, kind=kind_dbg).ap()
    yed = nc.dram_tensor("yed", [128, 64, D], BF16, kind=kind_dbg).ap()
    rscr = nc.dram_tensor("rscr", [16, T], BF16, kind="Internal").ap()
    sodram = nc.dram_tensor("l2_sod", [16, 128, T], BF16, kind=kind_dbg).ap()
    idxd = nc.dram_tensor("l2_idxd", [128, 64], U32, kind="Internal").ap()
    W = {k: nc.dram_tensor(k, list(v), F32, kind="ExternalInput").ap()
         for k, v in wshapes.items()}
    env = Env(nc)
    make_consts(env)
    env.rscr = rscr
    env.l2_sodram = sodram
    env.l2_idxd = idxd
    env.xn_fm = env.sb("xn_fm", [128, NCH * T], BF16)
    emit_ln_phase(env, x_in, xn_tok_dump=xnd)
    p = env.p
    sst = env.sb("ss2_st", [128, 512], BF16)
    stok = env.sb("ss2_tk", [128, 512], BF16)

    def s_sink_tok(dc, qc, ap, rkeys):
        p.tt(sst[:, :], ap,
             env.xn_fm[:, dc * T + qc * 512: dc * T + qc * 512 + 512],
             ALU.add, reads=list(rkeys) + ["xn_fm", "ss2_st"],
             writes=["ss2_st"], tag="ss2_add")
        trps = env.PS1.ap().bitcast(BF16)[:, 0:1024]
        for j in range(4):
            p.trans(trps[:, j * 128:(j + 1) * 128],
                    sst[:, j * 128:(j + 1) * 128], env.id128[:, :],
                    reads=["ss2_st", "id128"], writes=pk1(0, 512),
                    tag="ss2_tr")
        p.copy(stok[:, :], trps[:, 0:512], reads=pk1(0, 512),
               writes=["ss2_tk"], tag="ss2_cp")
        for j in range(4):
            tt = qc * 4 + j
            p.dma("gpsimd", s_scr2.rearrange("(n p) d -> n p d", p=128)
                  [tt, :, dc * 128:(dc + 1) * 128],
                  stok[:, j * 128:(j + 1) * 128],
                  reads=["ss2_tk"], writes=["s2w"], tag="ss2_out")
    emit_l2_attn(env, W, s_sink_tok)
    gb = env.sb("l2f_g", [32, 2 * D], BF16)
    sb2 = env.sb("l2f_s", [32, 2 * D], BF16)
    xb = env.sb("l2f_x", [32, 2 * D], F32)
    xiv = x_in.rearrange("(n p) d -> n p d", p=32)
    xov = x_out.rearrange("(n p) d -> n p d", p=32)
    siv = s_scr2.rearrange("(n p) d -> n p d", p=32)

    def finale(g, ap, rkeys):
        par = g % 2
        p.dma("sync", sb2[:, par * D:(par + 1) * D], siv[g, :, :],
              reads=["s2w", f"l2f_s{par}"], writes=[f"l2f_s{par}"],
              tag="l2f_lds")
        p.dma("sync", xb[:, par * D:(par + 1) * D], xiv[g, :, :],
              reads=[f"l2f_x{par}"], writes=[f"l2f_x{par}"], tag="l2f_ldx")
        p.act(gb[:, par * D:(par + 1) * D], ap, AF.Silu,
              reads=list(rkeys) + [f"l2f_g{par}"], writes=[f"l2f_g{par}"],
              tag="l2f_silu")
        p.tt(gb[:, par * D:(par + 1) * D], gb[:, par * D:(par + 1) * D],
             sb2[:, par * D:(par + 1) * D], ALU.mult,
             reads=[f"l2f_g{par}", f"l2f_s{par}"], writes=[f"l2f_g{par}"],
             tag="l2f_sg")
        p.tt(xb[:, par * D:(par + 1) * D], xb[:, par * D:(par + 1) * D],
             gb[:, par * D:(par + 1) * D], ALU.add,
             reads=[f"l2f_g{par}", f"l2f_x{par}"], writes=[f"l2f_x{par}"],
             tag="l2f_add")
        p.dma("gpsimd", xov[g, :, :], xb[:, par * D:(par + 1) * D],
              reads=[f"l2f_x{par}"], writes=[f"l2f_x{par}"], tag="out_x")
    emit_l2_ffn(env, W, xnd, yed, finale)
    env.p.build()
    return nc


LAYER_BUILDERS[2] = (host_prep_l2, build_layer2)
def host_prep_l0(inp):
    w = {}
    for nm, key in (("Wq", "l0_wq"), ("Wk", "l0_wk"), ("Wv", "l0_wv")):
        w[key] = np.ascontiguousarray(
            np.asarray(inp[f"L0_attn_{nm}"])).reshape(NCH, 128, D)
    Wo = np.asarray(inp["L0_attn_Wo"]).reshape(16, 64, D)
    perm = np.zeros((16, 64, D), Wo.dtype)
    pr = np.empty((NCH, 2), np.int64)
    for h in range(16):
        g, hl = h // 4, h % 4
        ch, rh = 2 * g + (hl & 1), hl >> 1
        perm[2 * ch + rh] = Wo[h]
    w["l0_wo"] = np.ascontiguousarray(perm.reshape(D, D)).reshape(NCH, 128, D)
    w["l0_ab"] = np.ascontiguousarray(np.concatenate(
        [np.asarray(inp["L0_attn_alpha_raw"]).reshape(1, 16),
         np.asarray(inp["L0_attn_beta_raw"]).reshape(1, 16)], 1))
    DEPTH, PAR, NN = 7, 8, 255
    W_in = np.asarray(inp["L0_ffn_W_in"])
    b_in = np.asarray(inp["L0_ffn_b_in"])
    W_out = np.asarray(inp["L0_ffn_W_out"])
    def bitrev(j, nb):
        r = 0
        for i in range(nb):
            r = (r << 1) | ((j >> i) & 1)
        return r
    LOFF = [0, 1, 2, 3, 4, 5, 7, 11]
    NPAD = 19 * 128
    Wp = np.zeros((NPAD, D), np.float32)
    bp = np.zeros((NPAD,), np.float32)
    Wop = np.zeros((NPAD, D), np.float32)
    for d in range(8):
        for j in range(2 ** d):
            node = (2 ** d - 1) + bitrev(j, d)
            for par in range(PAR):
                src = par * NN + node
                dst = LOFF[d] * 128 + j * 8 + par
                Wp[dst] = W_in[src]
                bp[dst] = b_in[src]
                Wop[dst] = W_out[:, src]
    w["l0_winT"] = np.ascontiguousarray(Wp.T).reshape(NCH, 128, NPAD)
    w["l0_bin"] = np.ascontiguousarray(bp.reshape(19, 128).T)
    w["l0_woutT"] = np.ascontiguousarray(Wop).reshape(19, 128, D)
    return w


def l0_make_consts(env):
    sel2 = np.zeros((2, 128), np.float32)
    sel2[0, 0:64] = 1.0
    sel2[1, 64:128] = 1.0
    env.pairsel2 = env.load_const_sb("pairsel2", sel2.astype(bfdt))
    sl = np.tril(np.ones((128, 128), np.float32), -1)
    su = np.triu(np.ones((128, 128), np.float32), 1)
    eye = np.eye(128, dtype=np.float32)
    env.l0_sl4 = env.load_const_sb("l0_sl4", np.tile(sl, (1, 4)).astype(bfdt))
    env.l0_su4 = env.load_const_sb("l0_su4", np.tile(su, (1, 4)).astype(bfdt))
    env.l0_id4 = env.load_const_sb("l0_id4", np.tile(eye, (1, 4)).astype(bfdt))
    env.l0_tree = {}
    for d in range(7):
        nin = 8 * 2 ** d
        nout = 8 * 2 ** (d + 1)
        rows_in = min(128, nin)
        for ot in range((nout + 127) // 128):
            orow = min(128, nout - ot * 128)
            js = range((ot * 128) // 8, (ot * 128 + orow) // 8)
            pj0 = (js.start) % (2 ** d)
            st = (pj0 * 8) // 128
            selp = np.zeros((rows_in, orow), np.float32)
            selge = np.zeros((rows_in, orow), np.float32)
            sellt = np.zeros((rows_in, orow), np.float32)
            for j in js:
                b, pj = j >> d, j % (2 ** d)
                prow = pj * 8 - st * 128
                assert 0 <= prow < rows_in
                for par in range(8):
                    selp[prow + par, (j * 8 + par) - ot * 128] = 1.0
                    (selge if b else sellt)[prow + par,
                                            (j * 8 + par) - ot * 128] = 1.0
            env.l0_tree[(d, ot)] = (st, rows_in,
                                    env.load_const_sb(f"l0_selp{d}_{ot}",
                                                      selp.astype(bfdt)),
                                    env.load_const_sb(f"l0_selge{d}_{ot}",
                                                      selge.astype(bfdt)),
                                    env.load_const_sb(f"l0_sellt{d}_{ot}",
                                                      sellt.astype(bfdt)))


def emit_l0_attn(env, W, s_sink):
    p, nc = env.p, env.nc
    PS1, PS2 = env.PS1.ap(), env.PS2.ap()
    p.barrier()
    from contextlib import ExitStack as _ES
    with _ES() as _es:
        _e = _es.enter_context
        y_fm = _e(nc.sbuf_tensor("l0_y", [128, NCH * T], BF16))
        abrow = _e(nc.sbuf_tensor("l0_abrow", [1, 48], F32))
        abrep = _e(nc.sbuf_tensor("l0_abrep", [128, 32], F32))
        rnb4 = _e(nc.sbuf_tensor("l0_rnb", [1, 32], BF16))
        _es_grp = _ES()
        _eg = _es_grp.enter_context
        qf = _eg(nc.sbuf_tensor("l0_qf", [64, 4 * T], BF16))
        kaf = _eg(nc.sbuf_tensor("l0_kaf", [64, 4 * T], BF16))
        ktok = _eg(nc.sbuf_tensor("l0_kt", [128, 4 * 1024], BF16))
        vtok = _eg(nc.sbuf_tensor("l0_vt", [128, 4 * 1024], BF16))
        wst = _eg(nc.sbuf_tensor("l0_wst", [128, 3 * NCH * 256], BF16))
        ptk = _eg(nc.sbuf_tensor("l0_ptk", [128, 512], BF16))
        sqs = _eg(nc.sbuf_tensor("l0_sqs", [128, 512], BF16))
        nrm = _eg(nc.sbuf_tensor("l0_nrm", [128, 64], F32))
        rnh = _eg(nc.sbuf_tensor("l0_rnh", [128, 64], F32))
        tl_b = _eg(nc.sbuf_tensor("l0_tl", [128, 2048], BF16))
        x_b = _eg(nc.sbuf_tensor("l0_x", [128, 4 * 2048], BF16))
        xt_b = _eg(nc.sbuf_tensor("l0_xt", [128, 2048], BF16))
        pm_b = _eg(nc.sbuf_tensor("l0_pm", [128, 4 * 2048], BF16))
        hbuf = _eg(nc.sbuf_tensor("l0_h", [128, 512], BF16))
        cs_b = _eg(nc.sbuf_tensor("l0_cs", [128, 4 * 64], BF16))
        us_b = _eg(nc.sbuf_tensor("l0_us", [128, 4 * 64], BF16))
        m_b = _eg(nc.sbuf_tensor("l0_m", [64, 4 * 64], BF16))
        p.dma("sync", abrow[0:1, 0:32], W["l0_ab"], writes=["l0_abr"],
              tag="l0ab")
        p.act(abrow[0:1, 0:32], abrow[0:1, 0:32], AF.Sigmoid,
              reads=["l0_abr"], writes=["l0_abs"], tag="l0abs")
        p.ts(abrow[0:1, 0:32], abrow[0:1, 0:32], 0.5, ALU.mult,
             reads=["l0_abs"], writes=["l0_ab2"], tag="l0ab2")
        p.act(abrow[0:1, 32:48], abrow[0:1, 0:16], AF.Sqrt,
              reads=["l0_ab2"], writes=["l0_sqa"], tag="l0sqa")
        p.op("vector", lambda e: e.reciprocal(abrow[0:1, 0:16],
                                              abrow[0:1, 32:48]),
             reads=["l0_sqa"], writes=["l0_rsqa"], tag="l0rsq")
        p.tt(abrow[0:1, 16:32], abrow[0:1, 16:32], abrow[0:1, 0:16], ALU.mult,
             reads=["l0_ab2", "l0_rsqa"], writes=["l0_bp"], tag="l0bp")
        p.act(rnb4[0:1, 0:16], abrow[0:1, 32:48], AF.Copy,
              reads=["l0_sqa"], writes=["l0_sqab"], tag="l0sqcast")
        p.act(rnb4[0:1, 16:32], abrow[0:1, 16:32], AF.Copy,
              reads=["l0_bp"], writes=["l0_bpb"], tag="l0bpcast")
        p.mm(PS2[:, 1536:1552], env.ones_row[:, :], rnb4[0:1, 0:16],
             reads=["l0_sqab", "ones_row"], writes=pk2(1536, 2048),
             tag="l0sqarep")
        p.act(abrep[:, 0:16], PS2[:, 1536:1552], AF.Copy,
              reads=pk2(1536, 2048), writes=["l0_sqarep"], tag="l0sqev")
        p.mm(PS2[:, 1536:1552], env.ones_row[:, :], rnb4[0:1, 16:32],
             reads=["l0_bpb", "ones_row"], writes=pk2(1536, 2048),
             tag="l0bprep")
        p.act(abrep[:, 16:32], PS2[:, 1536:1552], AF.Copy,
              reads=pk2(1536, 2048), writes=["l0_bprep"], tag="l0bpev")
        for g in range(4):
            p.barrier()
            for i, nm in enumerate(("l0_wq", "l0_wk", "l0_wv")):
                env.wload(wst[:, i * 2048:(i + 1) * 2048].rearrange(
                              "p (c o) -> p c o", c=NCH),
                          W[nm][:, :, g * 256:(g + 1) * 256].rearrange(
                              "c p o -> p c o"), "l0_wst")
            for tt in range(NT):
                pr = tt % 2
                psq = PS1[:, pr * 256: pr * 256 + 256]
                for kc in range(NCH):
                    p.mm(psq, env.xn_fm[:, kc * T + tt * 128: kc * T + (tt + 1) * 128],
                         wst[:, 0 * 2048 + kc * 256: 0 * 2048 + (kc + 1) * 256],
                         start=(kc == 0), stop=(kc == NCH - 1),
                         reads=["l0_wst", "xn_fm"], writes=[f"PA0{pr}"],
                         tag="l0pq")
                psk = PS1[:, 512 + pr * 256: 512 + pr * 256 + 256]
                for kc in range(NCH):
                    p.mm(psk, env.xn_fm[:, kc * T + tt * 128: kc * T + (tt + 1) * 128],
                         wst[:, 1 * 2048 + kc * 256: 1 * 2048 + (kc + 1) * 256],
                         start=(kc == 0), stop=(kc == NCH - 1),
                         reads=["l0_wst", "xn_fm"], writes=[f"PA1{pr}"],
                         tag="l0pk")
                psv = PS1[:, 1024 + pr * 256: 1024 + pr * 256 + 256]
                for kc in range(NCH):
                    p.mm(psv, env.xn_fm[:, kc * T + tt * 128: kc * T + (tt + 1) * 128],
                         wst[:, 2 * 2048 + kc * 256: 2 * 2048 + (kc + 1) * 256],
                         start=(kc == 0), stop=(kc == NCH - 1),
                         reads=["l0_wst", "xn_fm"], writes=[f"PA2{pr}"],
                         tag="l0pv")
                p.act(vtok[:, :].rearrange("p (h c o) -> p h c o", h=4, c=NT)
                      [:, :, tt, :],
                      psv.rearrange("p (h o) -> p h o", h=4),
                      AF.Copy, reads=[f"PA2{pr}"], writes=["l0_vt"],
                      tag="l0vev")
                ptkh = ptk[:, pr * 256: pr * 256 + 256]
                sqsh = sqs[:, pr * 256: pr * 256 + 256]
                p.act(ptkh, psk, AF.Copy, reads=[f"PA1{pr}", f"l0_ptk{pr}"],
                      writes=[f"l0_ptk{pr}"], tag="l0kev")
                p.act(sqsh, ptkh, AF.Square, reads=[f"l0_ptk{pr}", f"l0_sqs{pr}"],
                      writes=[f"l0_sqs{pr}"], tag="l0ksq")
                p.red(nrm[:, tt * 4:(tt + 1) * 4],
                      sqsh.rearrange("p (h o) -> p h o", h=4), ALU.add,
                      reads=[f"l0_sqs{pr}"], writes=["l0_nrm"], tag="l0knr")
                p.copy(ktok[:, :].rearrange("p (h c o) -> p h c o", h=4, c=NT)
                       [:, :, tt, :],
                       ptkh.rearrange("p (h o) -> p h o", h=4),
                       reads=[f"l0_ptk{pr}"], writes=["l0_ktraw"], tag="l0kst")
                for hl in range(4):
                    p.trans(PS1.bitcast(BF16)[0:64, 3072 + pr * 512 + hl * 128:
                                              3072 + pr * 512 + hl * 128 + 128],
                            ptkh[:, hl * 64:(hl + 1) * 64].bitcast(BF16)
                            if False else ptkh[:, hl * 64:(hl + 1) * 64],
                            env.id128[:, :],
                            reads=[f"l0_ptk{pr}", "id128"],
                            writes=[f"PA3{pr}"], tag="l0qtr") if False else None
                p.act(ptkh, psq, AF.Copy, reads=[f"PA0{pr}", f"l0_ptk{pr}",
                                                 "l0_ktraw"],
                      writes=[f"l0_ptk{pr}"], tag="l0qev")
                for hl in range(4):
                    p.trans(PS1.bitcast(BF16)[0:64, 3072 + pr * 512 + hl * 128:
                                              3072 + pr * 512 + hl * 128 + 128],
                            ptkh[:, hl * 64:(hl + 1) * 64],
                            env.id128[:, :],
                            reads=[f"l0_ptk{pr}", "id128"],
                            writes=[f"PA3{pr}"], tag="l0qtr")
                p.copy(qf[0:64, :].rearrange("p (h c o) -> p h c o", h=4, c=NT)
                       [:, :, tt, :],
                       PS1.bitcast(BF16)[0:64, 3072 + pr * 512:
                                         3072 + pr * 512 + 512].rearrange(
                           "p (h o) -> p h o", h=4),
                       reads=[f"PA3{pr}"], writes=["l0_qf"], tag="l0qtc")
            p.act(rnh[:, :], nrm[:, :], AF.Sqrt, reads=["l0_nrm"],
                  writes=["l0_rh1"], tag="l0rh1")
            p.ts(rnh[:, :], rnh[:, :], 1e-12, ALU.max, reads=["l0_rh1"],
                 writes=["l0_rh2"], tag="l0rh2")
            p.act(rnh[:, :], rnh[:, :], AF.Ln, reads=["l0_rh2"],
                  writes=["l0_rh3"], tag="l0rh3")
            p.act(rnh[:, :], rnh[:, :], AF.Exp, scale=-1.0, reads=["l0_rh3"],
                  writes=["l0_rh4"], tag="l0rh4")
            for hl in range(4):
                h = 4 * g + hl
                p.ts(rnh[:, :].rearrange("p (c h) -> p c h", c=NT)[:, :, hl],
                     rnh[:, :].rearrange("p (c h) -> p c h", c=NT)[:, :, hl],
                     abrep[:, h:h + 1], ALU.mult,
                     reads=["l0_rh4", "l0_sqarep"], writes=["l0_rh"],
                     tag="l0rh5")
            bpv = bass.AP(abrep, 16 + 4 * g, [[32, 128], [1, 4], [0, 1024]])
            p.tt(vtok[:, :].rearrange("p (h co) -> p h co", h=4),
                 vtok[:, :].rearrange("p (h co) -> p h co", h=4), bpv, ALU.mult,
                 reads=["l0_vt", "l0_bprep"], writes=["l0_vt"], tag="l0vbp")
            for tt in range(NT):
                pr = tt % 2
                ktv = ktok[:, :].rearrange("p (h c o) -> p h c o", h=4, c=NT)
                rnv = bass.AP(rnh, tt * 4,
                              [[64, 128], [1, 4], [0, 64]])
                p.tt(ktv[:, :, tt, :], ktv[:, :, tt, :], rnv, ALU.mult,
                     reads=["l0_ktraw", "l0_rh", "l0_kt"], writes=["l0_kt"],
                     tag="l0kap")
                for hl in range(4):
                    p.trans(PS1.bitcast(BF16)[0:64, 3072 + pr * 512 + hl * 128:
                                              3072 + pr * 512 + hl * 128 + 128],
                            ktv[:, hl, tt, :], env.id128[:, :],
                            reads=["l0_kt", "id128"], writes=[f"PA3{pr}"],
                            tag="l0katr")
                p.copy(kaf[0:64, :].rearrange("p (h c o) -> p h c o", h=4, c=NT)
                       [:, :, tt, :],
                       PS1.bitcast(BF16)[0:64, 3072 + pr * 512:
                                         3072 + pr * 512 + 512].rearrange(
                           "p (h o) -> p h o", h=4),
                       reads=[f"PA3{pr}"], writes=["l0_kaf"], tag="l0katc")
            p.barrier()
            for hl in range(4):
                kfh = kaf[0:64, hl * T:(hl + 1) * T]
                qfh = qf[0:64, hl * T:(hl + 1) * T]
                for cg in range(4):
                    ps = PS1[:, (cg % 2) * 512:(cg % 2) * 512 + 512]
                    pky = pk1((cg % 2) * 512, (cg % 2) * 512 + 512)
                    for j in range(4):
                        c = 4 * cg + j
                        p.mm(ps[:, j * 128:(j + 1) * 128],
                             kfh[:, c * 128:(c + 1) * 128],
                             kfh[:, c * 128:(c + 1) * 128],
                             reads=["l0_kaf"], writes=pky, tag="l0gram")
                    p.tt(hbuf[:, :], ps, env.l0_sl4[:, :], ALU.mult,
                         reads=pky + ["l0_sl4"], writes=["l0_hb"], tag="l0msl")
                    p.tt(tl_b[:, cg * 512:(cg + 1) * 512],
                         hbuf[:, :], env.l0_id4[:, :], ALU.add, engine="gpsimd",
                         reads=["l0_hb", "l0_id4"], writes=["l0_tl"], tag="l0tl")
                    p.stt(xt_b[:, cg * 512:(cg + 1) * 512],
                          hbuf[:, :], -1.0, env.l0_id4[:, :], ALU.mult, ALU.add,
                          reads=["l0_hb", "l0_id4"], writes=["l0_xt"],
                          tag="l0x0t")
                    p.tt(hbuf[:, :], ps, env.l0_su4[:, :], ALU.mult,
                         reads=pky + ["l0_su4"], writes=["l0_hb"], tag="l0msu")
                    p.stt(x_b[:, hl * 2048 + cg * 512: hl * 2048 + (cg + 1) * 512],
                          hbuf[:, :], -1.0, env.l0_id4[:, :], ALU.mult, ALU.add,
                          reads=["l0_hb", "l0_id4"],
                          writes=["l0_x"], tag="l0x0")
                    for j in range(4):
                        c = 4 * cg + j
                        p.mm(ps[:, j * 128:(j + 1) * 128],
                             kfh[:, c * 128:(c + 1) * 128],
                             qfh[:, c * 128:(c + 1) * 128],
                             reads=["l0_kaf", "l0_qf"], writes=pky, tag="l0pq2")
                    p.tt(pm_b[:, hl * 2048 + cg * 512: hl * 2048 + (cg + 1) * 512],
                         ps, env.l0_su4[:, :], ALU.mult,
                         reads=pky + ["l0_su4"], writes=["l0_pm"], tag="l0pmm")
                for it in range(2):
                    for cg in range(4):
                        psH = PS1[:, (cg % 2) * 512:(cg % 2) * 512 + 512]
                        pkH = pk1((cg % 2) * 512, (cg % 2) * 512 + 512)
                        base = hl * 2048 + cg * 512
                        tb = cg * 512
                        for j in range(4):
                            p.mm(psH[:, j * 128:(j + 1) * 128],
                                 tl_b[:, tb + j * 128: tb + (j + 1) * 128],
                                 x_b[:, base + j * 128: base + (j + 1) * 128],
                                 reads=["l0_tl", "l0_x"], writes=pkH,
                                 tag="l0nH")
                        p.act(hbuf[:, :], psH, AF.Copy, reads=pkH,
                              writes=["l0_hb"], tag="l0nHe")
                        psX = PS1[:, 1024 + (cg % 2) * 512:
                                  1024 + (cg % 2) * 512 + 512]
                        pkX = pk1(1024 + (cg % 2) * 512, 1536 + (cg % 2) * 512)
                        for j in range(4):
                            p.mm(psX[:, j * 128:(j + 1) * 128],
                                 xt_b[:, tb + j * 128: tb + (j + 1) * 128],
                                 hbuf[:, j * 128:(j + 1) * 128],
                                 reads=["l0_xt", "l0_hb"], writes=pkX,
                                 tag="l0nXH")
                        p.stt(x_b[:, base:base + 512], x_b[:, base:base + 512],
                              2.0, psX, ALU.mult, ALU.subtract,
                              reads=pkX + ["l0_x"], writes=["l0_x"], tag="l0nX")
                        if it < 1:
                            psT = PS1.bitcast(BF16)[:, (cg % 2) * 1024:
                                                    (cg % 2) * 1024 + 1024]
                            pkT = pk1((cg % 2) * 512, (cg % 2) * 512 + 512)
                            for j in range(4):
                                p.trans(psT[:, j * 256:j * 256 + 128],
                                        x_b[:, base + j * 128:
                                            base + (j + 1) * 128],
                                        env.id128[:, :],
                                        reads=["l0_x", "id128"], writes=pkT,
                                        tag="l0nT")
                            p.copy(xt_b[:, tb:tb + 512].rearrange(
                                       "p (j o) -> p j o", j=4),
                                   psT[:, 0:1024].rearrange(
                                       "p (j o) -> p j o", j=4)[:, :, 0:128],
                                   reads=pkT, writes=["l0_xt"], tag="l0nTe")
            for c in range(NT):
                for hl in range(4):
                    ch = 2 * g + (hl & 1)
                    rh = (hl >> 1) * 64
                    kfh = kaf[0:64, hl * T + c * 128: hl * T + (c + 1) * 128]
                    qfh = qf[0:64, hl * T + c * 128: hl * T + (c + 1) * 128]
                    vslice = vtok[:, (hl * NT + c) * 64:(hl * NT + c + 1) * 64]
                    kslice = ktok[:, (hl * NT + c) * 64:(hl * NT + c + 1) * 64]
                    PSB, c0, pkB = ((PS2, 1024, ["PB2"]) if hl == 0 else
                                    (PS2, 1536, ["PB3"]) if hl == 1 else
                                    (PS1, 1024, ["PA2"]) if hl == 2 else
                                    (PS1, 1536, ["PA3"]))
                    psC = PSB[:, c0: c0 + 64]
                    if c > 0:
                        p.mm(psC, kfh, m_b[0:64, hl * 64:(hl + 1) * 64],
                             reads=["l0_kaf", f"l0_m{hl}"], writes=pkB,
                             tag="l0C2")
                        p.tt(cs_b[:, hl * 64:(hl + 1) * 64], vslice, psC,
                             ALU.subtract, reads=pkB + ["l0_vt"],
                             writes=[f"l0_cs{hl}"], tag="l0Ce")
                    psU = PSB[:, c0 + 64: c0 + 128]
                    p.mm(psU, x_b[:, hl * 2048 + c * 128: hl * 2048 + (c + 1) * 128],
                         vslice if c == 0 else cs_b[:, hl * 64:(hl + 1) * 64],
                         reads=["l0_x", "l0_vt", f"l0_cs{hl}"], writes=pkB,
                         tag="l0U")
                    p.act(us_b[:, hl * 64:(hl + 1) * 64], psU, AF.Copy,
                          reads=pkB + [f"l0_us{hl}"], writes=[f"l0_us{hl}"],
                          tag="l0Ue")
                    psY = PSB[rh:rh + 64, c0 + 128: c0 + 256]
                    if c > 0:
                        p.mm(psY, m_b[0:64, hl * 64:(hl + 1) * 64], qfh,
                             start=True, stop=False,
                             reads=[f"l0_m{hl}", "l0_qf"], writes=pkB,
                             tag="l0Y1")
                    p.mm(psY, us_b[:, hl * 64:(hl + 1) * 64],
                         pm_b[:, hl * 2048 + c * 128: hl * 2048 + (c + 1) * 128],
                         start=(c == 0), stop=True,
                         reads=[f"l0_us{hl}", "l0_pm"], writes=pkB, tag="l0Y2")
                    p.copy(y_fm[rh:rh + 64, ch * T + c * 128: ch * T + (c + 1) * 128],
                           psY, reads=pkB + ["l0_yf"], writes=["l0_yf"],
                           tag="l0Ye")
                    if hl < 2:
                        psM = PS2[0:64, hl * 512: hl * 512 + 64]
                        pkM = [f"PB{hl}"]
                    else:
                        psM = PS1[0:64, (hl - 2) * 512: (hl - 2) * 512 + 64]
                        pkM = [f"PA{hl - 2}"]
                    p.mm(psM, kslice, us_b[:, hl * 64:(hl + 1) * 64],
                         start=(c == 0), stop=(c == NT - 1),
                         reads=["l0_kt", f"l0_us{hl}"], writes=pkM, tag="l0M")
                    if c < NT - 1:
                        p.act(m_b[0:64, hl * 64:(hl + 1) * 64], psM, AF.Copy,
                              reads=pkM + [f"l0_m{hl}"], writes=[f"l0_m{hl}"],
                              tag="l0Me")
        if DEBUG_DUMP:
            for nm_, tns, sh in (("dbg_y", y_fm, [128, NCH * T]),
                                 ("dbg_qf", qf, [64, 4 * T]),
                                 ("dbg_kaf", kaf, [64, 4 * T]),
                                 ("dbg_ktok", ktok, [128, 4096]),
                                 ("dbg_vtok", vtok, [128, 4096]),
                                 ("dbg_xb", x_b, [128, 4 * 2048]),
                                 ("dbg_pm", pm_b, [128, 4 * 2048])):
                dd_ = nc.dram_tensor(nm_, sh, BF16, kind="ExternalOutput").ap()
                p.dma("gpsimd", dd_, tns[0:sh[0], :],
                      reads=["l0_yf", "l0_qf", "l0_kaf", "l0_kt", "l0_vt",
                             "l0_x", "l0_pm"], tag=nm_)
        _es_grp.close()
        p.barrier()
        wo_sb = _e(nc.sbuf_tensor("l0_wo_sb", [128, NCH * D], BF16))
        env.wload(wo_sb[:, :].rearrange("p (c o) -> p c o", c=NCH),
                  W["l0_wo"].rearrange("c p o -> p c o"), "l0_wo")
        for dc in range(NCH):
            for qc in range(4):
                c0 = (qc % 2) * 512
                ps = PS2[:, c0:c0 + 512]
                for kc in range(NCH):
                    p.mm(ps, wo_sb[:, kc * D + dc * 128: kc * D + (dc + 1) * 128],
                         y_fm[:, kc * T + qc * 512: kc * T + qc * 512 + 512],
                         start=(kc == 0), stop=(kc == NCH - 1),
                         reads=["l0_wo", "l0_yf"], writes=pk2(c0, c0 + 512),
                         tag="l0omm")
                s_sink(dc, qc, ps, pk2(c0, c0 + 512))
    p.barrier()


def emit_l0_ffn(env, W, g_sink):
    p, nc = env.p, env.nc
    PS1, PS2 = env.PS1.ap(), env.PS2.ap()
    p.barrier()
    LOFF = [0, 1, 2, 3, 4, 5, 7, 11, 19]
    from contextlib import ExitStack as _ES
    with _ES() as _es:
        _e = _es.enter_context
        acts = _e(nc.sbuf_tensor("l0f_acts", [128, 19 * 512], BF16))
        Fb = _e(nc.sbuf_tensor("l0f_F", [128, 19 * 512], BF16))
        win = _e(nc.sbuf_tensor("l0f_win", [128, 2 * NCH * 128], BF16))
        wout = _e(nc.sbuf_tensor("l0f_wout", [128, 2 * 19 * 128], BF16))
        binc = _e(nc.sbuf_tensor("l0f_bin", [128, 19], F32))
        dd = _e(nc.sbuf_tensor("l0f_dd", [128, 512], BF16))
        ge_sb = _e(nc.sbuf_tensor("l0f_ge", [128, 8 * 512], BF16))
        lt_sb = _e(nc.sbuf_tensor("l0f_lt", [128, 8 * 512], BF16))
        p.dma("sync", binc[:, :], W["l0_bin"], writes=["l0f_bin"], tag="l0fb")
        for tb in range(4):
            c0 = tb * 512
            p.memset(Fb[0:8, 0:512], 1.0, reads=["l0f_F"], writes=["l0f_F"],
                     tag="l0fF0")
            for lt in range(19):
                env.wload(win[:, (lt % 2) * 1024:(lt % 2) * 1024 + 1024]
                          .rearrange("p (c o) -> p c o", c=NCH),
                          W["l0_winT"][:, :, lt * 128:(lt + 1) * 128]
                          .rearrange("c p o -> p c o"), f"l0f_win{lt % 2}")
                ps = PS1[:, (lt % 2) * 512:(lt % 2) * 512 + 512]
                pky = pk1((lt % 2) * 512, (lt % 2) * 512 + 512)
                for kc in range(NCH):
                    p.mm(ps, win[:, (lt % 2) * 1024 + kc * 128:
                                 (lt % 2) * 1024 + (kc + 1) * 128],
                         env.xn_fm[:, kc * T + c0: kc * T + c0 + 512],
                         start=(kc == 0), stop=(kc == NCH - 1),
                         reads=[f"l0f_win{lt % 2}", "xn_fm"], writes=pky,
                         tag="l0flmm")
                p.act(acts[:, lt * 512:(lt + 1) * 512], ps, AF.Silu,
                      bias=binc[:, lt:lt + 1], reads=pky + ["l0f_bin"],
                      writes=["l0f_acts"], tag="l0flev")
            for d in range(7):
                for st in range(max(1, (8 * 2 ** d) // 128)):
                    lt = LOFF[d] + st
                    nrow = min(128, 8 * 2 ** d)
                    p.ts(ge_sb[0:nrow, st * 512:(st + 1) * 512],
                         acts[0:nrow, lt * 512:(lt + 1) * 512], 0.0, ALU.is_ge,
                         reads=["l0f_acts"], writes=["l0f_ge"], tag="l0fge")
                    p.ts(lt_sb[0:nrow, st * 512:(st + 1) * 512],
                         acts[0:nrow, lt * 512:(lt + 1) * 512], 0.0, ALU.is_lt,
                         reads=["l0f_acts"], writes=["l0f_lt"], tag="l0flt")
                nin = 8 * 2 ** d
                nout = 8 * 2 ** (d + 1)
                for ot in range((nout + 127) // 128):
                    orow = min(128, nout - ot * 128)
                    psP = PS1[:, 1024:1536]
                    psD = PS1[:, 1536:2048]
                    st, rows, selp, selg, sell = env.l0_tree[(d, ot)]
                    p.mm(psP[0:orow, 0:512], selp[0:rows, 0:orow],
                         Fb[0:rows, (LOFF[d] + st) * 512:
                            (LOFF[d] + st + 1) * 512],
                         reads=["l0f_F", f"l0_selp{d}_{ot}"],
                         writes=pk1(1024, 1536), tag="l0fP")
                    p.mm(psD[0:orow, 0:512], selg[0:rows, 0:orow],
                         ge_sb[0:rows, st * 512:(st + 1) * 512],
                         start=True, stop=False,
                         reads=["l0f_ge", f"l0_selge{d}_{ot}"],
                         writes=pk1(1536, 2048), tag="l0fDg")
                    p.mm(psD[0:orow, 0:512], sell[0:rows, 0:orow],
                         lt_sb[0:rows, st * 512:(st + 1) * 512],
                         start=False, stop=True,
                         reads=["l0f_lt", f"l0_sellt{d}_{ot}"],
                         writes=pk1(1536, 2048), tag="l0fDl")
                    p.act(dd[0:orow, 0:512], psD[0:orow, 0:512], AF.Copy,
                          reads=pk1(1536, 2048), writes=["l0f_dd"], tag="l0fDe")
                    p.tt(Fb[0:orow, (LOFF[d + 1] + ot) * 512:
                            (LOFF[d + 1] + ot + 1) * 512],
                         psP[0:orow, 0:512], dd[0:orow, 0:512], ALU.mult,
                         reads=pk1(1024, 1536) + ["l0f_dd", "l0f_F"],
                         writes=["l0f_F"], tag="l0fFm")
            for lt in range(19):
                p.tt(acts[:, lt * 512:(lt + 1) * 512],
                     acts[:, lt * 512:(lt + 1) * 512],
                     Fb[:, lt * 512:(lt + 1) * 512], ALU.mult,
                     reads=["l0f_F", "l0f_acts"], writes=["l0f_acts"],
                     tag="l0fmask")
            for dc in range(NCH):
                env.wload(wout[:, (dc % 2) * 19 * 128:
                               (dc % 2) * 19 * 128 + 19 * 128]
                          .rearrange("p (l o) -> p l o", l=19),
                          W["l0_woutT"][:, :, dc * 128:(dc + 1) * 128]
                          .rearrange("l p o -> p l o"), f"l0f_wo{dc % 2}")
                ps = PS2[:, (dc % 2) * 512:(dc % 2) * 512 + 512]
                pky = pk2((dc % 2) * 512, (dc % 2) * 512 + 512)
                for lt in range(19):
                    p.mm(ps, wout[:, (dc % 2) * 19 * 128 + lt * 128:
                                  (dc % 2) * 19 * 128 + (lt + 1) * 128],
                         acts[:, lt * 512:(lt + 1) * 512],
                         start=(lt == 0), stop=(lt == 18),
                         reads=[f"l0f_wo{dc % 2}", "l0f_acts"], writes=pky,
                         tag="l0fomm")
                g_sink(dc, tb, ps, pky)
    p.barrier()


def build_layer0(wshapes):
    nc = bass.Bass("TRN2", target_bir_lowering=False)
    kind_dbg = "ExternalOutput" if DEBUG_DUMP else "Internal"
    x_in = nc.dram_tensor("x_in", [T, D], F32, kind="ExternalInput").ap()
    x_out = nc.dram_tensor("x_out", [T, D], F32, kind="ExternalOutput").ap()
    s_scr = nc.dram_tensor("s_scr", [NCH, 128, T], BF16, kind=kind_dbg).ap()
    rscr = nc.dram_tensor("rscr", [16, T], BF16, kind="Internal").ap()
    W = {k: nc.dram_tensor(k, list(v), F32, kind="ExternalInput").ap()
         for k, v in wshapes.items()}
    env = Env(nc)
    make_consts(env)
    l0_make_consts(env)
    env.rscr = rscr
    env.xn_fm = env.sb("xn_fm", [128, NCH * T], BF16)
    emit_ln_phase(env, x_in)
    if DEBUG_DUMP:
        dxn = nc.dram_tensor("dbg_xn", [128, NCH * T], BF16,
                             kind="ExternalOutput").ap()
        env.p.dma("gpsimd", dxn, env.xn_fm[:, :], reads=["xn_fm"], tag="dxn")
    emit_l0_attn(env, W, make_s_sink(env, s_scr, mode="plain"))
    g_sink = make_residual_streamer(env, x_in, x_out, s_scr, blk=512)
    emit_l0_ffn(env, W, g_sink)
    env.p.build()
    return nc


LAYER_BUILDERS[0] = (host_prep_l0, build_layer0)


def _dedupe_names(nc):
    seen = {}

    def uniq(name):
        n = seen.get(name, 0)
        seen[name] = n + 1
        return name if n == 0 else f"{name}__{n}"
    orig_sb = nc.sbuf_tensor
    nc.sbuf_tensor = lambda name, *a, **kw: orig_sb(uniq(name), *a, **kw)
    orig_asb = nc.alloc_sbuf_tensor
    nc.alloc_sbuf_tensor = lambda name, *a, **kw: orig_asb(uniq(name), *a, **kw)
    orig_it = nc.inline_tensor
    nc.inline_tensor = lambda arr, name=None, **kw: orig_it(
        arr, name=uniq(name), **kw)


def build_fused(wshapes):
    from contextlib import ExitStack
    nc = bass.Bass("TRN2", target_bir_lowering=False)
    _dedupe_names(nc)
    x_in = nc.dram_tensor("x_in", [T, D], F32, kind="ExternalInput").ap()
    x_out = nc.dram_tensor("x_out", [T, D], F32, kind="ExternalOutput").ap()
    x1 = nc.dram_tensor("x_b1", [T, D], F32, kind="Internal").ap()
    x2 = nc.dram_tensor("x_b2", [T, D], F32, kind="Internal").ap()
    x3 = nc.dram_tensor("x_b3", [T, D], F32, kind="Internal").ap()
    s_scr = nc.dram_tensor("s_scr", [NCH, 128, T], BF16, kind="Internal").ap()
    rscr = nc.dram_tensor("rscr", [16, T], BF16, kind="Internal").ap()
    s_scr2 = nc.dram_tensor("s_scr2", [T, D], BF16, kind="Internal").ap()
    xnd = nc.dram_tensor("xnd", [T, D], BF16, kind="Internal").ap()
    yed = nc.dram_tensor("yed", [128, 64, D], BF16, kind="Internal").ap()
    sodram = nc.dram_tensor("l2_sod", [16, 128, T], BF16, kind="Internal").ap()
    idxd = nc.dram_tensor("l2_idxd", [128, 64], U32, kind="Internal").ap()
    W = {k: nc.dram_tensor(k, list(v), F32, kind="ExternalInput").ap()
         for k, v in wshapes.items()}
    env = Env(nc)
    make_consts(env)
    env.rscr = rscr
    env.l2_sodram = sodram
    env.l2_idxd = idxd
    env.xn_fm = env.sb("xn_fm", [128, NCH * T], BF16)
    p = env.p

    env.scope = ExitStack()
    p.set_scope("L0ln")
    l0_make_consts(env)
    emit_ln_phase(env, x_in)
    p.set_scope("L0attn")
    emit_l0_attn(env, W, make_s_sink(env, s_scr, mode="plain"))
    p.set_scope("L0ffn")
    emit_l0_ffn(env, W, make_residual_streamer(env, x_in, x1, s_scr, blk=512))
    p.barrier()
    env.scope.close()
    env.scope = None

    env.scope = ExitStack()
    p.set_scope("L1ln")
    make_consts_l1(env)
    env.scr15 = env.sb("scr15", [15, 128], BF16)
    emit_ln_phase(env, x1)
    p.set_scope("L1attn")
    emit_l1_attn(env, W, make_s_sink(env, s_scr, mode="add_xn"))
    p.set_scope("L1ffn")
    emit_l1_ffn(env, W, make_residual_streamer(env, x1, x2, s_scr))
    p.barrier()
    env.scope.close()
    env.scope = None

    env.scope = ExitStack()
    p.set_scope("L2ln")
    emit_ln_phase(env, x2, xn_tok_dump=xnd)
    sst2 = env.sb("ss2_st", [128, 512], BF16)
    stok = env.sb("ss2_tk", [128, 512], BF16)

    def s_sink_tok(dc, qc, ap, rkeys):
        p.tt(sst2[:, :], ap,
             env.xn_fm[:, dc * T + qc * 512: dc * T + qc * 512 + 512],
             ALU.add, reads=list(rkeys) + ["xn_fm", "ss2_st"],
             writes=["ss2_st"], tag="ss2_add")
        trps = env.PS1.ap().bitcast(BF16)[:, 0:1024]
        for j in range(4):
            p.trans(trps[:, j * 128:(j + 1) * 128],
                    sst2[:, j * 128:(j + 1) * 128], env.id128[:, :],
                    reads=["ss2_st", "id128"], writes=pk1(0, 512),
                    tag="ss2_tr")
        p.copy(stok[:, :], trps[:, 0:512], reads=pk1(0, 512),
               writes=["ss2_tk"], tag="ss2_cp")
        for j in range(4):
            tt = qc * 4 + j
            p.dma("gpsimd", s_scr2.rearrange("(n p) d -> n p d", p=128)
                  [tt, :, dc * 128:(dc + 1) * 128],
                  stok[:, j * 128:(j + 1) * 128],
                  reads=["ss2_tk"], writes=["s2w"], tag="ss2_out")
    p.set_scope("L2attn")
    emit_l2_attn(env, W, s_sink_tok)
    gb2 = env.sb("l2f_g", [32, 2 * D], BF16)
    sb2 = env.sb("l2f_s", [32, 2 * D], BF16)
    xb2 = env.sb("l2f_x", [32, 2 * D], F32)
    xiv2 = x2.rearrange("(n p) d -> n p d", p=32)
    xov2 = x3.rearrange("(n p) d -> n p d", p=32)
    siv2 = s_scr2.rearrange("(n p) d -> n p d", p=32)

    def finale(g, ap, rkeys):
        par = g % 2
        p.dma("sync", sb2[:, par * D:(par + 1) * D], siv2[g, :, :],
              reads=["s2w", f"l2f_s{par}"], writes=[f"l2f_s{par}"],
              tag="l2f_lds")
        p.dma("sync", xb2[:, par * D:(par + 1) * D], xiv2[g, :, :],
              reads=[f"l2f_x{par}"], writes=[f"l2f_x{par}"], tag="l2f_ldx")
        p.act(gb2[:, par * D:(par + 1) * D], ap, AF.Silu,
              reads=list(rkeys) + [f"l2f_g{par}"], writes=[f"l2f_g{par}"],
              tag="l2f_silu")
        p.tt(gb2[:, par * D:(par + 1) * D], gb2[:, par * D:(par + 1) * D],
             sb2[:, par * D:(par + 1) * D], ALU.mult,
             reads=[f"l2f_g{par}", f"l2f_s{par}"], writes=[f"l2f_g{par}"],
             tag="l2f_sg")
        p.tt(xb2[:, par * D:(par + 1) * D], xb2[:, par * D:(par + 1) * D],
             gb2[:, par * D:(par + 1) * D], ALU.add,
             reads=[f"l2f_g{par}", f"l2f_x{par}"], writes=[f"l2f_x{par}"],
             tag="l2f_add")
        p.dma("gpsimd", xov2[g, :, :], xb2[:, par * D:(par + 1) * D],
              reads=[f"l2f_x{par}"], writes=[f"l2f_x{par}"], tag="out_x")
    p.set_scope("L2ffn")
    emit_l2_ffn(env, W, xnd, yed, finale)
    p.barrier()
    env.scope.close()
    env.scope = None

    env.scope = ExitStack()
    p.set_scope("L3ln")
    l3_make_consts(env)
    emit_ln_phase(env, x3)
    p.set_scope("L3attn")
    emit_l3_attn(env, W, make_s_sink(env, s_scr, mode="plain"))
    p.set_scope("L3ffn")
    emit_l3_ffn(env, W, make_residual_streamer(env, x3, x_out, s_scr, blk=256,
                                               trps_region=(0, 512)))
    p.set_scope(None)
    p.build()
    env.scope = None
    return nc


_FUSED_CACHE = {}


def kernel(**inputs):
    x = np.ascontiguousarray(np.asarray(inputs["L0_attn_x"], dtype=np.float32))
    B = x.shape[0]
    w = {}
    for li in (0, 1, 2, 3):
        w.update(LAYER_BUILDERS[li][0](inputs))
    wshapes = {k: v.shape for k, v in w.items()}
    key = tuple(sorted(wshapes.items()))
    if key not in _FUSED_CACHE:
        _FUSED_CACHE[key] = build_fused(wshapes)
    nc = _FUSED_CACHE[key]
    in_maps = []
    for c in range(B):
        m = {"x_in": np.ascontiguousarray(x[c], np.float32)}
        for k, v in w.items():
            m[k] = np.ascontiguousarray(v, np.float32)
        in_maps.append(m)
    res = run_bass_kernel_spmd(nc, in_maps, core_ids=list(range(B)))
    return np.stack([np.asarray(r["x_out"]) for r in res.results]).astype(
        np.float32)


# version 11 (final)
# speedup vs baseline: 1.7960x; 1.0082x over previous
import math
import numpy as np
import ml_dtypes
import concourse.bass as bass
import concourse.mybir as mybir
from concourse.bass_utils import run_bass_kernel_spmd

F32 = mybir.dt.float32
BF16 = mybir.dt.bfloat16
U32 = mybir.dt.uint32
AF = mybir.ActivationFunctionType
ALU = mybir.AluOpType
AX = mybir.AxisListType
bfdt = ml_dtypes.bfloat16

ENGINES = ("sync", "scalar", "vector", "tensor", "gpsimd")

T = 2048
D = 1024
NT = 16
NCH = 8


class Prog:
    def __init__(self, nc, same_engine_sync=True):
        self.nc = nc
        self.ops = []
        self.same_engine_sync = same_engine_sync

    def op(self, engine, fn, reads=(), writes=(), dma=False, tag=""):
        assert engine in ENGINES, engine
        self.ops.append(dict(engine=engine, fn=fn, reads=tuple(reads),
                             writes=tuple(writes), dma=bool(dma), tag=tag,
                             bar=self._bar_front, scope=self._scope))
        return len(self.ops) - 1

    _bar_front = None
    _scope = None

    def set_scope(self, name):
        self._scope = name

    def barrier(self):
        last = {}
        for i, o in enumerate(self.ops):
            last[o["engine"]] = i
        self._bar_front = tuple(sorted(last.values()))

    def dma(self, engine, out, in_, reads=(), writes=(), tag="dma", **kw):
        return self.op(engine, lambda e: e.dma_start(out, in_, **kw),
                       reads=reads, writes=writes, dma=True, tag=tag)

    def mm(self, out, lhsT, rhs, start=True, stop=True, reads=(), writes=(),
           tag="mm", **kw):
        return self.op("tensor",
                       lambda e: e.matmul(out, lhsT, rhs, start=start, stop=stop, **kw),
                       reads=reads, writes=writes, tag=tag)

    def trans(self, out, in_, ident, reads=(), writes=(), tag="tr", **kw):
        return self.op("tensor", lambda e: e.transpose(out, in_, ident, **kw),
                       reads=reads, writes=writes, tag=tag)

    def act(self, out, in_, func, bias=0.0, scale=1.0, engine="scalar",
            reads=(), writes=(), tag="act", **kw):
        return self.op(engine,
                       lambda e: e.activation(out, in_, func, bias=bias, scale=scale, **kw),
                       reads=reads, writes=writes, tag=tag)

    def tt(self, out, in0, in1, alu, engine="vector", reads=(), writes=(), tag="tt"):
        return self.op(engine, lambda e: e.tensor_tensor(out, in0, in1, alu),
                       reads=reads, writes=writes, tag=tag)

    def ts(self, out, in0, s1, alu, s2=None, alu2=None, engine="vector",
           reads=(), writes=(), tag="ts"):
        if s2 is None:
            return self.op(engine, lambda e: e.tensor_single_scalar(out, in0, s1, alu),
                           reads=reads, writes=writes, tag=tag)
        return self.op(engine, lambda e: e.tensor_scalar(out, in0, s1, s2, alu, alu2),
                       reads=reads, writes=writes, tag=tag)

    def stt(self, out, in0, scalar, in1, op0, op1, engine="vector",
            reads=(), writes=(), tag="stt"):
        return self.op(engine,
                       lambda e: e.scalar_tensor_tensor(out, in0, scalar, in1, op0, op1),
                       reads=reads, writes=writes, tag=tag)

    def scan(self, out, d0, d1, initial, op0, op1, engine="vector",
             reads=(), writes=(), tag="scan"):
        return self.op(engine,
                       lambda e: e.tensor_tensor_scan(out, d0, d1, initial, op0, op1),
                       reads=reads, writes=writes, tag=tag)

    def red(self, out, in_, alu, engine="vector", negate=False, reads=(), writes=(),
            tag="red"):
        return self.op(engine,
                       lambda e: e.tensor_reduce(out, in_, AX.X, alu, negate=negate),
                       reads=reads, writes=writes, tag=tag)

    def copy(self, out, in_, engine="vector", reads=(), writes=(), tag="cp"):
        return self.op(engine, lambda e: e.tensor_copy(out, in_),
                       reads=reads, writes=writes, tag=tag)

    def memset(self, ap, val, engine="vector", reads=(), writes=(), tag="memset"):
        return self.op(engine, lambda e: e.memset(ap, val),
                       reads=reads, writes=writes, tag=tag)

    def build(self):
        nc = self.nc
        n = len(self.ops)
        last_writer = {}
        readers_since = {}
        deps = [set() for _ in range(n)]
        for i, o in enumerate(self.ops):
            for k in o["reads"]:
                if k in last_writer:
                    deps[i].add(last_writer[k])
            for k in o["writes"]:
                if k in last_writer:
                    deps[i].add(last_writer[k])
                for r in readers_since.get(k, ()):
                    deps[i].add(r)
            for k in o["reads"]:
                readers_since.setdefault(k, []).append(i)
            for k in o["writes"]:
                last_writer[k] = i
                readers_since[k] = []
            if o.get("bar"):
                for q in o["bar"]:
                    deps[i].add(q)
            deps[i].discard(i)
        if not self.same_engine_sync:
            for i, o in enumerate(self.ops):
                if o["engine"] == "tensor" and not o["dma"]:
                    deps[i] = {q for q in deps[i]
                               if self.ops[q]["engine"] != "tensor"
                               or self.ops[q]["dma"]}
        needed = [False] * n
        for i in range(n):
            for q in deps[i]:
                needed[q] = True
        for i, o in enumerate(self.ops):
            if o["dma"]:
                needed[i] = True
        sems = {}
        for e in ENGINES:
            sems[(e, "c")] = nc.alloc_semaphore(name=f"sc_{e}")
            sems[(e, "d")] = nc.alloc_semaphore(name=f"sd_{e}")
        counts = [None] * n
        cum = {k: 0 for k in sems}
        for i, o in enumerate(self.ops):
            if needed[i]:
                k = (o["engine"], "d" if o["dma"] else "c")
                cum[k] += 16 if o["dma"] else 1
                counts[i] = (k, cum[k])
        waits = [dict() for _ in range(n)]
        eng_clock = {e: {} for e in ENGINES}
        op_clock = [None] * n
        for i, o in enumerate(self.ops):
            e = o["engine"]
            cl = eng_clock[e]
            for q in sorted(deps[i]):
                k, v = counts[q]
                if cl.get(k, 0) >= v:
                    continue
                waits[i][k] = max(waits[i].get(k, 0), v)
            for q in sorted(deps[i]):
                k, v = counts[q]
                if waits[i].get(k, 0) >= v and op_clock[q]:
                    for kk, vv in op_clock[q].items():
                        if cl.get(kk, 0) < vv:
                            cl[kk] = vv
            for k, v in waits[i].items():
                cl[k] = max(cl.get(k, 0), v)
            if counts[i] is not None:
                k, v = counts[i]
                snap = dict(cl)
                snap[k] = v
                op_clock[i] = snap
        final_counts = dict(cum)
        per_engine = {e: [] for e in ENGINES}
        for i, o in enumerate(self.ops):
            per_engine[o["engine"]].append(i)

        with nc.Block() as block:
            def mk(e_name):
                def body(eng):
                    for i in per_engine[e_name]:
                        o = self.ops[i]
                        for k, v in sorted(waits[i].items()):
                            eng.wait_ge(sems[k], v)
                        ins = o["fn"](eng)
                        if needed[i]:
                            k, v = counts[i]
                            ins.then_inc(sems[k], 16 if o["dma"] else 1)
                return body

            def gp_body(eng):
                mk("gpsimd")(eng)
                for k, v in final_counts.items():
                    if v > 0 and k != ("gpsimd", "c"):
                        eng.wait_ge(sems[k], v)
            block.sync(mk("sync"))
            block.scalar(mk("scalar"))
            block.vector(mk("vector"))
            block.tensor(mk("tensor"))
            block.gpsimd(gp_body)
        self._nwaits = sum(len(w) for w in waits)
        self._nincs = sum(1 for x in needed if x)
        return self


class Env:
    def __init__(self, nc):
        self.nc = nc
        self.p = Prog(nc, same_engine_sync=False)
        self.scope = None

    def sb(self, name, shape, dt=F32):
        if self.scope is not None:
            return self.scope.enter_context(self.nc.sbuf_tensor(name, shape, dt))
        return self.nc.alloc_sbuf_tensor(name, shape, dt)

    def psum(self, name, shape, dt=F32):
        return self.nc.alloc_psum_tensor(name, shape, dt)

    def const(self, name, arr):
        return self.nc.inline_tensor(np.ascontiguousarray(arr), name=name).ap()

    def load_const_sb(self, name, arr, engine="sync"):
        a = np.ascontiguousarray(arr)
        dt = {np.dtype(np.float32): F32, np.dtype(bfdt): BF16,
              np.dtype(np.uint32): U32}[a.dtype]
        t = self.sb(name, list(a.shape), dt)
        self.p.dma(engine, t[tuple(slice(None) for _ in a.shape)],
                   self.const(name + "_c", a), writes=[name], tag="ldc_" + name)
        return t

    def wload(self, dst_ap, src_ap, key, reads=(), engine="gpsimd"):
        self.p.dma(engine, dst_ap, src_ap, reads=reads, writes=[key],
                   tag="w_" + key)


def pk1(c0, c1):
    return ["PA%d" % b for b in range(c0 // 512, (c1 + 511) // 512)]


def pk2(c0, c1):
    return ["PB%d" % b for b in range(c0 // 512, (c1 + 511) // 512)]


def make_consts_l1(env):
    sel2 = np.zeros((2, 128), np.float32)
    sel2[0, 0:64] = 1.0
    sel2[1, 64:128] = 1.0
    env.pairsel2 = env.load_const_sb("pairsel2", sel2.astype(bfdt))
    ind = np.zeros((15, 15 * 128), np.float32)
    for e in range(15):
        ind[e, e * 128:(e + 1) * 128] = 1.0
    env.ind15 = env.load_const_sb("ind15", ind.astype(bfdt))
    env.ones512 = env.load_const_sb("ones512", np.ones((1, 512), bfdt))
    env.idf = env.load_const_sb("idf", np.eye(128, dtype=np.float32))


def make_consts(env):
    env.id128 = env.load_const_sb("id128", np.eye(128, dtype=bfdt))

    env.sumask = env.load_const_sb(
        "sumask", np.triu(np.ones((128, 128), np.float32), 1).astype(bfdt))
    env.causal_keep = env.load_const_sb(
        "causal_keep", np.triu(np.ones((128, 128), np.float32), 0).astype(bfdt))
    env.ones_row = env.load_const_sb("ones_row", np.ones((1, 128), bfdt))
    env.eps_ln = env.load_const_sb("eps_ln", np.full((128, 1), 1e-5, np.float32))
    env.eps8 = env.load_const_sb("eps8", np.full((128, 1), 1e-8, np.float32))
    env.ones_col128 = env.load_const_sb("ones_col128", np.ones((128, 1), bfdt))
    sel = np.zeros((128, 2), np.float32)
    sel[0:64, 0] = 1.0
    sel[64:128, 1] = 1.0
    env.pairselT = env.load_const_sb("pairselT", sel.astype(bfdt))

    env.scr15b = env.sb("scr15b", [128, 15], BF16)
    env.PS1 = env.psum("PS1", [128, 2048], F32)
    env.PS2 = env.psum("PS2", [128, 2048], F32)


def emit_ln_phase(env, x_dram, xnfm_key="xn_fm", xn_tok_dump=None):
    p, nc = env.p, env.nc
    with (
        nc.sbuf_tensor("g_xtok", [128, NT * D], F32) as x_tok,
        nc.sbuf_tensor("g_xntok", [128, NT * D], BF16) as xn_tok,
        nc.sbuf_tensor("g_lnst", [128, 5 * NT], F32) as st,
        nc.sbuf_tensor("g_lnsq", [128, D], F32) as sq,
    ):
        xv = x_dram.rearrange("(n p) d -> p n d", p=128)
        p.dma("sync", x_tok[:, :].rearrange("p (n d) -> p n d", n=NT), xv,
              writes=["x_tok"], tag="ld_x")
        s1, s2 = st[:, 0:NT], st[:, NT:2 * NT]
        rstd, nb = st[:, 2 * NT:3 * NT], st[:, 3 * NT:4 * NT]
        var = st[:, 4 * NT:5 * NT]
        for tt in range(NT):
            xt = x_tok[:, tt * D:(tt + 1) * D]
            p.op("scalar", lambda e, xt=xt, tt=tt: e.activation(
                    sq[:, :], xt, AF.Square, accum_out=s2[:, tt:tt + 1]),
                 reads=["x_tok"], writes=["ln_sq", "ln_s2"], tag=f"ln_sq{tt}")
            p.op("scalar", lambda e, xt=xt, tt=tt: e.activation(
                    sq[:, :], xt, AF.Copy, accum_out=s1[:, tt:tt + 1]),
                 reads=["x_tok", "ln_sq"], writes=["ln_sq", "ln_s1"],
                 tag=f"ln_s1{tt}")
        p.ts(s1, s1, 1.0 / D, ALU.mult, reads=["ln_s1"], writes=["ln_mean"],
             tag="ln_mean")
        p.tt(var, s1, s1, ALU.mult, reads=["ln_mean"], writes=["ln_m2"],
             tag="ln_m2")
        p.stt(var, s2, 1.0 / D, var, ALU.mult, ALU.subtract,
              reads=["ln_s2", "ln_m2"], writes=["ln_var"], tag="ln_var")
        p.act(rstd, var, AF.Sqrt, bias=env.eps_ln[:, 0:1],
              reads=["ln_var", "eps_ln"], writes=["ln_sd"], tag="ln_sd")
        p.op("vector", lambda e: e.reciprocal(rstd, rstd),
             reads=["ln_sd"], writes=["ln_rstd"], tag="ln_rstd")
        p.stt(nb, s1, -1.0, rstd, ALU.mult, ALU.mult,
              reads=["ln_mean", "ln_rstd"], writes=["ln_nb"], tag="ln_nb")
        trps = env.PS1.ap().bitcast(BF16)
        xnv = env.xn_fm[:, :].rearrange("p (c t) -> p c t", c=NCH)
        for tt in range(NT):
            p.op("scalar", lambda e, tt=tt: e.activation(
                    xn_tok[:, tt * D:(tt + 1) * D], x_tok[:, tt * D:(tt + 1) * D],
                    AF.Identity, bias=nb[:, tt:tt + 1], scale=rstd[:, tt:tt + 1]),
                 reads=["x_tok", "ln_rstd", "ln_nb"], writes=["xn_tok"],
                 tag=f"ln_ap{tt}")
            for c in range(NCH):
                p.trans(trps[:, c * 128:(c + 1) * 128],
                        xn_tok[:, tt * D + c * 128: tt * D + (c + 1) * 128],
                        env.id128[:, :],
                        reads=["xn_tok", "id128"], writes=pk1(0, 512), tag="ln_tr")
            p.copy(xnv[:, :, tt * 128:(tt + 1) * 128],
                   trps[:, 0:1024].rearrange("p (c t) -> p c t", c=NCH),
                   reads=pk1(0, 512), writes=[xnfm_key], tag=f"ln_trc{tt}")
            if xn_tok_dump is not None:
                p.dma("gpsimd",
                      xn_tok_dump.rearrange("(n p) d -> n p d", p=128)[tt],
                      xn_tok[:, tt * D:(tt + 1) * D],
                      reads=["xn_tok"], writes=["xnd_w"], tag="ln_xnd")


def emit_sg_residual(env, x_in_dram, x_out_dram, g_src, s_src, tag="sgr"):
    p, nc = env.p, env.nc
    xiv = x_in_dram.rearrange("(n p) d -> p n d", p=128)
    xov = x_out_dram.rearrange("(n p) d -> p n d", p=128)
    with (
        nc.sbuf_tensor("sg_buf", [128, NCH * 512], BF16) as sgb,
        nc.sbuf_tensor("sg_xt", [128, 2 * D], F32) as xt2,
    ):
        for tb in range(4):
            for dc in range(NCH):
                gap, gr = g_src(dc, tb)
                sap, sr = s_src(dc, tb)
                p.tt(sgb[:, dc * 512:(dc + 1) * 512], gap, sap, ALU.mult,
                     reads=list(gr) + list(sr), writes=["sg_buf"],
                     tag=tag + f"_m{tb}_{dc}")
            for j in range(4):
                tt = tb * 4 + j
                xt = xt2[:, (tt % 2) * D:(tt % 2) * D + D]
                p.dma("sync", xt, xiv[:, tt, :], reads=["sg_xt" + str(tt % 2)],
                      writes=["sg_x" + str(tt % 2)], tag=tag + "_ldx")
                trps = env.PS1.ap().bitcast(BF16)
                for c in range(NCH):
                    p.trans(trps[:, c * 128:(c + 1) * 128],
                            sgb[:, c * 512 + j * 128: c * 512 + (j + 1) * 128],
                            env.id128[:, :],
                            reads=["sg_buf", "id128"], writes=pk1(0, 512),
                            tag=tag + "_tr")
                p.tt(xt, xt, trps[:, 0:1024], ALU.add,
                     reads=pk1(0, 512) + ["sg_x" + str(tt % 2)],
                     writes=["sg_xt" + str(tt % 2)], tag=tag + f"_add{tt}")
                p.dma("gpsimd", xov[:, tt, :], xt,
                      reads=["sg_xt" + str(tt % 2)], writes=["sg_x" + str(tt % 2)],
                      tag="out_x")


def host_prep_l1(inp):
    w = {}
    wq = np.asarray(inp["L1_attn_w_q"])
    wk = np.asarray(inp["L1_attn_w_k"])
    wv = np.asarray(inp["L1_attn_w_v"])
    wo = np.asarray(inp["L1_attn_w_o"])
    w["l1_wq"] = np.ascontiguousarray(
        wq.transpose(1, 0, 2).reshape(D, D)).reshape(NCH, 128, D)
    w["l1_wk"] = np.ascontiguousarray(
        wk.transpose(1, 0, 2).reshape(D, D)).reshape(NCH, 128, D)
    w["l1_wv"] = np.ascontiguousarray(
        wv.transpose(1, 0, 2).reshape(D, D)).reshape(NCH, 128, D)
    w["l1_wo"] = np.ascontiguousarray(wo).reshape(NCH, 128, D)
    w["l1_routerW"] = np.ascontiguousarray(
        np.asarray(inp["L1_ffn_router_W"])).reshape(NCH, 128, 15)
    w["l1_routerb"] = np.ascontiguousarray(
        np.asarray(inp["L1_ffn_router_b"]).reshape(15, 1))
    W1all = np.concatenate([np.asarray(inp["L1_ffn_sW1"])[None],
                            np.asarray(inp["L1_ffn_eW1"])], 0)
    W2all = np.concatenate([np.asarray(inp["L1_ffn_sW2"])[None],
                            np.asarray(inp["L1_ffn_eW2"])], 0)
    b1all = np.concatenate([np.asarray(inp["L1_ffn_sb1"])[None],
                            np.asarray(inp["L1_ffn_eb1"])], 0)
    b2all = np.concatenate([np.asarray(inp["L1_ffn_sb2"])[None],
                            np.asarray(inp["L1_ffn_eb2"])], 0)
    w["l1_W1"] = np.ascontiguousarray(W1all.reshape(16, NCH, 128, D))
    w["l1_W2"] = np.ascontiguousarray(W2all.reshape(16, NCH, 128, D))
    w["l1_b1col"] = np.ascontiguousarray(
        b1all.reshape(16, NCH, 128).transpose(2, 0, 1).reshape(128, 16 * NCH))
    w["l1_eb2"] = np.ascontiguousarray(b2all[1:])
    w["l1_sb2row"] = np.ascontiguousarray(b2all[0:1])
    return w


L1_WSPECS = {
    "l1_wq": (NCH, 128, D), "l1_wk": (NCH, 128, D), "l1_wv": (NCH, 128, D),
    "l1_wo": (NCH, 128, D), "l1_routerW": (NCH, 128, 15), "l1_routerb": (15, 1),
    "l1_W1": (16, NCH, 128, D), "l1_W2": (16, NCH, 128, D),
    "l1_b1col": (128, 16 * NCH), "l1_b2": (16, D),
}


def emit_l1_attn(env, W, s_sink):
    p, nc = env.p, env.nc
    l1_bscr = nc.dram_tensor("l1_bscr", [2], F32, kind="Internal").ap()
    PS1, PS2 = env.PS1.ap(), env.PS2.ap()
    p.barrier()
    with (
        nc.sbuf_tensor("l1_y", [128, NCH * T], BF16) as y_fm,
        nc.sbuf_tensor("l1_sums", [16, T], F32) as sums_all,
        nc.sbuf_tensor("l1_r", [16, T], BF16) as r_bf,
        nc.sbuf_tensor("l1_srow", [128, 1024], F32) as srow,
        nc.sbuf_tensor("l1_rrep", [128, T], BF16) as rrep,
        nc.sbuf_tensor("l1_mcol", [2, 4], F32) as mcol,
        nc.sbuf_tensor("l1_qp", [128, T], BF16) as qp,
        nc.sbuf_tensor("l1_kp", [128, T], BF16) as kp,
        nc.sbuf_tensor("l1_vp", [128, NT * 130], BF16) as vp,
        nc.sbuf_tensor("l1_wp", [128, 3 * NCH * 128], BF16) as wpair,
        nc.sbuf_tensor("l1_pt", [128, 2048], BF16) as ptil,
        nc.sbuf_tensor("l1_sqt", [128, T], BF16) as sqt,

        nc.sbuf_tensor("l1_bh", [128, 2], F32) as bhneg,
        nc.sbuf_tensor("l1_br", [2, 2], F32) as brow2,
        nc.sbuf_tensor("l1_brb", [1, 2], BF16) as brow_bf,
        nc.sbuf_tensor("l1_wo_sb", [128, NCH * D], BF16) as wo_sb,
    ):
        env.wload(wo_sb[:, :].rearrange("p (c o) -> p c o", c=NCH),
                  W["l1_wo"].rearrange("c p o -> p c o"), "l1_wo")
        for hp in range(8):
            for i, nm in enumerate(("l1_wq", "l1_wk", "l1_wv")):
                env.wload(
                    wpair[:, i * 1024:(i + 1) * 1024].rearrange(
                        "p (c o) -> p c o", c=NCH),
                    W[nm][:, :, hp * 128:(hp + 1) * 128].rearrange(
                        "c p o -> p c o"),
                    f"l1_wpair{i}")
            for wi, (dst, dk) in enumerate(((qp, "l1_qp"), (kp, "l1_kp"))):
                for qc in range(4):
                    c0 = (qc % 2) * 512
                    ps = PS1[:, c0:c0 + 512]
                    for kc in range(NCH):
                        p.mm(ps,
                             wpair[:, wi * 1024 + kc * 128: wi * 1024 + (kc + 1) * 128],
                             env.xn_fm[:, kc * T + qc * 512: kc * T + qc * 512 + 512],
                             start=(kc == 0), stop=(kc == NCH - 1),
                             reads=[f"l1_wpair{wi}", "xn_fm"],
                             writes=pk1(c0, c0 + 512), tag=f"l1{dk}mm")
                    p.act(dst[:, qc * 512:(qc + 1) * 512], ps, AF.Copy,
                          reads=pk1(c0, c0 + 512), writes=[dk], tag=f"l1{dk}ev")
            for tt in range(NT):
                c0 = 1024 + (tt % 2) * 512
                ps = PS1[:, c0:c0 + 128]
                for kc in range(NCH):
                    p.mm(ps, env.xn_fm[:, kc * T + tt * 128: kc * T + (tt + 1) * 128],
                         wpair[:, 2048 + kc * 128: 2048 + (kc + 1) * 128],
                         start=(kc == 0), stop=(kc == NCH - 1),
                         reads=["l1_wpair2", "xn_fm"], writes=pk1(c0, c0 + 128),
                         tag=f"l1vmm{tt}")
                p.copy(vp[:, tt * 130:(tt + 1) * 130].rearrange(
                           "p (h j) -> p h j", h=2)[:, :, 0:64],
                       ps.rearrange("p (h j) -> p h j", h=2),
                       reads=pk1(c0, c0 + 128), writes=["l1_vp"],
                       tag=f"l1vev{tt}")
            p.memset(vp[:, :].rearrange("p (n h j) -> p n h j", n=NT, h=2)
                     [:, :, :, 64:65],
                     1.0, reads=["l1_vp"], writes=["l1_vp"], tag="l1_vones")
            for o, (srcb, sk) in enumerate(((qp, "l1_qp"), (kp, "l1_kp"))):
                p.act(sqt[:, :], srcb[:, :], AF.Square, reads=[sk],
                      writes=["l1_sqt"], tag=f"l1sq{o}")
                for qc in range(4):
                    ps = PS2[0:2, 1536:2048]
                    p.mm(ps, env.pairselT[:, :], sqt[:, qc * 512:(qc + 1) * 512],
                         reads=["l1_sqt", "pairselT"], writes=pk2(1536, 2048),
                         tag=f"l1nrm{o}")
                    p.red(mcol[:, qc:qc + 1], ps, ALU.max,
                          reads=pk2(1536, 2048), writes=["l1_mc"],
                          tag=f"l1nrmr{o}")
                p.red(brow2[:, o:o + 1], mcol[:, 0:4], ALU.max,
                      reads=["l1_mc"], writes=["l1_brq" if o == 0 else "l1_brk"],
                      tag=f"l1b{o}")
            p.tt(brow2[:, 0:1], brow2[:, 0:1], brow2[:, 1:2], ALU.mult,
                 reads=["l1_brq", "l1_brk"], writes=["l1_bp"], tag="l1bp")
            p.act(brow2[:, 0:1], brow2[:, 0:1], AF.Sqrt, reads=["l1_bp"],
                  writes=["l1_bs"], tag="l1bs")
            p.ts(brow2[:, 0:1], brow2[:, 0:1], -0.125, ALU.mult,
                 reads=["l1_bs"], writes=["l1_bn"], tag="l1bn")
            p.dma("gpsimd", l1_bscr, brow2[:, 0:1],
                  reads=["l1_bn", "l1_bh"], writes=["l1_bscrd"],
                  tag="l1btr")
            p.dma("sync", bhneg[:, 0:2],
                  bass.AP(l1_bscr.tensor, 0, [[0, 128], [1, 2]]),
                  reads=["l1_bscrd", "l1_bh"], writes=["l1_bh"],
                  tag="l1brep")
            for h in range(2):
                hg = 2 * hp + h
                for qh in range(2):
                    pi = (2 * h + qh) % 2
                    a0 = pi * 1024
                    av = PS2[0:65, a0:a0 + 1024]
                    for kt in range(NT):
                        s0 = (kt % 2) * 1024
                        pth = ptil[:, s0:s0 + 1024]
                        sc = PS1[:, s0:s0 + 1024]
                        for q2 in range(2):
                            p.mm(sc[:, q2 * 512:(q2 + 1) * 512],
                                 kp[64 * h:64 * h + 64, kt * 128:(kt + 1) * 128],
                                 qp[64 * h:64 * h + 64,
                                    qh * 1024 + q2 * 512: qh * 1024 + q2 * 512 + 512],
                                 reads=["l1_qp", "l1_kp"],
                                 writes=pk1(s0 + q2 * 512, s0 + q2 * 512 + 512),
                                 tag=f"l1sc{hg}")
                        p.act(pth, sc, AF.Exp, scale=0.125,
                              bias=bhneg[:, h:h + 1],
                              reads=pk1(s0, s0 + 1024) + ["l1_bh",
                                                          f"l1_pt{kt % 2}"],
                              writes=[f"l1_pt{kt % 2}"], tag=f"l1exp{hg}")
                        for q2 in range(2):
                            p.mm(av[:, q2 * 512:(q2 + 1) * 512],
                                 vp[:, kt * 130 + 65 * h: kt * 130 + 65 * h + 65],
                                 pth[:, q2 * 512:(q2 + 1) * 512],
                                 start=(kt == 0), stop=(kt == NT - 1),
                                 reads=[f"l1_pt{kt % 2}", "l1_vp"],
                                 writes=pk2(a0 + q2 * 512, a0 + q2 * 512 + 512),
                                 tag=f"l1av{hg}")
                    p.act(srow[64:65, :], av[64:65, :], AF.Copy,
                          reads=pk2(a0, a0 + 1024) + ["l1_srow"],
                          writes=["l1_srow"], tag=f"l1sumst{hg}")
                    p.dma("gpsimd",
                          sums_all[hg:hg + 1, qh * 1024: qh * 1024 + 1024],
                          srow[64:65, :], reads=["l1_srow"],
                          writes=["l1_sums", "l1_srow"], tag=f"l1sum{hg}")
                    p.copy(y_fm[64 * h:64 * h + 64,
                                hp * T + qh * 1024: hp * T + qh * 1024 + 1024],
                           av[0:64, :], reads=pk2(a0, a0 + 1024) + ["l1_yraw"],
                           writes=["l1_yraw"], tag=f"l1yev{hg}")
        p.act(sums_all[:, :], sums_all[:, :], AF.Ln, reads=["l1_sums"],
              writes=["l1_lns"], tag="l1_lns")
        p.act(r_bf[:, :], sums_all[:, :], AF.Exp, scale=-1.0,
              reads=["l1_lns"], writes=["l1_r"], tag="l1_rinv")
        p.dma("gpsimd", env.rscr, r_bf[:, :], reads=["l1_r"],
              writes=["l1_rscr"], tag="l1rst")
        for c in range(NCH):
            for h in range(2):
                srcap = bass.AP(env.rscr.tensor, (2 * c + h) * T,
                                [[0, 64], [1, T]])
                p.dma("sync", rrep[64 * h:64 * h + 64, :], srcap,
                      reads=["l1_rscr", "l1_rrep"], writes=["l1_rrep"],
                      tag="l1rbc")
            p.tt(y_fm[:, c * T:(c + 1) * T], y_fm[:, c * T:(c + 1) * T],
                 rrep[:, :], ALU.mult,
                 reads=["l1_rrep", "l1_yraw"], writes=["l1_y"],
                 tag=f"l1ynrm{c}")
        for dc in range(NCH):
            for qc in range(4):
                c0 = (qc % 2) * 512
                ps = PS2[:, c0:c0 + 512]
                for kc in range(NCH):
                    p.mm(ps, wo_sb[:, kc * D + dc * 128: kc * D + (dc + 1) * 128],
                         y_fm[:, kc * T + qc * 512: kc * T + qc * 512 + 512],
                         start=(kc == 0), stop=(kc == NCH - 1),
                         reads=["l1_wo", "l1_y"], writes=pk2(c0, c0 + 512),
                         tag="l1omm")
                s_sink(dc, qc, ps, pk2(c0, c0 + 512))
    p.barrier()


def emit_l1_ffn(env, W, g_sink):
    p, nc = env.p, env.nc
    PS1, PS2 = env.PS1.ap(), env.PS2.ap()
    p.barrier()
    with (
        nc.sbuf_tensor("l1_racc", [128, NCH * T], F32) as racc,
        nc.sbuf_tensor("l1_cw", [15, T], BF16) as cw_fm,
        nc.sbuf_tensor("l1_cwr", [128, 512], BF16) as cwrep,
        nc.sbuf_tensor("l1_hsb", [128, NCH * 512], BF16) as h_sb,
        nc.sbuf_tensor("l1_eW", [128, 2 * NCH * D], BF16) as eW,
        nc.sbuf_tensor("l1_b1c_sb", [128, 16 * NCH], F32) as b1c,
        nc.sbuf_tensor("l1_eb2_sb", [15, D], BF16) as eb2_sb,
        nc.sbuf_tensor("l1_sb2_sb", [1, D], BF16) as sb2_sb,
        nc.sbuf_tensor("l1_rt", [128, 65], F32) as rt,
        nc.sbuf_tensor("l1_s15f", [15, 128], F32) as scr15f,
        nc.sbuf_tensor("l1_rtb", [15, 1], F32) as rb_sb,
        nc.sbuf_tensor("l1_rW", [128, NCH * 16], BF16) as rW_sb,
    ):
        env.wload(b1c[:, :], W["l1_b1col"], "l1_b1c")
        env.wload(eb2_sb[:, :], W["l1_eb2"], "l1_eb2")
        env.wload(sb2_sb[:, :], W["l1_sb2row"], "l1_sb2")
        env.wload(rW_sb[:, :].rearrange("p (c o) -> p c o", c=NCH)[:, :, 0:15],
                  W["l1_routerW"].rearrange("c p o -> p c o"), "l1_rW")
        p.dma("sync", rb_sb[:, :], W["l1_routerb"], writes=["l1_rb"], tag="l1rbld")
        for qc in range(4):
            ps = PS1[0:15, 0:512]
            for kc in range(NCH):
                p.mm(ps, rW_sb[:, kc * 16: kc * 16 + 15],
                     env.xn_fm[:, kc * T + qc * 512: kc * T + qc * 512 + 512],
                     start=(kc == 0), stop=(kc == NCH - 1),
                     reads=["l1_rW", "xn_fm"], writes=pk1(0, 512), tag="l1rlmm")
            for j in range(4):
                tt = qc * 4 + j
                p.act(scr15f[0:15, 0:128], ps[0:15, j * 128:(j + 1) * 128],
                      AF.Identity, bias=rb_sb[:, 0:1],
                      reads=pk1(0, 512) + ["l1_rb"], writes=["scr15f"],
                      tag="l1rl15")
                trp = PS2[:, 1536:1664]
                p.trans(trp[0:128, 0:15], scr15f[0:15, 0:128],
                        env.idf[0:15, 0:15],
                        reads=["scr15f", "idf"], writes=pk2(1536, 2048),
                        tag="l1rtr")
                p.copy(rt[:, 0:15], trp[0:128, 0:15],
                       reads=pk2(1536, 2048), writes=["l1_rt"], tag="l1rtc")
                p.red(rt[:, 16:17], rt[:, 0:15], ALU.max, negate=True,
                      reads=["l1_rt"], writes=["l1_rtm"], tag="l1sm_m")
                p.op("scalar", lambda e: e.activation(
                        rt[:, 0:15], rt[:, 0:15], AF.Exp,
                        bias=rt[:, 16:17], accum_out=rt[:, 17:18]),
                     reads=["l1_rt", "l1_rtm"], writes=["l1_rt", "l1_rts"],
                     tag="l1sm_e")
                p.op("vector", lambda e: e.reciprocal(rt[:, 18:19], rt[:, 17:18]),
                     reads=["l1_rts"], writes=["l1_rtr"], tag="l1sm_r")
                p.ts(rt[:, 0:15], rt[:, 0:15], rt[:, 18:19], ALU.mult,
                     reads=["l1_rt", "l1_rtr"], writes=["l1_rt"], tag="l1sm_p")
                p.red(rt[:, 19:20], rt[:, 0:15], ALU.max, reads=["l1_rt"],
                      writes=["l1_m1"], tag="l1t_m1")
                p.ts(rt[:, 20:35], rt[:, 0:15], rt[:, 19:20], ALU.is_ge,
                     reads=["l1_rt", "l1_m1"], writes=["l1_e1"], tag="l1t_e1")
                p.tt(rt[:, 35:50], rt[:, 0:15], rt[:, 20:35], ALU.subtract,
                     reads=["l1_rt", "l1_e1"], writes=["l1_z1"], tag="l1t_z1")
                p.red(rt[:, 19:20], rt[:, 35:50], ALU.max, reads=["l1_z1"],
                      writes=["l1_m2"], tag="l1t_m2")
                p.ts(rt[:, 50:65], rt[:, 35:50], rt[:, 19:20], ALU.is_ge,
                     reads=["l1_z1", "l1_m2"], writes=["l1_e2"], tag="l1t_e2")
                p.tt(rt[:, 35:50], rt[:, 35:50], rt[:, 50:65], ALU.subtract,
                     reads=["l1_z1", "l1_e2"], writes=["l1_z2"], tag="l1t_z2")
                p.red(rt[:, 19:20], rt[:, 35:50], ALU.max, reads=["l1_z2"],
                      writes=["l1_m3"], tag="l1t_m3")
                p.ts(rt[:, 35:50], rt[:, 35:50], rt[:, 19:20], ALU.is_ge,
                     reads=["l1_z2", "l1_m3"], writes=["l1_e3"], tag="l1t_e3")
                p.tt(rt[:, 50:65], rt[:, 50:65], rt[:, 35:50], ALU.add,
                     reads=["l1_e2", "l1_e3"], writes=["l1_e23"], tag="l1t_e23")
                p.tt(rt[:, 50:65], rt[:, 50:65], rt[:, 20:35], ALU.add,
                     reads=["l1_e23", "l1_e1"], writes=["l1_e123"], tag="l1t_s")
                p.tt(rt[:, 50:65], rt[:, 50:65], rt[:, 0:15], ALU.mult,
                     reads=["l1_e123", "l1_rt"], writes=["l1_cwt"], tag="l1t_cw")
                p.copy(env.scr15b[0:128, 0:15], rt[:, 50:65],
                       reads=["l1_cwt"], writes=["scr15b"], tag="l1cwb")
                p.trans(PS2.bitcast(BF16)[0:15, 3072:3200],
                        env.scr15b[0:128, 0:15], env.id128[:, :],
                        reads=["scr15b", "id128"], writes=pk2(1536, 2048),
                        tag="l1cwtr")
                p.copy(cw_fm[0:15, tt * 128:(tt + 1) * 128],
                       PS2.bitcast(BF16)[0:15, 3072:3200],
                       reads=pk2(1536, 2048), writes=["l1_cw"], tag="l1cwc")
        for e in range(16):
            env.wload(eW[:, 0:NCH * D].rearrange("p (c o) -> p c o", c=NCH),
                      W["l1_W1"][e].rearrange("c p o -> p c o"), "l1_eW1")
            env.wload(eW[:, NCH * D:2 * NCH * D].rearrange("p (c o) -> p c o", c=NCH),
                      W["l1_W2"][e].rearrange("c p o -> p c o"), "l1_eW2")
            for tb in range(4):
                if e > 0:
                    p.mm(PS2[:, 1536:2048],
                         env.ind15[:, (e - 1) * 128: e * 128],
                         cw_fm[0:15, tb * 512:(tb + 1) * 512],
                         reads=["l1_cw", "ind15"], writes=pk2(1536, 2048),
                         tag="l1cwrep")
                    p.copy(cwrep[:, :], PS2[:, 1536:2048],
                           reads=pk2(1536, 2048), writes=["l1_cwr"],
                           tag="l1cwrc")
                for hc in range(NCH):
                    c0 = (hc % 2) * 512
                    ps = PS1[:, c0:c0 + 512]
                    for kc in range(NCH):
                        p.mm(ps, eW[:, kc * D + hc * 128: kc * D + (hc + 1) * 128],
                             env.xn_fm[:, kc * T + tb * 512: kc * T + tb * 512 + 512],
                             start=(kc == 0), stop=(kc == NCH - 1),
                             reads=["l1_eW1", "xn_fm"], writes=pk1(c0, c0 + 512),
                             tag="l1hmm")
                    p.act(h_sb[:, hc * 512:(hc + 1) * 512], ps, AF.Relu,
                          bias=b1c[:, e * NCH + hc: e * NCH + hc + 1],
                          reads=pk1(c0, c0 + 512) + ["l1_b1c"], writes=["l1_hs"],
                          tag="l1hev")
                    if e > 0:
                        p.tt(h_sb[:, hc * 512:(hc + 1) * 512],
                             h_sb[:, hc * 512:(hc + 1) * 512], cwrep[:, :],
                             ALU.mult, reads=["l1_hs", "l1_cwr"],
                             writes=["l1_h"], tag="l1hm")
                    else:
                        p.copy(h_sb[:, hc * 512:(hc + 1) * 512],
                               h_sb[:, hc * 512:(hc + 1) * 512],
                               reads=["l1_hs"], writes=["l1_h"], tag="l1hm0")
                for dc in range(NCH):
                    c0 = 1024 + (dc % 2) * 512
                    ps = PS1[:, c0:c0 + 512]
                    for hc in range(NCH):
                        p.mm(ps, eW[:, NCH * D + hc * D + dc * 128:
                                     NCH * D + hc * D + (dc + 1) * 128],
                             h_sb[:, hc * 512:(hc + 1) * 512],
                             start=(hc == 0), stop=(hc == NCH - 1),
                             reads=["l1_eW2", "l1_h"], writes=pk1(c0, c0 + 512),
                             tag="l1ymm")
                    dst = racc[:, dc * T + tb * 512: dc * T + tb * 512 + 512]
                    if e == 0:
                        p.copy(dst, ps, reads=pk1(c0, c0 + 512),
                               writes=["l1_racc"], tag="l1rc0")
                    else:
                        p.tt(dst, dst, ps, ALU.add,
                             reads=pk1(c0, c0 + 512) + ["l1_racc"],
                             writes=["l1_racc"], tag="l1rca")
        for dc in range(NCH):
            for qc in range(4):
                c0 = (qc % 2) * 512
                ps = PS1[:, c0:c0 + 512]
                p.mm(ps, eb2_sb[:, dc * 128:(dc + 1) * 128],
                     cw_fm[0:15, qc * 512:(qc + 1) * 512],
                     start=True, stop=False,
                     reads=["l1_eb2", "l1_cw"], writes=pk1(c0, c0 + 512),
                     tag="l1b2mm")
                p.mm(ps, sb2_sb[:, dc * 128:(dc + 1) * 128],
                     env.ones512[:, :], start=False, stop=True,
                     reads=["l1_sb2", "ones512"], writes=pk1(c0, c0 + 512),
                     tag="l1sb2mm")
                dst = racc[:, dc * T + qc * 512: dc * T + qc * 512 + 512]
                p.tt(dst, dst, ps, ALU.add,
                     reads=pk1(c0, c0 + 512) + ["l1_racc"], writes=["l1_racc"],
                     tag="l1b2a")
        if hasattr(env, "dbg_hook_ffn"):
            p.dma("gpsimd", env.dbg_hook_ffn, racc[:, :], reads=["l1_racc"],
                  tag="dracc")
            p.dma("gpsimd", env.dbg_cw, cw_fm[:, :], reads=["l1_cw"], tag="dcw")
        for tb in range(4):
            for dc in range(NCH):
                g_sink(dc, tb, racc[:, dc * T + tb * 512: dc * T + tb * 512 + 512],
                       ["l1_racc"])
    p.barrier()
def make_residual_streamer(env, x_in_dram, x_out_dram, s_scr, blk=512,
                           trps_region=None):
    p, nc = env.p, env.nc
    sgb = env.sb("rs_sgb", [128, NCH * blk], BF16)
    sld = env.sb("rs_sld", [128, 2 * blk], BF16)
    gtmp = env.sb("rs_gt", [128, 2 * blk], BF16)
    xt2 = env.sb("rs_xt", [128, D], F32)
    xiv = x_in_dram.rearrange("(n p) d -> p n d", p=128)
    xov = x_out_dram.rearrange("(n p) d -> p n d", p=128)
    ntile = blk // 128

    def g_sink(dc, tb, ap, rkeys):
        par = dc % 2
        sl = sld[:, par * blk:(par + 1) * blk]
        gt = gtmp[:, par * blk:(par + 1) * blk]
        p.dma("sync", sl, s_scr[dc, :, tb * blk:(tb + 1) * blk],
              reads=[f"rs_sld{par}"], writes=[f"rs_sld{par}", "s_scr_r"],
              tag="rs_lds")
        p.act(gt, ap, AF.Silu, reads=list(rkeys), writes=[f"rs_gt{par}"],
              tag="rs_silu")
        p.tt(sgb[:, dc * blk:(dc + 1) * blk], gt, sl, ALU.mult,
             reads=[f"rs_gt{par}", f"rs_sld{par}"], writes=["rs_sgb"],
             tag="rs_sg")
        if dc == NCH - 1:
            reg = trps_region if trps_region is not None else (0, 512)
            trps = env.PS1.ap().bitcast(BF16)[:, reg[0] * 2:reg[0] * 2 + 1024]
            for j in range(ntile):
                tt = tb * ntile + j
                xt = xt2[:, :]
                p.dma("sync", xt, xiv[:, tt, :], reads=["rs_x0"],
                      writes=["rs_x0"], tag="rs_ldx")
                for c in range(NCH):
                    p.trans(trps[:, c * 128:(c + 1) * 128],
                            sgb[:, c * blk + j * 128: c * blk + (j + 1) * 128],
                            env.id128[:, :],
                            reads=["rs_sgb", "id128"],
                            writes=pk1(reg[0], reg[1]), tag="rs_tr")
                p.tt(xt, xt, trps[:, 0:1024], ALU.add,
                     reads=pk1(reg[0], reg[1]) + ["rs_x0"],
                     writes=["rs_x0"], tag="rs_add")
                p.dma("gpsimd", xov[:, tt, :], xt,
                      reads=["rs_x0"], writes=["rs_x0"], tag="out_x")
    return g_sink


def make_s_sink(env, s_scr, mode="add_xn"):
    p = env.p
    sst = env.sb("ss_st", [128, 512], BF16)

    def s_sink(dc, qc, ap, rkeys):
        st = sst[:, 0:512]
        if mode == "add_xn":
            p.tt(st, ap, env.xn_fm[:, dc * T + qc * 512: dc * T + qc * 512 + 512],
                 ALU.add, reads=list(rkeys) + ["xn_fm", "ss_st"],
                 writes=["ss_st"], tag="ss_add")
        else:
            p.copy(st, ap, reads=list(rkeys) + ["ss_st"],
                   writes=["ss_st"], tag="ss_cp")
        p.dma("gpsimd", s_scr[dc, :, qc * 512:(qc + 1) * 512], st,
              reads=["ss_st"], writes=["ss_st", "s_scr_w"], tag="ss_out")
    return s_sink


DEBUG_DUMP = False


def build_layer1(wshapes):
    nc = bass.Bass("TRN2", target_bir_lowering=False)
    kind_dbg = "ExternalOutput" if DEBUG_DUMP else "Internal"
    x_in = nc.dram_tensor("x_in", [T, D], F32, kind="ExternalInput").ap()
    x_out = nc.dram_tensor("x_out", [T, D], F32, kind="ExternalOutput").ap()
    s_scr = nc.dram_tensor("s_scr", [NCH, 128, T], BF16, kind=kind_dbg).ap()
    rscr = nc.dram_tensor("rscr", [16, T], BF16, kind=kind_dbg).ap()
    W = {k: nc.dram_tensor(k, list(v), F32, kind="ExternalInput").ap()
         for k, v in wshapes.items()}
    env = Env(nc)
    make_consts(env)
    make_consts_l1(env)
    env.scr15 = env.sb("scr15", [15, 128], BF16)
    env.rscr = rscr
    env.xn_fm = env.sb("xn_fm", [128, NCH * T], BF16)
    emit_ln_phase(env, x_in)
    if DEBUG_DUMP:
        dxn = nc.dram_tensor("dbg_xn", [128, NCH * T], BF16,
                             kind="ExternalOutput").ap()
        env.p.dma("gpsimd", dxn, env.xn_fm[:, :], reads=["xn_fm"], tag="dxn")
    emit_l1_attn(env, W, make_s_sink(env, s_scr, mode="add_xn"))
    if DEBUG_DUMP:
        env.dbg_hook_ffn = nc.dram_tensor("dbg_racc", [128, NCH * T], F32,
                                          kind="ExternalOutput").ap()
        env.dbg_cw = nc.dram_tensor("dbg_cw", [15, T], BF16,
                                    kind="ExternalOutput").ap()
    g_sink = make_residual_streamer(env, x_in, x_out, s_scr)
    emit_l1_ffn(env, W, g_sink)
    env.p.build()
    return nc


LAYER_BUILDERS = {1: (host_prep_l1, build_layer1)}
_BUILD_CACHE = {}


def run_layer(li, x_np, inputs, cores=8):
    prep, builder = LAYER_BUILDERS[li]
    w = prep(inputs)
    wshapes = {k: v.shape for k, v in w.items()}
    key = (li, tuple(sorted(wshapes.items())))
    if key not in _BUILD_CACHE:
        _BUILD_CACHE[key] = builder(wshapes)
    nc = _BUILD_CACHE[key]
    in_maps = []
    for c in range(cores):
        m = {"x_in": np.ascontiguousarray(x_np[c], dtype=np.float32)}
        for k, v in w.items():
            m[k] = np.ascontiguousarray(v, dtype=np.float32)
        in_maps.append(m)
    res = run_bass_kernel_spmd(nc, in_maps, core_ids=list(range(cores)))
    return np.stack([r["x_out"] for r in res.results])


def kernel(**inputs):
    x = np.ascontiguousarray(np.asarray(inputs["L0_attn_x"], dtype=np.float32))
    B = x.shape[0]
    for li in (0, 1, 2, 3):
        if li in LAYER_BUILDERS:
            x = run_layer(li, x, inputs, cores=B)
    return x
def host_prep_l3(inp):
    w = {}
    Wqkv = np.asarray(inp["L3_attn_W_qkv"])
    H, DK = 16, 64
    wq = Wqkv[:, 0:D].reshape(D, H, DK)
    wk = Wqkv[:, D:2 * D].reshape(D, H, DK)
    wv = Wqkv[:, 2 * D:3 * D].reshape(D, H, DK)
    perm = np.concatenate([np.arange(0, DK, 2), np.arange(1, DK, 2)])
    wq = wq[:, :, perm].reshape(D, D)
    wk = wk[:, :, perm].reshape(D, D)
    wv = wv.reshape(D, D)
    w["l3_wq"] = np.ascontiguousarray(wq).reshape(NCH, 128, D)
    w["l3_wk"] = np.ascontiguousarray(wk).reshape(NCH, 128, D)
    w["l3_wv"] = np.ascontiguousarray(wv).reshape(NCH, 128, D)
    w["l3_wo"] = np.ascontiguousarray(np.asarray(inp["L3_attn_W_out"])).reshape(NCH, 128, D)
    def bitrev(j, nb):
        r = 0
        for i in range(nb):
            r = (r << 1) | ((j >> i) & 1)
        return r
    leafperm = [bitrev(j, 4) for j in range(16)]
    lvl_rows = {d: [(2 ** d - 1) + bitrev(j, d) for j in range(2 ** d)]
                for d in range(4)}
    for f, pfx in enumerate("qkv"):
        nw = np.asarray(inp[f"L3_ffn_{pfx}_node_w"])
        nb = np.asarray(inp[f"L3_ffn_{pfx}_node_b"])
        for d in range(4):
            rows = lvl_rows[d]
            w[f"l3_nw{f}{d}"] = np.ascontiguousarray(
                nw[rows].T.reshape(NCH, 128, 2 ** d))
            w[f"l3_nb{f}{d}"] = np.ascontiguousarray(
                nb[rows].reshape(2 ** d, 1))
        w1 = np.asarray(inp[f"L3_ffn_{pfx}_w1"])[leafperm]
        b1 = np.asarray(inp[f"L3_ffn_{pfx}_b1"])[leafperm]
        w2 = np.asarray(inp[f"L3_ffn_{pfx}_w2"])[leafperm]
        b2 = np.asarray(inp[f"L3_ffn_{pfx}_b2"])[leafperm]
        w[f"l3_w1{f}"] = np.ascontiguousarray(
            w1.reshape(16, NCH, 128, 64).transpose(1, 2, 0, 3).reshape(NCH, 128, 16 * 64))
        w[f"l3_b1{f}"] = np.ascontiguousarray(b1.T)
        w[f"l3_w2{f}"] = np.ascontiguousarray(w2)
        w[f"l3_b2{f}"] = np.ascontiguousarray(b2)
    return w


def l3_make_consts(env):
    pos = np.arange(T)
    i32 = np.arange(32)
    freqs = 1.0 / (10000.0 ** (2 * i32 / 64.0))
    ang = pos[None, :] * freqs[:, None]
    cos32, sin32 = np.cos(ang), np.sin(ang)
    C = np.concatenate([cos32, cos32, cos32, cos32], 0)
    S = np.concatenate([-sin32, sin32, -sin32, sin32], 0)
    env.l3_cos = env.load_const_sb("l3_cos", C.astype(bfdt))
    env.l3_sin = env.load_const_sb("l3_sin", S.astype(bfdt))

    for d in range(4):
        ge = np.zeros((2 ** d, 16), np.float32)
        lt = np.zeros((2 ** d, 16), np.float32)
        for j in range(16):
            pre = j % (2 ** d)
            if (j >> d) & 1:
                ge[pre, j] = 1.0
            else:
                lt[pre, j] = 1.0
        env.__dict__[f"l3_selge{d}"] = env.load_const_sb(f"l3_selge{d}", ge.astype(bfdt))
        env.__dict__[f"l3_sellt{d}"] = env.load_const_sb(f"l3_sellt{d}", lt.astype(bfdt))
    il = np.zeros((16, 16 * 64), np.float32)
    for l in range(16):
        il[l, l * 64:(l + 1) * 64] = 1.0
    env.l3_indl = env.load_const_sb("l3_indl", il.astype(bfdt))


def emit_l3_attn(env, W, s_sink):
    p, nc = env.p, env.nc
    PS1, PS2 = env.PS1.ap(), env.PS2.ap()
    l3_bscr = nc.dram_tensor("l3_bscr", [2], F32, kind="Internal").ap()
    p.barrier()
    from contextlib import ExitStack as _ES
    with _ES() as _es:
        _e = _es.enter_context
        y_fm = _e(nc.sbuf_tensor("l3_y", [128, NCH * T], BF16))
        sums_all = _e(nc.sbuf_tensor("l3_sums", [16, T], F32))
        r_bf = _e(nc.sbuf_tensor("l3_r", [16, T], BF16))
        srow = _e(nc.sbuf_tensor("l3_srow", [128, T], F32))
        rrep = _e(nc.sbuf_tensor("l3_rrep", [128, T], BF16))
        mcol = _e(nc.sbuf_tensor("l3_mcol", [2, 4], F32))
        qp = _e(nc.sbuf_tensor("l3_qp", [128, T], BF16))
        kp = _e(nc.sbuf_tensor("l3_kp", [128, T], BF16))
        vp = _e(nc.sbuf_tensor("l3_vp", [128, NT * 130], BF16))
        wpair = _e(nc.sbuf_tensor("l3_wp", [128, 3 * NCH * 128], BF16))
        ptil = _e(nc.sbuf_tensor("l3_pt", [128, 2 * 2048], BF16))
        swp = _e(nc.sbuf_tensor("l3_sw", [128, T], BF16))
        utmp = _e(nc.sbuf_tensor("l3_u", [128, T], BF16))
        sqt = _e(nc.sbuf_tensor("l3_sqt", [128, T], BF16))
        bhneg = _e(nc.sbuf_tensor("l3_bh", [128, 2], F32))
        brow2 = _e(nc.sbuf_tensor("l3_br", [2, 2], F32))
        brow_bf = _e(nc.sbuf_tensor("l3_brb", [1, 2], BF16))
        wo_sb = _e(nc.sbuf_tensor("l3_wo_sb", [128, NCH * D], BF16))
        env.wload(wo_sb[:, :].rearrange("p (c o) -> p c o", c=NCH),
                  W["l3_wo"].rearrange("c p o -> p c o"), "l3_wo")
        for hp in range(8):
            for i, nm in enumerate(("l3_wq", "l3_wk", "l3_wv")):
                env.wload(
                    wpair[:, i * 1024:(i + 1) * 1024].rearrange(
                        "p (c o) -> p c o", c=NCH),
                    W[nm][:, :, hp * 128:(hp + 1) * 128].rearrange(
                        "c p o -> p c o"),
                    f"l3_wpair{i}")
            for wi, (dst, dk) in enumerate(((qp, "l3_qp"), (kp, "l3_kp"))):
                for qc in range(4):
                    c0 = (qc % 2) * 512
                    ps = PS1[:, c0:c0 + 512]
                    for kc in range(NCH):
                        p.mm(ps,
                             wpair[:, wi * 1024 + kc * 128: wi * 1024 + (kc + 1) * 128],
                             env.xn_fm[:, kc * T + qc * 512: kc * T + qc * 512 + 512],
                             start=(kc == 0), stop=(kc == NCH - 1),
                             reads=[f"l3_wpair{wi}", "xn_fm"],
                             writes=pk1(c0, c0 + 512), tag=f"l3{dk}mm")
                    p.act(dst[:, qc * 512:(qc + 1) * 512], ps, AF.Copy,
                          reads=pk1(c0, c0 + 512), writes=[dk + "_raw"],
                          tag=f"l3{dk}ev")
                for b0, b1 in ((0, 32), (32, 0), (64, 96), (96, 64)):
                    p.dma("gpsimd", swp[b0:b0 + 32, :], dst[b1:b1 + 32, :],
                          reads=[dk + "_raw", "l3_swp"], writes=["l3_swp"],
                          tag=f"l3sw{dk}")
                eng = "vector" if wi == 0 else "gpsimd"
                p.tt(utmp[:, :], dst[:, :], env.l3_cos[:, :], ALU.mult,
                     engine=eng, reads=[dk + "_raw", "l3_cos", "l3_u"],
                     writes=["l3_u"], tag=f"l3rc{dk}")
                p.tt(swp[:, :], swp[:, :], env.l3_sin[:, :], ALU.mult,
                     engine=eng, reads=["l3_swp", "l3_sin"], writes=["l3_swp"],
                     tag=f"l3rs{dk}")
                p.tt(dst[:, :], utmp[:, :], swp[:, :], ALU.add,
                     engine=eng, reads=["l3_u", "l3_swp"],
                     writes=[dk, dk + "_raw"], tag=f"l3ra{dk}")
            for tt in range(NT):
                c0 = 1024 + (tt % 2) * 512
                ps = PS1[:, c0:c0 + 128]
                for kc in range(NCH):
                    p.mm(ps, env.xn_fm[:, kc * T + tt * 128: kc * T + (tt + 1) * 128],
                         wpair[:, 2048 + kc * 128: 2048 + (kc + 1) * 128],
                         start=(kc == 0), stop=(kc == NCH - 1),
                         reads=["l3_wpair2", "xn_fm"], writes=pk1(c0, c0 + 128),
                         tag=f"l3vmm{tt}")
                p.copy(vp[:, tt * 130:(tt + 1) * 130].rearrange(
                           "p (h j) -> p h j", h=2)[:, :, 0:64],
                       ps.rearrange("p (h j) -> p h j", h=2),
                       reads=pk1(c0, c0 + 128), writes=["l3_vp"],
                       tag=f"l3vev{tt}")
            p.memset(vp[:, :].rearrange("p (n h j) -> p n h j", n=NT, h=2)
                     [:, :, :, 64:65],
                     1.0, reads=["l3_vp"], writes=["l3_vp"], tag="l3_vones")
            for o, (srcb, sk) in enumerate(((qp, "l3_qp"), (kp, "l3_kp"))):
                p.act(sqt[:, :], srcb[:, :], AF.Square, reads=[sk],
                      writes=["l3_sqt"], tag=f"l3sq{o}")
                for qc in range(4):
                    ps = PS2[0:2, 1536:2048]
                    p.mm(ps, env.pairselT[:, :], sqt[:, qc * 512:(qc + 1) * 512],
                         reads=["l3_sqt", "pairselT"], writes=pk2(1536, 2048),
                         tag=f"l3nrm{o}")
                    p.red(mcol[:, qc:qc + 1], ps, ALU.max,
                          reads=pk2(1536, 2048), writes=["l3_mc"],
                          tag=f"l3nrmr{o}")
                p.red(brow2[:, o:o + 1], mcol[:, 0:4], ALU.max,
                      reads=["l3_mc"], writes=[f"l3_br{o}"], tag=f"l3b{o}")
            p.tt(brow2[:, 0:1], brow2[:, 0:1], brow2[:, 1:2], ALU.mult,
                 reads=["l3_br0", "l3_br1"], writes=["l3_bp"], tag="l3bp")
            p.act(brow2[:, 0:1], brow2[:, 0:1], AF.Sqrt, reads=["l3_bp"],
                  writes=["l3_bs"], tag="l3bs")
            p.ts(brow2[:, 0:1], brow2[:, 0:1], -0.125, ALU.mult,
                 reads=["l3_bs"], writes=["l3_bn"], tag="l3bn")
            p.dma("gpsimd", l3_bscr, brow2[:, 0:1],
                  reads=["l3_bn", "l3_bh"], writes=["l3_bscrd"],
                  tag="l3btr")
            p.dma("sync", bhneg[:, 0:2],
                  bass.AP(l3_bscr.tensor, 0, [[0, 128], [1, 2]]),
                  reads=["l3_bscrd", "l3_bh"], writes=["l3_bh"],
                  tag="l3brep")
            for h in range(2):
                hg = 2 * hp + h
                for kt in range(NT):
                    qcb0 = kt // 4
                    s0 = (kt % 2) * 1024
                    po = (kt % 2) * 2048
                    pky = f"l3_pt{kt % 2}"
                    for ch in range(qcb0, 4):
                        sc = PS1[:, s0 + (ch % 2) * 512: s0 + (ch % 2) * 512 + 512]
                        p.mm(sc,
                             kp[64 * h:64 * h + 64, kt * 128:(kt + 1) * 128],
                             qp[64 * h:64 * h + 64, ch * 512:(ch + 1) * 512],
                             reads=["l3_qp", "l3_kp"],
                             writes=pk1(s0 + (ch % 2) * 512,
                                        s0 + (ch % 2) * 512 + 512),
                             tag=f"l3sc{hg}")
                        p.act(ptil[:, po + ch * 512: po + (ch + 1) * 512],
                              sc, AF.Exp,
                              scale=0.125, bias=bhneg[:, h:h + 1],
                              reads=pk1(s0 + (ch % 2) * 512,
                                        s0 + (ch % 2) * 512 + 512) + ["l3_bh",
                                                                      pky],
                              writes=[pky], tag=f"l3exp{hg}")
                    m = kt % 4
                    if m > 0:
                        p.memset(ptil[:, po + qcb0 * 512: po + qcb0 * 512 + m * 128],
                                 0.0, reads=[pky], writes=[pky],
                                 tag=f"l3mz{hg}")
                    p.tt(ptil[:, po + kt * 128: po + (kt + 1) * 128],
                         ptil[:, po + kt * 128: po + (kt + 1) * 128],
                         env.causal_keep[:, :], ALU.mult,
                         reads=[pky, "causal_keep"], writes=[pky],
                         tag=f"l3msk{hg}")
                    for ch in range(qcb0, 4):
                        av = PS2[0:65, ch * 512:(ch + 1) * 512]
                        p.mm(av,
                             vp[:, kt * 130 + 65 * h: kt * 130 + 65 * h + 65],
                             ptil[:, po + ch * 512: po + (ch + 1) * 512],
                             start=(kt == 0), stop=(kt == min(4 * ch + 3, NT - 1)),
                             reads=[pky, "l3_vp"],
                             writes=pk2(ch * 512, (ch + 1) * 512),
                             tag=f"l3av{hg}")
                p.act(srow[64:65, 0:T], PS2[64:65, 0:T], AF.Copy,
                      reads=pk2(0, 2048) + ["l3_srow"],
                      writes=["l3_srow"], tag=f"l3sumst{hg}")
                p.dma("gpsimd", sums_all[hg:hg + 1, :], srow[64:65, 0:T],
                      reads=["l3_srow"], writes=["l3_sums", "l3_srow"],
                      tag=f"l3sum{hg}")
                p.copy(y_fm[64 * h:64 * h + 64, hp * T:(hp + 1) * T],
                       PS2[0:64, 0:T], reads=pk2(0, 2048) + ["l3_yraw"],
                       writes=["l3_yraw"], tag=f"l3yev{hg}")
        p.act(sums_all[:, :], sums_all[:, :], AF.Ln, reads=["l3_sums"],
              writes=["l3_lns"], tag="l3_lns")
        p.act(r_bf[:, :], sums_all[:, :], AF.Exp, scale=-1.0,
              reads=["l3_lns"], writes=["l3_r"], tag="l3_rinv")
        p.dma("gpsimd", env.rscr, r_bf[:, :], reads=["l3_r"],
              writes=["l3_rscr"], tag="l3rst")
        for c in range(NCH):
            for h in range(2):
                srcap = bass.AP(env.rscr.tensor, (2 * c + h) * T,
                                [[0, 64], [1, T]])
                p.dma("sync", rrep[64 * h:64 * h + 64, :], srcap,
                      reads=["l3_rscr", "l3_rrep"], writes=["l3_rrep"],
                      tag="l3rbc")
            p.tt(y_fm[:, c * T:(c + 1) * T], y_fm[:, c * T:(c + 1) * T],
                 rrep[:, :], ALU.mult,
                 reads=["l3_rrep", "l3_yraw"], writes=["l3_y"],
                 tag=f"l3ynrm{c}")
        for dc in range(NCH):
            for qc in range(4):
                c0 = (qc % 2) * 512
                ps = PS2[:, c0:c0 + 512]
                for kc in range(NCH):
                    p.mm(ps, wo_sb[:, kc * D + dc * 128: kc * D + (dc + 1) * 128],
                         y_fm[:, kc * T + qc * 512: kc * T + qc * 512 + 512],
                         start=(kc == 0), stop=(kc == NCH - 1),
                         reads=["l3_wo", "l3_y"], writes=pk2(c0, c0 + 512),
                         tag="l3omm")
                s_sink(dc, qc, ps, pk2(c0, c0 + 512))
    p.barrier()
def emit_l3_ffn(env, W, g_sink):
    p, nc = env.p, env.nc
    PS1, PS2 = env.PS1.ap(), env.PS2.ap()
    p.barrier()
    with (
        nc.sbuf_tensor("l3f_q", [128, NCH * T], BF16) as q_fm,
        nc.sbuf_tensor("l3f_k", [128, NCH * T], BF16) as k_fm,
        nc.sbuf_tensor("l3f_v", [128, NCH * T], BF16) as v_fm,
    ):
        vt = env.xn_fm
        from contextlib import ExitStack as _ES
        for f, (dst, dstk) in enumerate(((q_fm, "l3f_q"), (k_fm, "l3f_k"),
                                         (v_fm, "l3f_v"))):
            with _ES() as _es:
                _e = _es.enter_context
                nw_sb = _e(nc.sbuf_tensor(f"l3f_nw{f}", [128, NCH * 15], BF16))
                nb_sb = _e(nc.sbuf_tensor(f"l3f_nb{f}", [8, 8], F32))
                ge_sb = _e(nc.sbuf_tensor(f"l3f_ge{f}", [8, 512], BF16))
                lt_sb = _e(nc.sbuf_tensor(f"l3f_lt{f}", [8, 512], BF16))
                sel_sb = _e(nc.sbuf_tensor(f"l3f_sel{f}", [16, T], BF16))
                w1_sb = _e(nc.sbuf_tensor(f"l3f_w1{f}", [128, 16 * NCH * 64], BF16))
                w2_sb = _e(nc.sbuf_tensor(f"l3f_w2{f}", [64, 16 * D], BF16))
                b1_sb = _e(nc.sbuf_tensor(f"l3f_b1{f}", [64, 16], F32))
                b2_sb = _e(nc.sbuf_tensor(f"l3f_b2{f}", [16, D], BF16))
                h_sb = _e(nc.sbuf_tensor(f"l3f_h{f}", [64, 512], BF16))
                for d in range(4):
                    p.dma("gpsimd",
                          nw_sb[:, :].rearrange("p (c o) -> p c o", c=NCH)
                          [:, :, (2 ** d - 1):(2 ** (d + 1) - 1)],
                          W[f"l3_nw{f}{d}"].rearrange("c p o -> p c o"),
                          writes=["l3f_nw"], tag="w_l3f_nw",
                          allow_slow_non_contiguous=True)
                    p.dma("sync", nb_sb[0:2 ** d, d:d + 1], W[f"l3_nb{f}{d}"],
                          writes=["l3f_nb"], tag="l3fnb")
                for d in range(4):
                    p.ts(nb_sb[0:2 ** d, 4 + d:5 + d], nb_sb[0:2 ** d, d:d + 1],
                         -1.0, ALU.mult, reads=["l3f_nb"], writes=["l3f_nbn"],
                         tag="l3fnbn")
                env.wload(w1_sb[:, :].rearrange("p (c lo) -> p c lo", c=NCH),
                          W[f"l3_w1{f}"].rearrange("c p lo -> p c lo"),
                          "l3f_w1")
                env.wload(w2_sb[:, :].rearrange("p (l d) -> p l d", l=16),
                          W[f"l3_w2{f}"].rearrange("l p d -> p l d"), "l3f_w2")
                p.dma("sync", b1_sb[:, :], W[f"l3_b1{f}"], writes=["l3f_b1"],
                      tag="l3fb1")
                env.wload(b2_sb[:, :], W[f"l3_b2{f}"], "l3f_b2")
                for d in range(4):
                    nd = 2 ** d
                    for qc in range(4):
                        ps = PS1[0:nd, 0:512]
                        for kc in range(NCH):
                            p.mm(ps, nw_sb[:, kc * 15 + nd - 1:
                                           kc * 15 + 2 * nd - 1],
                                 env.xn_fm[:, kc * T + qc * 512:
                                           kc * T + qc * 512 + 512],
                                 start=(kc == 0), stop=(kc == NCH - 1),
                                 reads=["l3f_nw", "xn_fm"], writes=pk1(0, 512),
                                 tag=f"l3fsc{f}{d}")
                        p.ts(ge_sb[0:nd, :], ps, nb_sb[0:nd, 4 + d:5 + d],
                             ALU.is_ge, reads=pk1(0, 512) + ["l3f_nbn"],
                             writes=["l3f_ge"], tag=f"l3fge{d}")
                        p.ts(lt_sb[0:nd, :], ps, nb_sb[0:nd, 4 + d:5 + d],
                             ALU.is_lt, reads=pk1(0, 512) + ["l3f_nbn"],
                             writes=["l3f_lt"], tag=f"l3flt{d}")
                        ps2 = PS1[0:16, 512:1024]
                        p.mm(ps2, env.__dict__[f"l3_selge{d}"][:, :],
                             ge_sb[0:nd, :], start=True, stop=False,
                             reads=["l3f_ge", f"l3_selge{d}"],
                             writes=pk1(512, 1024), tag=f"l3fek{d}")
                        p.mm(ps2, env.__dict__[f"l3_sellt{d}"][:, :],
                             lt_sb[0:nd, :], start=False, stop=True,
                             reads=["l3f_lt", f"l3_sellt{d}"],
                             writes=pk1(512, 1024), tag=f"l3fek{d}b")
                        if d == 0:
                            p.copy(sel_sb[0:16, qc * 512:(qc + 1) * 512], ps2,
                                   reads=pk1(512, 1024), writes=["l3f_sel"],
                                   tag="l3fsel0")
                        else:
                            p.tt(sel_sb[0:16, qc * 512:(qc + 1) * 512],
                                 sel_sb[0:16, qc * 512:(qc + 1) * 512], ps2,
                                 ALU.mult, reads=pk1(512, 1024) + ["l3f_sel"],
                                 writes=["l3f_sel"], tag=f"l3fsel{d}")
                for tb in range(8):
                    c0 = tb * 256
                    for l in range(16):
                        psr = PS1[0:64, 0:256]
                        p.mm(psr, env.l3_indl[:, l * 64:(l + 1) * 64],
                             sel_sb[0:16, c0:c0 + 256],
                             reads=["l3f_sel", "l3_indl"], writes=pk1(0, 512),
                             tag="l3frep")
                        hps = PS1[0:64, 512 + (l % 2) * 256:
                                  512 + (l % 2) * 256 + 256]
                        for kc in range(NCH):
                            p.mm(hps, w1_sb[:, (kc * 16 + l) * 64:
                                            (kc * 16 + l + 1) * 64],
                                 env.xn_fm[:, kc * T + c0: kc * T + c0 + 256],
                                 start=(kc == 0), stop=(kc == NCH - 1),
                                 reads=["l3f_w1", "xn_fm"],
                                 writes=pk1(512 + (l % 2) * 256,
                                            768 + (l % 2) * 256),
                                 tag="l3fhmm")
                        hv = h_sb[0:64, (l % 2) * 256:(l % 2) * 256 + 256]
                        p.act(hv, hps, AF.Relu, bias=b1_sb[:, l:l + 1],
                              reads=pk1(512 + (l % 2) * 256,
                                        768 + (l % 2) * 256) + ["l3f_b1"],
                              writes=[f"l3f_h{l % 2}"], tag="l3fhev")
                        p.tt(hv, hv, psr, ALU.mult,
                             reads=pk1(0, 512) + [f"l3f_h{l % 2}"],
                             writes=[f"l3f_h{l % 2}"], tag="l3fhm")
                        for dc in range(NCH):
                            p.mm(PS2[:, dc * 256:(dc + 1) * 256],
                                 w2_sb[:, l * D + dc * 128: l * D + (dc + 1) * 128],
                                 hv, start=(l == 0), stop=False,
                                 reads=[f"l3f_h{l % 2}", "l3f_w2"],
                                 writes=pk2(dc * 256, (dc + 1) * 256),
                                 tag="l3fw2")
                    for dc in range(NCH):
                        p.mm(PS2[:, dc * 256:(dc + 1) * 256],
                             b2_sb[:, dc * 128:(dc + 1) * 128],
                             sel_sb[0:16, c0:c0 + 256], start=False, stop=True,
                             reads=["l3f_sel", "l3f_b2"],
                             writes=pk2(dc * 256, (dc + 1) * 256), tag="l3fb2")
                        p.act(dst[:, dc * T + c0: dc * T + c0 + 256],
                              PS2[:, dc * 256:(dc + 1) * 256], AF.Copy,
                              reads=pk2(dc * 256, (dc + 1) * 256),
                              writes=[dstk], tag="l3fout")
        p.barrier()
        trps = env.PS1.ap().bitcast(BF16)
        for tt in range(NT):
            for c in range(NCH):
                p.trans(trps[:, c * 128:(c + 1) * 128],
                        v_fm[:, c * T + tt * 128: c * T + (tt + 1) * 128],
                        env.id128[:, :],
                        reads=["l3f_v", "id128"], writes=pk1(0, 512),
                        tag="l3fvtr")
            p.copy(vt[:, tt * 1024:(tt + 1) * 1024], trps[:, 0:1024],
                   reads=pk1(0, 512), writes=["l3f_vt"], tag="l3fvtc")
        with _ES() as _es2:
            _e2 = _es2.enter_context
            sqt = _e2(nc.sbuf_tensor("l3f_sq", [128, T], BF16))
            bsc = _e2(nc.sbuf_tensor("l3f_b", [1, 10], F32))
            bscb = _e2(nc.sbuf_tensor("l3f_bb", [1, 1], BF16))
            bcol = _e2(nc.sbuf_tensor("l3f_bh", [128, 1], F32))
            ptil = _e2(nc.sbuf_tensor("l3f_pt", [128, 256], BF16))
            rbr = _e2(nc.sbuf_tensor("l3f_rw", [1, 256], BF16))
            rrep = _e2(nc.sbuf_tensor("l3f_rr", [128, 256], BF16))
            g_sb = _e2(nc.sbuf_tensor("l3f_g", [128, 2 * 256], F32))
            for o, (srcb, sk) in enumerate(((q_fm, "l3f_q"), (k_fm, "l3f_k"))):
                for qc in range(4):
                    ps = PS1[0:1, 1536:2048]
                    for c in range(NCH):
                        p.act(sqt[:, qc * 512:(qc + 1) * 512],
                              srcb[:, c * T + qc * 512: c * T + qc * 512 + 512],
                              AF.Square, reads=[sk], writes=["l3f_sq"],
                              tag=f"l3fsq{o}")
                        p.mm(ps, env.ones_col128[:, :],
                             sqt[:, qc * 512:(qc + 1) * 512],
                             start=(c == 0), stop=(c == NCH - 1),
                             reads=["l3f_sq", "ones_col128"],
                             writes=pk1(1536, 2048), tag=f"l3fn{o}")
                    p.red(bsc[0:1, 4 * o + qc:4 * o + qc + 1], ps, ALU.max,
                          reads=pk1(1536, 2048),
                          writes=[f"l3f_m{'qk'[o]}{qc}"], tag=f"l3fnr{o}")
            p.red(bsc[0:1, 8:9], bsc[0:1, 0:4], ALU.max,
                  reads=["l3f_mq0", "l3f_mq1", "l3f_mq2", "l3f_mq3"],
                  writes=["l3f_mq"], tag="l3fmq")
            p.red(bsc[0:1, 9:10], bsc[0:1, 4:8], ALU.max,
                  reads=["l3f_mk0", "l3f_mk1", "l3f_mk2", "l3f_mk3"],
                  writes=["l3f_mk"], tag="l3fmk")
            p.tt(bsc[0:1, 8:9], bsc[0:1, 8:9], bsc[0:1, 9:10], ALU.mult,
                 reads=["l3f_mq", "l3f_mk"], writes=["l3f_bp"], tag="l3fbp")
            p.act(bsc[0:1, 8:9], bsc[0:1, 8:9], AF.Sqrt, reads=["l3f_bp"],
                  writes=["l3f_bs"], tag="l3fbs")
            p.ts(bscb[0:1, 0:1], bsc[0:1, 8:9], -0.03125, ALU.mult,
                 reads=["l3f_bs"], writes=["l3f_bn"], tag="l3fbn")
            p.mm(PS1[:, 1536:1537], env.ones_row[:, :], bscb[0:1, 0:1],
                 reads=["l3f_bn", "ones_row"], writes=pk1(1536, 2048),
                 tag="l3fbrep")
            p.act(bcol[:, :], PS1[:, 1536:1537], AF.Copy,
                  reads=pk1(1536, 2048), writes=["l3f_bh"], tag="l3fbev")
            for qb in range(8):
                c0 = qb * 256
                for kt in range(NT):
                    s0 = 1024 + (kt % 2) * 256
                    sc = PS1[:, s0:s0 + 256]
                    sck = pk1(1024, 1536)
                    for kc in range(NCH):
                        p.mm(sc, k_fm[:, kc * T + kt * 128: kc * T + (kt + 1) * 128],
                             q_fm[:, kc * T + c0: kc * T + c0 + 256],
                             start=(kc == 0), stop=(kc == NCH - 1),
                             reads=["l3f_q", "l3f_k"], writes=sck,
                             tag="l3fasc")
                    p.act(ptil[:, :], sc, AF.Exp, scale=0.03125,
                          bias=bcol[:, 0:1], reads=sck + ["l3f_bh"],
                          writes=["l3f_pt"], tag="l3faexp")
                    p.mm(PS1[0:1, 1792:2048], env.ones_col128[:, :],
                         ptil[:, :], start=(kt == 0), stop=(kt == NT - 1),
                         reads=["l3f_pt", "ones_col128"], writes=pk1(1536, 2048),
                         tag="l3fasum")
                    for dc in range(NCH):
                        p.mm(PS2[:, dc * 256:(dc + 1) * 256],
                             vt[:, kt * 1024 + dc * 128: kt * 1024 + (dc + 1) * 128],
                             ptil[:, :], start=(kt == 0), stop=(kt == NT - 1),
                             reads=["l3f_pt", "l3f_vt"],
                             writes=pk2(dc * 256, (dc + 1) * 256), tag="l3fav")
                p.act(rbr[0:1, :], PS1[0:1, 1792:2048], AF.Ln,
                      reads=pk1(1536, 2048), writes=["l3f_ln"], tag="l3faln")
                p.act(rbr[0:1, :], rbr[0:1, :], AF.Exp, scale=-1.0,
                      reads=["l3f_ln"], writes=["l3f_rw"], tag="l3farw")
                p.mm(PS1[:, 1024:1280], env.ones_row[:, :], rbr[0:1, :],
                     reads=["l3f_rw", "ones_row"], writes=pk1(1024, 1536),
                     tag="l3farep")
                p.act(rrep[:, :], PS1[:, 1024:1280], AF.Copy,
                      reads=pk1(1024, 1536), writes=["l3f_rr"], tag="l3farev")
                for dc in range(NCH):
                    gv = g_sb[:, (dc % 2) * 256:(dc % 2) * 256 + 256]
                    p.tt(gv, PS2[:, dc * 256:(dc + 1) * 256], rrep[:, :],
                         ALU.mult,
                         reads=pk2(dc * 256, (dc + 1) * 256) + ["l3f_rr",
                                                                f"l3f_g{dc % 2}"],
                         writes=[f"l3f_g{dc % 2}"], tag="l3fgn")
                    g_sink(dc, qb, gv, [f"l3f_g{dc % 2}"])
    p.barrier()


def build_layer3(wshapes):
    nc = bass.Bass("TRN2", target_bir_lowering=False)
    kind_dbg = "ExternalOutput" if DEBUG_DUMP else "Internal"
    x_in = nc.dram_tensor("x_in", [T, D], F32, kind="ExternalInput").ap()
    x_out = nc.dram_tensor("x_out", [T, D], F32, kind="ExternalOutput").ap()
    s_scr = nc.dram_tensor("s_scr", [NCH, 128, T], BF16, kind=kind_dbg).ap()
    rscr = nc.dram_tensor("rscr", [16, T], BF16, kind="Internal").ap()
    W = {k: nc.dram_tensor(k, list(v), F32, kind="ExternalInput").ap()
         for k, v in wshapes.items()}
    env = Env(nc)
    make_consts(env)
    l3_make_consts(env)
    env.rscr = rscr
    env.xn_fm = env.sb("xn_fm", [128, NCH * T], BF16)
    emit_ln_phase(env, x_in)
    if DEBUG_DUMP:
        dxn = nc.dram_tensor("dbg_xn", [128, NCH * T], BF16,
                             kind="ExternalOutput").ap()
        env.p.dma("gpsimd", dxn, env.xn_fm[:, :], reads=["xn_fm"], tag="dxn")
    emit_l3_attn(env, W, make_s_sink(env, s_scr, mode="plain"))
    if DEBUG_DUMP:
        env.dbg_q = nc.dram_tensor("dbg_q", [128, NCH * T], BF16,
                                   kind="ExternalOutput").ap()
        env.dbg_sel = nc.dram_tensor("dbg_sel", [16, T], BF16,
                                     kind="ExternalOutput").ap()
    g_sink = make_residual_streamer(env, x_in, x_out, s_scr, blk=256,
                                    trps_region=(0, 512))
    emit_l3_ffn(env, W, g_sink)
    env.p.build()
    return nc


LAYER_BUILDERS[3] = (host_prep_l3, build_layer3)
def host_prep_l2(inp):
    w = {}
    w["l2_WK"] = np.ascontiguousarray(np.asarray(inp["L2_attn_W_K"])).reshape(NCH, 128, 2048)
    w["l2_Wugg"] = np.ascontiguousarray(np.asarray(inp["L2_attn_W_ugg"])).reshape(NCH, 128, 6144)
    w["l2_Wout"] = np.ascontiguousarray(np.asarray(inp["L2_attn_W_out"])).reshape(16, 128, D)
    w["l2_rms"] = np.ascontiguousarray(np.asarray(inp["L2_attn_rms_scale"])).reshape(NCH, 128).T.copy()
    w["l2_split"] = np.ascontiguousarray(np.asarray(inp["L2_attn_split_scale"])).reshape(NCH, 128).T.copy()
    w["l2_Wc"] = np.ascontiguousarray(np.asarray(inp["L2_ffn_Wc"])).reshape(NCH, 128, 128)
    w["l2_bc"] = np.ascontiguousarray(np.asarray(inp["L2_ffn_bc"])).reshape(128, 1)
    W1 = np.asarray(inp["L2_ffn_W1"]).reshape(128, D, 32)
    W2 = np.asarray(inp["L2_ffn_W2"]).reshape(128, D, 32)
    b1 = np.asarray(inp["L2_ffn_b1"]).reshape(128, 32)
    b2 = np.asarray(inp["L2_ffn_b2"]).reshape(128, D)
    w["l2_W1"] = np.ascontiguousarray(W1.transpose(0, 2, 1)
                                      .reshape(128, 32, NCH, 128)
                                      .transpose(0, 2, 3, 1))
    w["l2_W1"] = np.ascontiguousarray(W1.reshape(128, NCH, 128, 32))
    w["l2_W2t"] = np.ascontiguousarray(W2.transpose(0, 2, 1))
    w["l2_b1T"] = np.ascontiguousarray(b1.T)
    w["l2_b2"] = np.ascontiguousarray(b2)
    return w


def emit_l2_attn(env, W, s_sink):
    p, nc = env.p, env.nc
    PS1, PS2 = env.PS1.ap(), env.PS2.ap()
    p.barrier()
    from contextlib import ExitStack as _ES
    with _ES() as _es:
        _e = _es.enter_context
        xn2 = _e(nc.sbuf_tensor("l2_xn2", [128, NCH * T], BF16))
        so = _e(nc.sbuf_tensor("l2_so", [128, T], BF16))
        soq = _e(nc.sbuf_tensor("l2_soq", [128, 16 * 512], BF16))
        nrow = _e(nc.sbuf_tensor("l2_nr", [1, T], F32))
        rrep = _e(nc.sbuf_tensor("l2_rrep", [128, T], BF16))
        sv = _e(nc.sbuf_tensor("l2_sv_sb", [128, NCH], F32))
        sq = _e(nc.sbuf_tensor("l2_sq", [128, 512], BF16))
        wst = _e(nc.sbuf_tensor("l2_wst", [128, 2 * 4 * NCH * 128], BF16))
        kc_sb = _e(nc.sbuf_tensor("l2_kc", [128, T], F32))
        k1m = _e(nc.sbuf_tensor("l2_k1m", [128, T], F32))
        shf = _e(nc.sbuf_tensor("l2_shf", [128, T + 1], F32))
        ub = _e(nc.sbuf_tensor("l2_ub", [128, T], F32))
        t1 = _e(nc.sbuf_tensor("l2_t1", [128, T], F32))
        gsig = _e(nc.sbuf_tensor("l2_gs", [128, T], BF16))
        sct = _e(nc.sbuf_tensor("l2_sct", [128, T], F32))
        wo_sb = _e(nc.sbuf_tensor("l2_wo_sb", [128, 16 * D], BF16))
        env.wload(wo_sb[:, :].rearrange("p (c o) -> p c o", c=16),
                  W["l2_Wout"].rearrange("c p o -> p c o"), "l2_wo")
        p.dma("sync", sv[:, 0:NCH], W["l2_rms"], writes=["l2_rms"], tag="l2rms")
        p.dma("sync", xn2[:, 0:16].bitcast(F32), W["l2_split"],
              writes=["l2_splt"], tag="l2splt")
        p.tt(sv[:, :], sv[:, :], xn2[:, 0:16].bitcast(F32),
             ALU.mult, reads=["l2_rms", "l2_splt"], writes=["l2_sv"],
             tag="l2sv")
        for qc in range(4):
            ps = PS1[0:1, 1536:2048]
            for c in range(NCH):
                p.act(sq[:, 0:512],
                      env.xn_fm[:, c * T + qc * 512: c * T + qc * 512 + 512],
                      AF.Square, reads=["xn_fm"], writes=["l2_sq"], tag="l2sq")
                p.mm(ps, env.ones_col128[:, :], sq[:, 0:512],
                     start=(c == 0), stop=(c == NCH - 1),
                     reads=["l2_sq", "ones_col128"], writes=pk1(1536, 2048),
                     tag="l2nm")
            p.act(nrow[0:1, qc * 512:(qc + 1) * 512], ps, AF.Sqrt,
                  scale=1.0 / D, reads=pk1(1536, 2048), writes=["l2_nr"],
                  tag="l2nsq")
        p.act(nrow[0:1, :], nrow[0:1, :], AF.Ln, bias=env.eps8[0:1, 0:1],
              reads=["l2_nr", "eps8"], writes=["l2_nl"], tag="l2nl")
        p.act(nrow[0:1, :], nrow[0:1, :], AF.Exp, scale=-1.0,
              reads=["l2_nl"], writes=["l2_nr2"], tag="l2nr2")
        p.copy(sct[0:1, 0:T], nrow[0:1, :], reads=["l2_nr2"], writes=["l2_rbf"],
               tag="l2rbf")
        p.dma("gpsimd", env.rscr[0:1, :], sct[0:1, 0:T], reads=["l2_rbf"],
              writes=["l2_rd"], tag="l2rst")
        srcap = bass.AP(env.rscr.tensor, 0, [[0, 128], [1, T]])
        p.dma("sync", rrep[:, :], srcap, reads=["l2_rd"], writes=["l2_rrep"],
              tag="l2rbc")
        for c in range(NCH):
            p.stt(xn2[:, c * T:(c + 1) * T],
                  env.xn_fm[:, c * T:(c + 1) * T], sv[:, c:c + 1],
                  rrep[:, :], ALU.mult, ALU.mult,
                  reads=["xn_fm", "l2_sv", "l2_rrep", "l2_splt"],
                  writes=["l2_xn2"], tag=f"l2xn2_{c}")
        for hc in range(16):
            env.wload(wst[:, 0:1024].rearrange("p (c o) -> p c o", c=NCH),
                      W["l2_WK"][:, :, hc * 128:(hc + 1) * 128].rearrange(
                          "c p o -> p c o"), "l2_wstK")
            for j, off in enumerate((0, 2048, 4096)):
                env.wload(wst[:, (j + 1) * 1024:(j + 2) * 1024].rearrange(
                              "p (c o) -> p c o", c=NCH),
                          W["l2_Wugg"][:, :, off + hc * 128: off + (hc + 1) * 128]
                          .rearrange("c p o -> p c o"), f"l2_wstU{j + 1}")
            for qc in range(4):
                c0 = (qc % 2) * 512
                ps = PS1[:, c0:c0 + 512]
                for kc in range(NCH):
                    p.mm(ps, wst[:, kc * 128:(kc + 1) * 128],
                         xn2[:, kc * T + qc * 512: kc * T + qc * 512 + 512],
                         start=(kc == 0), stop=(kc == NCH - 1),
                         reads=["l2_wstK", "l2_xn2"], writes=pk1(c0, c0 + 512),
                         tag="l2kmm")
                p.act(kc_sb[:, qc * 512:(qc + 1) * 512], ps, AF.Sigmoid,
                      reads=pk1(c0, c0 + 512), writes=["l2_kc"], tag="l2kev")
                p.act(k1m[:, qc * 512:(qc + 1) * 512], ps, AF.Sigmoid,
                      scale=-1.0, reads=pk1(c0, c0 + 512), writes=["l2_k1m"],
                      tag="l2k1ev")
            for wi, (dst, dk, fn) in enumerate((
                    (t1, "l2_t1", AF.Sigmoid), (ub, "l2_ub", AF.Copy),
                    (gsig, "l2_gs", AF.Sigmoid))):
                wofs = (2 if wi == 0 else (1 if wi == 1 else 3)) * 1024
                for qc in range(4):
                    c0 = (qc % 2) * 512
                    ps = PS1[:, c0:c0 + 512]
                    for kc in range(NCH):
                        p.mm(ps, wst[:, wofs + kc * 128: wofs + (kc + 1) * 128],
                             xn2[:, kc * T + qc * 512: kc * T + qc * 512 + 512],
                             start=(kc == 0), stop=(kc == NCH - 1),
                             reads=[f"l2_wstU{wofs // 1024}", "l2_xn2"],
                             writes=pk1(c0, c0 + 512), tag=f"l2umm{wi}")
                    p.act(dst[:, qc * 512:(qc + 1) * 512], ps, fn,
                          reads=pk1(c0, c0 + 512), writes=[dk], tag=f"l2uev{wi}")
            p.tt(ub[:, :], ub[:, :], t1[:, :], ALU.mult,
                 reads=["l2_ub", "l2_t1"], writes=["l2_ub"], tag="l2ub1")
            p.tt(ub[:, :], ub[:, :], k1m[:, :], ALU.mult,
                 reads=["l2_ub", "l2_k1m"], writes=["l2_ub"], tag="l2ub2")
            p.memset(shf[:, 0:1], 0.0, writes=["l2_shf"], tag="l2shz")
            p.copy(shf[:, 1:T + 1], kc_sb[:, :], reads=["l2_kc", "l2_shf"],
                   writes=["l2_shf"], tag="l2shc")
            p.scan(sct[:, :], shf[:, 0:T], ub[:, :], 0.0, ALU.mult, ALU.add,
                   reads=["l2_shf", "l2_ub"], writes=["l2_sct", "l2_rbf"],
                   tag="l2scan")
            p.tt(so[:, :], sct[:, :], gsig[:, :], ALU.mult,
                 reads=["l2_sct", "l2_gs", "l2_so"], writes=["l2_so"],
                 tag="l2soev")
            p.dma("gpsimd", env.l2_sodram[hc], so[:, :],
                  reads=["l2_so"], writes=["l2_sod"], tag="l2sost")
        for qc in range(4):
            p.dma("sync", soq[:, :].rearrange("p (h j) -> p h j", h=16),
                  env.l2_sodram.rearrange("h p j -> p h j")
                  [:, :, qc * 512:(qc + 1) * 512],
                  reads=["l2_sod", "l2_soq"], writes=["l2_soq"], tag="l2sold")
            for dc in range(NCH):
                c0 = (dc % 2) * 512
                ps = PS2[:, c0:c0 + 512]
                for hc in range(16):
                    p.mm(ps, wo_sb[:, hc * D + dc * 128: hc * D + (dc + 1) * 128],
                         soq[:, hc * 512:(hc + 1) * 512],
                         start=(hc == 0), stop=(hc == 15),
                         reads=["l2_wo", "l2_soq"], writes=pk2(c0, c0 + 512),
                         tag="l2omm")
                s_sink(dc, qc, ps, pk2(c0, c0 + 512))
    p.barrier()
def emit_l2_ffn(env, W, xnd, yed, finale):
    p, nc = env.p, env.nc
    PS1, PS2 = env.PS1.ap(), env.PS2.ap()
    p.barrier()
    from contextlib import ExitStack as _ES
    from concourse.bass_core_rust import IndirectOffsetOnAxis
    tie = env.load_const_sb(
        "l2_tie", np.ascontiguousarray(np.broadcast_to(
            np.tile(np.linspace(0.0, 1e-6, 32, dtype=np.float32), 64),
            (128, T))))
    iota32 = env.load_const_sb(
        "l2_iota32", np.ascontiguousarray(np.broadcast_to(
            np.tile(np.arange(32, dtype=np.float32), 64), (128, T))
            .astype(bfdt)))
    g32 = env.load_const_sb(
        "l2_g32", np.ascontiguousarray(np.broadcast_to(
            (np.arange(64, dtype=np.float32) * 32), (128, 64))))
    with _ES() as _es:
        _e = _es.enter_context
        wc_sb = _e(nc.sbuf_tensor("l2_wc_sb", [128, NCH * 128], BF16))
        bc_sb = _e(nc.sbuf_tensor("l2_bc_sb", [128, 1], F32))
        lg = _e(nc.sbuf_tensor("l2_lg", [128, T], F32))
        gmax = _e(nc.sbuf_tensor("l2_gmax", [128, 64], F32))
        perm = _e(nc.sbuf_tensor("l2_perm", [128, T], BF16))
        loc = _e(nc.sbuf_tensor("l2_loc", [128, 64], F32))
        idxb = _e(nc.sbuf_tensor("l2_idx", [128, 64], U32))
        idx2 = _e(nc.sbuf_tensor("l2_idx2", [128, 64], U32))
        vtok = _e(nc.sbuf_tensor("l2_vtok", [128, 2 * D], BF16))
        xsel = _e(nc.sbuf_tensor("l2_xsel", [128, 2 * D], BF16))
        w1_sb = _e(nc.sbuf_tensor("l2_w1s", [128, 4 * NCH * 32], BF16))
        w2_sb = _e(nc.sbuf_tensor("l2_w2s", [32, 4 * D], BF16))
        b1_sb = _e(nc.sbuf_tensor("l2_b1s", [32, 128], F32))
        b2_sb = _e(nc.sbuf_tensor("l2_b2s", [1, 4 * D], BF16))
        h_sb = _e(nc.sbuf_tensor("l2_hs", [32, 4 * 64], BF16))
        ye_sb = _e(nc.sbuf_tensor("l2_yes", [64, 2 * D], BF16))
        yeg = _e(nc.sbuf_tensor("l2_yeg", [128, 2 * D], BF16))
        env.wload(wc_sb[:, :].rearrange("p (c o) -> p c o", c=NCH),
                  W["l2_Wc"].rearrange("c p o -> p c o"), "l2_wcw")
        p.dma("sync", bc_sb[:, :], W["l2_bc"], writes=["l2_bcb"], tag="l2bc")
        p.dma("sync", b1_sb[:, :], W["l2_b1T"], writes=["l2_b1s"], tag="l2b1")
        for qc in range(4):
            c0 = (qc % 2) * 512
            ps = PS1[:, c0:c0 + 512]
            for kc in range(NCH):
                p.mm(ps, wc_sb[:, kc * 128:(kc + 1) * 128],
                     env.xn_fm[:, kc * T + qc * 512: kc * T + qc * 512 + 512],
                     start=(kc == 0), stop=(kc == NCH - 1),
                     reads=["l2_wcw", "xn_fm"], writes=pk1(c0, c0 + 512),
                     tag="l2lmm")
            p.act(lg[:, qc * 512:(qc + 1) * 512], ps, AF.Identity,
                  bias=bc_sb[:, 0:1], reads=pk1(c0, c0 + 512) + ["l2_bcb"],
                  writes=["l2_lg"], tag="l2lev")
        p.tt(lg[:, :], lg[:, :], tie[:, :], ALU.add,
             reads=["l2_lg", "l2_tie"], writes=["l2_lg"], tag="l2tie")
        lg3 = lg[:, :].rearrange("p (g j) -> p g j", g=64)
        p.red(gmax[:, :], lg3, ALU.max, reads=["l2_lg"], writes=["l2_gm"],
              tag="l2gm")
        gmb = bass.AP(gmax, 0, [[64, 128], [1, 64], [0, 32]])
        p.tt(perm[:, :].rearrange("p (g j) -> p g j", g=64), lg3, gmb,
             ALU.is_ge, reads=["l2_lg", "l2_gm"], writes=["l2_perm"],
             tag="l2pm")
        p.tt(lg[:, :], perm[:, :], iota32[:, :], ALU.mult,
             reads=["l2_perm", "l2_iota32", "l2_lg"], writes=["l2_lgi"],
             tag="l2li")
        p.red(loc[:, :], lg3, ALU.add, reads=["l2_lgi"], writes=["l2_loc"],
              tag="l2loc")
        p.tt(loc[:, :], loc[:, :], g32[:, :], ALU.add,
             reads=["l2_loc", "l2_g32"], writes=["l2_loc"], tag="l2locg")
        p.copy(idxb[:, :], loc[:, :], reads=["l2_loc"], writes=["l2_idxb"],
               tag="l2idx")
        p.dma("gpsimd", env.l2_idxd, idxb[:, :], reads=["l2_idxb"],
              writes=["l2_idxd"], tag="l2idxst")
        for eh in range(2):
            srcap = bass.AP(env.l2_idxd.tensor, eh * 64,
                            [[1, 64], [128, 64]])
            p.dma("sync", idx2[eh * 64:(eh + 1) * 64, :], srcap,
                  reads=["l2_idxd"], writes=["l2_idx2"], tag="l2idxld",
                  allow_slow_non_contiguous=True)
        for gi in range(64):
            par = gi % 2
            p.op("gpsimd", lambda e, gi=gi, par=par: e.indirect_dma_start(
                    vtok[:, par * D:(par + 1) * D], None, xnd,
                    IndirectOffsetOnAxis(idx2[:, gi:gi + 1], 0)),
                 reads=["l2_idx2", "xnd_w", f"l2_vt{par}"],
                 writes=[f"l2_vt{par}"], dma=True, tag="l2gather")
            trps = PS1.bitcast(BF16)[:, 2048 + par * 1024:
                                     2048 + par * 1024 + 1024]
            tk = pk1(1024 + par * 512, 1536 + par * 512)
            for c in range(NCH):
                p.trans(trps[:, c * 128:(c + 1) * 128],
                        vtok[:, par * D + c * 128: par * D + (c + 1) * 128],
                        env.id128[:, :], reads=[f"l2_vt{par}", "id128"],
                        writes=tk, tag="l2vtr")
            p.copy(xsel[:, par * D:(par + 1) * D], trps[:, :],
                   reads=tk, writes=[f"l2_xs{par}"], tag="l2xsc")
            for eh in range(2):
                es = 2 * gi + eh
                sl = 2 * par + eh
                env.wload(w1_sb[:, sl * NCH * 32:(sl + 1) * NCH * 32]
                          .rearrange("p (c o) -> p c o", c=NCH),
                          W["l2_W1"][es].rearrange("c p o -> p c o"),
                          f"l2_w1{sl}")
                env.wload(w2_sb[0:32, sl * D:(sl + 1) * D], W["l2_W2t"][es],
                          f"l2_w2{sl}")
                env.wload(b2_sb[0:1, sl * D:(sl + 1) * D],
                          W["l2_b2"][es:es + 1, :], f"l2_b2{sl}")
                hps = PS1[0:32, 512 + sl * 64: 512 + sl * 64 + 64]
                hk = pk1(512, 1024)
                for kc in range(NCH):
                    p.mm(hps, w1_sb[:, sl * NCH * 32 + kc * 32:
                                    sl * NCH * 32 + (kc + 1) * 32],
                         xsel[:, par * D + kc * 128 + eh * 64:
                              par * D + kc * 128 + eh * 64 + 64],
                         start=(kc == 0), stop=(kc == NCH - 1),
                         reads=[f"l2_w1{sl}", f"l2_xs{par}"],
                         writes=hk, tag="l2hmm")
                p.act(h_sb[0:32, sl * 64:(sl + 1) * 64], hps, AF.Relu,
                      bias=b1_sb[:, es:es + 1], reads=hk + ["l2_b1s"],
                      writes=[f"l2_h{sl}"], tag="l2hev")
                yps = PS2[0:64, eh * 1024:(eh + 1) * 1024]
                for nh in range(2):
                    yk = pk2(eh * 1024 + nh * 512, eh * 1024 + nh * 512 + 512)
                    p.mm(yps[:, nh * 512:(nh + 1) * 512],
                         h_sb[0:32, sl * 64:(sl + 1) * 64],
                         w2_sb[0:32, sl * D + nh * 512: sl * D + nh * 512 + 512],
                         start=True, stop=False,
                         reads=[f"l2_h{sl}", f"l2_w2{sl}"], writes=yk,
                         tag="l2ymm")
                    p.mm(yps[:, nh * 512:(nh + 1) * 512],
                         env.ones_row[:, 0:64],
                         b2_sb[0:1, sl * D + nh * 512: sl * D + nh * 512 + 512],
                         start=False, stop=True,
                         reads=[f"l2_b2{sl}", "ones_row"], writes=yk,
                         tag="l2yb2")
                p.act(ye_sb[0:64, eh * D:(eh + 1) * D], yps, AF.Copy,
                      reads=pk2(eh * 1024, (eh + 1) * 1024),
                      writes=[f"l2_ye{eh}"], tag="l2yev")
                p.dma("gpsimd", yed[es, :, :], ye_sb[0:64, eh * D:(eh + 1) * D],
                      reads=[f"l2_ye{eh}"], writes=["l2_yed"], tag="l2yout")
        for g in range(64):
            par = g % 2
            p.dma("sync", yeg[:, par * D:(par + 1) * D], yed[:, g, :],
                  reads=["l2_yed", f"l2_yg{par}"], writes=[f"l2_yg{par}"],
                  tag="l2ygl")
            ps = PS1[0:32, par * 1024: par * 1024 + 1024]
            for nh in range(2):
                p.mm(ps[:, nh * 512:(nh + 1) * 512],
                     perm[:, g * 32:(g + 1) * 32],
                     yeg[:, par * D + nh * 512: par * D + nh * 512 + 512],
                     reads=["l2_perm", f"l2_yg{par}"],
                     writes=pk1(par * 1024 + nh * 512,
                                par * 1024 + nh * 512 + 512), tag="l2fmm")
            finale(g, ps, pk1(par * 1024, par * 1024 + 1024))
    p.barrier()


def build_layer2(wshapes):
    nc = bass.Bass("TRN2", target_bir_lowering=False)
    kind_dbg = "ExternalOutput" if DEBUG_DUMP else "Internal"
    x_in = nc.dram_tensor("x_in", [T, D], F32, kind="ExternalInput").ap()
    x_out = nc.dram_tensor("x_out", [T, D], F32, kind="ExternalOutput").ap()
    s_scr2 = nc.dram_tensor("s_scr2", [T, D], BF16, kind=kind_dbg).ap()
    xnd = nc.dram_tensor("xnd", [T, D], BF16, kind=kind_dbg).ap()
    yed = nc.dram_tensor("yed", [128, 64, D], BF16, kind=kind_dbg).ap()
    rscr = nc.dram_tensor("rscr", [16, T], BF16, kind="Internal").ap()
    sodram = nc.dram_tensor("l2_sod", [16, 128, T], BF16, kind=kind_dbg).ap()
    idxd = nc.dram_tensor("l2_idxd", [128, 64], U32, kind="Internal").ap()
    W = {k: nc.dram_tensor(k, list(v), F32, kind="ExternalInput").ap()
         for k, v in wshapes.items()}
    env = Env(nc)
    make_consts(env)
    env.rscr = rscr
    env.l2_sodram = sodram
    env.l2_idxd = idxd
    env.xn_fm = env.sb("xn_fm", [128, NCH * T], BF16)
    emit_ln_phase(env, x_in, xn_tok_dump=xnd)
    p = env.p
    sst = env.sb("ss2_st", [128, 512], BF16)
    stok = env.sb("ss2_tk", [128, 512], BF16)

    def s_sink_tok(dc, qc, ap, rkeys):
        p.tt(sst[:, :], ap,
             env.xn_fm[:, dc * T + qc * 512: dc * T + qc * 512 + 512],
             ALU.add, reads=list(rkeys) + ["xn_fm", "ss2_st"],
             writes=["ss2_st"], tag="ss2_add")
        trps = env.PS1.ap().bitcast(BF16)[:, 0:1024]
        for j in range(4):
            p.trans(trps[:, j * 128:(j + 1) * 128],
                    sst[:, j * 128:(j + 1) * 128], env.id128[:, :],
                    reads=["ss2_st", "id128"], writes=pk1(0, 512),
                    tag="ss2_tr")
        p.copy(stok[:, :], trps[:, 0:512], reads=pk1(0, 512),
               writes=["ss2_tk"], tag="ss2_cp")
        for j in range(4):
            tt = qc * 4 + j
            p.dma("gpsimd", s_scr2.rearrange("(n p) d -> n p d", p=128)
                  [tt, :, dc * 128:(dc + 1) * 128],
                  stok[:, j * 128:(j + 1) * 128],
                  reads=["ss2_tk"], writes=["s2w"], tag="ss2_out")
    emit_l2_attn(env, W, s_sink_tok)
    gb = env.sb("l2f_g", [32, 2 * D], BF16)
    sb2 = env.sb("l2f_s", [32, 2 * D], BF16)
    xb = env.sb("l2f_x", [32, 2 * D], F32)
    xiv = x_in.rearrange("(n p) d -> n p d", p=32)
    xov = x_out.rearrange("(n p) d -> n p d", p=32)
    siv = s_scr2.rearrange("(n p) d -> n p d", p=32)

    def finale(g, ap, rkeys):
        par = g % 2
        p.dma("sync", sb2[:, par * D:(par + 1) * D], siv[g, :, :],
              reads=["s2w", f"l2f_s{par}"], writes=[f"l2f_s{par}"],
              tag="l2f_lds")
        p.dma("sync", xb[:, par * D:(par + 1) * D], xiv[g, :, :],
              reads=[f"l2f_x{par}"], writes=[f"l2f_x{par}"], tag="l2f_ldx")
        p.act(gb[:, par * D:(par + 1) * D], ap, AF.Silu,
              reads=list(rkeys) + [f"l2f_g{par}"], writes=[f"l2f_g{par}"],
              tag="l2f_silu")
        p.tt(gb[:, par * D:(par + 1) * D], gb[:, par * D:(par + 1) * D],
             sb2[:, par * D:(par + 1) * D], ALU.mult,
             reads=[f"l2f_g{par}", f"l2f_s{par}"], writes=[f"l2f_g{par}"],
             tag="l2f_sg")
        p.tt(xb[:, par * D:(par + 1) * D], xb[:, par * D:(par + 1) * D],
             gb[:, par * D:(par + 1) * D], ALU.add,
             reads=[f"l2f_g{par}", f"l2f_x{par}"], writes=[f"l2f_x{par}"],
             tag="l2f_add")
        p.dma("gpsimd", xov[g, :, :], xb[:, par * D:(par + 1) * D],
              reads=[f"l2f_x{par}"], writes=[f"l2f_x{par}"], tag="out_x")
    emit_l2_ffn(env, W, xnd, yed, finale)
    env.p.build()
    return nc


LAYER_BUILDERS[2] = (host_prep_l2, build_layer2)
def host_prep_l0(inp):
    w = {}
    for nm, key in (("Wq", "l0_wq"), ("Wk", "l0_wk"), ("Wv", "l0_wv")):
        w[key] = np.ascontiguousarray(
            np.asarray(inp[f"L0_attn_{nm}"])).reshape(NCH, 128, D)
    Wo = np.asarray(inp["L0_attn_Wo"]).reshape(16, 64, D)
    perm = np.zeros((16, 64, D), Wo.dtype)
    pr = np.empty((NCH, 2), np.int64)
    for h in range(16):
        g, hl = h // 4, h % 4
        ch, rh = 2 * g + (hl & 1), hl >> 1
        perm[2 * ch + rh] = Wo[h]
    w["l0_wo"] = np.ascontiguousarray(perm.reshape(D, D)).reshape(NCH, 128, D)
    w["l0_ab"] = np.ascontiguousarray(np.concatenate(
        [np.asarray(inp["L0_attn_alpha_raw"]).reshape(1, 16),
         np.asarray(inp["L0_attn_beta_raw"]).reshape(1, 16)], 1))
    DEPTH, PAR, NN = 7, 8, 255
    W_in = np.asarray(inp["L0_ffn_W_in"])
    b_in = np.asarray(inp["L0_ffn_b_in"])
    W_out = np.asarray(inp["L0_ffn_W_out"])
    def bitrev(j, nb):
        r = 0
        for i in range(nb):
            r = (r << 1) | ((j >> i) & 1)
        return r
    LOFF = [0, 1, 2, 3, 4, 5, 7, 11]
    NPAD = 19 * 128
    Wp = np.zeros((NPAD, D), np.float32)
    bp = np.zeros((NPAD,), np.float32)
    Wop = np.zeros((NPAD, D), np.float32)
    for d in range(8):
        for j in range(2 ** d):
            node = (2 ** d - 1) + bitrev(j, d)
            for par in range(PAR):
                src = par * NN + node
                dst = LOFF[d] * 128 + j * 8 + par
                Wp[dst] = W_in[src]
                bp[dst] = b_in[src]
                Wop[dst] = W_out[:, src]
    w["l0_winT"] = np.ascontiguousarray(Wp.T).reshape(NCH, 128, NPAD)
    w["l0_bin"] = np.ascontiguousarray(bp.reshape(19, 128).T)
    w["l0_woutT"] = np.ascontiguousarray(Wop).reshape(19, 128, D)
    return w


def l0_make_consts(env):
    sel2 = np.zeros((2, 128), np.float32)
    sel2[0, 0:64] = 1.0
    sel2[1, 64:128] = 1.0
    env.pairsel2 = env.load_const_sb("pairsel2", sel2.astype(bfdt))
    sl = np.tril(np.ones((128, 128), np.float32), -1)
    su = np.triu(np.ones((128, 128), np.float32), 1)
    eye = np.eye(128, dtype=np.float32)
    env.l0_sl4 = env.load_const_sb("l0_sl4", np.tile(sl, (1, 4)).astype(bfdt))
    env.l0_su4 = env.load_const_sb("l0_su4", np.tile(su, (1, 4)).astype(bfdt))
    env.l0_id4 = env.load_const_sb("l0_id4", np.tile(eye, (1, 4)).astype(bfdt))
    env.l0_tree = {}
    for d in range(7):
        nin = 8 * 2 ** d
        nout = 8 * 2 ** (d + 1)
        rows_in = min(128, nin)
        for ot in range((nout + 127) // 128):
            orow = min(128, nout - ot * 128)
            js = range((ot * 128) // 8, (ot * 128 + orow) // 8)
            pj0 = (js.start) % (2 ** d)
            st = (pj0 * 8) // 128
            selp = np.zeros((rows_in, orow), np.float32)
            selge = np.zeros((rows_in, orow), np.float32)
            sellt = np.zeros((rows_in, orow), np.float32)
            for j in js:
                b, pj = j >> d, j % (2 ** d)
                prow = pj * 8 - st * 128
                assert 0 <= prow < rows_in
                for par in range(8):
                    selp[prow + par, (j * 8 + par) - ot * 128] = 1.0
                    (selge if b else sellt)[prow + par,
                                            (j * 8 + par) - ot * 128] = 1.0
            env.l0_tree[(d, ot)] = (st, rows_in,
                                    env.load_const_sb(f"l0_selp{d}_{ot}",
                                                      selp.astype(bfdt)),
                                    env.load_const_sb(f"l0_selge{d}_{ot}",
                                                      selge.astype(bfdt)),
                                    env.load_const_sb(f"l0_sellt{d}_{ot}",
                                                      sellt.astype(bfdt)))


def emit_l0_attn(env, W, s_sink):
    p, nc = env.p, env.nc
    PS1, PS2 = env.PS1.ap(), env.PS2.ap()
    p.barrier()
    from contextlib import ExitStack as _ES
    with _ES() as _es:
        _e = _es.enter_context
        y_fm = _e(nc.sbuf_tensor("l0_y", [128, NCH * T], BF16))
        abrow = _e(nc.sbuf_tensor("l0_abrow", [1, 48], F32))
        abrep = _e(nc.sbuf_tensor("l0_abrep", [128, 32], F32))
        rnb4 = _e(nc.sbuf_tensor("l0_rnb", [1, 32], BF16))
        _es_grp = _ES()
        _eg = _es_grp.enter_context
        qf = _eg(nc.sbuf_tensor("l0_qf", [64, 4 * T], BF16))
        kaf = _eg(nc.sbuf_tensor("l0_kaf", [64, 4 * T], BF16))
        ktok = _eg(nc.sbuf_tensor("l0_kt", [128, 4 * 1024], BF16))
        vtok = _eg(nc.sbuf_tensor("l0_vt", [128, 4 * 1024], BF16))
        wst = _eg(nc.sbuf_tensor("l0_wst", [128, 3 * NCH * 256], BF16))
        ptk = _eg(nc.sbuf_tensor("l0_ptk", [128, 512], BF16))
        sqs = _eg(nc.sbuf_tensor("l0_sqs", [128, 512], BF16))
        nrm = _eg(nc.sbuf_tensor("l0_nrm", [128, 64], F32))
        rnh = _eg(nc.sbuf_tensor("l0_rnh", [128, 64], F32))
        tl_b = _eg(nc.sbuf_tensor("l0_tl", [128, 2048], BF16))
        x_b = _eg(nc.sbuf_tensor("l0_x", [128, 4 * 2048], BF16))
        xt_b = _eg(nc.sbuf_tensor("l0_xt", [128, 2048], BF16))
        pm_b = _eg(nc.sbuf_tensor("l0_pm", [128, 4 * 2048], BF16))
        hbuf = _eg(nc.sbuf_tensor("l0_h", [128, 512], BF16))
        cs_b = _eg(nc.sbuf_tensor("l0_cs", [128, 4 * 64], BF16))
        us_b = _eg(nc.sbuf_tensor("l0_us", [128, 4 * 64], BF16))
        m_b = _eg(nc.sbuf_tensor("l0_m", [64, 4 * 64], BF16))
        p.dma("sync", abrow[0:1, 0:32], W["l0_ab"], writes=["l0_abr"],
              tag="l0ab")
        p.act(abrow[0:1, 0:32], abrow[0:1, 0:32], AF.Sigmoid,
              reads=["l0_abr"], writes=["l0_abs"], tag="l0abs")
        p.ts(abrow[0:1, 0:32], abrow[0:1, 0:32], 0.5, ALU.mult,
             reads=["l0_abs"], writes=["l0_ab2"], tag="l0ab2")
        p.act(abrow[0:1, 32:48], abrow[0:1, 0:16], AF.Sqrt,
              reads=["l0_ab2"], writes=["l0_sqa"], tag="l0sqa")
        p.op("vector", lambda e: e.reciprocal(abrow[0:1, 0:16],
                                              abrow[0:1, 32:48]),
             reads=["l0_sqa"], writes=["l0_rsqa"], tag="l0rsq")
        p.tt(abrow[0:1, 16:32], abrow[0:1, 16:32], abrow[0:1, 0:16], ALU.mult,
             reads=["l0_ab2", "l0_rsqa"], writes=["l0_bp"], tag="l0bp")
        p.act(rnb4[0:1, 0:16], abrow[0:1, 32:48], AF.Copy,
              reads=["l0_sqa"], writes=["l0_sqab"], tag="l0sqcast")
        p.act(rnb4[0:1, 16:32], abrow[0:1, 16:32], AF.Copy,
              reads=["l0_bp"], writes=["l0_bpb"], tag="l0bpcast")
        p.mm(PS2[:, 1536:1552], env.ones_row[:, :], rnb4[0:1, 0:16],
             reads=["l0_sqab", "ones_row"], writes=pk2(1536, 2048),
             tag="l0sqarep")
        p.act(abrep[:, 0:16], PS2[:, 1536:1552], AF.Copy,
              reads=pk2(1536, 2048), writes=["l0_sqarep"], tag="l0sqev")
        p.mm(PS2[:, 1536:1552], env.ones_row[:, :], rnb4[0:1, 16:32],
             reads=["l0_bpb", "ones_row"], writes=pk2(1536, 2048),
             tag="l0bprep")
        p.act(abrep[:, 16:32], PS2[:, 1536:1552], AF.Copy,
              reads=pk2(1536, 2048), writes=["l0_bprep"], tag="l0bpev")
        for g in range(4):
            p.barrier()
            for i, nm in enumerate(("l0_wq", "l0_wk", "l0_wv")):
                env.wload(wst[:, i * 2048:(i + 1) * 2048].rearrange(
                              "p (c o) -> p c o", c=NCH),
                          W[nm][:, :, g * 256:(g + 1) * 256].rearrange(
                              "c p o -> p c o"), "l0_wst")
            for tt in range(NT):
                pr = tt % 2
                psq = PS1[:, pr * 256: pr * 256 + 256]
                for kc in range(NCH):
                    p.mm(psq, env.xn_fm[:, kc * T + tt * 128: kc * T + (tt + 1) * 128],
                         wst[:, 0 * 2048 + kc * 256: 0 * 2048 + (kc + 1) * 256],
                         start=(kc == 0), stop=(kc == NCH - 1),
                         reads=["l0_wst", "xn_fm"], writes=[f"PA0{pr}"],
                         tag="l0pq")
                psk = PS1[:, 512 + pr * 256: 512 + pr * 256 + 256]
                for kc in range(NCH):
                    p.mm(psk, env.xn_fm[:, kc * T + tt * 128: kc * T + (tt + 1) * 128],
                         wst[:, 1 * 2048 + kc * 256: 1 * 2048 + (kc + 1) * 256],
                         start=(kc == 0), stop=(kc == NCH - 1),
                         reads=["l0_wst", "xn_fm"], writes=[f"PA1{pr}"],
                         tag="l0pk")
                psv = PS1[:, 1024 + pr * 256: 1024 + pr * 256 + 256]
                for kc in range(NCH):
                    p.mm(psv, env.xn_fm[:, kc * T + tt * 128: kc * T + (tt + 1) * 128],
                         wst[:, 2 * 2048 + kc * 256: 2 * 2048 + (kc + 1) * 256],
                         start=(kc == 0), stop=(kc == NCH - 1),
                         reads=["l0_wst", "xn_fm"], writes=[f"PA2{pr}"],
                         tag="l0pv")
                p.act(vtok[:, :].rearrange("p (h c o) -> p h c o", h=4, c=NT)
                      [:, :, tt, :],
                      psv.rearrange("p (h o) -> p h o", h=4),
                      AF.Copy, reads=[f"PA2{pr}"], writes=["l0_vt"],
                      tag="l0vev")
                ptkh = ptk[:, pr * 256: pr * 256 + 256]
                sqsh = sqs[:, pr * 256: pr * 256 + 256]
                p.act(ptkh, psk, AF.Copy, reads=[f"PA1{pr}", f"l0_ptk{pr}"],
                      writes=[f"l0_ptk{pr}"], tag="l0kev")
                p.act(sqsh, ptkh, AF.Square, reads=[f"l0_ptk{pr}", f"l0_sqs{pr}"],
                      writes=[f"l0_sqs{pr}"], tag="l0ksq")
                p.red(nrm[:, tt * 4:(tt + 1) * 4],
                      sqsh.rearrange("p (h o) -> p h o", h=4), ALU.add,
                      reads=[f"l0_sqs{pr}"], writes=["l0_nrm"], tag="l0knr")
                p.copy(ktok[:, :].rearrange("p (h c o) -> p h c o", h=4, c=NT)
                       [:, :, tt, :],
                       ptkh.rearrange("p (h o) -> p h o", h=4),
                       reads=[f"l0_ptk{pr}"], writes=["l0_ktraw"], tag="l0kst")
                for hl in range(4):
                    p.trans(PS1.bitcast(BF16)[0:64, 3072 + pr * 512 + hl * 128:
                                              3072 + pr * 512 + hl * 128 + 128],
                            ptkh[:, hl * 64:(hl + 1) * 64].bitcast(BF16)
                            if False else ptkh[:, hl * 64:(hl + 1) * 64],
                            env.id128[:, :],
                            reads=[f"l0_ptk{pr}", "id128"],
                            writes=[f"PA3{pr}"], tag="l0qtr") if False else None
                p.act(ptkh, psq, AF.Copy, reads=[f"PA0{pr}", f"l0_ptk{pr}",
                                                 "l0_ktraw"],
                      writes=[f"l0_ptk{pr}"], tag="l0qev")
                for hl in range(4):
                    p.trans(PS1.bitcast(BF16)[0:64, 3072 + pr * 512 + hl * 128:
                                              3072 + pr * 512 + hl * 128 + 128],
                            ptkh[:, hl * 64:(hl + 1) * 64],
                            env.id128[:, :],
                            reads=[f"l0_ptk{pr}", "id128"],
                            writes=[f"PA3{pr}"], tag="l0qtr")
                p.copy(qf[0:64, :].rearrange("p (h c o) -> p h c o", h=4, c=NT)
                       [:, :, tt, :],
                       PS1.bitcast(BF16)[0:64, 3072 + pr * 512:
                                         3072 + pr * 512 + 512].rearrange(
                           "p (h o) -> p h o", h=4),
                       reads=[f"PA3{pr}"], writes=["l0_qf"], tag="l0qtc")
            p.act(rnh[:, :], nrm[:, :], AF.Sqrt, reads=["l0_nrm"],
                  writes=["l0_rh1"], tag="l0rh1")
            p.ts(rnh[:, :], rnh[:, :], 1e-12, ALU.max, reads=["l0_rh1"],
                 writes=["l0_rh2"], tag="l0rh2")
            p.act(rnh[:, :], rnh[:, :], AF.Ln, reads=["l0_rh2"],
                  writes=["l0_rh3"], tag="l0rh3")
            p.act(rnh[:, :], rnh[:, :], AF.Exp, scale=-1.0, reads=["l0_rh3"],
                  writes=["l0_rh4"], tag="l0rh4")
            for hl in range(4):
                h = 4 * g + hl
                p.ts(rnh[:, :].rearrange("p (c h) -> p c h", c=NT)[:, :, hl],
                     rnh[:, :].rearrange("p (c h) -> p c h", c=NT)[:, :, hl],
                     abrep[:, h:h + 1], ALU.mult,
                     reads=["l0_rh4", "l0_sqarep"], writes=["l0_rh"],
                     tag="l0rh5")
            bpv = bass.AP(abrep, 16 + 4 * g, [[32, 128], [1, 4], [0, 1024]])
            p.tt(vtok[:, :].rearrange("p (h co) -> p h co", h=4),
                 vtok[:, :].rearrange("p (h co) -> p h co", h=4), bpv, ALU.mult,
                 reads=["l0_vt", "l0_bprep"], writes=["l0_vt"], tag="l0vbp")
            for tt in range(NT):
                pr = tt % 2
                ktv = ktok[:, :].rearrange("p (h c o) -> p h c o", h=4, c=NT)
                rnv = bass.AP(rnh, tt * 4,
                              [[64, 128], [1, 4], [0, 64]])
                p.tt(ktv[:, :, tt, :], ktv[:, :, tt, :], rnv, ALU.mult,
                     reads=["l0_ktraw", "l0_rh", "l0_kt"], writes=["l0_kt"],
                     tag="l0kap")
                for hl in range(4):
                    p.trans(PS1.bitcast(BF16)[0:64, 3072 + pr * 512 + hl * 128:
                                              3072 + pr * 512 + hl * 128 + 128],
                            ktv[:, hl, tt, :], env.id128[:, :],
                            reads=["l0_kt", "id128"], writes=[f"PA3{pr}"],
                            tag="l0katr")
                p.copy(kaf[0:64, :].rearrange("p (h c o) -> p h c o", h=4, c=NT)
                       [:, :, tt, :],
                       PS1.bitcast(BF16)[0:64, 3072 + pr * 512:
                                         3072 + pr * 512 + 512].rearrange(
                           "p (h o) -> p h o", h=4),
                       reads=[f"PA3{pr}"], writes=["l0_kaf"], tag="l0katc")
            p.barrier()
            for hl in range(4):
                kfh = kaf[0:64, hl * T:(hl + 1) * T]
                qfh = qf[0:64, hl * T:(hl + 1) * T]
                for cg in range(4):
                    ps = PS1[:, (cg % 2) * 512:(cg % 2) * 512 + 512]
                    pky = pk1((cg % 2) * 512, (cg % 2) * 512 + 512)
                    for j in range(4):
                        c = 4 * cg + j
                        p.mm(ps[:, j * 128:(j + 1) * 128],
                             kfh[:, c * 128:(c + 1) * 128],
                             kfh[:, c * 128:(c + 1) * 128],
                             reads=["l0_kaf"], writes=pky, tag="l0gram")
                    p.tt(hbuf[:, :], ps, env.l0_sl4[:, :], ALU.mult,
                         reads=pky + ["l0_sl4"], writes=["l0_hb"], tag="l0msl")
                    p.tt(tl_b[:, cg * 512:(cg + 1) * 512],
                         hbuf[:, :], env.l0_id4[:, :], ALU.add, engine="gpsimd",
                         reads=["l0_hb", "l0_id4"], writes=["l0_tl"], tag="l0tl")
                    p.stt(xt_b[:, cg * 512:(cg + 1) * 512],
                          hbuf[:, :], -1.0, env.l0_id4[:, :], ALU.mult, ALU.add,
                          reads=["l0_hb", "l0_id4"], writes=["l0_xt"],
                          tag="l0x0t")
                    p.tt(hbuf[:, :], ps, env.l0_su4[:, :], ALU.mult,
                         reads=pky + ["l0_su4"], writes=["l0_hb"], tag="l0msu")
                    p.stt(x_b[:, hl * 2048 + cg * 512: hl * 2048 + (cg + 1) * 512],
                          hbuf[:, :], -1.0, env.l0_id4[:, :], ALU.mult, ALU.add,
                          reads=["l0_hb", "l0_id4"],
                          writes=["l0_x"], tag="l0x0")
                    for j in range(4):
                        c = 4 * cg + j
                        p.mm(ps[:, j * 128:(j + 1) * 128],
                             kfh[:, c * 128:(c + 1) * 128],
                             qfh[:, c * 128:(c + 1) * 128],
                             reads=["l0_kaf", "l0_qf"], writes=pky, tag="l0pq2")
                    p.tt(pm_b[:, hl * 2048 + cg * 512: hl * 2048 + (cg + 1) * 512],
                         ps, env.l0_su4[:, :], ALU.mult,
                         reads=pky + ["l0_su4"], writes=["l0_pm"], tag="l0pmm")
                for it in range(2):
                    for cg in range(4):
                        psH = PS1[:, (cg % 2) * 512:(cg % 2) * 512 + 512]
                        pkH = pk1((cg % 2) * 512, (cg % 2) * 512 + 512)
                        base = hl * 2048 + cg * 512
                        tb = cg * 512
                        for j in range(4):
                            p.mm(psH[:, j * 128:(j + 1) * 128],
                                 tl_b[:, tb + j * 128: tb + (j + 1) * 128],
                                 x_b[:, base + j * 128: base + (j + 1) * 128],
                                 reads=["l0_tl", "l0_x"], writes=pkH,
                                 tag="l0nH")
                        p.act(hbuf[:, :], psH, AF.Copy, reads=pkH,
                              writes=["l0_hb"], tag="l0nHe")
                        psX = PS1[:, 1024 + (cg % 2) * 512:
                                  1024 + (cg % 2) * 512 + 512]
                        pkX = pk1(1024 + (cg % 2) * 512, 1536 + (cg % 2) * 512)
                        for j in range(4):
                            p.mm(psX[:, j * 128:(j + 1) * 128],
                                 xt_b[:, tb + j * 128: tb + (j + 1) * 128],
                                 hbuf[:, j * 128:(j + 1) * 128],
                                 reads=["l0_xt", "l0_hb"], writes=pkX,
                                 tag="l0nXH")
                        p.stt(x_b[:, base:base + 512], x_b[:, base:base + 512],
                              2.0, psX, ALU.mult, ALU.subtract,
                              reads=pkX + ["l0_x"], writes=["l0_x"], tag="l0nX")
                        if it < 1:
                            psT = PS1.bitcast(BF16)[:, (cg % 2) * 1024:
                                                    (cg % 2) * 1024 + 1024]
                            pkT = pk1((cg % 2) * 512, (cg % 2) * 512 + 512)
                            for j in range(4):
                                p.trans(psT[:, j * 256:j * 256 + 128],
                                        x_b[:, base + j * 128:
                                            base + (j + 1) * 128],
                                        env.id128[:, :],
                                        reads=["l0_x", "id128"], writes=pkT,
                                        tag="l0nT")
                            p.copy(xt_b[:, tb:tb + 512].rearrange(
                                       "p (j o) -> p j o", j=4),
                                   psT[:, 0:1024].rearrange(
                                       "p (j o) -> p j o", j=4)[:, :, 0:128],
                                   reads=pkT, writes=["l0_xt"], tag="l0nTe")
            for c in range(NT):
                for hl in range(4):
                    ch = 2 * g + (hl & 1)
                    rh = (hl >> 1) * 64
                    kfh = kaf[0:64, hl * T + c * 128: hl * T + (c + 1) * 128]
                    qfh = qf[0:64, hl * T + c * 128: hl * T + (c + 1) * 128]
                    vslice = vtok[:, (hl * NT + c) * 64:(hl * NT + c + 1) * 64]
                    kslice = ktok[:, (hl * NT + c) * 64:(hl * NT + c + 1) * 64]
                    PSB, c0, pkB = ((PS2, 1024, ["PB2"]) if hl == 0 else
                                    (PS2, 1536, ["PB3"]) if hl == 1 else
                                    (PS1, 1024, ["PA2"]) if hl == 2 else
                                    (PS1, 1536, ["PA3"]))
                    psC = PSB[:, c0: c0 + 64]
                    if c > 0:
                        p.mm(psC, kfh, m_b[0:64, hl * 64:(hl + 1) * 64],
                             reads=["l0_kaf", f"l0_m{hl}"], writes=pkB,
                             tag="l0C2")
                        p.tt(cs_b[:, hl * 64:(hl + 1) * 64], vslice, psC,
                             ALU.subtract, reads=pkB + ["l0_vt"],
                             writes=[f"l0_cs{hl}"], tag="l0Ce")
                    psU = PSB[:, c0 + 64: c0 + 128]
                    p.mm(psU, x_b[:, hl * 2048 + c * 128: hl * 2048 + (c + 1) * 128],
                         vslice if c == 0 else cs_b[:, hl * 64:(hl + 1) * 64],
                         reads=["l0_x", "l0_vt", f"l0_cs{hl}"], writes=pkB,
                         tag="l0U")
                    p.act(us_b[:, hl * 64:(hl + 1) * 64], psU, AF.Copy,
                          reads=pkB + [f"l0_us{hl}"], writes=[f"l0_us{hl}"],
                          tag="l0Ue")
                    psY = PSB[rh:rh + 64, c0 + 128: c0 + 256]
                    if c > 0:
                        p.mm(psY, m_b[0:64, hl * 64:(hl + 1) * 64], qfh,
                             start=True, stop=False,
                             reads=[f"l0_m{hl}", "l0_qf"], writes=pkB,
                             tag="l0Y1")
                    p.mm(psY, us_b[:, hl * 64:(hl + 1) * 64],
                         pm_b[:, hl * 2048 + c * 128: hl * 2048 + (c + 1) * 128],
                         start=(c == 0), stop=True,
                         reads=[f"l0_us{hl}", "l0_pm"], writes=pkB, tag="l0Y2")
                    p.copy(y_fm[rh:rh + 64, ch * T + c * 128: ch * T + (c + 1) * 128],
                           psY, reads=pkB + ["l0_yf"], writes=["l0_yf"],
                           tag="l0Ye")
                    if hl < 2:
                        psM = PS2[0:64, hl * 512: hl * 512 + 64]
                        pkM = [f"PB{hl}"]
                    else:
                        psM = PS1[0:64, (hl - 2) * 512: (hl - 2) * 512 + 64]
                        pkM = [f"PA{hl - 2}"]
                    p.mm(psM, kslice, us_b[:, hl * 64:(hl + 1) * 64],
                         start=(c == 0), stop=(c == NT - 1),
                         reads=["l0_kt", f"l0_us{hl}"], writes=pkM, tag="l0M")
                    if c < NT - 1:
                        p.act(m_b[0:64, hl * 64:(hl + 1) * 64], psM, AF.Copy,
                              reads=pkM + [f"l0_m{hl}"], writes=[f"l0_m{hl}"],
                              tag="l0Me")
        if DEBUG_DUMP:
            for nm_, tns, sh in (("dbg_y", y_fm, [128, NCH * T]),
                                 ("dbg_qf", qf, [64, 4 * T]),
                                 ("dbg_kaf", kaf, [64, 4 * T]),
                                 ("dbg_ktok", ktok, [128, 4096]),
                                 ("dbg_vtok", vtok, [128, 4096]),
                                 ("dbg_xb", x_b, [128, 4 * 2048]),
                                 ("dbg_pm", pm_b, [128, 4 * 2048])):
                dd_ = nc.dram_tensor(nm_, sh, BF16, kind="ExternalOutput").ap()
                p.dma("gpsimd", dd_, tns[0:sh[0], :],
                      reads=["l0_yf", "l0_qf", "l0_kaf", "l0_kt", "l0_vt",
                             "l0_x", "l0_pm"], tag=nm_)
        _es_grp.close()
        p.barrier()
        wo_sb = _e(nc.sbuf_tensor("l0_wo_sb", [128, NCH * D], BF16))
        env.wload(wo_sb[:, :].rearrange("p (c o) -> p c o", c=NCH),
                  W["l0_wo"].rearrange("c p o -> p c o"), "l0_wo")
        for dc in range(NCH):
            for qc in range(4):
                c0 = (qc % 2) * 512
                ps = PS2[:, c0:c0 + 512]
                for kc in range(NCH):
                    p.mm(ps, wo_sb[:, kc * D + dc * 128: kc * D + (dc + 1) * 128],
                         y_fm[:, kc * T + qc * 512: kc * T + qc * 512 + 512],
                         start=(kc == 0), stop=(kc == NCH - 1),
                         reads=["l0_wo", "l0_yf"], writes=pk2(c0, c0 + 512),
                         tag="l0omm")
                s_sink(dc, qc, ps, pk2(c0, c0 + 512))
    p.barrier()


def emit_l0_ffn(env, W, g_sink):
    p, nc = env.p, env.nc
    PS1, PS2 = env.PS1.ap(), env.PS2.ap()
    p.barrier()
    LOFF = [0, 1, 2, 3, 4, 5, 7, 11, 19]
    from contextlib import ExitStack as _ES
    with _ES() as _es:
        _e = _es.enter_context
        acts = _e(nc.sbuf_tensor("l0f_acts", [128, 19 * 512], BF16))
        Fb = _e(nc.sbuf_tensor("l0f_F", [128, 19 * 512], BF16))
        win = _e(nc.sbuf_tensor("l0f_win", [128, 2 * NCH * 128], BF16))
        wout = _e(nc.sbuf_tensor("l0f_wout", [128, 2 * 19 * 128], BF16))
        binc = _e(nc.sbuf_tensor("l0f_bin", [128, 19], F32))
        dd = _e(nc.sbuf_tensor("l0f_dd", [128, 512], BF16))
        ge_sb = _e(nc.sbuf_tensor("l0f_ge", [128, 8 * 512], BF16))
        lt_sb = _e(nc.sbuf_tensor("l0f_lt", [128, 8 * 512], BF16))
        p.dma("sync", binc[:, :], W["l0_bin"], writes=["l0f_bin"], tag="l0fb")
        for tb in range(4):
            c0 = tb * 512
            p.memset(Fb[0:8, 0:512], 1.0, reads=["l0f_F"], writes=["l0f_F"],
                     tag="l0fF0")
            for lt in range(19):
                env.wload(win[:, (lt % 2) * 1024:(lt % 2) * 1024 + 1024]
                          .rearrange("p (c o) -> p c o", c=NCH),
                          W["l0_winT"][:, :, lt * 128:(lt + 1) * 128]
                          .rearrange("c p o -> p c o"), f"l0f_win{lt % 2}")
                ps = PS1[:, (lt % 2) * 512:(lt % 2) * 512 + 512]
                pky = pk1((lt % 2) * 512, (lt % 2) * 512 + 512)
                for kc in range(NCH):
                    p.mm(ps, win[:, (lt % 2) * 1024 + kc * 128:
                                 (lt % 2) * 1024 + (kc + 1) * 128],
                         env.xn_fm[:, kc * T + c0: kc * T + c0 + 512],
                         start=(kc == 0), stop=(kc == NCH - 1),
                         reads=[f"l0f_win{lt % 2}", "xn_fm"], writes=pky,
                         tag="l0flmm")
                p.act(acts[:, lt * 512:(lt + 1) * 512], ps, AF.Silu,
                      bias=binc[:, lt:lt + 1], reads=pky + ["l0f_bin"],
                      writes=["l0f_acts"], tag="l0flev")
            for d in range(7):
                for st in range(max(1, (8 * 2 ** d) // 128)):
                    lt = LOFF[d] + st
                    nrow = min(128, 8 * 2 ** d)
                    p.ts(ge_sb[0:nrow, st * 512:(st + 1) * 512],
                         acts[0:nrow, lt * 512:(lt + 1) * 512], 0.0, ALU.is_ge,
                         reads=["l0f_acts"], writes=["l0f_ge"], tag="l0fge")
                    p.ts(lt_sb[0:nrow, st * 512:(st + 1) * 512],
                         acts[0:nrow, lt * 512:(lt + 1) * 512], 0.0, ALU.is_lt,
                         reads=["l0f_acts"], writes=["l0f_lt"], tag="l0flt")
                nin = 8 * 2 ** d
                nout = 8 * 2 ** (d + 1)
                for ot in range((nout + 127) // 128):
                    orow = min(128, nout - ot * 128)
                    psP = PS1[:, 1024:1536]
                    psD = PS1[:, 1536:2048]
                    st, rows, selp, selg, sell = env.l0_tree[(d, ot)]
                    p.mm(psP[0:orow, 0:512], selp[0:rows, 0:orow],
                         Fb[0:rows, (LOFF[d] + st) * 512:
                            (LOFF[d] + st + 1) * 512],
                         reads=["l0f_F", f"l0_selp{d}_{ot}"],
                         writes=pk1(1024, 1536), tag="l0fP")
                    p.mm(psD[0:orow, 0:512], selg[0:rows, 0:orow],
                         ge_sb[0:rows, st * 512:(st + 1) * 512],
                         start=True, stop=False,
                         reads=["l0f_ge", f"l0_selge{d}_{ot}"],
                         writes=pk1(1536, 2048), tag="l0fDg")
                    p.mm(psD[0:orow, 0:512], sell[0:rows, 0:orow],
                         lt_sb[0:rows, st * 512:(st + 1) * 512],
                         start=False, stop=True,
                         reads=["l0f_lt", f"l0_sellt{d}_{ot}"],
                         writes=pk1(1536, 2048), tag="l0fDl")
                    p.act(dd[0:orow, 0:512], psD[0:orow, 0:512], AF.Copy,
                          reads=pk1(1536, 2048), writes=["l0f_dd"], tag="l0fDe")
                    p.tt(Fb[0:orow, (LOFF[d + 1] + ot) * 512:
                            (LOFF[d + 1] + ot + 1) * 512],
                         psP[0:orow, 0:512], dd[0:orow, 0:512], ALU.mult,
                         reads=pk1(1024, 1536) + ["l0f_dd", "l0f_F"],
                         writes=["l0f_F"], tag="l0fFm")
            for lt in range(19):
                p.tt(acts[:, lt * 512:(lt + 1) * 512],
                     acts[:, lt * 512:(lt + 1) * 512],
                     Fb[:, lt * 512:(lt + 1) * 512], ALU.mult,
                     reads=["l0f_F", "l0f_acts"], writes=["l0f_acts"],
                     tag="l0fmask")
            for dc in range(NCH):
                env.wload(wout[:, (dc % 2) * 19 * 128:
                               (dc % 2) * 19 * 128 + 19 * 128]
                          .rearrange("p (l o) -> p l o", l=19),
                          W["l0_woutT"][:, :, dc * 128:(dc + 1) * 128]
                          .rearrange("l p o -> p l o"), f"l0f_wo{dc % 2}")
                ps = PS2[:, (dc % 2) * 512:(dc % 2) * 512 + 512]
                pky = pk2((dc % 2) * 512, (dc % 2) * 512 + 512)
                for lt in range(19):
                    p.mm(ps, wout[:, (dc % 2) * 19 * 128 + lt * 128:
                                  (dc % 2) * 19 * 128 + (lt + 1) * 128],
                         acts[:, lt * 512:(lt + 1) * 512],
                         start=(lt == 0), stop=(lt == 18),
                         reads=[f"l0f_wo{dc % 2}", "l0f_acts"], writes=pky,
                         tag="l0fomm")
                g_sink(dc, tb, ps, pky)
    p.barrier()


def build_layer0(wshapes):
    nc = bass.Bass("TRN2", target_bir_lowering=False)
    kind_dbg = "ExternalOutput" if DEBUG_DUMP else "Internal"
    x_in = nc.dram_tensor("x_in", [T, D], F32, kind="ExternalInput").ap()
    x_out = nc.dram_tensor("x_out", [T, D], F32, kind="ExternalOutput").ap()
    s_scr = nc.dram_tensor("s_scr", [NCH, 128, T], BF16, kind=kind_dbg).ap()
    rscr = nc.dram_tensor("rscr", [16, T], BF16, kind="Internal").ap()
    W = {k: nc.dram_tensor(k, list(v), F32, kind="ExternalInput").ap()
         for k, v in wshapes.items()}
    env = Env(nc)
    make_consts(env)
    l0_make_consts(env)
    env.rscr = rscr
    env.xn_fm = env.sb("xn_fm", [128, NCH * T], BF16)
    emit_ln_phase(env, x_in)
    if DEBUG_DUMP:
        dxn = nc.dram_tensor("dbg_xn", [128, NCH * T], BF16,
                             kind="ExternalOutput").ap()
        env.p.dma("gpsimd", dxn, env.xn_fm[:, :], reads=["xn_fm"], tag="dxn")
    emit_l0_attn(env, W, make_s_sink(env, s_scr, mode="plain"))
    g_sink = make_residual_streamer(env, x_in, x_out, s_scr, blk=512)
    emit_l0_ffn(env, W, g_sink)
    env.p.build()
    return nc


LAYER_BUILDERS[0] = (host_prep_l0, build_layer0)


def _dedupe_names(nc):
    seen = {}

    def uniq(name):
        n = seen.get(name, 0)
        seen[name] = n + 1
        return name if n == 0 else f"{name}__{n}"
    orig_sb = nc.sbuf_tensor
    nc.sbuf_tensor = lambda name, *a, **kw: orig_sb(uniq(name), *a, **kw)
    orig_asb = nc.alloc_sbuf_tensor
    nc.alloc_sbuf_tensor = lambda name, *a, **kw: orig_asb(uniq(name), *a, **kw)
    orig_it = nc.inline_tensor
    nc.inline_tensor = lambda arr, name=None, **kw: orig_it(
        arr, name=uniq(name), **kw)


def build_fused(wshapes):
    from contextlib import ExitStack
    nc = bass.Bass("TRN2", target_bir_lowering=False)
    _dedupe_names(nc)
    x_in = nc.dram_tensor("x_in", [T, D], F32, kind="ExternalInput").ap()
    x_out = nc.dram_tensor("x_out", [T, D], F32, kind="ExternalOutput").ap()
    x1 = nc.dram_tensor("x_b1", [T, D], F32, kind="Internal").ap()
    x2 = nc.dram_tensor("x_b2", [T, D], F32, kind="Internal").ap()
    x3 = nc.dram_tensor("x_b3", [T, D], F32, kind="Internal").ap()
    s_scr = nc.dram_tensor("s_scr", [NCH, 128, T], BF16, kind="Internal").ap()
    rscr = nc.dram_tensor("rscr", [16, T], BF16, kind="Internal").ap()
    s_scr2 = nc.dram_tensor("s_scr2", [T, D], BF16, kind="Internal").ap()
    xnd = nc.dram_tensor("xnd", [T, D], BF16, kind="Internal").ap()
    yed = nc.dram_tensor("yed", [128, 64, D], BF16, kind="Internal").ap()
    sodram = nc.dram_tensor("l2_sod", [16, 128, T], BF16, kind="Internal").ap()
    idxd = nc.dram_tensor("l2_idxd", [128, 64], U32, kind="Internal").ap()
    W = {k: nc.dram_tensor(k, list(v), F32, kind="ExternalInput").ap()
         for k, v in wshapes.items()}
    env = Env(nc)
    make_consts(env)
    env.rscr = rscr
    env.l2_sodram = sodram
    env.l2_idxd = idxd
    env.xn_fm = env.sb("xn_fm", [128, NCH * T], BF16)
    p = env.p

    env.scope = ExitStack()
    p.set_scope("L0ln")
    l0_make_consts(env)
    emit_ln_phase(env, x_in)
    p.set_scope("L0attn")
    emit_l0_attn(env, W, make_s_sink(env, s_scr, mode="plain"))
    p.set_scope("L0ffn")
    emit_l0_ffn(env, W, make_residual_streamer(env, x_in, x1, s_scr, blk=512))
    p.barrier()
    env.scope.close()
    env.scope = None

    env.scope = ExitStack()
    p.set_scope("L1ln")
    make_consts_l1(env)
    env.scr15 = env.sb("scr15", [15, 128], BF16)
    emit_ln_phase(env, x1)
    p.set_scope("L1attn")
    emit_l1_attn(env, W, make_s_sink(env, s_scr, mode="add_xn"))
    p.set_scope("L1ffn")
    emit_l1_ffn(env, W, make_residual_streamer(env, x1, x2, s_scr))
    p.barrier()
    env.scope.close()
    env.scope = None

    env.scope = ExitStack()
    p.set_scope("L2ln")
    emit_ln_phase(env, x2, xn_tok_dump=xnd)
    sst2 = env.sb("ss2_st", [128, 512], BF16)
    stok = env.sb("ss2_tk", [128, 512], BF16)

    def s_sink_tok(dc, qc, ap, rkeys):
        p.tt(sst2[:, :], ap,
             env.xn_fm[:, dc * T + qc * 512: dc * T + qc * 512 + 512],
             ALU.add, reads=list(rkeys) + ["xn_fm", "ss2_st"],
             writes=["ss2_st"], tag="ss2_add")
        trps = env.PS1.ap().bitcast(BF16)[:, 0:1024]
        for j in range(4):
            p.trans(trps[:, j * 128:(j + 1) * 128],
                    sst2[:, j * 128:(j + 1) * 128], env.id128[:, :],
                    reads=["ss2_st", "id128"], writes=pk1(0, 512),
                    tag="ss2_tr")
        p.copy(stok[:, :], trps[:, 0:512], reads=pk1(0, 512),
               writes=["ss2_tk"], tag="ss2_cp")
        for j in range(4):
            tt = qc * 4 + j
            p.dma("gpsimd", s_scr2.rearrange("(n p) d -> n p d", p=128)
                  [tt, :, dc * 128:(dc + 1) * 128],
                  stok[:, j * 128:(j + 1) * 128],
                  reads=["ss2_tk"], writes=["s2w"], tag="ss2_out")
    p.set_scope("L2attn")
    emit_l2_attn(env, W, s_sink_tok)
    gb2 = env.sb("l2f_g", [32, 2 * D], BF16)
    sb2 = env.sb("l2f_s", [32, 2 * D], BF16)
    xb2 = env.sb("l2f_x", [32, 2 * D], F32)
    xiv2 = x2.rearrange("(n p) d -> n p d", p=32)
    xov2 = x3.rearrange("(n p) d -> n p d", p=32)
    siv2 = s_scr2.rearrange("(n p) d -> n p d", p=32)

    def finale(g, ap, rkeys):
        par = g % 2
        p.dma("sync", sb2[:, par * D:(par + 1) * D], siv2[g, :, :],
              reads=["s2w", f"l2f_s{par}"], writes=[f"l2f_s{par}"],
              tag="l2f_lds")
        p.dma("sync", xb2[:, par * D:(par + 1) * D], xiv2[g, :, :],
              reads=[f"l2f_x{par}"], writes=[f"l2f_x{par}"], tag="l2f_ldx")
        p.act(gb2[:, par * D:(par + 1) * D], ap, AF.Silu,
              reads=list(rkeys) + [f"l2f_g{par}"], writes=[f"l2f_g{par}"],
              tag="l2f_silu")
        p.tt(gb2[:, par * D:(par + 1) * D], gb2[:, par * D:(par + 1) * D],
             sb2[:, par * D:(par + 1) * D], ALU.mult,
             reads=[f"l2f_g{par}", f"l2f_s{par}"], writes=[f"l2f_g{par}"],
             tag="l2f_sg")
        p.tt(xb2[:, par * D:(par + 1) * D], xb2[:, par * D:(par + 1) * D],
             gb2[:, par * D:(par + 1) * D], ALU.add,
             reads=[f"l2f_g{par}", f"l2f_x{par}"], writes=[f"l2f_x{par}"],
             tag="l2f_add")
        p.dma("gpsimd", xov2[g, :, :], xb2[:, par * D:(par + 1) * D],
              reads=[f"l2f_x{par}"], writes=[f"l2f_x{par}"], tag="out_x")
    p.set_scope("L2ffn")
    emit_l2_ffn(env, W, xnd, yed, finale)
    p.barrier()
    env.scope.close()
    env.scope = None

    env.scope = ExitStack()
    p.set_scope("L3ln")
    l3_make_consts(env)
    emit_ln_phase(env, x3)
    p.set_scope("L3attn")
    emit_l3_attn(env, W, make_s_sink(env, s_scr, mode="plain"))
    p.set_scope("L3ffn")
    emit_l3_ffn(env, W, make_residual_streamer(env, x3, x_out, s_scr, blk=256,
                                               trps_region=(0, 512)))
    p.set_scope(None)
    p.build()
    env.scope = None
    return nc


_FUSED_CACHE = {}


def kernel(**inputs):
    x = np.ascontiguousarray(np.asarray(inputs["L0_attn_x"], dtype=np.float32))
    B = x.shape[0]
    w = {}
    for li in (0, 1, 2, 3):
        w.update(LAYER_BUILDERS[li][0](inputs))
    wshapes = {k: v.shape for k, v in w.items()}
    key = tuple(sorted(wshapes.items()))
    if key not in _FUSED_CACHE:
        _FUSED_CACHE[key] = build_fused(wshapes)
    nc = _FUSED_CACHE[key]
    in_maps = []
    for c in range(B):
        m = {"x_in": np.ascontiguousarray(x[c], np.float32)}
        for k, v in w.items():
            m[k] = np.ascontiguousarray(v, np.float32)
        in_maps.append(m)
    res = run_bass_kernel_spmd(nc, in_maps, core_ids=list(range(B)))
    return np.stack([np.asarray(r["x_out"]) for r in res.results]).astype(
        np.float32)
